# version 07 (final)
# baseline (speedup 1.0000x reference)
import numpy as np
import ml_dtypes
import concourse.bass as bass
import concourse.mybir as mybir
from concourse.bass_utils import run_bass_kernel_spmd
from concourse.bass_core_rust import IndirectOffsetOnAxis

F32 = mybir.dt.float32
BF16 = mybir.dt.bfloat16
I32 = mybir.dt.int32
AF = mybir.ActivationFunctionType
ALU = mybir.AluOpType
AX = mybir.AxisListType
BF = ml_dtypes.bfloat16

NCORE = 8
T = 2048
D = 1024
NSEQ = 4096
MASK_C = float(2 ** 18)

ENGINES = ("sync", "tensor", "vector", "scalar", "gpsimd")
DEBUG_DUMPS = False


class Tok:
    __slots__ = ("sem", "val", "clk")

    def __init__(self, sem, val, clk=None):
        self.sem = sem
        self.val = val
        self.clk = clk


class Prog:

    def __init__(self, name="prog", fused=False):
        self.nc = bass.Bass("TRN2", target_bir_lowering=False, name=name,
                            dynamic_dma_scratch_size=DDS if fused else 16384)
        self.queues = {e: [] for e in ENGINES}
        self.sems = {}
        self.cnt = {}
        self.waited = {e: {} for e in ENGINES}
        for e in ENGINES:
            self._mksem("p_" + e)
        for e in ("sync", "scalar", "gpsimd"):
            self._mksem("d_" + e)
        self._mksem("cc")
        self.inputs = {}
        self.outputs = {}
        self.out_toks = []
        self.fused = fused
        self.stage = 0
        self._arena_base = None
        self._arena_off = 0
        self._shared_ps = None
        self.io = None
        self._mksem("bar")
        self.clock = {e: {} for e in ENGINES}
        self.hb_viol = []

    def new_stage(self):
        self.stage += 1
        k = self.stage
        pre = [Tok(s, self.cnt[s]) for s in self.cnt
               if self.cnt[s] > 0 and s != "bar"]
        for e in ENGINES:
            self.op(e, (lambda ee: lambda eng: eng.engine_nop()
                        if hasattr(eng, "engine_nop") else eng.nop())(e),
                    deps=pre, inc=("bar", 1))
        bar_t = Tok("bar", self.cnt["bar"])
        for e in ENGINES:
            self.queues[e].append(([bar_t], lambda eng: None, self.sems["bar"], 0, e))
        self._arena_off = 0
        return bar_t

    def _mksem(self, name):
        self.sems[name] = self.nc.alloc_semaphore(name)
        self.cnt[name] = 0

    def inp(self, name, shape, dtype):
        ap = self.nc.dram_tensor(name, list(shape), dtype, kind="ExternalInput").ap()
        self.inputs[name] = ap
        return ap

    def outp(self, name, shape, dtype):
        ap = self.nc.dram_tensor(name, list(shape), dtype, kind="ExternalOutput").ap()
        self.outputs[name] = ap
        return ap

    def get_in(self, name, shape, dtype):
        if self.io is not None and name in self.io:
            return self.io[name]
        if self.fused:
            return self.inp("s%d_%s" % (self.stage, name), shape, dtype)
        return self.inp(name, shape, dtype)

    def get_out(self, name, shape, dtype):
        if self.io is not None and name in self.io:
            return self.io[name]
        return self.outp(name, shape, dtype)

    def dram(self, name, shape, dtype, addr_space="Local"):
        if self.fused:
            name = "s%d_%s" % (self.stage, name)
        return self.nc.dram_tensor(name, list(shape), dtype, addr_space=addr_space).ap()

    ARENA = 212000

    def sb(self, name, shape, dtype):
        if not self.fused:
            return self.nc.alloc_sbuf_tensor(name, list(shape), dtype)
        if self._arena_base is None:
            self._arena_base = self.nc.bump_sbuf(self.ARENA)[0]
        nbytes = int(np.prod(shape[1:])) * (2 if dtype == BF16 else 4)
        off = (self._arena_off + 31) // 32 * 32
        self._arena_off = off + nbytes
        assert self._arena_off <= self.ARENA, (name, self._arena_off)
        return self.nc.alloc_sbuf_tensor_at(
            "s%d_%s" % (self.stage, name), list(shape), dtype,
            offset=self._arena_base + off)

    def ps(self, name, shape, dtype=F32):
        if not self.fused:
            return self.nc.alloc_psum_tensor(name, list(shape), dtype)
        key = (name, tuple(shape), dtype)
        if self._shared_ps is None:
            self._shared_ps = {}
        if key not in self._shared_ps:
            self._shared_ps[key] = self.nc.alloc_psum_tensor(
                "ps_%s" % name, list(shape), dtype)
        return self._shared_ps[key]

    def op(self, eng, fn, deps=(), inc=None, n=1):
        if inc is None:
            inc = ("p_" + eng, 1)
        sem_name, amount = inc
        deps = [Tok(d.sem, self.cnt[d.sem], getattr(d, "clk", None))
                if (d.sem.startswith("d_") or d.sem.startswith("cc")) else d
                for d in deps if d is not None]
        ec = self.clock[eng]
        upd = {}
        for d in deps:
            if d.sem[0] == "d" and d.sem[1] == "_" and d.val > ec.get(d.sem, 0) \
                    and d.val > upd.get(d.sem, 0):
                upd[d.sem] = d.val
            if d.clk:
                for s, v in d.clk.items():
                    if v > ec.get(s, 0) and v > upd.get(s, 0):
                        upd[s] = v
        if upd:
            ec = dict(ec)
            ec.update(upd)
            self.clock[eng] = ec
        if sem_name[0] == "d" and sem_name[1] == "_":
            prior = self.cnt[sem_name]
            if ec.get(sem_name, 0) < prior:
                import traceback
                fr = traceback.extract_stack()[-3]
                self.hb_viol.append(
                    "HB RACE: %s DMA on %s: prior count %d, queue %s only knows %d"
                    " (stage %d, %s:%d)" % (eng, sem_name, prior, eng,
                                            ec.get(sem_name, 0), self.stage,
                                            fr.filename.split("/")[-1], fr.lineno))
        self.cnt[sem_name] += amount * n
        tokclk = ec
        if sem_name[0] == "d" and sem_name[1] == "_":
            tokclk = dict(ec)
            tokclk[sem_name] = self.cnt[sem_name]
        tok = Tok(sem_name, self.cnt[sem_name], tokclk)
        self.queues[eng].append((deps, fn, self.sems[sem_name], amount, eng))
        return tok

    STREAM_ALIAS = {
        "cp1": "cpg", "cp2": "cp", "cp3": "cp", "cp4": "cp", "cp5": "cp",
        "hl0": "rhb0", "hl1": "rhb1", "hs0": "st0", "hs1": "st1",
        "oa0": "xs0", "oa1": "xs1", "ob0": "ys0", "ob1": "ys1",
        "os0": "st0", "os1": "st1",
        "fx0": "xs0", "fx1": "xs1", "fp0": "ys0", "fp1": "ys1",
        "fs0": "st0", "fs1": "st1",
        "mrt0": "mrt", "mrt1": "mrt", "mst": "mrt", "gw": "mrt", "ow": "mrt",
        "irtl00": "irt0", "irtl01": "irt0", "irtl10": "irt1", "irtl11": "irt1",
        "yrd0": "xs0", "yrd1": "xs1", "fyr0": "xs0", "fyr1": "xs1",
        "fys0": "yst0", "fys1": "yst1", "fy20": "ys0", "fy21": "ys1",
        "fst0": "st0", "fst1": "st1", "frb0": "rhb0",
        "hlst0": "shst0", "hlst1": "shst1",
        "oos0": "ost0", "oos1": "ost1",
        "w2s": "wv", "w1l": "wblk", "w3l": "wv", "w2l": "konstg",
        "xgr0": "rhb0", "xgr1": "rhb1", "hrr": "rhb0",
        "mt0": "xt0", "mt1": "ys1", "or0": "irt0", "or1": "irt1",
        "grl0": "irt0", "grl1": "irt1",
        "pst0": "yst0", "pst1": "yst1",
        "rr0": "rhb0", "rr1": "rhb1",
    }

    def dma(self, eng, fn, deps=(), stream=None):
        stream = self.STREAM_ALIAS.get(stream, stream)
        name = "d_" + (stream if stream is not None else eng)
        if name not in self.sems:
            self._mksem(name)
        return self.op(eng, fn, deps=deps, inc=(name, 16))

    def coll(self, fn, deps=(), stream="cc", sem="cc"):
        if sem not in self.sems:
            self._mksem(sem)
        return self.op("gpsimd", fn, deps=deps, inc=(sem, 1))

    def soft_stage(self):
        self.stage += 1

    def out_dma(self, eng, fn, deps=(), stream=None):
        t = self.dma(eng, fn, deps=deps, stream=stream)
        self.out_toks.append(t)
        return t

    def barrier_tok(self):
        return [Tok(s, self.cnt[s]) for s in self.cnt if self.cnt[s] > 0]

    def build(self):
        nc = self.nc
        with nc.Block() as block:
            def run(eng_name):
                def body(eng):
                    waited = self.waited[eng_name]
                    for deps, fn, sem, amount, _ in self.queues[eng_name]:
                        for d in deps:
                            if waited.get(d.sem, 0) >= d.val:
                                continue
                            eng.wait_ge(self.sems[d.sem], d.val)
                            waited[d.sem] = d.val
                        inst = fn(eng)
                        if inst is not None:
                            inst.then_inc(sem, amount)
                    if eng_name == "sync":
                        fin = {}
                        for t in self.out_toks:
                            fin[t.sem] = max(fin.get(t.sem, 0), self.cnt[t.sem])
                        for s_, v_ in fin.items():
                            if waited.get(s_, 0) < v_:
                                eng.wait_ge(self.sems[s_], v_)
                                waited[s_] = v_
                return body
            block.sync(run("sync"))
            block.tensor(run("tensor"))
            block.vector(run("vector"))
            block.scalar(run("scalar"))
            block.gpsimd(run("gpsimd"))
        return nc



def emit_load2(p, dst_sb, dst_cols, src_ap, deps=(), eng="sync", stream=None):
    def fn(e):
        return e.dma_start(out=dst_sb[:, dst_cols[0]:dst_cols[1]], in_=src_ap)
    return p.dma(eng, fn, deps=deps, stream=stream)


def emit_ln_stats(p, x_sb, xsq_sb, nchunk, t, ones_sb, stat_ps, mean_sb, rstd_sb,
                  scale_sb, dep, eps, extra_rms_eps=None, sq_scr=None):
    raise NotImplementedError



def np_bf(x):
    return np.asarray(x, dtype=BF)


def causal_mask_V(delta, qn=512):
    j = np.arange(128)[:, None]
    q = np.arange(qn)[None, :]
    return np_bf(np.where(q + delta < j, -MASK_C, 0.0))


def anticausal_mask_V(delta, qn=512):
    j = np.arange(128)[:, None]
    q = np.arange(qn)[None, :]
    return np_bf(np.where(j < q + delta, -MASK_C, 0.0))


def rope_tables(positions, dh=64, theta=10000.0):
    inv = 1.0 / theta ** (np.arange(0, dh, 2, dtype=np.float64) / dh)
    fr = positions[None, :].astype(np.float64) * inv[:, None]
    cos = np.repeat(np.cos(fr), 2, axis=0)
    sin = np.repeat(np.sin(fr), 2, axis=0)
    return (np.tile(cos, (2, 1)).astype(np.float32),
            np.tile(sin, (2, 1)).astype(np.float32))


def rot_interleaved_R():
    R = np.zeros((128, 128), dtype=np.float32)
    for i in range(64):
        R[2 * i + 1, 2 * i] = -1.0
        R[2 * i, 2 * i + 1] = 1.0
    return np_bf(R)


def rot_half_R():
    R = np.zeros((128, 128), dtype=np.float32)
    for h in range(2):
        b = h * 64
        for i in range(32):
            R[b + 32 + i, b + i] = -1.0
            R[b + i, b + 32 + i] = 1.0
    return np_bf(R)


def xpos_tables(positions, dh=64, window=512, nseq=4096):
    half = np.arange(0, dh, 2, dtype=np.float64)
    inv = 1.0 / 10000.0 ** (half / dh)
    t = positions.astype(np.float64)
    fr = t[None, :] * inv[:, None]
    fr = np.concatenate([fr, fr], axis=0)
    base = (half + 0.4 * dh) / (1.4 * dh)
    power = (t - nseq // 2) / float(window // 2)
    scale = base[:, None] ** power[None, :]
    scale = np.concatenate([scale, scale], axis=0)
    cos, sin = np.cos(fr), np.sin(fr)
    out = (cos * scale, sin * scale, cos / scale, sin / scale)
    return [np.tile(o, (2, 1)).astype(np.float32) for o in out]




def emit_module_ln(p, x_fm, bk, xs, sqs, cones_f, onesrow_f, rt_dram, rt_row,
                   Sbb, Bbb, app_scr, xo, nw_sb=None, rms_eps=None, eps=1e-5,
                   pre_deps=(), tag="", W=None, src_bf16=False, gains=None,
                   out_dram=None, ost=None, out_gpsimd=False, out_dram2=None):
    W = T if W is None else W
    ntt = W // 512
    pre = list(pre_deps)
    slot_cons = [pre, pre]
    sq_cons = [[], []]
    last_m1 = last_m2 = None
    for kc in range(8):
        for tt in range(ntt):
            sl = (kc * ntt + tt) % 2
            ssl = sl
            ld = p.dma("sync", (lambda kcc, ttt, s: lambda e: e.dma_start(
                out=xs[:, s * 512:(s + 1) * 512],
                in_=x_fm[kcc * 128:(kcc + 1) * 128, ttt * 512:(ttt + 1) * 512]))(kc, tt, sl),
                deps=slot_cons[sl], stream="xs%d" % sl)
            slot_cons[sl] = []
            t_sq = p.op("scalar", (lambda s, s2: lambda e: e.activation(
                sqs[:, s2 * 512:(s2 + 1) * 512],
                xs[:, s * 512:(s + 1) * 512],
                AF.Square))(sl, ssl), deps=[ld] + sq_cons[ssl] + (pre if kc == 0 else []))
            sq_cons[ssl] = []
            first = (kc == 0 and tt == 0)
            last = (kc == 7 and tt == ntt - 1)
            t_m1 = p.op("tensor", (lambda s, ttt, f, l: lambda e: e.matmul(
                bk["sx"][0:ntt, :], cones_f[:, ntt - 1 - ttt: 2 * ntt - 1 - ttt],
                xs[:, s * 512:(s + 1) * 512],
                start=f, stop=l))(sl, tt, first, last),
                deps=[ld] + (bk.get("sx_free", []) if first else []))
            t_m2 = p.op("tensor", (lambda ttt, s2, f, l: lambda e: e.matmul(
                bk["sx2"][0:ntt, :], cones_f[:, ntt - 1 - ttt: 2 * ntt - 1 - ttt],
                sqs[:, s2 * 512:(s2 + 1) * 512],
                start=f, stop=l))(tt, ssl, first, last),
                deps=[t_sq] + (bk.get("sx2_free", []) if first else []))
            sq_cons[ssl].append(t_m2)
            slot_cons[sl] += [t_sq, t_m1]
            last_m1, last_m2 = t_m1, t_m2
    sA = sqs[0:ntt, 0:512]
    sB = sqs[0:ntt, 512:1024]
    sC = app_scr[0:ntt, 0:512]
    sD = app_scr[0:ntt, 512:1024]
    t1 = p.op("vector", lambda e: e.tensor_scalar_mul(sA, bk["sx"][0:ntt, :], 1.0 / D), deps=[last_m1, last_m2])
    t2 = p.op("vector", lambda e: e.tensor_scalar_mul(sB, bk["sx2"][0:ntt, :], 1.0 / D), deps=[last_m2])
    t3 = p.op("vector", lambda e: e.tensor_mul(sC, sA, sA), deps=[t1])
    t4 = p.op("vector", lambda e: e.tensor_sub(sC, sB, sC), deps=[t2, t3])
    t5 = p.op("vector", lambda e: e.tensor_scalar_add(sD, sC, eps), deps=[t4])
    t6 = p.op("scalar", lambda e: e.activation(sD, sD, AF.Sqrt), deps=[t5])
    t7 = p.op("vector", lambda e: e.reciprocal(sD, sD), deps=[t6])
    toks_scale = t7
    if rms_eps is not None:
        t8 = p.op("vector", lambda e: e.tensor_mul(sC, sC, sD), deps=[t7, t4])
        t9 = p.op("vector", lambda e: e.tensor_mul(sC, sC, sD), deps=[t8])
        t10 = p.op("vector", lambda e: e.tensor_scalar_add(sC, sC, rms_eps), deps=[t9])
        t11 = p.op("scalar", lambda e: e.activation(sC, sC, AF.Sqrt), deps=[t10])
        t12 = p.op("vector", lambda e: e.reciprocal(sC, sC), deps=[t11])
        toks_scale = p.op("vector", lambda e: e.tensor_mul(sD, sD, sC), deps=[t12])
    t13 = p.op("vector", lambda e: e.scalar_tensor_tensor(
        sA, sA, -1.0, sD, ALU.mult, ALU.mult), deps=[toks_scale, t3])
    w1 = p.dma("sync", lambda e: e.dma_start(out=rt_dram[0:ntt, :], in_=sD), deps=[toks_scale, t13], stream="rtw" + tag)
    w2 = p.dma("sync", lambda e: e.dma_start(out=rt_dram[ntt:2 * ntt, :], in_=sA), deps=[t13], stream="rtw" + tag)
    xln_toks = {}
    xln_toks2 = {}
    slot_cons = [[], []]
    scr_cons = [[], []]
    ost_cons = [[], []]
    bc_evac = {}
    rtslot_busy = [[], []]
    bcslot_cons = [[], []]
    for tt in range(ntt):
        bsl = tt % 2
        sl_ = 0
        r1 = p.dma("sync", (lambda ttt, s: lambda e: e.dma_start(
            out=rt_row[0:1, s * 1024:s * 1024 + 512],
            in_=rt_dram[ttt:ttt + 1, :]))(tt, sl_), deps=[w1] + rtslot_busy[sl_],
            stream="rt%d%s" % (sl_, tag))
        r2 = p.dma("sync", (lambda ttt, s: lambda e: e.dma_start(
            out=rt_row[0:1, s * 1024 + 512:s * 1024 + 1024],
            in_=rt_dram[ntt + ttt:ntt + ttt + 1, :]))(tt, sl_), deps=[w2] + rtslot_busy[sl_],
            stream="rt%d%s" % (sl_, tag))
        m1 = p.op("tensor", (lambda s: lambda e: e.matmul(
            bk["b1"][:, :], onesrow_f[:, :], rt_row[0:1, s * 1024:s * 1024 + 512],
            start=True, stop=True))(sl_), deps=[r1] + (bk.get("b1_free", []) if tt == 0 else []) + list(bc_evac.get((tt - 1, 0), [])))
        e1 = p.op("scalar", (lambda b_: lambda e: e.copy(
            Sbb[:, b_ * 512:(b_ + 1) * 512], bk["b1"][:, :]))(bsl),
            deps=[m1] + bcslot_cons[bsl])
        m2 = p.op("tensor", (lambda s: lambda e: e.matmul(
            bk["b2"][:, :], onesrow_f[:, :], rt_row[0:1, s * 1024 + 512:s * 1024 + 1024],
            start=True, stop=True))(sl_), deps=[r2] + (bk.get("b2_free", []) if tt == 0 else []) + list(bc_evac.get((tt - 1, 1), [])))
        e2 = p.op("scalar", (lambda b_: lambda e: e.copy(
            Bbb[:, b_ * 512:(b_ + 1) * 512], bk["b2"][:, :]))(bsl),
            deps=[m2] + bcslot_cons[bsl])
        bc_evac[(tt, 0)] = [e1]
        bc_evac[(tt, 1)] = [e2]
        rtslot_busy[sl_] = [m1, m2]
        bcslot_cons[bsl] = []
        for kc in range(8):
            sl = (tt * 8 + kc) % 2
            ssl = sl
            ldb = p.dma("sync", (lambda kcc, ttt, s: lambda e: e.dma_start(
                out=xs[:, s * 512:(s + 1) * 512],
                in_=x_fm[kcc * 128:(kcc + 1) * 128, ttt * 512:(ttt + 1) * 512]))(kc, tt, sl),
                deps=slot_cons[sl] + ([last_m1, last_m2] if tt * 8 + kc < 2 else []), stream="xs%d" % sl)
            slot_cons[sl] = []
            if nw_sb is not None:
                d1 = p.op("vector", (lambda kcc, s, b_, s2: lambda e: e.scalar_tensor_tensor(
                    app_scr[:, s2 * 512:(s2 + 1) * 512],
                    xs[:, s * 512:(s + 1) * 512],
                    nw_sb[:, kcc:kcc + 1],
                    Sbb[:, b_ * 512:(b_ + 1) * 512], ALU.mult, ALU.mult))(kc, sl, bsl, ssl),
                    deps=[ldb, e1, w1, w2] + scr_cons[ssl])
                d2 = p.op("vector", (lambda kcc, ttt, b_, s2: lambda e: e.scalar_tensor_tensor(
                    xo[:, kcc * W + ttt * 512: kcc * W + (ttt + 1) * 512],
                    Bbb[:, b_ * 512:(b_ + 1) * 512],
                    nw_sb[:, kcc:kcc + 1],
                    app_scr[:, s2 * 512:(s2 + 1) * 512], ALU.mult, ALU.add))(kc, tt, bsl, ssl),
                    deps=[d1, e2])
            elif out_dram is not None:
                d1 = p.op("vector", (lambda s, b_, s2: lambda e: e.tensor_mul(
                    app_scr[:, s2 * 512:(s2 + 1) * 512],
                    xs[:, s * 512:(s + 1) * 512],
                    Sbb[:, b_ * 512:(b_ + 1) * 512]))(sl, bsl, ssl),
                    deps=[ldb, e1, w1, w2] + scr_cons[ssl])
                d2 = p.op("vector", (lambda b_, s2: lambda e: e.tensor_add(
                    ost[:, s2 * 512:(s2 + 1) * 512],
                    app_scr[:, s2 * 512:(s2 + 1) * 512],
                    Bbb[:, b_ * 512:(b_ + 1) * 512]))(bsl, ssl),
                    deps=[d1, e2] + ost_cons[ssl])
                dst = p.dma("gpsimd" if out_gpsimd else "sync",
                    (lambda kcc, ttt, s2: lambda e: e.dma_start(
                    out=out_dram[kcc * 128:(kcc + 1) * 128, ttt * 512:(ttt + 1) * 512],
                    in_=ost[:, s2 * 512:(s2 + 1) * 512]))(kc, tt, ssl),
                    deps=[d2], stream=("lngst%d%s" if out_gpsimd else "lnst%d%s") % (ssl, tag))
                ost_cons[ssl] = [dst]
                if out_dram2 is not None:
                    dst2 = p.dma("sync", (lambda kcc, ttt, s2: lambda e: e.dma_start(
                        out=out_dram2[kcc * 128:(kcc + 1) * 128, ttt * 512:(ttt + 1) * 512],
                        in_=ost[:, s2 * 512:(s2 + 1) * 512]))(kc, tt, ssl),
                        deps=[d2], stream="ln2st%d%s" % (ssl, tag))
                    ost_cons[ssl].append(dst2)
                    xln_toks2[(kc, tt)] = dst2
                d2 = dst
            else:
                d1 = p.op("vector", (lambda s, b_, s2: lambda e: e.tensor_mul(
                    app_scr[:, s2 * 512:(s2 + 1) * 512],
                    xs[:, s * 512:(s + 1) * 512],
                    Sbb[:, b_ * 512:(b_ + 1) * 512]))(sl, bsl, ssl),
                    deps=[ldb, e1, w1, w2] + scr_cons[ssl])
                d2 = p.op("vector", (lambda kcc, ttt, b_, s2: lambda e: e.tensor_add(
                    xo[:, kcc * W + ttt * 512: kcc * W + (ttt + 1) * 512],
                    app_scr[:, s2 * 512:(s2 + 1) * 512],
                    Bbb[:, b_ * 512:(b_ + 1) * 512]))(kc, tt, bsl, ssl),
                    deps=[d1, e2])
            scr_cons[ssl] = [d2]
            slot_cons[sl].append(d2)
            bcslot_cons[bsl].append(d2)
            xln_toks[(kc, tt)] = d2
    if out_dram2 is not None:
        return xln_toks, [w1, w2] + [t for v in bc_evac.values() for t in v], xln_toks2
    return xln_toks, [w1, w2] + [t for v in bc_evac.values() for t in v]
def emit_l0_attn(p: Prog):
    x_fm = p.get_in("x_fm", [D, T], F32)
    wqkv = p.get_in("wqkv", [D, 3 * D], F32)
    wout = p.get_in("wout", [D, D], F32)
    nw = p.get_in("nw", [128, 8], F32)
    bout = p.get_in("bout", [128, 8], F32)
    pmkT = p.get_in("pmkT", [D, 16], F32)
    pmv65 = p.get_in("pmv65", [16, 16 * 65], F32)
    rc = p.get_in("rc", [128, T], BF16)
    rs = p.get_in("rs", [128, T], BF16)
    cU = p.get_in("c_U", [128, 128], BF16)
    cVfam = p.get_in("c_Vfam", [128, 4 * 512], BF16)
    cselb = p.get_in("c_selb", [128, 31], F32)
    cR = p.get_in("c_R", [128, 128], BF16)
    ccones = p.get_in("c_cones", [128, 7], F32)
    x1_fm = p.get_out("x1_fm", [D, T], F32)
    mrow_d = p.dram("mrow_d", [16, 1], F32)
    rt_d = p.dram("rt_d", [8, 512], F32)

    xs = p.sb("xs", [128, 2 * T], F32)
    sqs = p.sb("sqs", [128, 2 * 512], F32)
    xo = p.sb("xo", [128, 8 * T], BF16)
    Wst = p.sb("Wst", [128, 4096], BF16)
    Qf = p.sb("Qf", [128, 8 * T], BF16)
    Kf = p.sb("Kf", [128, 8 * T], BF16)
    V65 = p.sb("V65", [128, 16 * 1040], BF16)
    rcb = p.sb("rcb", [128, T], BF16)
    rsb = p.sb("rsb", [128, T], BF16)
    Ub = p.sb("Ub", [128, 128], BF16)
    Vfb = p.sb("Vfb", [128, 4 * 512], BF16)
    selb = p.sb("selb", [128, 31], F32)
    Rb = p.sb("Rb", [128, 128], BF16)
    conesb = p.sb("conesb", [128, 7], F32)
    pmk_sb = p.sb("pmk_sb", [128, 8 * 16], BF16)
    pmv_sb = p.sb("pmv_sb", [16, 16 * 65], BF16)
    nw_sb = p.sb("nw_sb", [128, 8], F32)
    bout_sb = p.sb("bout_sb", [128, 8], F32)
    onesrow_f = p.sb("onesrow_f", [1, 128], F32)
    onesrow_b2 = p.sb("onesrow_b2", [1, 128], BF16)
    bvrow_sb = p.sb("bvrow_sb", [1, 512], BF16)
    rt_row = p.sb("rt_row", [1, 1024], F32)
    Sbb = p.sb("Sbb", [128, 2 * 512], BF16)
    Bbb = p.sb("Bbb", [128, 2 * 512], BF16)
    app_scr = p.sb("app_scr", [128, 2 * 512], F32)
    qsmax = p.sb("qsmax", [16, 1], F32)
    ksmax = p.sb("ksmax", [16, 1], F32)
    hred = p.sb("hred", [16, 1], F32)
    negM = p.sb("negM", [128, 16], F32)
    mrow = p.sb("mrow", [1, 16], F32)
    Pt = p.sb("Pt", [128, 2 * 512], BF16)
    Ppm = p.sb("Ppm", [16, 512], BF16)
    xr = p.sb("xr", [128, 2 * 512], BF16)
    ost = p.sb("ost", [128, 2 * 512], F32)
    recb = app_scr
    rrow = app_scr

    ps = [p.ps(f"ps{i}", [128, 512], F32) for i in range(8)]

    c1 = [emit_load2(p, rcb, (0, T), rc[:, :], stream="konst"),
          emit_load2(p, rsb, (0, T), rs[:, :], stream="konst"),
          emit_load2(p, Ub, (0, 128), cU[:, :], stream="konst"),
          emit_load2(p, Vfb, (0, 4 * 512), cVfam[:, :], stream="konst"),
          emit_load2(p, selb, (0, 31), cselb[:, :], stream="konst"),
          emit_load2(p, Rb, (0, 128), cR[:, :], stream="konst"),
          emit_load2(p, conesb, (0, 7), ccones[:, :], stream="konst"),
          emit_load2(p, nw_sb, (0, 8), nw[:, :], stream="konst"),
          emit_load2(p, bout_sb, (0, 8), bout[:, :], stream="konst")]
    c2 = [p.dma("gpsimd", lambda e: e.dma_start(
              out=pmk_sb[:, :].rearrange("p (c m) -> p c m", c=8),
              in_=pmkT.rearrange("(c p) m -> p c m", p=128)), stream="konstg"),
          p.dma("gpsimd", lambda e: e.dma_start(out=pmv_sb[:, :], in_=pmv65[:, :]), stream="konstg")]
    c3 = [p.op("vector", lambda e: e.memset(onesrow_f[:, :], 1.0)),
          p.op("vector", lambda e: e.memset(V65[:, :], 1.0))]
    t_ones = c3[0]
    t_v65ones = c3[1]
    t_cones = c1[6]

    bk = {"sx": ps[0], "sx2": ps[1], "b1": ps[2], "b2": ps[3],
          "sx_free": [], "sx2_free": [], "b1_free": [], "b2_free": []}
    xln_toks, ln_done = emit_module_ln(p, x_fm, bk, xs, sqs, conesb, onesrow_f, rt_d,
                              rt_row, Sbb, Bbb, app_scr, xo, nw_sb=nw_sb,
                              rms_eps=1e-6, eps=1e-5,
                              pre_deps=[t_cones, t_ones, c1[7]], tag="l0a")
    xln_chunk = [xln_toks[(kc, T // 512 - 1)] for kc in range(8)]
    xln_all = [xln_toks[(kc, tt)] for kc in range(8) for tt in range(T // 512)]

    wv_ld = [None, None]
    pv_rot = [[], []]
    v_mms = []
    for ft in range(2):
        wv_ld[ft] = p.dma("gpsimd", (lambda f: lambda e: e.dma_start(
            out=Wst[:, :].rearrange("p (c m) -> p c m", c=8),
            in_=wqkv[:, 2 * D + f * 512:2 * D + (f + 1) * 512].rearrange(
                "(c p) m -> p c m", p=128)))(ft), deps=list(v_mms), stream="wv")
        v_mms = []
        for tc in range(16):
            bnk = tc % 2
            mm = None
            for kc in range(8):
                mm = p.op("tensor", (lambda tcc, k, b: lambda e: e.matmul(
                    ps[6 + b][:, :],
                    xo[:, k * T + tcc * 128:k * T + (tcc + 1) * 128],
                    Wst[:, k * 512:(k + 1) * 512],
                    start=(k == 0), stop=(k == 7)))(tc, kc, bnk),
                    deps=([wv_ld[ft], xln_chunk[kc]] + (pv_rot[bnk] if kc == 0 else [])))
            v_mms.append(mm)
            pv_rot[bnk] = []
            for hh in range(8):
                h = ft * 8 + hh
                ev = p.op("scalar", (lambda tcc, hhh, hg, b: lambda e: e.copy(
                    V65[:, tcc * 1040 + hg * 65:tcc * 1040 + hg * 65 + 64],
                    ps[6 + b][:, hhh * 64:(hhh + 1) * 64]))(tc, hh, h, bnk),
                    deps=[mm, t_v65ones])
                pv_rot[bnk].append(ev)
    v65_done = [t for b in pv_rot for t in b]

    qk_rot = [[], []]
    wcol_ld = {}
    qk_evac = {}
    for fc in range(16):
        sl = fc % 2
        if fc >= 2:
            dep_w = [qk_evac[(fc - 2, tt)] for tt in range(4)]
        else:
            dep_w = list(v65_done)
        wcol_ld[fc] = p.dma("gpsimd", (lambda f, s: lambda e: e.dma_start(
            out=Wst[:, s * 1024:s * 1024 + 1024].rearrange("p (c m) -> p c m", c=8),
            in_=wqkv[:, f * 128:(f + 1) * 128].rearrange("(c p) m -> p c m", p=128)))(fc, sl),
            deps=dep_w, stream="w%d" % sl)
        for tt in range(4):
            bnk = (fc * 4 + tt) % 2
            mm = None
            for kc in range(8):
                mm = p.op("tensor", (lambda f, s, t_, k, b: lambda e: e.matmul(
                    ps[4 + b][:, :],
                    Wst[:, s * 1024 + k * 128:s * 1024 + (k + 1) * 128],
                    xo[:, k * T + t_ * 512:k * T + (t_ + 1) * 512],
                    start=(k == 0), stop=(k == 7)))(fc, sl, tt, kc, bnk),
                    deps=[wcol_ld[fc], xln_toks[(kc, tt)]] + (qk_rot[bnk] if kc == 0 else []))
            dstT = Qf if fc < 8 else Kf
            dc = fc if fc < 8 else fc - 8
            ev = p.op("scalar", (lambda dT, d, t_, b: lambda e: e.copy(
                dT[:, d * T + t_ * 512:d * T + (t_ + 1) * 512], ps[4 + b][:, :]))(dstT, dc, tt, bnk),
                deps=[mm])
            qk_rot[bnk] = [ev]
            qk_evac[(fc, tt)] = ev

    rope_done = {}
    rot_free = []
    xr_cons = [[], []]
    for ci in range(16):
        src = Qf if ci < 8 else Kf
        cc = ci % 8
        for tt in range(4):
            dep0 = [qk_evac[(ci, tt)], c1[5]]
            m = p.op("tensor", (lambda s_, c_, t_: lambda e: e.matmul(
                ps[6][:, :], Rb[:, :], s_[:, c_ * T + t_ * 512:c_ * T + (t_ + 1) * 512],
                start=True, stop=True))(src, cc, tt), deps=dep0 + rot_free)
            a1 = p.op("vector", (lambda s_, c_, t_, sl: lambda e: e.tensor_mul(
                xr[:, sl * 512:(sl + 1) * 512],
                s_[:, c_ * T + t_ * 512:c_ * T + (t_ + 1) * 512],
                rcb[:, t_ * 512:(t_ + 1) * 512]))(src, cc, tt, (ci * 4 + tt) % 2),
                deps=[qk_evac[(ci, tt)], c1[0]] + xr_cons[(ci * 4 + tt) % 2])
            a2 = p.op("vector", (lambda s_, c_, t_, sl: lambda e: e.scalar_tensor_tensor(
                s_[:, c_ * T + t_ * 512:c_ * T + (t_ + 1) * 512],
                ps[6][:, :], 1.0, rsb[:, t_ * 512:(t_ + 1) * 512],
                ALU.mult, ALU.mult))(src, cc, tt, 0), deps=[m, c1[1], a1])
            a3 = p.op("vector", (lambda s_, c_, t_, sl: lambda e: e.tensor_add(
                s_[:, c_ * T + t_ * 512:c_ * T + (t_ + 1) * 512],
                s_[:, c_ * T + t_ * 512:c_ * T + (t_ + 1) * 512],
                xr[:, sl * 512:(sl + 1) * 512]))(src, cc, tt, (ci * 4 + tt) % 2),
                deps=[a2])
            rot_free = [a2]
            xr_cons[(ci * 4 + tt) % 2] = [a3]
            rope_done[(ci, tt)] = a3

    dbg_toks = []
    if DEBUG_DUMPS:
        dxln = p.get_out("dbg_xln", [D, T], BF16)
        dq = p.get_out("dbg_q", [D, T], BF16)
        dk = p.get_out("dbg_k", [D, T], BF16)
        dv = p.get_out("dbg_v65", [128, 16 * 1040], BF16)
        for kc in range(8):
            dbg_toks.append(p.out_dma("scalar", (lambda k: lambda e: e.dma_start(
                out=dxln[k * 128:(k + 1) * 128, :], in_=xo[:, k * T:(k + 1) * T]))(kc),
                deps=[xln_toks[(kc, tt)] for tt in range(4)], stream="dbg"))
            dbg_toks.append(p.out_dma("scalar", (lambda k: lambda e: e.dma_start(
                out=dq[k * 128:(k + 1) * 128, :], in_=Qf[:, k * T:(k + 1) * T]))(kc),
                deps=[rope_done[(kc, tt)] for tt in range(4)], stream="dbg"))
            dbg_toks.append(p.out_dma("scalar", (lambda k: lambda e: e.dma_start(
                out=dk[k * 128:(k + 1) * 128, :], in_=Kf[:, k * T:(k + 1) * T]))(kc),
                deps=[rope_done[(kc + 8, tt)] for tt in range(4)], stream="dbg"))
        dbg_toks.append(p.out_dma("scalar", lambda e: e.dma_start(
            out=dv[:, :], in_=V65[:, :]), deps=v65_done, stream="dbg"))

    sq_slot_cons = [list(ln_done), list(ln_done)]
    bank_free = []
    last_up = None
    for ti, (tens, outmax, cbase) in enumerate(
            [(Qf, qsmax, 0), (Kf, ksmax, 8)] ):
        for tt in range(4):
            mm = None
            for kc in range(8):
                sl = kc % 2
                sq = p.op("scalar", (lambda s_, c_, t_, s2: lambda e: e.activation(
                    sqs[:, s2 * 512:(s2 + 1) * 512],
                    s_[:, c_ * T + t_ * 512:c_ * T + (t_ + 1) * 512],
                    AF.Square))(tens, kc, tt, sl),
                    deps=[rope_done[(kc + cbase, tt)]] + sq_slot_cons[sl])
                mm = p.op("tensor", (lambda k, s2, f, l: lambda e: e.matmul(
                    ps[2][0:16, :], selb[:, 15 - 2 * k:31 - 2 * k],
                    sqs[:, s2 * 512:(s2 + 1) * 512],
                    start=f, stop=l))(kc, sl, kc == 0, kc == 7),
                    deps=[sq, c1[4]] + (bank_free if kc == 0 else []))
                sq_slot_cons[sl] = [mm]
            rd = p.op("vector", lambda e: e.reduce_max(
                hred[:, :], ps[2][0:16, :], axis=AX.X), deps=[mm] + ([last_up] if last_up else []))
            if tt == 0:
                up = p.op("vector", (lambda om: lambda e: e.tensor_copy(
                    om[:, :], hred[:, :]))(outmax), deps=[rd])
            else:
                up = p.op("vector", (lambda om: lambda e: e.tensor_max(
                    om[:, :], om[:, :], hred[:, :]))(outmax), deps=[rd])
            bank_free = [rd]
            last_up = up
    pm_mm = None
    for kc in range(8):
        sl = kc % 2
        sq = p.op("scalar", (lambda k, s2: lambda e: e.activation(
            sqs[:, s2 * 512:s2 * 512 + 16], pmk_sb[:, k * 16:(k + 1) * 16],
            AF.Square))(kc, sl), deps=[c2[0]] + sq_slot_cons[sl])
        pm_mm = p.op("tensor", (lambda k, s2, f, l: lambda e: e.matmul(
            ps[3][0:16, 0:16], selb[:, 15 - 2 * k:31 - 2 * k],
            sqs[:, s2 * 512:s2 * 512 + 16],
            start=f, stop=l))(kc, sl, kc == 0, kc == 7),
            deps=[sq])
        sq_slot_cons[sl] = [pm_mm]
    rd = p.op("vector", lambda e: e.reduce_max(hred[:, :], ps[3][0:16, 0:16], axis=AX.X),
              deps=[pm_mm, last_up])
    up = p.op("vector", lambda e: e.tensor_max(ksmax[:, :], ksmax[:, :], hred[:, :]),
              deps=[rd])
    s1 = p.op("scalar", lambda e: e.activation(qsmax[:, :], qsmax[:, :], AF.Sqrt),
              deps=[up])
    s2_ = p.op("scalar", lambda e: e.activation(ksmax[:, :], ksmax[:, :], AF.Sqrt), deps=[up, s1])
    s3 = p.op("vector", lambda e: e.tensor_mul(hred[:, :], qsmax[:, :], ksmax[:, :]),
              deps=[s1, s2_, up])
    s4 = p.op("vector", lambda e: e.tensor_scalar_mul(hred[:, :], hred[:, :], -0.1250002), deps=[s3])
    w = p.dma("sync", lambda e: e.dma_start(out=mrow_d[:, :], in_=hred[:, :]), deps=[s4], stream="mrt")
    r = p.dma("sync", lambda e: e.dma_start(
        out=mrow[:, :], in_=mrow_d.rearrange("p o -> o p")), deps=[w], stream="mrt")
    bcm = p.op("tensor", lambda e: e.matmul(ps[2][:, 0:16], onesrow_f[:, :], mrow[:, :],
                                            start=True, stop=True), deps=[r, rd, t_ones])
    negm_tok = p.op("scalar", lambda e: e.copy(negM[:, :], ps[2][:, 0:16]), deps=[bcm])

    ofm_toks = {}
    sfree = [[negm_tok], [negm_tok]]
    pt_cons = [[], []]
    ppm_cons = []
    pm_free = [negm_tok]
    o_free = [up]
    rb_free = v65_done[-1:]
    rb_free = [rope_done[(15, 3)]]
    for h in range(16):
        base = (h % 2) * 64
        ch = h // 2
        for seg in range(4):
            pvs = []
            for kc in range(4):
                b = kc % 2
                t_s = p.op("tensor", (lambda bb, c_, s_, k, b_: lambda e: e.matmul(
                    ps[b_][:, :],
                    Kf[bb:bb + 64, c_ * T + s_ * 512 + k * 128:c_ * T + s_ * 512 + (k + 1) * 128],
                    Qf[bb:bb + 64, c_ * T + s_ * 512:c_ * T + (s_ + 1) * 512],
                    start=True, stop=False))(base, ch, seg, kc, b),
                    deps=[rope_done[(ch, seg)], rope_done[(ch + 8, seg)]] + sfree[b])
                t_mk = p.op("tensor", (lambda k, b_: lambda e: e.matmul(
                    ps[b_][:, :], Ub[:, :], Vfb[:, k * 512:(k + 1) * 512],
                    start=False, stop=True))(kc, b), deps=[t_s, c1[2], c1[3]])
                t_e = p.op("scalar", (lambda h_, k, b_: lambda e: e.activation(
                    Pt[:, b_ * 512:(b_ + 1) * 512], ps[b_][:, :], AF.Exp,
                    bias=negM[:, h_:h_ + 1], scale=0.125))(h, kc, b),
                    deps=[t_mk] + pt_cons[b])
                sfree[b] = [t_e]
                t_pv = p.op("tensor", (lambda h_, c_, s_, k, b_: lambda e: e.matmul(
                    ps[3][0:65, :],
                    V65[:, (s_ * 4 + k) * 1040 + h_ * 65:(s_ * 4 + k) * 1040 + h_ * 65 + 65],
                    Pt[:, b_ * 512:(b_ + 1) * 512],
                    start=(k == 0), stop=False))(h, ch, seg, kc, b),
                    deps=[t_e] + v65_done + (o_free if kc == 0 else []))
                pt_cons[b] = [t_pv]
                pvs.append(t_pv)
            t_spm = p.op("tensor", (lambda bb, c_, s_: lambda e: e.matmul(
                ps[2][0:16, :], pmk_sb[bb:bb + 64, c_ * 16:(c_ + 1) * 16],
                Qf[bb:bb + 64, c_ * T + s_ * 512:c_ * T + (s_ + 1) * 512],
                start=True, stop=True))(base, ch, seg),
                deps=[c2[0], rope_done[(ch, seg)]] + pm_free)
            t_epm = p.op("scalar", (lambda h_: lambda e: e.activation(
                Ppm[:, :], ps[2][0:16, :], AF.Exp,
                bias=negM[0:16, h_:h_ + 1], scale=0.125))(h),
                deps=[t_spm] + ppm_cons)
            pm_free = [t_epm]
            t_pvm = p.op("tensor", (lambda h_: lambda e: e.matmul(
                ps[3][0:65, :], pmv_sb[0:16, h_ * 65:h_ * 65 + 65], Ppm[0:16, :],
                start=False, stop=True))(h), deps=[t_epm, c2[1]] + pvs)
            ppm_cons = [t_pvm]
            t_rc = p.op("vector", lambda e: e.reciprocal(rrow[0:1, 0:512], ps[3][64:65, :]),
                        deps=[t_pvm])
            t_bc = p.op("tensor", lambda e: e.matmul(
                ps[6][0:64, :], onesrow_f[0:1, 0:64], rrow[0:1, 0:512], start=True, stop=True),
                deps=[t_rc, t_ones] + rb_free)
            t_ev = p.op("scalar", lambda e: e.copy(recb[0:64, 512:1024], ps[6][0:64, :]), deps=[t_bc])
            rb_free = [t_ev]
            t_on = p.op("vector", (lambda bb, c_, s_: lambda e: e.tensor_mul(
                xo[bb:bb + 64, c_ * T + s_ * 512:c_ * T + (s_ + 1) * 512],
                ps[3][0:64, :], recb[0:64, 512:1024]))(base, ch, seg),
                deps=[t_ev, qk_evac[(15, 3)], v65_done[-1]] + dbg_toks[-1:])
            o_free = [t_on]
            ofm_toks[(h, seg)] = t_on

    if DEBUG_DUMPS:
        dofm = p.get_out("dbg_ofm", [D, T], BF16)
        dnegm = p.get_out("dbg_negm", [128, 16], F32)
        for kc in range(8):
            dbg_toks.append(p.out_dma("scalar", (lambda k: lambda e: e.dma_start(
                out=dofm[k * 128:(k + 1) * 128, :], in_=xo[:, k * T:(k + 1) * T]))(kc),
                deps=[ofm_toks[(2 * kc, sg)] for sg in range(4)] +
                     [ofm_toks[(2 * kc + 1, sg)] for sg in range(4)], stream="dbg"))
        dbg_toks.append(p.out_dma("scalar", lambda e: e.dma_start(
            out=dnegm[:, :], in_=negM[:, :]), deps=[negm_tok], stream="dbg"))

    op_rot = [[], []]
    wo_ld = {}
    xsl = {}
    st_cons = [[], []]
    xs_cons = [[], []]
    for fc in range(8):
        sl = fc % 2
        depw = [wo_ld[fc - 2]] if fc >= 2 else []
        if fc >= 2:
            depw = op_rot[0] + op_rot[1]
        elif fc < 2:
            depw = [qk_evac[(15, 3)]]
        wo_ld[fc] = p.dma("gpsimd", (lambda f, s: lambda e: e.dma_start(
            out=Wst[:, s * 1024:s * 1024 + 1024].rearrange("p (c m) -> p c m", c=8),
            in_=wout[:, f * 128:(f + 1) * 128].rearrange("(c p) m -> p c m", p=128)))(fc, sl),
            deps=depw, stream="w%d" % sl)
        xsl[fc] = p.dma("sync", (lambda f, s: lambda e: e.dma_start(
            out=xs[:, s * T:(s + 1) * T], in_=x_fm[f * 128:(f + 1) * 128, :]))(fc, sl),
            deps=xs_cons[sl], stream="xs%d" % sl)
        xs_cons[sl] = []
        for tt in range(4):
            bnk = (fc * 4 + tt) % 2
            mm = None
            for ic in range(8):
                deps = [wo_ld[fc]] + [ofm_toks[(2 * ic, tt)], ofm_toks[(2 * ic + 1, tt)]]
                mm = p.op("tensor", (lambda f, s, t_, k, b: lambda e: e.matmul(
                    ps[4 + b][:, :],
                    Wst[:, s * 1024 + k * 128:s * 1024 + (k + 1) * 128],
                    xo[:, k * T + t_ * 512:k * T + (t_ + 1) * 512],
                    start=(k == 0), stop=(k == 7)))(fc, sl, tt, ic, bnk),
                    deps=deps + (op_rot[bnk] if ic == 0 else []))
            sslot = (fc * 4 + tt) % 2
            ev = p.op("vector", (lambda f, s, t_, b, s2: lambda e: e.scalar_tensor_tensor(
                ost[:, s2 * 512:(s2 + 1) * 512], ps[4 + b][:, :],
                bout_sb[:, f:f + 1],
                xs[:, s * T + t_ * 512:s * T + (t_ + 1) * 512],
                ALU.add, ALU.add))(fc, sl, tt, bnk, sslot),
                deps=[mm, xsl[fc], c1[8], ofm_toks[(15, 3)]] + st_cons[sslot])
            op_rot[bnk] = [ev]
            xs_cons[sl].append(ev)
            st = p.out_dma("sync", (lambda f, t_, s2: lambda e: e.dma_start(
                out=x1_fm[f * 128:(f + 1) * 128, t_ * 512:(t_ + 1) * 512],
                in_=ost[:, s2 * 512:(s2 + 1) * 512]))(fc, tt, sslot),
                deps=[ev], stream="st%d" % sslot)
            st_cons[sslot] = [st]


def prep_l0_attn_inputs(inputs):
    pm_k = inputs["L0_attn_pm_k"]
    pm_v = inputs["L0_attn_pm_v"]
    pmkT = np.ascontiguousarray(np.transpose(pm_k, (0, 2, 1)).reshape(D, 16)).astype(np.float32)
    pmv65 = np.zeros((16, 16 * 65), dtype=np.float32)
    for h in range(16):
        pmv65[:, h * 65:h * 65 + 64] = pm_v[h]
        pmv65[:, h * 65 + 64] = 1.0
    U = np_bf((np.arange(128)[:, None] <= np.arange(128)[None, :]).astype(np.float32))
    Vfam = np.concatenate([causal_mask_V(-128 * kc) for kc in range(4)], axis=1)
    selb = np.zeros((128, 31), dtype=np.float32)
    selb[0:64, 15] = 1.0
    selb[64:128, 16] = 1.0
    cones = np.zeros((128, 7), dtype=np.float32)
    cones[:, 3] = 1.0
    R = rot_interleaved_R()
    nw = np.ascontiguousarray(inputs["L0_attn_norm_w"].reshape(8, 128).T).astype(np.float32)
    bout = np.ascontiguousarray(inputs["L0_attn_b_out"].reshape(8, 128).T).astype(np.float32)
    x = inputs["L0_attn_seq"]
    in_maps = []
    for c in range(NCORE):
        b, half = c // 2, c % 2
        s = half * T
        rc_, rs_ = rope_tables(np.arange(s, s + T))
        in_maps.append(dict(
            x_fm=np.ascontiguousarray(x[b, s:s + T, :].T),
            wqkv=inputs["L0_attn_w_qkv"],
            wout=inputs["L0_attn_w_out"],
            nw=nw, bout=bout, pmkT=pmkT, pmv65=pmv65,
            rc=np_bf(rc_), rs=np_bf(rs_),
            c_U=U, c_Vfam=Vfam, c_selb=selb, c_R=R, c_cones=cones,
        ))
    return in_maps


def build_l0_attn():
    p = Prog("l0attn")
    emit_l0_attn(p)
    return p


W2 = 3584
NT2 = 7
QKSC = 8.0


def emit_l0_ffn(p: Prog):
    x1h = p.get_in("x1h", [D, W2], F32)
    fw_qkv = p.get_in("fw_qkv", [2 * D, 1536], F32)
    fw_o = p.get_in("fw_o", [2 * 512, D], F32)
    fw_1 = p.get_in("fw_1", [2 * D, 5460], F32)
    fw_2 = p.get_in("fw_2", [2 * 2730, D], F32)
    f_qs = p.get_in("f_qs", [128, 2], F32)
    f_ks = p.get_in("f_ks", [128, 2], F32)
    f_lg = p.get_in("f_lg", [128, 16], F32)
    f_lb = p.get_in("f_lb", [128, 16], F32)
    f_b1a = p.get_in("f_b1a", [128, 44], F32)
    f_b1g = p.get_in("f_b1g", [128, 44], F32)
    f_b2 = p.get_in("f_b2", [128, 16], F32)
    xq_cs = p.get_in("xq_cs", [128, W2], BF16)
    xq_ss = p.get_in("xq_ss", [128, W2], BF16)
    xk_cs = p.get_in("xk_cs", [128, W2], BF16)
    xk_ss = p.get_in("xk_ss", [128, W2], BF16)
    cU = p.get_in("c_U", [128, 128], BF16)
    cVfam = p.get_in("c_Vfam", [128, 4 * 512], BF16)
    cUa = p.get_in("c_Ua", [128, 128], BF16)
    cVafam = p.get_in("c_Vafam", [128, 4 * 512], BF16)
    cselb = p.get_in("c_selb", [128, 31], F32)
    csel2T = p.get_in("c_sel2T", [8, 4 * 128], F32)
    cR2 = p.get_in("c_R2", [128, 128], BF16)
    ccones = p.get_in("c_cones", [128, 13], F32)
    cpw = p.get_in("c_pw", [1, 7 * 128], BF16)
    cconesb = p.get_in("c_conesb", [128, 13], BF16)
    x2_fm = p.get_out("x2_fm", [D, T], F32)

    y_d = p.dram("y_d", [D, W2], F32)
    sh_d = p.dram("sh_d", [D, W2], BF16)
    o_d = p.dram("o_d", [512, W2], BF16)
    hl_d = p.dram("hl_d", [D, W2], BF16)
    rt_d = p.dram("rt_d", [2 * NT2, 512], F32)
    mrow_d = p.dram("mrow_d", [8, 1], F32)

    xs = p.sb("xs", [128, 2 * 512], F32)
    ys = p.sb("ys", [128, 2 * 513], F32)
    sqs = p.sb("sqs", [128, 2 * 512], F32)
    app_scr = p.sb("app_scr", [128, 2 * 512], F32)
    rt_row = p.sb("rt_row", [1, 1024], F32)
    ost = p.sb("ost", [128, 2 * 512], F32)
    Sbb = p.sb("Sbb", [128, 2 * 512], BF16)
    Bbb = p.sb("Bbb", [128, 2 * 512], BF16)
    Qf = p.sb("Qf", [128, 4 * W2], BF16)
    Kf = p.sb("Kf", [128, 4 * W2], BF16)
    V65 = p.sb("V65", [128, 28 * 520], BF16)
    xtb = p.sb("xtb", [128, 1024], BF16)
    Wblk = p.sb("Wblk", [128, 4096], BF16)
    Wv = p.sb("Wv", [128, 8 * 512], BF16)
    rhb = p.sb("rhb", [128, 8 * 512], BF16)
    shm = p.sb("shm", [128, 2 * 512], BF16)
    Wo = p.sb("Wo", [128, 2 * 512], BF16)
    wa = p.sb("wa", [128, 2 * 1024], BF16)
    wg = p.sb("wg", [128, 2 * 1024], BF16)
    w2b = p.sb("w2b", [128, 2816], BF16)
    h_sb = p.sb("h_sb", [128, 22 * 512], BF16)
    Pt = p.sb("Pt", [128, 2 * 512], BF16)
    Ub = p.sb("Ub", [128, 128], BF16)
    Vfb = p.sb("Vfb", [128, 4 * 512], BF16)
    Uab = p.sb("Uab", [128, 128], BF16)
    Vafb = p.sb("Vafb", [128, 4 * 512], BF16)
    selb = p.sb("selb", [128, 31], F32)
    sel2T = p.sb("sel2T", [8, 4 * 128], F32)
    R2b = p.sb("R2b", [128, 128], BF16)
    conesb = p.sb("conesb", [128, 13], F32)
    pwb = p.sb("pwb", [1, 7 * 128], BF16)
    onesrow_f = p.sb("onesrow_f", [1, 128], F32)
    onesrow_b2 = p.sb("onesrow_b2", [1, 128], BF16)
    bvrow_sb = p.sb("bvrow_sb", [1, 512], BF16)
    onesrow_b = p.sb("onesrow_b", [1, 512], BF16)
    qs_sb = p.sb("qs_sb", [128, 2], F32)
    ks_sb = p.sb("ks_sb", [128, 2], F32)
    lg_sb = p.sb("lg_sb", [128, 16], F32)
    lb_sb = p.sb("lb_sb", [128, 16], F32)
    b1a_sb = p.sb("b1a_sb", [128, 44], F32)
    b1g_sb = p.sb("b1g_sb", [128, 44], F32)
    b2_sb = p.sb("b2_sb", [128, 16], F32)
    nrm = p.sb("nrm", [16, 512], F32)
    nq2 = p.sb("nq2", [16, 512], F32)
    qs2_sb = p.sb("qs2_sb", [128, 4], F32)
    qsmax = p.sb("qsmax", [16, 1], F32)
    ksmax = p.sb("ksmax", [16, 1], F32)
    hred = p.sb("hred", [16, 1], F32)
    negM = p.sb("negM", [128, 8], F32)
    mrow = p.sb("mrow", [1, 8], F32)
    arecb = p.sb("arecb", [64, 512], F32)
    arec = nq2
    gact = p.sb("gact", [128, 2 * 512], BF16)
    conesbb = p.sb("conesbb", [128, 13], BF16)
    hlb = gact

    ps = [p.ps(f"ps{i}", [128, 512], F32) for i in range(8)]

    cl = [emit_load2(p, Ub, (0, 128), cU[:, :], stream="konst"),
          emit_load2(p, Vfb, (0, 2048), cVfam[:, :], stream="konst"),
          emit_load2(p, Uab, (0, 128), cUa[:, :], stream="konst"),
          emit_load2(p, Vafb, (0, 2048), cVafam[:, :], stream="konst"),
          emit_load2(p, selb, (0, 31), cselb[:, :], stream="konst"),
          emit_load2(p, sel2T, (0, 512), csel2T[:, :], stream="konst"),
          emit_load2(p, R2b, (0, 128), cR2[:, :], stream="konst"),
          emit_load2(p, conesb, (0, 13), ccones[:, :], stream="konst"),
          emit_load2(p, pwb, (0, 896), cpw[:, :], stream="konst"),
          emit_load2(p, qs_sb, (0, 2), f_qs[:, :], stream="konst"),
          emit_load2(p, ks_sb, (0, 2), f_ks[:, :], stream="konst"),
          emit_load2(p, lg_sb, (0, 16), f_lg[:, :], stream="konst"),
          emit_load2(p, lb_sb, (0, 16), f_lb[:, :], stream="konst"),
          emit_load2(p, b1a_sb, (0, 44), f_b1a[:, :], stream="konst"),
          emit_load2(p, b1g_sb, (0, 44), f_b1g[:, :], stream="konst"),
          emit_load2(p, b2_sb, (0, 16), f_b2[:, :], stream="konst"),
          emit_load2(p, conesbb, (0, 13), cconesb[:, :], stream="konst")]
    mos = [p.op("vector", lambda e: e.memset(onesrow_f[:, :], 1.0)),
           p.op("vector", lambda e: e.memset(onesrow_b[:, :], 1.0)),
           p.op("vector", lambda e: e.memset(V65[:, :], 1.0))]
    t_ones = mos[0]
    t_onesb = mos[1]
    t_v65ones = mos[2]

    bk = {"sx": ps[0], "sx2": ps[1], "b1": ps[2], "b2": ps[3]}
    y0_toks, ln_done = emit_module_ln(
        p, x1h, bk, xs, sqs, conesb, onesrow_f, rt_d, rt_row, Sbb, Bbb,
        app_scr, None, nw_sb=None, rms_eps=None, eps=1e-5,
        pre_deps=[cl[7], t_ones], tag="m", W=W2, out_dram=y_d, ost=ost)

    def stream_tiles_ln(src_dram, tiles, tag2, extra_pre, gains=None, out_d=None,
                        src16=True):
        raise NotImplementedError

    prev_done = [y0_toks[(kc, tt)] for kc in range(8) for tt in range(NT2)]

    resid_out_toks = None
    for l in range(2):
        RQ = list(range(2 * l, NT2))
        RWn = list(range(1 + 2 * l, NT2))
        RFr = list(range(1 + 2 * l, NT2))

        if TRUNC2 is not None and TRUNC2 <= l * 6 + 0:
            return
        sh_toks = {}
        ys_cons = [[], []]
        shm_cons = [[], []]
        for ti, tt in enumerate(RQ):
            for kc in range(8):
                sl = (ti * 8 + kc) % 2
                lo = tt * 512 - 1
                if lo < 0:
                    t_l = p.dma("sync", (lambda k, s: lambda e: e.dma_start(
                        out=ys[:, s * 513 + 1:(s + 1) * 513],
                        in_=y_d[k * 128:(k + 1) * 128, 0:512]))(kc, sl),
                        deps=ys_cons[sl] + prev_done, stream="ys%d" % sl)
                    t_z = p.op("vector", (lambda s: lambda e: e.memset(
                        ys[:, s * 513:s * 513 + 1], 0.0))(sl), deps=ys_cons[sl])
                    dep_src = [t_l, t_z]
                else:
                    t_l = p.dma("sync", (lambda k, s, o: lambda e: e.dma_start(
                        out=ys[:, s * 513:(s + 1) * 513],
                        in_=y_d[k * 128:(k + 1) * 128, o:o + 513]))(kc, sl, lo),
                        deps=ys_cons[sl] + prev_done, stream="ys%d" % sl)
                    dep_src = [t_l]
                off = 1 if kc < 4 else 0
                t_c = p.op("vector", (lambda s, s2, o: lambda e: e.tensor_copy(
                    shm[:, s2 * 512:(s2 + 1) * 512],
                    ys[:, s * 513 + o:s * 513 + o + 512]))(sl, sl, off),
                    deps=dep_src + shm_cons[sl])
                ys_cons[sl] = [t_c]
                t_s = p.dma("scalar", (lambda k, ttt, s2: lambda e: e.dma_start(
                    out=sh_d[k * 128:(k + 1) * 128, ttt * 512:(ttt + 1) * 512],
                    in_=shm[:, s2 * 512:(s2 + 1) * 512]))(kc, tt, sl),
                    deps=[t_c], stream="shst%d" % sl)
                shm_cons[sl] = [t_s]
                sh_toks[(kc, tt)] = t_s

        if TRUNC2 is not None and TRUNC2 <= l * 6 + 1:
            return
        wv_ld = p.dma("gpsimd", (lambda ll: lambda e: e.dma_start(
            out=Wv[:, :].rearrange("p (c m) -> p c m", c=8),
            in_=fw_qkv[ll * D:(ll + 1) * D, 1024:1536].rearrange(
                "(c p) m -> p c m", p=128)))(l), stream="konstg",
            deps=prev_done)
        qk_evac = {}
        v_evac = {}
        rhb_cons = [[], []]
        blk_rot = [[], []]
        vps_rot = [[], []]
        wblk_cons = []
        for blk in range(2):
            wsl = 0
            wblk_ld = p.dma("gpsimd", (lambda ll, b: lambda e: e.dma_start(
                out=Wblk[:, :].rearrange("p (c m) -> p c m", c=8),
                in_=fw_qkv[ll * D:(ll + 1) * D, b * 512:(b + 1) * 512].rearrange(
                    "(c p) m -> p c m", p=128)))(l, blk),
                deps=prev_done + wblk_cons, stream="wblk")
            for ti, tt in enumerate(RQ):
                rsl = 0
                t_r = p.dma("sync", (lambda ttt: lambda e: e.dma_start(
                    out=rhb[:, :].rearrange("p (c m) -> p c m", c=8),
                    in_=sh_d[:, ttt * 512:(ttt + 1) * 512].rearrange(
                        "(c p) m -> p c m", p=128)))(tt),
                    deps=rhb_cons[0] + [sh_toks[(kc, tt)] for kc in range(8)],
                    stream="rhb0")
                rhb_cons[0] = []
                for fcq in range(4 if not SKIP_B_QK else 0):
                    bnk = (blk * 28 + ti * 4 + fcq) % 2
                    mm = None
                    for kc in range(8):
                        mm = p.op("tensor", (lambda f, k, b: lambda e: e.matmul(
                            ps[4 + b][:, :],
                            Wblk[:, k * 512 + f * 128:k * 512 + (f + 1) * 128],
                            rhb[:, k * 512:(k + 1) * 512],
                            start=(k == 0), stop=(k == 7)))(fcq, kc, bnk),
                            deps=[wblk_ld, t_r] + (blk_rot[bnk] if kc == 0 else []))
                    dstT = Qf if blk == 0 else Kf
                    ev = p.op("scalar", (lambda dT, d, t_, b: lambda e: e.copy(
                        dT[:, d * W2 + t_ * 512:d * W2 + (t_ + 1) * 512], ps[4 + b][:, :]))(
                        dstT, fcq, tt, bnk), deps=[mm])
                    blk_rot[bnk] = [ev]
                    rhb_cons[0].append(ev)
                    wblk_cons.append(ev)
                    qk_evac[(blk, fcq, tt)] = ev
                if blk == 0 and not SKIP_B_V:
                    for j in range(4):
                        vb = (ti * 4 + j) % 2
                        mmv = None
                        for kc in range(8):
                            mmv = p.op("tensor", (lambda jj, k, b: lambda e: e.matmul(
                                ps[6 + b][:, :],
                                rhb[:, k * 512 + jj * 128:k * 512 + (jj + 1) * 128],
                                Wv[:, k * 512:(k + 1) * 512],
                                start=(k == 0), stop=(k == 7)))(j, kc, vb),
                                deps=[wv_ld, t_r] + (vps_rot[vb] if kc == 0 else []))
                        vps_rot[vb] = []
                        tg = tt * 4 + j
                        for hh in range(8):
                            evv = p.op("scalar", (lambda tgg, hhh, b: lambda e: e.copy(
                                V65[:, tgg * 520 + hhh * 65:tgg * 520 + hhh * 65 + 64],
                                ps[6 + b][:, hhh * 64:(hhh + 1) * 64]))(tg, hh, vb),
                                deps=[mmv, t_v65ones])
                            vps_rot[vb].append(evv)
                            rhb_cons[0].append(evv)
                            v_evac[(tg, hh)] = evv
        v_all = [v_evac[k] for k in v_evac]
        qk_all = [qk_evac[k] for k in qk_evac]
        last_qk = qk_evac.get((1, 3, RQ[-1]))

        if TRUNC2 is not None and TRUNC2 <= l * 6 + 2:
            return
        t_qs2 = p.op("vector", (lambda ll: lambda e: e.tensor_mul(
            qs2_sb[:, 0:1], qs_sb[:, ll:ll + 1], qs_sb[:, ll:ll + 1]))(l),
            deps=[cl[9], last_qk])
        t_ks2 = p.op("vector", (lambda ll: lambda e: e.tensor_mul(
            qs2_sb[:, 1:2], ks_sb[:, ll:ll + 1], ks_sb[:, ll:ll + 1]))(l),
            deps=[cl[10], t_qs2])
        xpos_done = {}
        nrm_cons = []
        n2_cons = []
        mmax_init = {0: False, 1: False}
        m_bank_free = []
        sq_cons = [[], []]
        xt_cons = [[], []]
        rot_free = []
        scr_cons = [[], []]
        rec_free = []
        for qk in range(2):
            src = Qf if qk == 0 else Kf
            cstab = xq_cs if qk == 0 else xk_cs
            sstab = xq_ss if qk == 0 else xk_ss
            scap = qs_sb if qk == 0 else ks_sb
            s2col = qk
            outmax = qsmax if qk == 0 else ksmax
            for ti, tt in enumerate(RQ):
                xsl = 0
                t_xt = p.dma("sync", (lambda t_, ct: lambda e: e.dma_start(
                    out=xtb[:, 0:512], in_=ct[:, t_ * 512:(t_ + 1) * 512]))(
                    tt, cstab), deps=xt_cons[0], stream="xt0")
                t_xt2 = p.dma("sync", (lambda t_, st_: lambda e: e.dma_start(
                    out=xtb[:, 512:1024], in_=st_[:, t_ * 512:(t_ + 1) * 512]))(
                    tt, sstab), deps=xt_cons[0], stream="xt0")
                xt_cons[0] = []
                mmN = mmM = None
                for cc in range(4):
                    ssl = cc % 2
                    t_sq = p.op("scalar", (lambda s_, c_, t_, s2: lambda e: e.activation(
                        sqs[:, s2 * 512:(s2 + 1) * 512],
                        s_[:, c_ * W2 + t_ * 512:c_ * W2 + (t_ + 1) * 512],
                        AF.Square))(src, cc, tt, ssl),
                        deps=[qk_evac[(qk, cc, tt)]] + sq_cons[ssl])
                    mmN = p.op("tensor", (lambda c_, s2, f: lambda e: e.matmul(
                        ps[2][0:16, :], selb[:, 15 - 2 * c_:31 - 2 * c_],
                        sqs[:, s2 * 512:(s2 + 1) * 512],
                        start=f, stop=(c_ == 3)))(cc, ssl, cc == 0),
                        deps=[t_sq, cl[4]] + (rec_free if cc == 0 else []))
                    t_w = p.op("vector", (lambda s2, sc: lambda e: e.tensor_scalar_mul(
                        sqs[:, s2 * 512:(s2 + 1) * 512],
                        sqs[:, s2 * 512:(s2 + 1) * 512],
                        qs2_sb[:, sc:sc + 1]))(ssl, s2col),
                        deps=[mmN, t_qs2, t_ks2])
                    mmM = p.op("tensor", (lambda c_, s2, f: lambda e: e.matmul(
                        ps[3][0:16, :], selb[:, 15 - 2 * c_:31 - 2 * c_],
                        sqs[:, s2 * 512:(s2 + 1) * 512],
                        start=f, stop=(c_ == 3)))(cc, ssl, cc == 0),
                        deps=[t_w] + (m_bank_free if cc == 0 else []))
                    sq_cons[ssl] = [mmM]
                r1 = p.op("scalar", lambda e: e.activation(nrm[:, :], ps[2][0:16, :], AF.Sqrt),
                          deps=[mmN] + nrm_cons)
                r2 = p.op("vector", lambda e: e.tensor_scalar_max(nrm[:, :], nrm[:, :], 1e-12),
                          deps=[r1])
                r3 = p.op("vector", lambda e: e.reciprocal(nrm[:, :], nrm[:, :]), deps=[r2])
                n1 = p.op("vector", lambda e: e.tensor_mul(nq2[:, :], ps[3][0:16, :], nrm[0:16, :]),
                          deps=[r3, mmM] + n2_cons)
                n2 = p.op("vector", lambda e: e.tensor_mul(nq2[:, :], nq2[:, :], nrm[0:16, :]),
                          deps=[n1])
                m_bank_free = [n1]
                n3 = p.op("vector", lambda e: e.reduce_max(hred[:, :], nq2[:, :], axis=AX.X),
                          deps=[n2])
                if not mmax_init[qk]:
                    n4 = p.op("vector", (lambda om: lambda e: e.tensor_copy(om[:, :], hred[:, :]))(outmax),
                              deps=[n3])
                    mmax_init[qk] = True
                else:
                    n4 = p.op("vector", (lambda om: lambda e: e.tensor_max(om[:, :], om[:, :], hred[:, :]))(outmax),
                              deps=[n3])
                n2_cons = [n4]
                for cc in range(4):
                    bcm = p.op("tensor", (lambda c_: lambda e: e.matmul(
                        ps[2][:, :], sel2T[:, c_ * 128:(c_ + 1) * 128], nrm[0:8, :],
                        start=True, stop=True))(cc),
                        deps=[r3, cl[5], mmN, n1])
                    qn = p.op("vector", (lambda s_, c_, t_, sc_, ll: lambda e: e.scalar_tensor_tensor(
                        s_[:, c_ * W2 + t_ * 512:c_ * W2 + (t_ + 1) * 512],
                        s_[:, c_ * W2 + t_ * 512:c_ * W2 + (t_ + 1) * 512],
                        sc_[:, ll:ll + 1], ps[2][:, :], ALU.mult, ALU.mult))(src, cc, tt, scap, l),
                        deps=[bcm] + sq_cons[0] + sq_cons[1] + [n2])
                    rec_free = [qn]
                    rmm = p.op("tensor", (lambda s_, c_, t_: lambda e: e.matmul(
                        ps[6][:, :], R2b[:, :],
                        s_[:, c_ * W2 + t_ * 512:c_ * W2 + (t_ + 1) * 512],
                        start=True, stop=True))(src, cc, tt),
                        deps=[qn, cl[6]] + rot_free)
                    a1 = p.op("vector", (lambda s_, c_, t_, s2: lambda e: e.tensor_mul(
                        app_scr[:, s2 * 512:(s2 + 1) * 512],
                        s_[:, c_ * W2 + t_ * 512:c_ * W2 + (t_ + 1) * 512],
                        xtb[:, 0:512]))(src, cc, tt, cc % 2),
                        deps=[qn, t_xt] + scr_cons[cc % 2])
                    a2 = p.op("vector", (lambda s_, c_, t_: lambda e: e.scalar_tensor_tensor(
                        s_[:, c_ * W2 + t_ * 512:c_ * W2 + (t_ + 1) * 512],
                        ps[6][:, :], 1.0,
                        xtb[:, 512:1024], ALU.mult, ALU.mult))(src, cc, tt),
                        deps=[rmm, t_xt2, a1])
                    rot_free = [a2]
                    a3 = p.op("vector", (lambda s_, c_, t_, s2: lambda e: e.tensor_add(
                        s_[:, c_ * W2 + t_ * 512:c_ * W2 + (t_ + 1) * 512],
                        s_[:, c_ * W2 + t_ * 512:c_ * W2 + (t_ + 1) * 512],
                        app_scr[:, s2 * 512:(s2 + 1) * 512]))(src, cc, tt, cc % 2),
                        deps=[a2])
                    scr_cons[cc % 2] = [a3]
                    xt_cons[0].append(a3)
                    xpos_done[(qk, cc, tt)] = a3
                nrm_cons = [a3, n2]
        s3 = p.op("vector", lambda e: e.tensor_mul(hred[:, :], qsmax[:, :], ksmax[:, :]),
                  deps=n2_cons + [xpos_done[(1, 3, RQ[-1])]])
        s4 = p.op("scalar", lambda e: e.activation(hred[:, :], hred[:, :], AF.Sqrt), deps=[s3])
        s5 = p.op("vector", lambda e: e.tensor_scalar_mul(hred[:, :], hred[:, :], -QKSC * 1.000001),
                  deps=[s4])
        w = p.dma("sync", lambda e: e.dma_start(out=mrow_d[:, :], in_=hred[0:8, :]), deps=[s5],
                  stream="mrt%d" % l)
        r = p.dma("sync", lambda e: e.dma_start(
            out=mrow[:, :], in_=mrow_d.rearrange("p o -> o p")), deps=[w], stream="mrt%d" % l)
        bcm = p.op("tensor", lambda e: e.matmul(ps[2][:, 0:8], onesrow_f[:, :], mrow[:, :],
                                                start=True, stop=True),
                   deps=[r, t_ones, xpos_done[(1, 3, RQ[-1])]])
        negm_tok = p.op("scalar", lambda e: e.copy(negM[:, :], ps[2][:, 0:8]), deps=[bcm])

        if TRUNC2 is not None and TRUNC2 <= l * 6 + 3:
            return
        on_toks = {}
        sfree = [[negm_tok], [negm_tok]]
        pt_cons = [[], []]
        o_free = [negm_tok]
        rb_free = [xpos_done[(1, 3, RQ[-1])]]
        shm_cons2 = [[], []]
        ost_free = []
        for h in range(8):
            base = (h % 2) * 64
            ch = h // 2
            for w in RWn:
                kgs = [((w - 1) * 4 + kc, kc, True) for kc in range(4)] + \
                      [(w * 4 + kc, kc, False) for kc in range(4)]
                pvs = []
                for i, (kg, kc, prev) in enumerate(kgs):
                    b = i % 2
                    t_s = p.op("tensor", (lambda bb, c_, w_, kg_, b_: lambda e: e.matmul(
                        ps[b_][:, :],
                        Kf[bb:bb + 64, c_ * W2 + kg_ * 128:c_ * W2 + (kg_ + 1) * 128],
                        Qf[bb:bb + 64, c_ * W2 + w_ * 512:c_ * W2 + (w_ + 1) * 512],
                        start=True, stop=False))(base, ch, w, kg, b),
                        deps=[xpos_done[(0, ch, w)], xpos_done[(1, ch, (kg // 4))]] + sfree[b])
                    if prev:
                        t_mk = p.op("tensor", (lambda k, b_: lambda e: e.matmul(
                            ps[b_][:, :], Uab[:, :], Vafb[:, k * 512:(k + 1) * 512],
                            start=False, stop=False))(kc, b), deps=[t_s, cl[2], cl[3]])
                        t_mk = p.op("tensor", (lambda w_, b_: lambda e: e.matmul(
                            ps[b_][:, :], pwb[0:1, w_ * 128:(w_ + 1) * 128], onesrow_b[:, :],
                            start=False, stop=True))(w, b), deps=[t_mk, cl[8], t_onesb])
                    else:
                        t_mk = p.op("tensor", (lambda k, b_: lambda e: e.matmul(
                            ps[b_][:, :], Ub[:, :], Vfb[:, k * 512:(k + 1) * 512],
                            start=False, stop=True))(kc, b), deps=[t_s, cl[0], cl[1]])
                    t_e = p.op("scalar", (lambda h_, b_: lambda e: e.activation(
                        Pt[:, b_ * 512:(b_ + 1) * 512], ps[b_][:, :], AF.Exp,
                        bias=negM[:, h_:h_ + 1], scale=QKSC))(h, b),
                        deps=[t_mk] + pt_cons[b])
                    sfree[b] = [t_e]
                    t_pv = p.op("tensor", (lambda h_, kg_, b_, f, lst: lambda e: e.matmul(
                        ps[3][0:65, :],
                        V65[:, kg_ * 520 + h_ * 65:kg_ * 520 + h_ * 65 + 65],
                        Pt[:, b_ * 512:(b_ + 1) * 512],
                        start=f, stop=lst))(h, kg, b, i == 0, i == 7),
                        deps=[t_e, v_evac[(kg, h)]] + (o_free if i == 0 else []))
                    pt_cons[b] = [t_pv]
                    pvs.append(t_pv)
                t_rc = p.op("vector", lambda e: e.reciprocal(arec[0:1, :], ps[3][64:65, :]),
                            deps=[pvs[-1]] + ost_free)
                t_bc = p.op("tensor", lambda e: e.matmul(
                    ps[6][0:64, :], onesrow_f[0:1, 0:64], arec[0:1, :],
                    start=True, stop=True), deps=[t_rc, t_ones] + rb_free)
                t_ev = p.op("scalar", lambda e: e.copy(arecb[:, :], ps[6][0:64, :]),
                            deps=[t_bc])
                rb_free = [t_ev]
                sl2 = (h * len(RWn) + w) % 2
                t_on = p.op("vector", (lambda s2: lambda e: e.tensor_mul(
                    shm[0:64, s2 * 512:(s2 + 1) * 512],
                    ps[3][0:64, :], arecb[:, :]))(sl2),
                    deps=[t_ev] + shm_cons2[sl2])
                o_free = [t_on]
                ost_free = [t_on]
                t_os = p.dma("scalar", (lambda h_, w_, s2: lambda e: e.dma_start(
                    out=o_d[h_ * 64:(h_ + 1) * 64, w_ * 512:(w_ + 1) * 512],
                    in_=shm[0:64, s2 * 512:(s2 + 1) * 512]))(h, w, sl2),
                    deps=[t_on], stream="ost%d" % sl2)
                shm_cons2[sl2] = [t_os]
                on_toks[(h, w)] = t_os

        if TRUNC2 is not None and TRUNC2 <= l * 6 + 4:
            return
        resid_done = {}
        wo_cons = [[], []]
        rhb_cons2 = [[], []]
        op_rot = [[], []]
        ys_cons2 = [[], []]
        ost_cons2 = [[], []]
        for fi, fc in enumerate(range(8)):
            wsl = fc % 2
            t_wo = p.dma("gpsimd", (lambda ll, f, s: lambda e: e.dma_start(
                out=Wo[:, s * 512:s * 512 + 512].rearrange("p (c m) -> p c m", c=4),
                in_=fw_o[ll * 512:(ll + 1) * 512, f * 128:(f + 1) * 128].rearrange(
                    "(c p) m -> p c m", p=128)))(l, fc, wsl),
                deps=wo_cons[wsl], stream="wo%d" % wsl)
            wo_cons[wsl] = []
            for ti, tt in enumerate(RWn):
                rsl = 0
                t_r = p.dma("sync", (lambda ttt: lambda e: e.dma_start(
                    out=rhb[:, 0:2048].rearrange("p (c m) -> p c m", c=4),
                    in_=o_d[:, ttt * 512:(ttt + 1) * 512].rearrange(
                        "(c p) m -> p c m", p=128)))(tt),
                    deps=rhb_cons2[0] + [on_toks[(h, tt)] for h in range(8)],
                    stream="orb0")
                rhb_cons2[0] = []
                ysl = (fi * len(RWn) + ti) % 2
                t_y = p.dma("sync", (lambda f, ttt, s: lambda e: e.dma_start(
                    out=xs[:, s * 512:(s + 1) * 512],
                    in_=y_d[f * 128:(f + 1) * 128, ttt * 512:(ttt + 1) * 512]))(fc, tt, ysl),
                    deps=ys_cons2[ysl] + prev_done, stream="yrd%d" % ysl)
                ys_cons2[ysl] = []
                bnk = (fi * len(RWn) + ti) % 2
                mm = None
                for ic in range(4):
                    mm = p.op("tensor", (lambda s, k, b: lambda e: e.matmul(
                        ps[4 + b][:, :],
                        Wo[:, s * 512 + k * 128:s * 512 + (k + 1) * 128],
                        rhb[:, k * 512:(k + 1) * 512],
                        start=(k == 0), stop=(k == 3)))(wsl, ic, bnk),
                        deps=[t_wo, t_r] + (op_rot[bnk] if ic == 0 else []))
                osl = (fi * len(RWn) + ti) % 2
                ev = p.op("vector", (lambda b, s, s2: lambda e: e.tensor_add(
                    ost[:, s2 * 512:(s2 + 1) * 512], ps[4 + b][:, :],
                    xs[:, s * 512:(s + 1) * 512]))(bnk, ysl, osl),
                    deps=[mm, t_y] + ost_cons2[osl])
                op_rot[bnk] = [ev]
                rhb_cons2[0].append(ev)
                ys_cons2[ysl].append(ev)
                t_st = p.dma("sync", (lambda f, ttt, s2: lambda e: e.dma_start(
                    out=y_d[f * 128:(f + 1) * 128, ttt * 512:(ttt + 1) * 512],
                    in_=ost[:, s2 * 512:(s2 + 1) * 512]))(fc, tt, osl),
                    deps=[ev], stream="yst%d" % osl)
                ost_cons2[osl] = [t_st]
                wo_cons[wsl].append(ev)
                resid_done[(fc, tt)] = t_st
        prev_done = [resid_done[k] for k in resid_done]

        if TRUNC2 is not None and TRUNC2 <= l * 6 + 5:
            return
        sh2_toks = {}
        ys_cons = [prev_done, prev_done]
        shm_c = [shm_cons2[0] + [on_toks[(7, RWn[-1])]], shm_cons2[1]]
        for ti, tt in enumerate(RFr):
            for kc in range(8):
                sl = (ti * 8 + kc) % 2
                lo = tt * 512 - 1
                t_l = p.dma("sync", (lambda k, s, o: lambda e: e.dma_start(
                    out=ys[:, s * 513:(s + 1) * 513],
                    in_=y_d[k * 128:(k + 1) * 128, o:o + 513]))(kc, sl, lo),
                    deps=ys_cons[sl] + prev_done, stream="ys%d" % sl)
                off = 1 if kc < 4 else 0
                t_c = p.op("vector", (lambda s, o: lambda e: e.tensor_copy(
                    shm[:, s * 512:(s + 1) * 512],
                    ys[:, s * 513 + o:s * 513 + o + 512]))(sl, off),
                    deps=[t_l] + shm_c[sl])
                ys_cons[sl] = [t_c]
                t_s = p.dma("scalar", (lambda k, ttt, s2: lambda e: e.dma_start(
                    out=sh_d[k * 128:(k + 1) * 128, ttt * 512:(ttt + 1) * 512],
                    in_=shm[:, s2 * 512:(s2 + 1) * 512]))(kc, tt, sl),
                    deps=[t_c], stream="shst%d" % sl)
                shm_c[sl] = [t_s]
                sh2_toks[(kc, tt)] = t_s

        nt = len(RFr)
        sl_cons = [[], []]
        sqc = [[], []]
        lm1 = lm2 = None
        for kc in range(8):
            for ti, tt in enumerate(RFr):
                sl = (kc * nt + ti) % 2
                ld = p.dma("gpsimd", (lambda k, ttt, s: lambda e: e.dma_start(
                    out=hlb[:, s * 512:(s + 1) * 512],
                    in_=sh_d[k * 128:(k + 1) * 128, ttt * 512:(ttt + 1) * 512]))(kc, tt, sl),
                    deps=sl_cons[sl] + [sh2_toks[(kc, tt)]], stream="hlb%d" % sl)
                sl_cons[sl] = []
                t_sq = p.op("scalar", (lambda s: lambda e: e.activation(
                    sqs[:, s * 512:(s + 1) * 512], hlb[:, s * 512:(s + 1) * 512],
                    AF.Square))(sl), deps=[ld] + sqc[sl])
                sqc[sl] = []
                first = (kc == 0 and ti == 0)
                lastt = (kc == 7 and ti == nt - 1)
                lm1 = p.op("tensor", (lambda s, ii, f, lst: lambda e: e.matmul(
                    ps[0][0:nt, :], conesbb[:, 6 - ii: 6 - ii + nt],
                    hlb[:, s * 512:(s + 1) * 512], start=f, stop=lst))(sl, ti, first, lastt),
                    deps=[ld, cl[16]] + ([negm_tok] if first else []))
                lm2 = p.op("tensor", (lambda s, ii, f, lst: lambda e: e.matmul(
                    ps[1][0:nt, :], conesb[:, 6 - ii: 6 - ii + nt],
                    sqs[:, s * 512:(s + 1) * 512], start=f, stop=lst))(sl, ti, first, lastt),
                    deps=[t_sq, cl[7]] + ([negm_tok] if first else []))
                sqc[sl] = [lm2]
                sl_cons[sl] += [t_sq, lm1]
        sA = sqs[0:nt, 0:512]
        sB = sqs[0:nt, 512:1024]
        sC = app_scr[0:nt, 0:512]
        sD = app_scr[0:nt, 512:1024]
        u1 = p.op("vector", lambda e: e.tensor_scalar_mul(sA, ps[0][0:nt, :], 1.0 / D), deps=[lm1, lm2])
        u2 = p.op("vector", lambda e: e.tensor_scalar_mul(sB, ps[1][0:nt, :], 1.0 / D), deps=[lm2, u1])
        u3 = p.op("vector", lambda e: e.tensor_mul(sC, sA, sA), deps=[u1])
        u4 = p.op("vector", lambda e: e.tensor_sub(sC, sB, sC), deps=[u2, u3])
        u5 = p.op("vector", lambda e: e.tensor_scalar_add(sD, sC, 1e-5), deps=[u4])
        u6 = p.op("scalar", lambda e: e.activation(sD, sD, AF.Sqrt), deps=[u5])
        u7 = p.op("vector", lambda e: e.reciprocal(sD, sD), deps=[u6])
        u8 = p.op("vector", lambda e: e.scalar_tensor_tensor(sA, sA, -1.0, sD, ALU.mult, ALU.mult),
                  deps=[u7, u3])
        v1 = p.dma("sync", (lambda: lambda e: e.dma_start(out=rt_d[0:nt, :], in_=sD))(),
                   deps=[u7, u8], stream="irt%d" % l)
        v2 = p.dma("sync", (lambda: lambda e: e.dma_start(out=rt_d[7:7 + nt, :], in_=sA))(),
                   deps=[u8], stream="irt%d" % l)
        hl_toks = {}
        sl_cons = [[], []]
        scrc = [[], []]
        shm_c3 = [shm_c[0], shm_c[1]]
        rtb = [[], []]
        bcc = [[], []]
        ps23_free = []
        for ti, tt in enumerate(RFr):
            sl_ = 0
            bsl = ti % 2
            q1 = p.dma("sync", (lambda ii, s: lambda e: e.dma_start(
                out=rt_row[0:1, s * 1024:s * 1024 + 512], in_=rt_d[ii:ii + 1, :]))(ti, sl_),
                deps=[v1] + rtb[sl_], stream="irtl%d%d" % (sl_, l))
            q2 = p.dma("sync", (lambda ii, s: lambda e: e.dma_start(
                out=rt_row[0:1, s * 1024 + 512:s * 1024 + 1024], in_=rt_d[7 + ii:8 + ii, :]))(ti, sl_),
                deps=[v2] + rtb[sl_], stream="irtl%d%d" % (sl_, l))
            b1_ = p.op("tensor", (lambda s: lambda e: e.matmul(
                ps[2][:, :], onesrow_f[:, :], rt_row[0:1, s * 1024:s * 1024 + 512],
                start=True, stop=True))(sl_),
                deps=[q1] + ([negm_tok] if ti == 0 else []) + ps23_free)
            e1 = p.op("scalar", (lambda b_: lambda e: e.copy(
                Sbb[:, b_ * 512:(b_ + 1) * 512], ps[2][:, :]))(bsl), deps=[b1_] + bcc[bsl])
            b2_ = p.op("tensor", (lambda s: lambda e: e.matmul(
                ps[3][:, :], onesrow_f[:, :], rt_row[0:1, s * 1024 + 512:s * 1024 + 1024],
                start=True, stop=True))(sl_),
                deps=[q2] + ([negm_tok] if ti == 0 else []) + ps23_free)
            e2 = p.op("scalar", (lambda b_: lambda e: e.copy(
                Bbb[:, b_ * 512:(b_ + 1) * 512], ps[3][:, :]))(bsl), deps=[b2_] + bcc[bsl])
            ps23_free = [e1, e2]
            rtb[sl_] = [b1_, b2_]
            bcc[bsl] = []
            for kc in range(8):
                sl = (ti * 8 + kc) % 2
                ld = p.dma("gpsimd", (lambda k, ttt, s: lambda e: e.dma_start(
                    out=hlb[:, s * 512:(s + 1) * 512],
                    in_=sh_d[k * 128:(k + 1) * 128, ttt * 512:(ttt + 1) * 512]))(kc, tt, sl),
                    deps=sl_cons[sl] + [lm1, lm2], stream="hlb%d" % sl)
                sl_cons[sl] = []
                g1 = p.op("vector", (lambda k, b_, s, ll: lambda e: e.scalar_tensor_tensor(
                    app_scr[:, s * 512:(s + 1) * 512],
                    hlb[:, s * 512:(s + 1) * 512],
                    lg_sb[:, ll * 8 + k:ll * 8 + k + 1],
                    Sbb[:, b_ * 512:(b_ + 1) * 512], ALU.mult, ALU.mult))(kc, bsl, sl, l),
                    deps=[ld, e1, v1, v2, cl[11]] + scrc[sl])
                g2 = p.op("vector", (lambda k, b_, s, ll: lambda e: e.scalar_tensor_tensor(
                    app_scr[:, s * 512:(s + 1) * 512],
                    Bbb[:, b_ * 512:(b_ + 1) * 512],
                    lg_sb[:, ll * 8 + k:ll * 8 + k + 1],
                    app_scr[:, s * 512:(s + 1) * 512], ALU.mult, ALU.add))(kc, bsl, sl, l),
                    deps=[g1, e2])
                g3 = p.op("vector", (lambda k, s, ll: lambda e: e.tensor_scalar_add(
                    shm[:, s * 512:(s + 1) * 512],
                    app_scr[:, s * 512:(s + 1) * 512],
                    lb_sb[:, ll * 8 + k:ll * 8 + k + 1]))(kc, sl, l),
                    deps=[g2, cl[12]] + shm_c3[sl])
                scrc[sl] = [g3]
                sl_cons[sl].append(g3)
                bcc[bsl].append(g3)
                t_s = p.dma("scalar", (lambda k, ttt, s: lambda e: e.dma_start(
                    out=hl_d[k * 128:(k + 1) * 128, ttt * 512:(ttt + 1) * 512],
                    in_=shm[:, s * 512:(s + 1) * 512]))(kc, tt, sl),
                    deps=[g3], stream="hlst%d" % sl)
                shm_c3[sl] = [t_s]
                hl_toks[(kc, tt)] = t_s

        wac = [[], []]
        h_cons = {}
        rhb_c3 = [[], []]
        ysc = [[], []]
        ostc = [[], []]
        w2c = [[], []]
        gact_c = [[], []]
        ffn_done = {}
        for ti, tt in enumerate(RFr):
            rsl = 0
            t_r = p.dma("sync", (lambda ttt: lambda e: e.dma_start(
                out=rhb[:, :].rearrange("p (c m) -> p c m", c=8),
                in_=hl_d[:, ttt * 512:(ttt + 1) * 512].rearrange(
                    "(c p) m -> p c m", p=128)))(tt),
                deps=rhb_c3[0] + [hl_toks[(kc, tt)] for kc in range(8)], stream="frb0")
            rhb_c3[0] = []
            for hc in range(22):
                wd = 128 if hc < 21 else 42
                wsl = hc % 2
                depw = wac[wsl]
                t_wa = p.dma("gpsimd", (lambda ll, h_, s, w_: lambda e: e.dma_start(
                    out=wa[:, s * 1024:s * 1024 + 8 * w_].rearrange("p (c m) -> p c m", c=8),
                    in_=fw_1[ll * D:(ll + 1) * D, h_ * 128:h_ * 128 + w_].rearrange(
                        "(c p) m -> p c m", p=128)))(l, hc, wsl, wd),
                    deps=depw, stream="wa%d" % wsl)
                t_wg = p.dma("gpsimd", (lambda ll, h_, s, w_: lambda e: e.dma_start(
                    out=wg[:, s * 1024:s * 1024 + 8 * w_].rearrange("p (c m) -> p c m", c=8),
                    in_=fw_1[ll * D:(ll + 1) * D, 2730 + h_ * 128:2730 + h_ * 128 + w_].rearrange(
                        "(c p) m -> p c m", p=128)))(l, hc, wsl, wd),
                    deps=depw, stream="wg%d" % wsl)
                wac[wsl] = []
                bnk = hc % 2
                mma = mmg = None
                for kc in range(8):
                    mma = p.op("tensor", (lambda s, k, b, w_: lambda e: e.matmul(
                        ps[4 + b][0:w_, :],
                        wa[:, s * 1024 + k * w_:s * 1024 + (k + 1) * w_],
                        rhb[:, k * 512:(k + 1) * 512],
                        start=(k == 0), stop=(k == 7)))(wsl, kc, bnk, wd),
                        deps=[t_wa, t_r] + (h_cons.get((bnk, "a"), []) if kc == 0 else []))
                    mmg = p.op("tensor", (lambda s, k, b, w_: lambda e: e.matmul(
                        ps[6 + b][0:w_, :],
                        wg[:, s * 1024 + k * w_:s * 1024 + (k + 1) * w_],
                        rhb[:, k * 512:(k + 1) * 512],
                        start=(k == 0), stop=(k == 7)))(wsl, kc, bnk, wd),
                        deps=[t_wg, t_r] + (h_cons.get((bnk, "g"), []) if kc == 0 else []))
                gsl = hc % 2
                t_g = p.op("scalar", (lambda b, s, w_, h_, ll: lambda e: e.activation(
                    gact[0:w_, s * 512:s * 512 + 512], ps[6 + b][0:w_, :], AF.Gelu,
                    bias=b1g_sb[0:w_, ll * 22 + h_:ll * 22 + h_ + 1]))(bnk, gsl, wd, hc, l),
                    deps=[mmg, cl[14]] + gact_c[gsl])
                t_h = p.op("vector", (lambda b, s, w_, h_, ll: lambda e: e.scalar_tensor_tensor(
                    h_sb[0:w_, h_ * 512:(h_ + 1) * 512], ps[4 + b][0:w_, :],
                    b1a_sb[0:w_, ll * 22 + h_:ll * 22 + h_ + 1],
                    gact[0:w_, s * 512:s * 512 + 512], ALU.add, ALU.mult))(bnk, gsl, wd, hc, l),
                    deps=[t_g, mma, cl[13]] + h_cons.get(("h", hc), []))
                gact_c[gsl] = [t_h]
                h_cons[(bnk, "a")] = [t_h]
                h_cons[(bnk, "g")] = [t_g]
                h_cons[("h", hc)] = []
                wac[wsl] = [mma, mmg]
                h_cons.setdefault("tile", []).append(t_h)
            h_all = h_cons.pop("tile")
            rhb_c3[0] = list(h_all)
            for fc in range(8):
                wsl2 = 0
                t_w2a = p.dma("gpsimd", (lambda ll, f: lambda e: e.dma_start(
                    out=w2b[:, 0:2688].rearrange("p (c m) -> p c m", c=21),
                    in_=fw_2[ll * 2730:ll * 2730 + 2688, f * 128:(f + 1) * 128].rearrange(
                        "(c p) m -> p c m", p=128)))(l, fc),
                    deps=w2c[0], stream="w2s")
                t_w2b = p.dma("gpsimd", (lambda ll, f: lambda e: e.dma_start(
                    out=w2b[0:42, 2688:2816],
                    in_=fw_2[ll * 2730 + 2688:ll * 2730 + 2730, f * 128:(f + 1) * 128]))(l, fc),
                    deps=w2c[0], stream="w2s")
                w2c[0] = []
                ysl = fc % 2
                t_y = p.dma("sync", (lambda f, ttt, s: lambda e: e.dma_start(
                    out=xs[:, s * 512:(s + 1) * 512],
                    in_=y_d[f * 128:(f + 1) * 128, ttt * 512:(ttt + 1) * 512]))(fc, tt, ysl),
                    deps=ysc[ysl] + [resid_done[(fc, tt)]] if l == 0 or True else [], stream="fyr%d" % ysl)
                ysc[ysl] = []
                bnk2 = fc % 2
                mm = None
                for hc in range(22):
                    wd = 128 if hc < 21 else 42
                    mm = p.op("tensor", (lambda h_, b, w_: lambda e: e.matmul(
                        ps[0 + b][:, :],
                        w2b[0:w_, h_ * 128:h_ * 128 + 128],
                        h_sb[0:w_, h_ * 512:(h_ + 1) * 512],
                        start=(h_ == 0), stop=(h_ == 21)))(hc, bnk2, wd),
                        deps=[t_w2a, t_w2b, h_all[hc]] + (ostc[bnk2] if hc == 0 else []))
                w2c[0] = [mm]
                osl = fc % 2
                ev = p.op("vector", (lambda b, sy, s2, f, ll: lambda e: e.scalar_tensor_tensor(
                    ost[:, s2 * 512:(s2 + 1) * 512], ps[0 + b][:, :],
                    b2_sb[:, ll * 8 + f:ll * 8 + f + 1],
                    xs[:, sy * 512:(sy + 1) * 512], ALU.add, ALU.add))(bnk2, ysl, osl, fc, l),
                    deps=[mm, t_y, cl[15]])
                ostc[bnk2] = [ev]
                ysc[ysl].append(ev)
                t_st = p.dma("sync", (lambda f, ttt, s2: lambda e: e.dma_start(
                    out=y_d[f * 128:(f + 1) * 128, ttt * 512:(ttt + 1) * 512],
                    in_=ost[:, s2 * 512:(s2 + 1) * 512]))(fc, tt, osl),
                    deps=[ev], stream="fys%d" % osl)
                ostc[bnk2] = [ev, t_st]
                ysc[ysl].append(t_st)
                ffn_done[(fc, tt)] = t_st
            for hc in range(22):
                h_cons[("h", hc)] = [mm]
        prev_done = [ffn_done[k] for k in ffn_done]

    if TRUNC2 is not None and TRUNC2 <= 12:
        return
    f_ys = [[], []]
    f_xs = [[], []]
    f_os = [[], []]
    for kc in range(8):
        for tt in range(4):
            sl = (kc * 4 + tt) % 2
            t_a = p.dma("sync", (lambda k, t_, s: lambda e: e.dma_start(
                out=xs[:, s * 512:(s + 1) * 512],
                in_=x1h[k * 128:(k + 1) * 128, 1536 + t_ * 512:1536 + (t_ + 1) * 512]))(kc, tt, sl),
                deps=f_xs[sl] + prev_done, stream="fx%d" % sl)
            t_b = p.dma("sync", (lambda k, t_, s: lambda e: e.dma_start(
                out=ys[:, s * 513:s * 513 + 512],
                in_=y_d[k * 128:(k + 1) * 128, 1536 + t_ * 512:1536 + (t_ + 1) * 512]))(kc, tt, sl),
                deps=f_ys[sl] + prev_done, stream="fy2%d" % sl)
            ev = p.op("vector", (lambda s: lambda e: e.tensor_add(
                ost[:, s * 512:(s + 1) * 512], xs[:, s * 512:(s + 1) * 512],
                ys[:, s * 513:s * 513 + 512]))(sl), deps=[t_a, t_b] + f_os[sl])
            f_xs[sl] = [ev]
            f_ys[sl] = [ev]
            st = p.out_dma("sync", (lambda k, t_, s: lambda e: e.dma_start(
                out=x2_fm[k * 128:(k + 1) * 128, t_ * 512:(t_ + 1) * 512],
                in_=ost[:, s * 512:(s + 1) * 512]))(kc, tt, sl),
                deps=[ev], stream="fst%d" % sl)
            f_os[sl] = [st]


def build_l0_ffn():
    p = Prog("l0ffn")
    emit_l0_ffn(p)
    return p


def prep_l0_ffn_inputs(inputs, x1_full):
    C22 = float(2 ** 22)
    U = np_bf((np.arange(128)[:, None] <= np.arange(128)[None, :]).astype(np.float32))
    Ua = np_bf((np.arange(128)[:, None] > np.arange(128)[None, :]).astype(np.float32))
    j = np.arange(128)[:, None]
    q = np.arange(512)[None, :]
    Vfam = np.concatenate(
        [np_bf(np.where(q + (-128 * kc) < j, -C22, 0.0)) for kc in range(4)], axis=1)
    Vafam = np.concatenate(
        [np_bf(np.where(j <= q + (-128 * kc), -C22, 0.0)) for kc in range(4)], axis=1)
    selb = np.zeros((128, 31), dtype=np.float32)
    selb[0:64, 15] = 1.0
    selb[64:128, 16] = 1.0
    sel2T = np.zeros((8, 4 * 128), dtype=np.float32)
    for cc in range(4):
        sel2T[2 * cc, cc * 128:cc * 128 + 64] = 1.0
        sel2T[2 * cc + 1, cc * 128 + 64:cc * 128 + 128] = 1.0
    cones = np.zeros((128, 13), dtype=np.float32)
    cones[:, 6] = 1.0
    R2 = rot_half_R()
    qs = inputs["L0_ffn_q_scale"]
    ks = inputs["L0_ffn_k_scale"]
    f_qs = np.stack([np.tile(qs[l], 2) for l in range(2)], axis=1).astype(np.float32)
    f_ks = np.stack([np.tile(ks[l], 2) for l in range(2)], axis=1).astype(np.float32)
    def chunked(v, n):
        return np.ascontiguousarray(v.reshape(n, 128).T).astype(np.float32)
    f_lg = np.concatenate([chunked(inputs["L0_ffn_ln_g"][l], 8) for l in range(2)], axis=1)
    f_lb = np.concatenate([chunked(inputs["L0_ffn_ln_b"][l], 8) for l in range(2)], axis=1)
    b1 = inputs["L0_ffn_b1"]
    f_b1a = np.zeros((128, 44), dtype=np.float32)
    f_b1g = np.zeros((128, 44), dtype=np.float32)
    for l in range(2):
        for hc in range(22):
            wd = 128 if hc < 21 else 42
            f_b1a[0:wd, l * 22 + hc] = b1[l, hc * 128:hc * 128 + wd]
            f_b1g[0:wd, l * 22 + hc] = b1[l, 2730 + hc * 128:2730 + hc * 128 + wd]
    f_b2 = np.concatenate([chunked(inputs["L0_ffn_b2"][l], 8) for l in range(2)], axis=1)
    fw_qkv = np.ascontiguousarray(inputs["L0_ffn_wqkv"].reshape(2 * D, 1536))
    fw_o = np.ascontiguousarray(inputs["L0_ffn_wo"].reshape(2 * 512, D))
    fw_1 = np.ascontiguousarray(inputs["L0_ffn_w1"].reshape(2 * D, 5460))
    fw_2 = np.ascontiguousarray(inputs["L0_ffn_w2"].reshape(2 * 2730, D))
    in_maps = []
    for c in range(NCORE):
        b, half = c // 2, c % 2
        s = half * T
        pos = np.arange(s - 1536, s + 2048)
        tabs = xpos_tables(pos)
        x1h = np.zeros((D, W2), dtype=np.float32)
        lo = max(0, s - 1536)
        x1h[:, lo - (s - 1536):] = x1_full[b, lo:s + 2048, :].T
        pw = np.zeros((1, 7 * 128), dtype=np.float32)
        if half == 0:
            pw[0, 3 * 128:4 * 128] = -C22
        in_maps.append(dict(
            x1h=x1h, fw_qkv=fw_qkv, fw_o=fw_o, fw_1=fw_1, fw_2=fw_2,
            f_qs=f_qs, f_ks=f_ks, f_lg=f_lg, f_lb=f_lb,
            f_b1a=f_b1a, f_b1g=f_b1g, f_b2=f_b2,
            xq_cs=np_bf(tabs[0]), xq_ss=np_bf(tabs[1]),
            xk_cs=np_bf(tabs[2]), xk_ss=np_bf(tabs[3]),
            c_U=U, c_Vfam=Vfam, c_Ua=Ua, c_Vafam=Vafam,
            c_selb=selb, c_sel2T=sel2T, c_R2=R2, c_cones=cones,
            c_conesb=np_bf(cones), c_pw=np_bf(pw),
        ))
    return in_maps


W3 = 4096


def emit_l1_attn(p: Prog):
    xfull = p.get_in("xfull", [D, W3], F32)
    wq = p.get_in("wq", [D, 512], F32)
    wk = p.get_in("wk", [D, 512], F32)
    wv = p.get_in("wv", [D, 512], F32)
    bq = p.get_in("bq", [128, 4], F32)
    bk = p.get_in("bk", [128, 4], F32)
    bv = p.get_in("bv", [128, 4], F32)
    bv_row = p.get_in("bv_row", [1, 512], BF16)
    cU = p.get_in("c_U", [128, 128], BF16)
    cVfam = p.get_in("c_Vfam", [128, 4 * 512], BF16)
    cselb = p.get_in("c_selb", [128, 31], F32)
    ccones = p.get_in("c_cones", [128, 15], F32)
    o_out = p.get_out("o_out", [512, W3], BF16)
    xln_d = p.dram("xln_d", [D, W3], F32)
    rt_d = p.dram("rt_d", [16, 512], F32)
    mrow_d = p.dram("mrow_d", [8, 1], F32)

    xs = p.sb("xs", [128, 2 * 512], F32)
    sqs = p.sb("sqs", [128, 2 * 512], F32)
    app_scr = p.sb("app_scr", [128, 2 * 512], F32)
    rt_row = p.sb("rt_row", [1, 1024], F32)
    ost = p.sb("ost", [128, 2 * 512], F32)
    Sbb = p.sb("Sbb", [128, 2 * 512], BF16)
    Bbb = p.sb("Bbb", [128, 2 * 512], BF16)
    Qf = p.sb("Qf", [128, 4 * W3], BF16)
    Kf = p.sb("Kf", [128, 4 * W3], BF16)
    V65 = p.sb("V65", [128, 32 * 520], BF16)
    Wqb = p.sb("Wqb", [128, 8 * 512], BF16)
    Wkb = p.sb("Wkb", [128, 8 * 512], BF16)
    Wvb = p.sb("Wvb", [128, 8 * 512], BF16)
    rhb = p.sb("rhb", [128, 2 * 8 * 512], BF16)
    Ub = p.sb("Ub", [128, 128], BF16)
    Vfb = p.sb("Vfb", [128, 4 * 512], BF16)
    selb = p.sb("selb", [128, 31], F32)
    conesb = p.sb("conesb", [128, 15], F32)
    onesrow_f = p.sb("onesrow_f", [1, 128], F32)
    onesrow_b2 = p.sb("onesrow_b2", [1, 128], BF16)
    bvrow_sb = p.sb("bvrow_sb", [1, 512], BF16)
    bq_sb = p.sb("bq_sb", [128, 4], F32)
    bk_sb = p.sb("bk_sb", [128, 4], F32)
    bv_sb = p.sb("bv_sb", [128, 4], F32)
    qsmax = p.sb("qsmax", [16, 1], F32)
    ksmax = p.sb("ksmax", [16, 1], F32)
    hred = p.sb("hred", [16, 1], F32)
    negM = p.sb("negM", [128, 8], F32)
    mrow = p.sb("mrow", [1, 8], F32)
    Pt = p.sb("Pt", [128, 2 * 512], BF16)
    arecb = p.sb("arecb", [64, 512], F32)
    arec = p.sb("arec", [1, 512], F32)
    oshm = p.sb("oshm", [128, 2 * 512], BF16)

    ps = [p.ps(f"ps{i}", [128, 512], F32) for i in range(8)]

    cl = [emit_load2(p, Ub, (0, 128), cU[:, :], stream="konst"),
          emit_load2(p, Vfb, (0, 2048), cVfam[:, :], stream="konst"),
          emit_load2(p, selb, (0, 31), cselb[:, :], stream="konst"),
          emit_load2(p, conesb, (0, 15), ccones[:, :], stream="konst"),
          emit_load2(p, bq_sb, (0, 4), bq[:, :], stream="konst"),
          emit_load2(p, bk_sb, (0, 4), bk[:, :], stream="konst"),
          emit_load2(p, bv_sb, (0, 4), bv[:, :], stream="konst"),
          emit_load2(p, bvrow_sb, (0, 512), bv_row[:, :], stream="konst")]
    wl = [p.dma("gpsimd", (lambda w_, dst: lambda e: e.dma_start(
        out=dst[:, :].rearrange("p (c m) -> p c m", c=8),
        in_=w_.rearrange("(c p) m -> p c m", p=128)))(w_, dst), stream="konstg")
        for (w_, dst) in ((wq, Wqb), (wk, Wkb), (wv, Wvb))]
    t_ones = p.op("vector", lambda e: e.memset(onesrow_f[:, :], 1.0))
    t_onesb2 = p.op("vector", lambda e: e.memset(onesrow_b2[:, :], 1.0))
    t_v65 = p.op("vector", lambda e: e.memset(V65[:, :], 1.0))

    bkk = {"sx": ps[0], "sx2": ps[1], "b1": ps[2], "b2": ps[3]}
    xln_toks, ln_done = emit_module_ln(
        p, xfull, bkk, xs, sqs, conesb, onesrow_f, rt_d, rt_row, Sbb, Bbb,
        app_scr, None, nw_sb=None, rms_eps=None, eps=1e-5,
        pre_deps=[cl[3], t_ones], tag="m", W=W3, out_dram=xln_d, ost=ost)
    prev = [xln_toks[(kc, tt)] for kc in range(8) for tt in range(8)]

    qk_evac = {}
    v_evac = {}
    rhb_cons = [[], []]
    rot = [[], []]
    vrot = [[], []]
    for tt in range(8):
        rsl = tt % 2
        t_r = p.dma("gpsimd", (lambda ttt, s: lambda e: e.dma_start(
            out=rhb[:, s * 4096:(s + 1) * 4096].rearrange("p (c m) -> p c m", c=8),
            in_=xln_d[:, ttt * 512:(ttt + 1) * 512].rearrange(
                "(c p) m -> p c m", p=128)))(tt, rsl),
            deps=rhb_cons[rsl] + prev, stream="rhbg%d" % rsl)
        rhb_cons[rsl] = []
        for (ix, Wb, dstT, bias) in ((0, Wqb, Qf, bq_sb), (1, Wkb, Kf, bk_sb)):
            for fcq in range(4):
                bnk = fcq % 2
                mm = None
                for kc in range(8):
                    mm = p.op("tensor", (lambda W_, f, k, b, r: lambda e: e.matmul(
                        ps[4 + b][:, :],
                        W_[:, k * 512 + f * 128:k * 512 + (f + 1) * 128],
                        rhb[:, r * 4096 + k * 512:r * 4096 + (k + 1) * 512],
                        start=(k == 0), stop=(k == 7)))(Wb, fcq, kc, bnk, rsl),
                        deps=[wl[ix], t_r] + (rot[bnk] if kc == 0 else []))
                ev = p.op("vector", (lambda dT, f, t_, b, bi: lambda e: e.tensor_scalar_add(
                    dT[:, f * W3 + t_ * 512:f * W3 + (t_ + 1) * 512], ps[4 + b][:, :],
                    bi[:, f:f + 1]))(dstT, fcq, tt, bnk, bias),
                    deps=[mm, cl[4 + ix]])
                rot[bnk] = [ev]
                rhb_cons[rsl].append(ev)
                qk_evac[(ix, fcq, tt)] = ev
        for j in range(4):
            vb = j % 2
            mmv = None
            for kc in range(8):
                mmv = p.op("tensor", (lambda jj, k, b, r: lambda e: e.matmul(
                    ps[6 + b][:, :],
                    rhb[:, r * 4096 + k * 512 + jj * 128:r * 4096 + k * 512 + (jj + 1) * 128],
                    Wvb[:, k * 512:(k + 1) * 512],
                    start=(k == 0), stop=False))(j, kc, vb, rsl),
                    deps=[wl[2], t_r] + (vrot[vb] if kc == 0 else []))
            mmv = p.op("tensor", (lambda b: lambda e: e.matmul(
                ps[6 + b][:, :], onesrow_b2[:, :], bvrow_sb[0:1, :],
                start=False, stop=True))(vb), deps=[mmv, t_onesb2, cl[7]])
            tg = tt * 4 + j
            vrot[vb] = []
            for hh in range(8):
                evv = p.op("scalar", (lambda tgg, hhh, b: lambda e: e.copy(
                    V65[:, tgg * 520 + hhh * 65:tgg * 520 + hhh * 65 + 64],
                    ps[6 + b][:, hhh * 64:(hhh + 1) * 64]))(tg, hh, vb),
                    deps=[mmv, t_v65])
                vrot[vb].append(evv)
                rhb_cons[rsl].append(evv)
                v_evac[(tg, hh)] = evv


    sqc = [[], []]
    bankf = []
    lastup = None
    for ix, (tens, outmax) in enumerate(((Qf, qsmax), (Kf, ksmax))):
        for tt in range(8):
            mm = None
            for cc in range(4):
                sl = cc % 2
                sq = p.op("scalar", (lambda s_, c_, t_, s2: lambda e: e.activation(
                    sqs[:, s2 * 512:(s2 + 1) * 512],
                    s_[:, c_ * W3 + t_ * 512:c_ * W3 + (t_ + 1) * 512],
                    AF.Square))(tens, cc, tt, sl),
                    deps=[qk_evac[(ix, cc, tt)]] + sqc[sl] + ln_done[-1:])
                mm = p.op("tensor", (lambda c_, s2, f, lst: lambda e: e.matmul(
                    ps[2][0:16, :], selb[:, 15 - 2 * c_:31 - 2 * c_],
                    sqs[:, s2 * 512:(s2 + 1) * 512],
                    start=f, stop=lst))(cc, sl, cc == 0, cc == 3),
                    deps=[sq, cl[2]] + (bankf if cc == 0 else []))
                sqc[sl] = [mm]
            rd = p.op("vector", lambda e: e.reduce_max(hred[:, :], ps[2][0:16, :], axis=AX.X),
                      deps=[mm] + ([lastup] if lastup else []))
            if tt == 0:
                up = p.op("vector", (lambda om: lambda e: e.tensor_copy(om[:, :], hred[:, :]))(outmax),
                          deps=[rd])
            else:
                up = p.op("vector", (lambda om: lambda e: e.tensor_max(om[:, :], om[:, :], hred[:, :]))(outmax),
                          deps=[rd])
            bankf = [rd]
            lastup = up
    s3 = p.op("vector", lambda e: e.tensor_mul(hred[:, :], qsmax[:, :], ksmax[:, :]), deps=[lastup])
    s4 = p.op("scalar", lambda e: e.activation(hred[:, :], hred[:, :], AF.Sqrt), deps=[s3])
    s5 = p.op("vector", lambda e: e.tensor_scalar_mul(hred[:, :], hred[:, :], -0.1250002), deps=[s4])
    w = p.dma("sync", lambda e: e.dma_start(out=mrow_d[:, :], in_=hred[0:8, :]), deps=[s5], stream="mrt")
    r = p.dma("sync", lambda e: e.dma_start(out=mrow[:, :], in_=mrow_d.rearrange("p o -> o p")),
              deps=[w], stream="mrt")
    bcm = p.op("tensor", lambda e: e.matmul(ps[2][:, 0:8], onesrow_f[:, :], mrow[:, :],
                                            start=True, stop=True), deps=[r, t_ones, bankf[0]])
    negm_tok = p.op("scalar", lambda e: e.copy(negM[:, :], ps[2][:, 0:8]), deps=[bcm])

    sfree = [[negm_tok], [negm_tok]]
    pt_cons = [[], []]
    o_free = [negm_tok]
    rb_free = [qk_evac[(1, 3, 7)]]
    oshm_c = [[], []]
    ost_free = []
    for h in range(8):
        base = (h % 2) * 64
        ch = h // 2
        for qt in range(8):
            nkc = 4 * (qt + 1)
            pvs = []
            for i in range(nkc):
                b = i % 2
                t_s = p.op("tensor", (lambda bb, c_, q_, kg_, b_: lambda e: e.matmul(
                    ps[b_][:, :],
                    Kf[bb:bb + 64, c_ * W3 + kg_ * 128:c_ * W3 + (kg_ + 1) * 128],
                    Qf[bb:bb + 64, c_ * W3 + q_ * 512:c_ * W3 + (q_ + 1) * 512],
                    start=True, stop=(kg_ < q_ * 4)))(base, ch, qt, i, b),
                    deps=[qk_evac[(0, ch, qt)], qk_evac[(1, ch, i // 4)]] + sfree[b])
                if i >= qt * 4:
                    t_s = p.op("tensor", (lambda k, b_: lambda e: e.matmul(
                        ps[b_][:, :], Ub[:, :], Vfb[:, k * 512:(k + 1) * 512],
                        start=False, stop=True))(i - qt * 4, b), deps=[t_s, cl[0], cl[1]])
                t_e = p.op("scalar", (lambda h_, b_: lambda e: e.activation(
                    Pt[:, b_ * 512:(b_ + 1) * 512], ps[b_][:, :], AF.Exp,
                    bias=negM[:, h_:h_ + 1], scale=0.125))(h, b),
                    deps=[t_s] + pt_cons[b])
                sfree[b] = [t_e]
                t_pv = p.op("tensor", (lambda h_, kg_, b_, f, lst: lambda e: e.matmul(
                    ps[3][0:65, :],
                    V65[:, kg_ * 520 + h_ * 65:kg_ * 520 + h_ * 65 + 65],
                    Pt[:, b_ * 512:(b_ + 1) * 512],
                    start=f, stop=lst))(h, i, b, i == 0, i == nkc - 1),
                    deps=[t_e, v_evac[(i, h)]] + (o_free if i == 0 else []))
                pt_cons[b] = [t_pv]
                pvs.append(t_pv)
            t_rc = p.op("vector", lambda e: e.reciprocal(arec[0:1, :], ps[3][64:65, :]),
                        deps=[pvs[-1]] + ost_free)
            t_bc = p.op("tensor", lambda e: e.matmul(
                ps[6][0:64, :], onesrow_f[0:1, 0:64], arec[0:1, :], start=True, stop=True),
                deps=[t_rc, t_ones] + rb_free)
            t_ev = p.op("scalar", lambda e: e.copy(arecb[:, :], ps[6][0:64, :]), deps=[t_bc])
            rb_free = [t_ev]
            sl2 = (h * 8 + qt) % 2
            t_on = p.op("vector", (lambda s2: lambda e: e.tensor_mul(
                oshm[0:64, s2 * 512:(s2 + 1) * 512], ps[3][0:64, :], arecb[:, :]))(sl2),
                deps=[t_ev] + oshm_c[sl2])
            o_free = [t_on]
            ost_free = [t_on]
            t_os = p.out_dma("scalar", (lambda h_, q_, s2: lambda e: e.dma_start(
                out=o_out[h_ * 64:(h_ + 1) * 64, q_ * 512:(q_ + 1) * 512],
                in_=oshm[0:64, s2 * 512:(s2 + 1) * 512]))(h, qt, sl2),
                deps=[t_on], stream="oos%d" % sl2)
            oshm_c[sl2] = [t_os]


def build_l1_attn():
    p = Prog("l1attn")
    emit_l1_attn(p)
    return p


def prep_l1_attn_inputs(inputs, x2_full):
    U = np_bf((np.arange(128)[:, None] <= np.arange(128)[None, :]).astype(np.float32))
    Vfam = np.concatenate([causal_mask_V(-128 * kc) for kc in range(4)], axis=1)
    selb = np.zeros((128, 31), dtype=np.float32)
    selb[0:64, 15] = 1.0
    selb[64:128, 16] = 1.0
    cones = np.zeros((128, 15), dtype=np.float32)
    cones[:, 7] = 1.0
    Wa = inputs["L1_attn_W_attn"]
    ba = inputs["L1_attn_b_attn"]
    in_maps = []
    for c in range(NCORE):
        b, hg = c // 2, c % 2
        cs = hg * 512
        in_maps.append(dict(
            xfull=np.ascontiguousarray(x2_full[b].T),
            wq=np.ascontiguousarray(Wa[:, cs:cs + 512]),
            wk=np.ascontiguousarray(Wa[:, 1024 + cs:1024 + cs + 512]),
            wv=np.ascontiguousarray(Wa[:, 2048 + cs:2048 + cs + 512]),
            bq=np.ascontiguousarray(ba[cs:cs + 512].reshape(4, 128).T).astype(np.float32),
            bk=np.ascontiguousarray(ba[1024 + cs:1024 + cs + 512].reshape(4, 128).T).astype(np.float32),
            bv=np.ascontiguousarray(ba[2048 + cs:2048 + cs + 512].reshape(4, 128).T).astype(np.float32),
            bv_row=np_bf(ba[2048 + cs:2048 + cs + 512].reshape(1, 512)),
            c_U=U, c_Vfam=Vfam, c_selb=selb, c_cones=cones,
        ))
    return in_maps



def emit_l1_proj_router(p: Prog):
    x2own = p.get_in("x2own", [D, T], F32)
    o_mine = p.get_in("o_mine", [D, T], BF16)
    wproj = p.get_in("wproj", [D, D], F32)
    bproj = p.get_in("bproj", [128, 8], F32)
    wr = p.get_in("wr", [D, 8], F32)
    ccones = p.get_in("c_cones", [128, 7], F32)
    ciota8 = p.get_in("c_iota8", [128, 8], F32)
    ciota8m = p.get_in("c_iota8m", [128, 8], F32)
    cid8 = p.get_in("c_id8", [8, 8], F32)
    x3_fm = p.get_out("x3_fm", [D, T], F32)
    xln4 = p.get_out("xln4", [D, T], BF16)
    meta = p.get_out("meta", [T, 4], F32)
    x3_d = p.dram("x3_d", [D, T], F32)
    xlnf_d = p.dram("xlnf_d", [D, T], F32)
    rt_d = p.dram("rt_d", [8, 512], F32)

    xs = p.sb("xs", [128, 2 * 512], F32)
    sqs = p.sb("sqs", [128, 2 * 512], F32)
    app_scr = p.sb("app_scr", [128, 2 * 512], F32)
    rt_row = p.sb("rt_row", [1, 1024], F32)
    ost = p.sb("ost", [128, 2 * 512], F32)
    Sbb = p.sb("Sbb", [128, 2 * 512], BF16)
    Bbb = p.sb("Bbb", [128, 2 * 512], BF16)
    rhb = p.sb("rhb", [128, 2 * 8 * 512], BF16)
    Wst = p.sb("Wst", [128, 2 * 1024], BF16)
    Wrb = p.sb("Wrb", [128, 64], F32)
    conesb = p.sb("conesb", [128, 7], F32)
    iota8 = p.sb("iota8", [128, 8], F32)
    iota8m = p.sb("iota8m", [128, 8], F32)
    id8 = p.sb("id8", [8, 8], F32)
    onesrow_f = p.sb("onesrow_f", [1, 128], F32)
    bproj_sb = p.sb("bproj_sb", [128, 8], F32)
    lgt = p.sb("lgt", [8, T], F32)
    xlnf = p.sb("xlnf", [128, 2 * 4096], F32)
    lgt2 = p.sb("lgt2", [128, 8], F32)
    l2s = p.sb("l2s", [128, 8], F32)
    m1s = p.sb("m1s", [128, 8], F32)
    mt = p.sb("mt", [128, 4], F32)

    ps = [p.ps(f"ps{i}", [128, 512], F32) for i in range(8)]

    cl = [emit_load2(p, conesb, (0, 7), ccones[:, :], stream="konst"),
          emit_load2(p, iota8, (0, 8), ciota8[:, :], stream="konst"),
          emit_load2(p, iota8m, (0, 8), ciota8m[:, :], stream="konst"),
          emit_load2(p, id8, (0, 8), cid8[:, :], stream="konst"),
          emit_load2(p, bproj_sb, (0, 8), bproj[:, :], stream="konst")]
    wr_ld = p.dma("sync", lambda e: e.dma_start(
        out=Wrb[:, :].rearrange("p (c m) -> p c m", c=8),
        in_=wr.rearrange("(c p) m -> p c m", p=128)), stream="konst")
    t_ones = p.op("vector", lambda e: e.memset(onesrow_f[:, :], 1.0))

    x3_toks = {}
    wcons = [[], []]
    rhbc = [[], []]
    oprot = [[], []]
    xsc = [[], []]
    ostc = [[], []]
    for fc in range(8):
        wsl = fc % 2
        t_w = p.dma("gpsimd", (lambda f, s: lambda e: e.dma_start(
            out=Wst[:, s * 1024:s * 1024 + 1024].rearrange("p (c m) -> p c m", c=8),
            in_=wproj[:, f * 128:(f + 1) * 128].rearrange("(c p) m -> p c m", p=128)))(fc, wsl),
            deps=wcons[wsl], stream="w%d" % wsl)
        wcons[wsl] = []
        for tt in range(4):
            rsl = (fc * 4 + tt) % 2
            t_r = p.dma("sync", (lambda ttt, s: lambda e: e.dma_start(
                out=rhb[:, s * 4096:(s + 1) * 4096].rearrange("p (c m) -> p c m", c=8),
                in_=o_mine[:, ttt * 512:(ttt + 1) * 512].rearrange(
                    "(c p) m -> p c m", p=128)))(tt, rsl),
                deps=rhbc[rsl], stream="rhb%d" % rsl)
            rhbc[rsl] = []
            t_x = p.dma("sync", (lambda f, ttt, s: lambda e: e.dma_start(
                out=xs[:, s * 512:(s + 1) * 512],
                in_=x2own[f * 128:(f + 1) * 128, ttt * 512:(ttt + 1) * 512]))(fc, tt, rsl),
                deps=xsc[rsl], stream="xs%d" % rsl)
            xsc[rsl] = []
            bnk = (fc * 4 + tt) % 2
            mm = None
            for kc in range(8):
                mm = p.op("tensor", (lambda s, k, b, r: lambda e: e.matmul(
                    ps[4 + b][:, :],
                    Wst[:, s * 1024 + k * 128:s * 1024 + (k + 1) * 128],
                    rhb[:, r * 4096 + k * 512:r * 4096 + (k + 1) * 512],
                    start=(k == 0), stop=(k == 7)))(wsl, kc, bnk, rsl),
                    deps=[t_w, t_r] + (oprot[bnk] if kc == 0 else []))
            ev = p.op("vector", (lambda b, s, f: lambda e: e.scalar_tensor_tensor(
                ost[:, s * 512:(s + 1) * 512], ps[4 + b][:, :],
                bproj_sb[:, f:f + 1],
                xs[:, s * 512:(s + 1) * 512], ALU.add, ALU.add))(bnk, rsl, fc),
                deps=[mm, t_x, cl[4]] + ostc[rsl])
            oprot[bnk] = [ev]
            rhbc[rsl].append(ev)
            xsc[rsl].append(ev)
            wcons[wsl].append(ev)
            st1 = p.out_dma("sync", (lambda f, ttt, s: lambda e: e.dma_start(
                out=x3_fm[f * 128:(f + 1) * 128, ttt * 512:(ttt + 1) * 512],
                in_=ost[:, s * 512:(s + 1) * 512]))(fc, tt, rsl), deps=[ev], stream="st%d" % rsl)
            st2 = p.dma("sync", (lambda f, ttt, s: lambda e: e.dma_start(
                out=x3_d[f * 128:(f + 1) * 128, ttt * 512:(ttt + 1) * 512],
                in_=ost[:, s * 512:(s + 1) * 512]))(fc, tt, rsl), deps=[ev], stream="st%d" % rsl)
            ostc[rsl] = [st1, st2]
            x3_toks[(fc, tt)] = st2

    bkk = {"sx": ps[0], "sx2": ps[1], "b1": ps[2], "b2": ps[3]}
    xln_toks, ln_done, xln_toks2 = emit_module_ln(
        p, x3_d, bkk, xs, sqs, conesb, onesrow_f, rt_d, rt_row, Sbb, Bbb,
        app_scr, None, nw_sb=None, rms_eps=None, eps=1e-5,
        pre_deps=[cl[0], t_ones] + [x3_toks[k] for k in x3_toks], tag="m",
        W=T, out_dram=xln4, ost=ost, out_gpsimd=True, out_dram2=xlnf_d)
    for k in xln_toks:
        p.out_toks.append(xln_toks[k])

    lg_evac = {}
    lgc = []
    rhbc2 = [[], []]
    for tt in range(4):
        rsl = tt % 2
        t_r = p.dma("sync", (lambda ttt, s: lambda e: e.dma_start(
            out=xlnf[:, s * 4096:(s + 1) * 4096].rearrange("p (c m) -> p c m", c=8),
            in_=xlnf_d[:, ttt * 512:(ttt + 1) * 512].rearrange(
                "(c p) m -> p c m", p=128)))(tt, rsl),
            deps=[xln_toks2[(kc, tt)] for kc in range(8)] + rhbc2[rsl], stream="rr%d" % rsl)
        mm = None
        for kc in range(8):
            mm = p.op("tensor", (lambda k, r: lambda e: e.matmul(
                ps[4][0:8, :], Wrb[:, k * 8:(k + 1) * 8],
                xlnf[:, r * 4096 + k * 512:r * 4096 + (k + 1) * 512],
                start=(k == 0), stop=(k == 7)))(kc, rsl),
                deps=[wr_ld, t_r] + (lgc if kc == 0 else []))
        ev = p.op("vector", (lambda ttt: lambda e: e.tensor_copy(
            lgt[0:8, ttt * 512:(ttt + 1) * 512], ps[4][0:8, :]))(tt), deps=[mm])
        lgc = [ev]
        rhbc2[rsl] = [mm]
        lg_evac[tt] = ev
    c1_ = p.op("vector", lambda e: e.tensor_scalar_min(lgt[0:8, :], lgt[0:8, :], 10000.0),
               deps=list(lg_evac.values()))
    c2_ = p.op("vector", lambda e: e.tensor_scalar_max(lgt[0:8, :], lgt[0:8, :], -10000.0),
               deps=[c1_])
    tp_free = []
    mtc = []
    for j in range(16):
        tp = p.op("tensor", (lambda jj: lambda e: e.transpose(
            ps[5][0:128, 0:8], lgt[0:8, jj * 128:(jj + 1) * 128], id8[:, :]))(j),
            deps=[c2_, cl[3]] + tp_free)
        g0 = p.op("vector", lambda e: e.tensor_copy(lgt2[:, :], ps[5][0:128, 0:8]), deps=[tp] + mtc)
        tp_free = [g0]
        g1 = p.op("vector", lambda e: e.reduce_max(m1s[:, 0:1], lgt2[:, :], axis=AX.X), deps=[g0])
        g2 = p.op("vector", lambda e: e.tensor_scalar(l2s[:, :], lgt2[:, :], m1s[:, 0:1], None,
                                                      ALU.is_equal), deps=[g1])
        g3 = p.op("vector", lambda e: e.tensor_mul(l2s[:, :], l2s[:, :], iota8m[:, :]),
                  deps=[g2, cl[2]])
        g4 = p.op("vector", lambda e: e.tensor_reduce(m1s[:, 1:2], l2s[:, :], AX.X, ALU.min),
                  deps=[g3])
        g5 = p.op("vector", lambda e: e.tensor_scalar_add(m1s[:, 1:2], m1s[:, 1:2], 999.0),
                  deps=[g4])
        g6 = p.op("vector", lambda e: e.tensor_scalar(l2s[:, :], iota8[:, :], m1s[:, 1:2], None,
                                                      ALU.is_equal), deps=[g5, cl[1]])
        g7 = p.op("vector", lambda e: e.scalar_tensor_tensor(l2s[:, :], l2s[:, :], -1e30,
                                                             lgt2[:, :], ALU.mult, ALU.add),
                  deps=[g6])
        g8 = p.op("vector", lambda e: e.reduce_max(m1s[:, 2:3], l2s[:, :], axis=AX.X), deps=[g7])
        g9 = p.op("vector", lambda e: e.tensor_scalar(l2s[:, :], l2s[:, :], m1s[:, 2:3], None,
                                                      ALU.is_equal), deps=[g8])
        g10 = p.op("vector", lambda e: e.tensor_mul(l2s[:, :], l2s[:, :], iota8m[:, :]), deps=[g9])
        g11 = p.op("vector", lambda e: e.tensor_reduce(m1s[:, 3:4], l2s[:, :], AX.X, ALU.min),
                   deps=[g10])
        g12 = p.op("vector", lambda e: e.tensor_scalar_add(m1s[:, 3:4], m1s[:, 3:4], 999.0),
                   deps=[g11])
        g13 = p.op("vector", lambda e: e.tensor_sub(m1s[:, 4:5], m1s[:, 2:3], m1s[:, 0:1]),
                   deps=[g12, g8, g1])
        g14 = p.op("scalar", lambda e: e.activation(m1s[:, 4:5], m1s[:, 4:5], AF.Exp), deps=[g13])
        g15 = p.op("vector", lambda e: e.tensor_scalar_add(m1s[:, 5:6], m1s[:, 4:5],
                                                           1.0 + 1e-12), deps=[g14])
        g16 = p.op("vector", lambda e: e.reciprocal(m1s[:, 5:6], m1s[:, 5:6]), deps=[g15])
        g17 = p.op("vector", lambda e: e.tensor_mul(m1s[:, 6:7], m1s[:, 4:5], m1s[:, 5:6]),
                   deps=[g16])
        h1 = p.op("vector", lambda e: e.tensor_copy(mt[:, 0:1], m1s[:, 1:2]), deps=[g17] + mtc)
        h2 = p.op("vector", lambda e: e.tensor_copy(mt[:, 1:2], m1s[:, 3:4]), deps=[g12, h1])
        h3 = p.op("vector", lambda e: e.tensor_copy(mt[:, 2:3], m1s[:, 5:6]), deps=[g16, h2])
        h4 = p.op("vector", lambda e: e.tensor_copy(mt[:, 3:4], m1s[:, 6:7]), deps=[g17, h3])
        st = p.out_dma("sync", (lambda jj: lambda e: e.dma_start(
            out=meta[jj * 128:(jj + 1) * 128, :], in_=mt[:, :]))(j), deps=[h4], stream="mst")
        mtc = [st]


def build_l1_proj_router():
    p = Prog("l1pr")
    emit_l1_proj_router(p)
    return p


def prep_l1_proj_router_inputs(inputs, x2_full, o_outs):
    cones = np.zeros((128, 7), dtype=np.float32)
    cones[:, 3] = 1.0
    iota8 = np.broadcast_to(np.arange(8, dtype=np.float32)[None, :], (128, 8)).copy()
    in_maps = []
    for c in range(NCORE):
        b, half = c // 2, c % 2
        s = half * T
        o_mine = np.concatenate([
            np.asarray(o_outs[2 * b])[:, s:s + T],
            np.asarray(o_outs[2 * b + 1])[:, s:s + T]], axis=0)
        in_maps.append(dict(
            x2own=np.ascontiguousarray(x2_full[b, s:s + T, :].T),
            o_mine=np.ascontiguousarray(o_mine),
            wproj=inputs["L1_attn_W_proj"],
            bproj=np.ascontiguousarray(inputs["L1_attn_b_proj"].reshape(8, 128).T).astype(np.float32),
            wr=inputs["L1_ffn_Wr"],
            c_cones=cones, c_iota8=iota8, c_iota8m=iota8 - 999.0,
            c_id8=np.eye(8, dtype=np.float32),
        ))
    return in_maps


TG = 16384
CAP = 5120.0


def emit_moe_dense(p: Prog):
    xg = p.get_in("xg", [D, TG], BF16)
    meta_all = p.get_in("meta_all", [TG, 4], F32)
    w1e = p.get_in("w1e", [D, 4096], F32)
    w3e = p.get_in("w3e", [D, 4096], F32)
    w2e = p.get_in("w2e", [4096, D], F32)
    myexp = p.get_in("myexp", [128, 1], F32)
    ctris = p.get_in("c_tris", [128, 128], BF16)
    ccone255 = p.get_in("c_cone255", [128, 255], BF16)
    ciota8 = p.get_in("c_iota8", [128, 8], F32)
    part = p.get_out("part", [D, TG], BF16)
    h_d = p.dram("h_d", [4096, TG], BF16)
    g_d = p.dram("g_d", [128, 128], F32)
    offs_d = p.dram("offs_d", [128, 8], F32)

    w2sb = p.sb("w2sb", [128, 32 * 1024], BF16)
    hrb = p.sb("hrb", [128, 32 * 512], BF16)
    w1b = p.sb("w1b", [128, 8 * 1024], BF16)
    w3b = p.sb("w3b", [128, 8 * 1024], BF16)
    rhb = p.sb("rhb", [128, 2 * 8 * 512], BF16)
    hst = p.sb("hst", [128, 2 * 512], BF16)
    gact = p.sb("gact", [128, 2 * 512], BF16)
    cums = p.sb("cums", [128, 8 * 128], F32)
    g_sb = p.sb("g_sb", [128, 128], F32)
    gT_sb = p.sb("gT_sb", [128, 128], F32)
    grow = p.sb("grow", [1, 2 * 512], F32)
    gb = p.sb("gb", [128, 2 * 512], F32)
    mt = p.sb("mt", [128, 2 * 4], F32)
    oh1 = p.sb("oh1", [128, 8], F32)
    oh2 = p.sb("oh2", [128, 8], F32)
    obf = p.sb("obf", [128, 8], BF16)
    scr8 = p.sb("scr8", [128, 8 * 4], F32)
    trisb = p.sb("trisb", [128, 128], BF16)
    cone255 = p.sb("cone255", [128, 255], BF16)
    iota8 = p.sb("iota8", [128, 8], F32)
    idb = p.sb("idb", [128, 128], F32)
    myx = p.sb("myx", [128, 1], F32)
    totbf = p.sb("totbf", [128, 8], BF16)
    orow = p.sb("orow", [1, 2 * 8], F32)
    ost = p.sb("ost", [128, 2 * 512], BF16)
    onesr = p.sb("onesr", [1, 128], F32)

    ps = [p.ps(f"ps{i}", [128, 512], F32) for i in range(8)]

    cl = [emit_load2(p, trisb, (0, 128), ctris[:, :], stream="konst"),
          emit_load2(p, cone255, (0, 255), ccone255[:, :], stream="konst"),
          emit_load2(p, iota8, (0, 8), ciota8[:, :], stream="konst"),
          emit_load2(p, myx, (0, 1), myexp[:, :], stream="konst")]
    t_id = p.op("vector", lambda e: e.memset(idb[:, :], 0.0))
    t_onesr = p.op("vector", lambda e: e.memset(onesr[:, :], 1.0))
    ciota128 = p.get_in("c_iota128", [128, 128], F32)
    il = emit_load2(p, idb, (0, 128), ciota128[:, :], stream="konst", deps=[t_id])

    md = (p.io or {}).get("meta_dep", []) if p.io else []
    xgd = (p.io or {}).get("xg_dep", []) if p.io else []
    mtc = [[], []]
    cum_evac = {}
    ps_rot = [[], []]
    tot_last = None
    oh_cons = []
    for tile in range(128):
        sl = tile % 2
        ld = p.dma("sync", (lambda t_, s: lambda e: e.dma_start(
            out=mt[:, s * 4:(s + 1) * 4], in_=meta_all[t_ * 128:(t_ + 1) * 128, :]))(tile, sl),
            deps=mtc[sl] + (md if tile < 2 else []), stream="mt%d" % sl)
        o1 = p.op("vector", (lambda s: lambda e: e.tensor_scalar(
            oh1[:, :], iota8[:, :], mt[:, s * 4:s * 4 + 1], None, ALU.is_equal))(sl),
            deps=[ld, cl[2]] + oh_cons)
        o2 = p.op("vector", (lambda s: lambda e: e.tensor_scalar(
            oh2[:, :], iota8[:, :], mt[:, s * 4 + 1:s * 4 + 2], None, ALU.is_equal))(sl),
            deps=[ld, o1])
        ob = p.op("vector", lambda e: e.tensor_add(obf[:, :], oh1[:, :], oh2[:, :]),
                  deps=[o2])
        mtc[sl] = [o1, o2]
        oh_cons = [ob]
        bnk = tile % 2
        mm1 = p.op("tensor", (lambda b: lambda e: e.matmul(
            ps[4 + b][:, 0:8], trisb[:, :], obf[:, :], start=True, stop=True))(bnk),
            deps=[ob, cl[0]] + ps_rot[bnk])
        ev = p.op("vector", (lambda t_, b: lambda e: e.tensor_copy(
            cums[:, t_ * 8:(t_ + 1) * 8], ps[4 + b][:, 0:8]))(tile, bnk), deps=[mm1])
        ps_rot[bnk] = [ev]
        cum_evac[tile] = ev
        tot_last = p.op("tensor", (lambda t_: lambda e: e.matmul(
            ps[6][:, 0:8], cone255[:, 127 - t_:255 - t_], obf[:, :],
            start=(t_ == 0), stop=(t_ == 127)))(tile),
            deps=[ob, cl[1]])
        oh_cons.append(tot_last)
    tev = p.op("vector", lambda e: e.tensor_copy(scr8[:, 0:8], ps[6][:, 0:8]), deps=[tot_last])
    tbf = p.op("vector", lambda e: e.tensor_copy(totbf[:, :], scr8[:, 0:8]), deps=[tev])
    om = p.op("tensor", lambda e: e.matmul(ps[6][:, 0:8], trisb[:, :], totbf[:, :],
                                           start=True, stop=True), deps=[tbf, cl[0], tev])
    oev = p.op("vector", lambda e: e.tensor_copy(scr8[:, 8:16], ps[6][:, 0:8]), deps=[om])
    ow = p.dma("sync", lambda e: e.dma_start(out=offs_d[:, :], in_=scr8[:, 8:16]), deps=[oev],
               stream="ow")

    g_done = {}
    mtc = [[], []]
    orc = [[], []]
    gcons = []
    for tile in range(128):
        sl = tile % 2
        ld = p.dma("sync", (lambda t_, s: lambda e: e.dma_start(
            out=mt[:, s * 4:(s + 1) * 4], in_=meta_all[t_ * 128:(t_ + 1) * 128, :]))(tile, sl),
            deps=mtc[sl], stream="mt%d" % sl)
        orl = p.dma("sync", (lambda t_, s: lambda e: e.dma_start(
            out=orow[0:1, s * 8:(s + 1) * 8], in_=offs_d[t_:t_ + 1, :]))(tile, sl),
            deps=[ow] + orc[sl], stream="or%d" % sl)
        o1 = p.op("vector", (lambda s: lambda e: e.tensor_scalar(
            oh1[:, :], iota8[:, :], mt[:, s * 4:s * 4 + 1], None, ALU.is_equal))(sl),
            deps=[ld] + gcons)
        o2 = p.op("vector", (lambda s: lambda e: e.tensor_scalar(
            oh2[:, :], iota8[:, :], mt[:, s * 4 + 1:s * 4 + 2], None, ALU.is_equal))(sl),
            deps=[ld, o1])
        bcm = p.op("tensor", (lambda s: lambda e: e.matmul(
            ps[5][:, 0:8], onesr[:, :], orow[0:1, s * 8:(s + 1) * 8],
            start=True, stop=True))(sl), deps=[orl, t_onesr] + gcons)
        cg = p.op("vector", (lambda t_: lambda e: e.tensor_add(
            scr8[:, 16:24], cums[:, t_ * 8:(t_ + 1) * 8], ps[5][:, 0:8]))(tile),
            deps=[bcm, cum_evac[tile]] + gcons)
        x1_ = p.op("vector", lambda e: e.tensor_mul(scr8[:, 24:32], scr8[:, 16:24], oh1[:, :]),
                   deps=[cg, o1])
        r1 = p.op("vector", lambda e: e.reduce_sum(g_sb[0:128, 0:1] if False else scr8[:, 0:1],
                                                   scr8[:, 24:32], axis=AX.X), deps=[x1_])
        k1 = p.op("vector", lambda e: e.tensor_scalar(scr8[:, 1:2], scr8[:, 0:1], CAP, None,
                                                      ALU.is_lt), deps=[r1])
        x2_ = p.op("vector", lambda e: e.tensor_mul(scr8[:, 24:32], scr8[:, 16:24], oh2[:, :]),
                   deps=[cg, o2, r1])
        r2 = p.op("vector", lambda e: e.reduce_sum(scr8[:, 2:3], scr8[:, 24:32], axis=AX.X),
                  deps=[x2_])
        k2 = p.op("vector", lambda e: e.tensor_scalar(scr8[:, 3:4], scr8[:, 2:3], CAP, None,
                                                      ALU.is_lt), deps=[r2])
        m1_ = p.op("vector", (lambda s: lambda e: e.tensor_scalar(
            scr8[:, 4:5], mt[:, s * 4:s * 4 + 1], myx[:, 0:1], None, ALU.is_equal))(sl),
            deps=[ld, cl[3], k1])
        m2_ = p.op("vector", (lambda s: lambda e: e.tensor_scalar(
            scr8[:, 5:6], mt[:, s * 4 + 1:s * 4 + 2], myx[:, 0:1], None, ALU.is_equal))(sl),
            deps=[ld, m1_])
        g1 = p.op("vector", (lambda s: lambda e: e.tensor_mul(
            scr8[:, 4:5], scr8[:, 4:5], mt[:, s * 4 + 2:s * 4 + 3]))(sl), deps=[m1_, k1])
        g1b = p.op("vector", lambda e: e.tensor_mul(scr8[:, 4:5], scr8[:, 4:5], scr8[:, 1:2]),
                   deps=[g1])
        g2 = p.op("vector", (lambda s: lambda e: e.tensor_mul(
            scr8[:, 5:6], scr8[:, 5:6], mt[:, s * 4 + 3:s * 4 + 4]))(sl), deps=[m2_, k2])
        g2b = p.op("vector", lambda e: e.tensor_mul(scr8[:, 5:6], scr8[:, 5:6], scr8[:, 3:4]),
                   deps=[g2])
        gf = p.op("vector", (lambda t_: lambda e: e.tensor_add(
            g_sb[:, t_:t_ + 1], scr8[:, 4:5], scr8[:, 5:6]))(tile), deps=[g1b, g2b])
        mtc[sl] = [m1_, m2_, g1, g2]
        orc[sl] = [bcm]
        gcons = [gf]
        g_done[tile] = gf
    gt = p.op("tensor", lambda e: e.transpose(ps[5][:, 0:128], g_sb[:, :], idb[:, :]),
              deps=[il] + list(g_done.values()))
    gte = p.op("vector", lambda e: e.tensor_copy(gT_sb[:, :], ps[5][:, 0:128]), deps=[gt])
    gw = p.dma("sync", lambda e: e.dma_start(out=g_d[:, :], in_=gT_sb[:, :]), deps=[gte],
               stream="gw")

    h_toks = {}
    rhbc = [[], []]
    hstc = [[], []]
    gac = [[], []]
    wcons = []
    for hb in range(4):
        lw1 = p.dma("gpsimd", (lambda b: lambda e: e.dma_start(
            out=w1b[:, :].rearrange("p (c m) -> p c m", c=8),
            in_=w1e[:, b * 1024:(b + 1) * 1024].rearrange("(c p) m -> p c m", p=128)))(hb),
            deps=wcons, stream="w1l")
        lw3 = p.dma("gpsimd", (lambda b: lambda e: e.dma_start(
            out=w3b[:, :].rearrange("p (c m) -> p c m", c=8),
            in_=w3e[:, b * 1024:(b + 1) * 1024].rearrange("(c p) m -> p c m", p=128)))(hb),
            deps=wcons, stream="w3l")
        wcons = []
        for tt in range(32):
            rsl = tt % 2
            t_r = p.dma("sync", (lambda ttt, s: lambda e: e.dma_start(
                out=rhb[:, s * 4096:(s + 1) * 4096].rearrange("p (c m) -> p c m", c=8),
                in_=xg[:, ttt * 512:(ttt + 1) * 512].rearrange(
                    "(c p) m -> p c m", p=128)))(tt, rsl),
                deps=rhbc[rsl], stream="xgr%d" % rsl)
            rhbc[rsl] = []
            for hcs in range(8):
                bnk = hcs % 2
                mma = mmg = None
                for kc in range(8):
                    mma = p.op("tensor", (lambda s_, k, b, r: lambda e: e.matmul(
                        ps[4 + b][:, :],
                        w1b[:, s_ * 1024 + k * 128:s_ * 1024 + (k + 1) * 128] if False else
                        w1b[:, k * 1024 + s_ * 128:k * 1024 + (s_ + 1) * 128],
                        rhb[:, r * 4096 + k * 512:r * 4096 + (k + 1) * 512],
                        start=(k == 0), stop=(k == 7)))(hcs, kc, bnk, rsl),
                        deps=[lw1, t_r] + (gac[bnk] if kc == 0 else []))
                    mmg = p.op("tensor", (lambda s_, k, b, r: lambda e: e.matmul(
                        ps[6 + b][:, :],
                        w3b[:, k * 1024 + s_ * 128:k * 1024 + (s_ + 1) * 128],
                        rhb[:, r * 4096 + k * 512:r * 4096 + (k + 1) * 512],
                        start=(k == 0), stop=(k == 7)))(hcs, kc, bnk, rsl),
                        deps=[lw3, t_r] + (gac[bnk] if kc == 0 else []))
                sl2 = (tt * 8 + hcs) % 2
                t_g = p.op("scalar", (lambda b, s2: lambda e: e.activation(
                    gact[:, s2 * 512:(s2 + 1) * 512], ps[4 + b][:, :], AF.Silu))(bnk, sl2),
                    deps=[mma] + hstc[sl2])
                t_h = p.op("vector", (lambda b, s2: lambda e: e.tensor_mul(
                    hst[:, s2 * 512:(s2 + 1) * 512], gact[:, s2 * 512:(s2 + 1) * 512],
                    ps[6 + b][:, :]))(bnk, sl2), deps=[t_g, mmg])
                gac[bnk] = [t_h]
                rhbc[rsl].append(t_h)
                wcons.append(t_h)
                t_s = p.dma("scalar", (lambda b, h_, ttt, s2: lambda e: e.dma_start(
                    out=h_d[b * 1024 + h_ * 128:b * 1024 + (h_ + 1) * 128,
                            ttt * 512:(ttt + 1) * 512],
                    in_=hst[:, s2 * 512:(s2 + 1) * 512]))(hb, hcs, tt, sl2),
                    deps=[t_h], stream="hs%d" % sl2)
                hstc[sl2] = [t_s]
                h_toks[(hb, hcs, tt)] = t_s

    lw2 = p.dma("gpsimd", lambda e: e.dma_start(
        out=w2sb[:, :].rearrange("p (c m) -> p c m", c=32),
        in_=w2e.rearrange("(c p) m -> p c m", p=128)), stream="w2l")
    hrc = []
    gbc = []
    oc = [[], []]
    prot = [[], []]
    for tt in range(32):
        t_h = p.dma("sync", (lambda ttt: lambda e: e.dma_start(
            out=hrb[:, :].rearrange("p (c m) -> p c m", c=32),
            in_=h_d[:, ttt * 512:(ttt + 1) * 512].rearrange(
                "(c p) m -> p c m", p=128)))(tt),
            deps=hrc + [h_toks[(hb, hcs, tt)] for hb in range(4) for hcs in range(8)],
            stream="hrr")
        hrc = []
        gsl = tt % 2
        grl = p.dma("sync", (lambda ttt, s: lambda e: e.dma_start(
            out=grow[0:1, s * 512:(s + 1) * 512],
            in_=g_d[ttt * 4:(ttt + 1) * 4, :].rearrange("a b -> (a b)").rearrange("(o n) -> o n", o=1)))(tt, gsl),
            deps=[gw] + gbc, stream="grl%d" % gsl)
        bcg = p.op("tensor", (lambda s: lambda e: e.matmul(
            ps[5][:, :], onesr[:, :], grow[0:1, s * 512:(s + 1) * 512],
            start=True, stop=True))(gsl), deps=[grl, t_onesr] + gbc)
        geb = p.op("scalar", (lambda s: lambda e: e.copy(gb[:, s * 512:(s + 1) * 512],
                                                         ps[5][:, :]))(tt % 2),
                   deps=[bcg] + oc[tt % 2])
        gbc = [geb]
        for fc in range(8):
            bnk = fc % 2
            mm = None
            for hc in range(32):
                mm = p.op("tensor", (lambda f, h_, b: lambda e: e.matmul(
                    ps[0 + b][:, :],
                    w2sb[:, h_ * 1024 + f * 128:h_ * 1024 + (f + 1) * 128],
                    hrb[:, h_ * 512:(h_ + 1) * 512],
                    start=(h_ == 0), stop=(h_ == 31)))(fc, hc, bnk),
                    deps=[lw2, t_h] + (prot[bnk] if hc == 0 else []))
            sl2 = (tt * 8 + fc) % 2
            ev = p.op("vector", (lambda b, s, s2: lambda e: e.tensor_mul(
                ost[:, s2 * 512:(s2 + 1) * 512], ps[0 + b][:, :],
                gb[:, s * 512:(s + 1) * 512]))(bnk, tt % 2, sl2),
                deps=[mm, geb] + oc[sl2])
            prot[bnk] = [ev]
            hrc.append(ev)
            if p.io is not None and p.io.get("part_rs_layout"):
                ro = (tt // 4) * 1024 + fc * 128
                co = (tt % 4) * 512
            else:
                ro = fc * 128
                co = tt * 512
            st = p.out_dma("sync", (lambda r_, c_, s2: lambda e: e.dma_start(
                out=part[r_:r_ + 128, c_:c_ + 512],
                in_=ost[:, s2 * 512:(s2 + 1) * 512]))(ro, co, sl2), deps=[ev],
                stream="pst%d" % sl2)
            oc[sl2] = [st, ev]


def build_moe_dense():
    p = Prog("moed")
    emit_moe_dense(p)
    return p



NS = 5120
SWS = ["lngst", "cpg", "rhbg0", "rhbg1"]



def _bc_reg(p, e, val):
    if not hasattr(p, "_bc"):
        p._bc = {}
    if val not in p._bc:
        r = e.alloc_register()
        e.reg_mov(r, val)
        p._bc[val] = r
    return p._bc[val]


def emit_moe_ep_a(p: Prog):
    from concourse.bass_core_rust import IndirectOffsetOnAxis as IOA
    xg_tok = p.io["xg_tok"]
    meta_all = p.io["meta_all"]
    part = p.io["part"]
    w1e = p.get_in("w1e", [D, 4096], F32)
    w3e = p.get_in("w3e", [D, 4096], F32)
    w2e = p.get_in("w2e", [4096, D], F32)
    myexp = p.get_in("myexp", [128, 1], F32)
    ctris = p.get_in("c_tris", [128, 128], BF16)
    ccone255 = p.get_in("c_cone255", [128, 255], BF16)
    ciota8 = p.get_in("c_iota8", [128, 8], F32)
    ciota128 = p.get_in("c_iota128", [128, 128], F32)
    cpcol = p.get_in("c_pcol", [128, 1], F32)
    xm_d = p.dram("xm_d", [NS, 2], F32)
    he_d = p.dram("he_d", [4096, NS], BF16)
    offs_d = p.dram("offs_d", [128, 8], F32)

    cums = p.sb("cums", [128, 8 * 128], F32)
    mt = p.sb("mt", [128, 2 * 4], F32)
    oh1 = p.sb("oh1", [128, 8], F32)
    oh2 = p.sb("oh2", [128, 8], F32)
    obf = p.sb("obf", [128, 8], BF16)
    scr8 = p.sb("scr8", [128, 8 * 4], F32)
    trisb = p.sb("trisb", [128, 128], BF16)
    cone255 = p.sb("cone255", [128, 255], BF16)
    iota8 = p.sb("iota8", [128, 8], F32)
    myx = p.sb("myx", [128, 1], F32)
    pcol = p.sb("pcol", [128, 1], F32)
    totbf = p.sb("totbf", [128, 8], BF16)
    orow = p.sb("orow", [1, 2 * 8], F32)
    onesr = p.sb("onesr", [1, 128], F32)
    gm = p.sb("gm", [128, 4 * 2], F32)
    iix = p.sb("iix", [128, 4], I32)
    zb = p.sb("zb", [128, 1024], BF16)
    zf = p.sb("zf", [128, 80], F32)
    w1b = p.sb("w1b", [128, 8 * 4096], BF16)
    w3b = p.sb("w3b", [128, 8 * 4096], BF16)
    rxe = p.sb("rxe", [128, 4 * 1024], BF16)
    rxf = p.sb("rxf", [128, 4 * 1024], F32)
    rhf = p.sb("rhf", [128, 8 * 512], BF16)
    xms = p.sb("xms", [128, 8], F32)
    iig = p.sb("iig", [128, 4], I32)
    gsc = p.sb("gsc", [128, 8], F32)
    hst = p.sb("hst", [128, 2 * 512], BF16)
    gact = p.sb("gact", [128, 2 * 512], BF16)

    ps = [p.ps(f"ps{i}", [128, 512], F32) for i in range(8)]

    cl = [emit_load2(p, trisb, (0, 128), ctris[:, :], stream="konst"),
          emit_load2(p, cone255, (0, 255), ccone255[:, :], stream="konst"),
          emit_load2(p, iota8, (0, 8), ciota8[:, :], stream="konst"),
          emit_load2(p, myx, (0, 1), myexp[:, :], stream="konst"),
          emit_load2(p, pcol, (0, 1), cpcol[:, :], stream="konst")]
    t_onesr = p.op("vector", lambda e: e.memset(onesr[:, :], 1.0))
    idf32 = p.sb("idf32", [128, 128], F32)
    cl.append(emit_load2(p, idf32, (0, 128), ciota128[:, :], stream="konst"))
    t_idbf = cl[5]

    t_zb = p.op("vector", lambda e: e.memset(zb[:, :], 0.0))
    t_zf = p.op("vector", lambda e: e.memset(zf[:, :], 0.0))
    ZST = ["rr0", "rr1", "ys0", "yst0", "yst1", "cp"]
    zch = {s: [] for s in ZST}
    part_zero = []
    for rb in range(128):
        s = ZST[rb % 6]
        t = p.dma("sync", (lambda r: lambda e: e.dma_start(
            out=part[r * 128:(r + 1) * 128, :], in_=zb[:, :]))(rb),
            deps=[t_zb] + zch[s], stream=s)
        zch[s] = [t]
        part_zero.append(t)
    xm_zero = p.dma("sync", lambda e: e.dma_start(
        out=xm_d[:, :].rearrange("(c p) m -> p c m", p=128), in_=zf[:, :].rearrange(
            "p (c m) -> p c m", c=40)), deps=[t_zf], stream="cp")
    zch["cp"].append(xm_zero)

    md = (p.io or {}).get("meta_dep", []) if p.io else []
    xgd = (p.io or {}).get("xg_dep", []) if p.io else []
    mtc = [[], []]
    cum_evac = {}
    ps_rot = [[], []]
    tot_last = None
    oh_cons = []
    for tile in range(128):
        sl = tile % 2
        ld = p.dma("sync", (lambda t_, s: lambda e: e.dma_start(
            out=mt[:, s * 4:(s + 1) * 4], in_=meta_all[t_ * 128:(t_ + 1) * 128, :]))(tile, sl),
            deps=mtc[sl] + (md if tile < 2 else []), stream="mt%d" % sl)
        o1 = p.op("vector", (lambda s: lambda e: e.tensor_scalar(
            oh1[:, :], iota8[:, :], mt[:, s * 4:s * 4 + 1], None, ALU.is_equal))(sl),
            deps=[ld, cl[2]] + oh_cons)
        o2 = p.op("vector", (lambda s: lambda e: e.tensor_scalar(
            oh2[:, :], iota8[:, :], mt[:, s * 4 + 1:s * 4 + 2], None, ALU.is_equal))(sl),
            deps=[ld, o1])
        ob = p.op("vector", lambda e: e.tensor_add(obf[:, :], oh1[:, :], oh2[:, :]),
                  deps=[o2])
        mtc[sl] = [o1, o2]
        oh_cons = [ob]
        bnk = tile % 2
        mm1 = p.op("tensor", (lambda b: lambda e: e.matmul(
            ps[4 + b][:, 0:8], trisb[:, :], obf[:, :], start=True, stop=True))(bnk),
            deps=[ob, cl[0]] + ps_rot[bnk])
        ev = p.op("vector", (lambda t_, b: lambda e: e.tensor_copy(
            cums[:, t_ * 8:(t_ + 1) * 8], ps[4 + b][:, 0:8]))(tile, bnk), deps=[mm1])
        ps_rot[bnk] = [ev]
        cum_evac[tile] = ev
        tot_last = p.op("tensor", (lambda t_: lambda e: e.matmul(
            ps[6][:, 0:8], cone255[:, 127 - t_:255 - t_], obf[:, :],
            start=(t_ == 0), stop=(t_ == 127)))(tile),
            deps=[ob, cl[1]])
        oh_cons.append(tot_last)
    tev = p.op("vector", lambda e: e.tensor_copy(scr8[:, 0:8], ps[6][:, 0:8]), deps=[tot_last])
    tbf = p.op("vector", lambda e: e.tensor_copy(totbf[:, :], scr8[:, 0:8]), deps=[tev])
    om = p.op("tensor", lambda e: e.matmul(ps[6][:, 0:8], trisb[:, :], totbf[:, :],
                                           start=True, stop=True), deps=[tbf, cl[0], tev])
    oev = p.op("vector", lambda e: e.tensor_copy(scr8[:, 8:16], ps[6][:, 0:8]), deps=[om])
    ow = p.dma("sync", lambda e: e.dma_start(out=offs_d[:, :], in_=scr8[:, 8:16]), deps=[oev],
               stream="ow")

    mtc = [[], []]
    orc = [[], []]
    gcons = []
    sc_last = {s: [] for s in SWS}
    ps5_free = []
    orl_last = None
    for tile in range(128):
        sl = tile % 2
        sw = SWS[tile % 4]
        ld = p.dma("sync", (lambda t_, s: lambda e: e.dma_start(
            out=mt[:, s * 4:(s + 1) * 4], in_=meta_all[t_ * 128:(t_ + 1) * 128, :]))(tile, sl),
            deps=mtc[sl], stream="mt%d" % sl)
        orl = p.dma("sync", (lambda t_, s: lambda e: e.dma_start(
            out=orow[0:1, s * 8:(s + 1) * 8], in_=offs_d[t_:t_ + 1, :]))(tile, sl),
            deps=[ow] + orc[sl], stream="or%d" % sl)
        orl_last = orl
        o1 = p.op("vector", (lambda s: lambda e: e.tensor_scalar(
            oh1[:, :], iota8[:, :], mt[:, s * 4:s * 4 + 1], None, ALU.is_equal))(sl),
            deps=[ld] + gcons)
        o2 = p.op("vector", (lambda s: lambda e: e.tensor_scalar(
            oh2[:, :], iota8[:, :], mt[:, s * 4 + 1:s * 4 + 2], None, ALU.is_equal))(sl),
            deps=[ld, o1])
        bcm = p.op("tensor", (lambda s: lambda e: e.matmul(
            ps[5][:, 0:8], onesr[:, :], orow[0:1, s * 8:(s + 1) * 8],
            start=True, stop=True))(sl), deps=[orl, t_onesr] + gcons)
        cg = p.op("vector", (lambda t_: lambda e: e.tensor_add(
            scr8[:, 16:24], cums[:, t_ * 8:(t_ + 1) * 8], ps[5][:, 0:8]))(tile),
            deps=[bcm, cum_evac[tile]] + gcons)
        ps5_free = [cg]
        x1_ = p.op("vector", lambda e: e.tensor_mul(scr8[:, 24:32], scr8[:, 16:24], oh1[:, :]),
                   deps=[cg, o1])
        r1 = p.op("vector", lambda e: e.reduce_sum(scr8[:, 0:1], scr8[:, 24:32], axis=AX.X),
                  deps=[x1_])
        k1 = p.op("vector", lambda e: e.tensor_scalar(scr8[:, 1:2], scr8[:, 0:1], CAP, None,
                                                      ALU.is_lt), deps=[r1])
        x2_ = p.op("vector", lambda e: e.tensor_mul(scr8[:, 24:32], scr8[:, 16:24], oh2[:, :]),
                   deps=[cg, o2, r1])
        r2 = p.op("vector", lambda e: e.reduce_sum(scr8[:, 2:3], scr8[:, 24:32], axis=AX.X),
                  deps=[x2_])
        k2 = p.op("vector", lambda e: e.tensor_scalar(scr8[:, 3:4], scr8[:, 2:3], CAP, None,
                                                      ALU.is_lt), deps=[r2])
        m1_ = p.op("vector", (lambda s: lambda e: e.tensor_scalar(
            scr8[:, 4:5], mt[:, s * 4:s * 4 + 1], myx[:, 0:1], None, ALU.is_equal))(sl),
            deps=[ld, cl[3], k1])
        m2_ = p.op("vector", (lambda s: lambda e: e.tensor_scalar(
            scr8[:, 5:6], mt[:, s * 4 + 1:s * 4 + 2], myx[:, 0:1], None, ALU.is_equal))(sl),
            deps=[ld, m1_])
        s1 = p.op("vector", lambda e: e.tensor_mul(scr8[:, 6:7], scr8[:, 4:5], scr8[:, 1:2]),
                  deps=[m1_, k1])
        s2 = p.op("vector", lambda e: e.tensor_mul(scr8[:, 7:8], scr8[:, 5:6], scr8[:, 3:4]),
                  deps=[m2_, k2])
        g1 = p.op("vector", (lambda s: lambda e: e.tensor_mul(
            scr8[:, 4:5], scr8[:, 4:5], mt[:, s * 4 + 2:s * 4 + 3]))(sl), deps=[m1_, k1, s1])
        g1b = p.op("vector", lambda e: e.tensor_mul(scr8[:, 4:5], scr8[:, 4:5], scr8[:, 1:2]),
                   deps=[g1])
        g2 = p.op("vector", (lambda s: lambda e: e.tensor_mul(
            scr8[:, 5:6], scr8[:, 5:6], mt[:, s * 4 + 3:s * 4 + 4]))(sl), deps=[m2_, g1, s2])
        g2b = p.op("vector", lambda e: e.tensor_mul(scr8[:, 5:6], scr8[:, 5:6], scr8[:, 3:4]),
                   deps=[g2])
        sel = p.op("vector", lambda e: e.tensor_add(scr8[:, 8:9], scr8[:, 6:7], scr8[:, 7:8]),
                   deps=[s1, s2])
        v1 = p.op("vector", lambda e: e.tensor_mul(scr8[:, 9:10], scr8[:, 0:1], scr8[:, 6:7]),
                  deps=[s1, r1, g1b])
        v2 = p.op("vector", lambda e: e.tensor_mul(scr8[:, 10:11], scr8[:, 2:3], scr8[:, 7:8]),
                  deps=[s2, r2, g2b])
        pv_ = p.op("vector", lambda e: e.tensor_add(scr8[:, 11:12], scr8[:, 9:10],
                                                    scr8[:, 10:11]), deps=[v1, v2])
        tk = p.op("vector", (lambda t_: lambda e: e.tensor_scalar(
            scr8[:, 14:15], pcol[:, 0:1], float(t_ * 128), None, ALU.add))(tile),
            deps=[cl[4]] + gcons)
        av = p.op("vector", lambda e: e.tensor_scalar(scr8[:, 12:13], scr8[:, 8:9], 0.0, None,
                                                      ALU.is_equal), deps=[sel])
        bv = p.op("vector", lambda e: e.tensor_scalar(scr8[:, 15:16], scr8[:, 14:15], 1.0e6,
                                                      None, ALU.add), deps=[tk])
        ab = p.op("vector", lambda e: e.tensor_mul(scr8[:, 13:14], scr8[:, 12:13],
                                                   scr8[:, 15:16]), deps=[av, bv])
        ixf = p.op("vector", lambda e: e.tensor_add(scr8[:, 9:10], scr8[:, 11:12],
                                                    scr8[:, 13:14]), deps=[pv_, ab])
        sl4 = tile % 4
        cv = p.op("vector", (lambda s: lambda e: e.tensor_copy(
            iix[:, s:s + 1], scr8[:, 9:10]))(sl4), deps=[ixf] + sc_last[sw])
        gmc = p.op("vector", (lambda s: lambda e: e.tensor_add(
            gm[:, 2 * s:2 * s + 1], scr8[:, 4:5], scr8[:, 5:6]))(sl4),
            deps=[g1b, g2b] + sc_last[sw])
        gmt = p.op("vector", (lambda s: lambda e: e.tensor_scalar(
            gm[:, 2 * s + 1:2 * s + 2], scr8[:, 14:15], 1.0, None, ALU.add))(sl4),
            deps=[tk] + sc_last[sw])
        sc = p.dma("gpsimd", (lambda s: lambda e: e.indirect_dma_start(
            out=xm_d[:, :], out_offset=IOA(iix[:, s:s + 1], 0),
            in_=gm[:, 2 * s:2 * s + 2], in_offset=None,
            bounds_check=_bc_reg(p, e, NS - 1), oob_is_err=False))(sl4),
            deps=[cv, gmc, gmt, xm_zero] + sc_last[sw], stream=sw)
        sc_last[sw] = [sc]
        mtc[sl] = [m1_, m2_, g1, g2, o1, o2, tk]
        orc[sl] = [bcm]
        gcons = [cv, gmt, gmc]
    all_sc = [t for s in SWS for t in sc_last[s]]

    lw1 = p.dma("gpsimd", lambda e: e.dma_start(
        out=w1b[:, :].rearrange("p (c m) -> p c m", c=8),
        in_=w1e.rearrange("(c p) m -> p c m", p=128)), stream="w1l")
    lw3 = p.dma("gpsimd", lambda e: e.dma_start(
        out=w3b[:, :].rearrange("p (c m) -> p c m", c=8),
        in_=w3e.rearrange("(c p) m -> p c m", p=128)), stream="w3l")
    rxe_cons = []
    rxf_cons = []
    rhf_cons = []
    xms_cons = []
    gath_last = {s: list(sc_last[s]) for s in SWS}
    trot = [list((p.io or {}).get("ps23_seed", [[], []])[i]) for i in range(2)]
    mrot = [[], []]
    hstc = [list((p.io or {}).get("st_seed", [])), []]
    he_toks = []
    for st5 in range(10):
        lm = p.dma("sync", (lambda t_: lambda e: e.dma_start(
            out=xms[:, :].rearrange("p (c m) -> p c m", c=4),
            in_=xm_d[t_ * 512:(t_ + 1) * 512, :].rearrange("(c p) m -> p c m", p=128)))(st5),
            deps=all_sc + xms_cons + [orl_last], stream="or0")
        gths = []
        for j in range(4):
            sw = SWS[j]
            v = p.op("vector", (lambda jj: lambda e: e.tensor_scalar(
                gsc[:, jj:jj + 1], xms[:, 2 * jj + 1:2 * jj + 2], 0.0, None,
                ALU.is_equal))(j), deps=[lm] + xms_cons)
            m_ = p.op("vector", (lambda jj: lambda e: e.tensor_scalar(
                gsc[:, 4 + jj:5 + jj], gsc[:, jj:jj + 1], 1000001.0, None,
                ALU.mult))(j), deps=[v])
            a_ = p.op("vector", (lambda jj: lambda e: e.tensor_add(
                gsc[:, 4 + jj:5 + jj], gsc[:, 4 + jj:5 + jj],
                xms[:, 2 * jj + 1:2 * jj + 2]))(j), deps=[m_])
            a2 = p.op("vector", (lambda jj: lambda e: e.tensor_scalar(
                gsc[:, 4 + jj:5 + jj], gsc[:, 4 + jj:5 + jj], -1.0, None,
                ALU.add))(j), deps=[a_])
            cvg = p.op("vector", (lambda jj: lambda e: e.tensor_copy(
                iig[:, jj:jj + 1], gsc[:, 4 + jj:5 + jj]))(j),
                deps=[a2] + rxe_cons)
            gt_ = p.dma("gpsimd", (lambda jj: lambda e: e.indirect_dma_start(
                out=rxe[:, jj * 1024:(jj + 1) * 1024], out_offset=None,
                in_=xg_tok[:, :], in_offset=IOA(iig[:, jj:jj + 1], 0),
                bounds_check=_bc_reg(p, e, TG - 1), oob_is_err=False))(j),
                deps=[cvg] + gath_last[sw] + rxe_cons +
                     (xgd if st5 == 0 else []), stream=sw)
            gath_last[sw] = [gt_]
            gths.append(gt_)
        xms_cons = gths
        cjs = []
        for j in range(4):
            cj = p.op("vector", (lambda jj: lambda e: e.tensor_copy(
                rxf[:, jj * 1024:(jj + 1) * 1024],
                rxe[:, jj * 1024:(jj + 1) * 1024]))(j),
                deps=[gths[j]] + rxf_cons)
            cjs.append(cj)
        rxf_cons = []
        evs = {}
        for j in range(4):
            for kc in range(8):
                b2 = (j * 8 + kc) % 2
                tr = p.op("tensor", (lambda jj, k, b: lambda e: e.transpose(
                    ps[2 + b][:, 0:128], rxf[:, jj * 1024 + k * 128:jj * 1024 + (k + 1) * 128],
                    idf32[:, :]))(j, kc, b2),
                    deps=[cjs[j], t_idbf] + trot[b2])
                ev = p.op("scalar", (lambda jj, k, b: lambda e: e.copy(
                    rhf[:, k * 512 + jj * 128:k * 512 + (jj + 1) * 128],
                    ps[2 + b][:, 0:128]))(j, kc, b2), deps=[tr] + rhf_cons)
                trot[b2] = [ev]
                evs[(j, kc)] = ev
        rhf_cons = []
        rxe_cons = list(cjs)
        rxf_cons = [evs[(j, kc)] for j in range(4) for kc in range(8)]
        ev_all = list(rxf_cons)
        mm_last = None
        for hc in range(32):
            bnk = hc % 2
            mma = mmg = None
            for kc in range(8):
                mma = p.op("tensor", (lambda h_, k, b: lambda e: e.matmul(
                    ps[4 + b][:, :], w1b[:, k * 4096 + h_ * 128:k * 4096 + (h_ + 1) * 128],
                    rhf[:, k * 512:(k + 1) * 512],
                    start=(k == 0), stop=(k == 7)))(hc, kc, bnk),
                    deps=[lw1] + ev_all + (mrot[bnk] if kc == 0 else []) +
                         ([ps_rot[bnk][0]] if (st5 == 0 and hc < 2 and ps_rot[bnk]) else []) +
                         (ps5_free if (st5 == 0 and hc == 1 and kc == 0) else []))
                mmg = p.op("tensor", (lambda h_, k, b: lambda e: e.matmul(
                    ps[6 + b][:, :], w3b[:, k * 4096 + h_ * 128:k * 4096 + (h_ + 1) * 128],
                    rhf[:, k * 512:(k + 1) * 512],
                    start=(k == 0), stop=(k == 7)))(hc, kc, bnk),
                    deps=[lw3] + ev_all + (mrot[bnk] if kc == 0 else []) +
                         ([oev] if (st5 == 0 and hc < 2) else []))
            ev_all = []
            sl2 = hc % 2
            t_g = p.op("scalar", (lambda b, s2: lambda e: e.activation(
                gact[:, s2 * 512:(s2 + 1) * 512], ps[4 + b][:, :], AF.Silu))(bnk, sl2),
                deps=[mma] + hstc[sl2])
            t_h = p.op("vector", (lambda b, s2: lambda e: e.tensor_mul(
                hst[:, s2 * 512:(s2 + 1) * 512], gact[:, s2 * 512:(s2 + 1) * 512],
                ps[6 + b][:, :]))(bnk, sl2), deps=[t_g, mmg])
            mrot[bnk] = [t_h]
            mm_last = mma
            rhf_cons.append(t_h)
            t_s = p.dma("scalar", (lambda h_, t_, s2: lambda e: e.dma_start(
                out=he_d[h_ * 128:(h_ + 1) * 128, t_ * 512:(t_ + 1) * 512],
                in_=hst[:, s2 * 512:(s2 + 1) * 512]))(hc, st5, sl2),
                deps=[t_h], stream="hs%d" % sl2)
            hstc[sl2] = [t_s]
            he_toks.append(t_s)
        rhf_cons = rhf_cons[-2:]
    return dict(he_d=he_d, xm_d=xm_d, w2e=w2e, he_toks=he_toks,
                part_zero=part_zero, all_sc=all_sc)


def emit_moe_ep_b(p: Prog, H):
    from concourse.bass_core_rust import IndirectOffsetOnAxis as IOA
    he_d, xm_d, w2e = H["he_d"], H["xm_d"], H["w2e"]
    part = p.io["part"]
    w2sb = p.sb("w2sb", [128, 32 * 1024], BF16)
    hle = p.sb("hle", [128, 2 * 4096], BF16)
    yet = p.sb("yet", [128, 2 * 1024], BF16)
    xmt = p.sb("xmt", [128, 2 * 2], F32)
    iib = p.sb("iib", [128, 4], I32)
    scb = p.sb("scb", [128, 8], F32)
    ps = [p.ps(f"ps{i}", [128, 512], F32) for i in range(8)]

    lw2 = p.dma("gpsimd", lambda e: e.dma_start(
        out=w2sb[:, :].rearrange("p (c m) -> p c m", c=32),
        in_=w2e.rearrange("(c p) m -> p c m", p=128)), stream="w2l")
    hle_cons = [[], []]
    xmt_cons = [[], []]
    yet_cons = [[], []]
    sc_last = {s: [] for s in SWS}
    prot = [[], []]
    for st in range(40):
        sl = st % 2
        sw = SWS[st % 4]
        ld_h = p.dma("sync", (lambda t_, s: lambda e: e.dma_start(
            out=hle[:, s * 4096:(s + 1) * 4096].rearrange("p (c m) -> p c m", c=32),
            in_=he_d[:, t_ * 128:(t_ + 1) * 128].rearrange("(c p) m -> p c m", p=128)))(st, sl),
            deps=hle_cons[sl] + (H["he_toks"] if st < 2 else []),
            stream=["hrr", "xgr1"][sl])
        ld_m = p.dma("sync", (lambda t_, s: lambda e: e.dma_start(
            out=xmt[:, 2 * s:2 * s + 2], in_=xm_d[t_ * 128:(t_ + 1) * 128, :]))(st, sl),
            deps=xmt_cons[sl] + (H["all_sc"] if st < 2 else []),
            stream=["or0", "or1"][sl])
        v = p.op("vector", (lambda s: lambda e: e.tensor_scalar(
            scb[:, s:s + 1], xmt[:, 2 * s + 1:2 * s + 2], 0.0, None, ALU.is_equal))(sl),
            deps=[ld_m])
        m_ = p.op("vector", (lambda s: lambda e: e.tensor_scalar(
            scb[:, 2 + s:3 + s], scb[:, s:s + 1], 1000001.0, None, ALU.mult))(sl),
            deps=[v])
        a_ = p.op("vector", (lambda s: lambda e: e.tensor_add(
            scb[:, 2 + s:3 + s], scb[:, 2 + s:3 + s], xmt[:, 2 * s + 1:2 * s + 2]))(sl),
            deps=[m_])
        a2 = p.op("vector", (lambda s: lambda e: e.tensor_scalar(
            scb[:, 2 + s:3 + s], scb[:, 2 + s:3 + s], -1.0, None, ALU.add))(sl),
            deps=[a_])
        sl4 = st % 4
        cv = p.op("vector", (lambda s, s4: lambda e: e.tensor_copy(
            iib[:, s4:s4 + 1], scb[:, 2 + s:3 + s]))(sl, sl4), deps=[a2] + sc_last[sw])
        evs = []
        for dc in range(2):
            mm = None
            for hc in range(32):
                mm = p.op("tensor", (lambda d_, h_, s: lambda e: e.matmul(
                    ps[d_][:, :], hle[:, s * 4096 + h_ * 128:s * 4096 + (h_ + 1) * 128],
                    w2sb[:, h_ * 1024 + d_ * 512:h_ * 1024 + d_ * 512 + 512],
                    start=(h_ == 0), stop=(h_ == 31)))(dc, hc, sl),
                    deps=[lw2, ld_h] + (prot[dc] if hc == 0 else []))
            ev = p.op("vector", (lambda d_, s: lambda e: e.tensor_scalar(
                yet[:, s * 1024 + d_ * 512:s * 1024 + (d_ + 1) * 512], ps[d_][:, :],
                xmt[:, 2 * s:2 * s + 1], None, ALU.mult))(dc, sl),
                deps=[mm, ld_m] + yet_cons[sl])
            prot[dc] = [ev]
            evs.append(ev)
        hle_cons[sl] = [mm]
        sc = p.dma("gpsimd", (lambda s, s4: lambda e: e.indirect_dma_start(
            out=part[:, :], out_offset=IOA(iib[:, s4:s4 + 1], 0),
            in_=yet[:, s * 1024:(s + 1) * 1024], in_offset=None,
            bounds_check=_bc_reg(p, e, TG - 1), oob_is_err=False))(sl, sl4),
            deps=evs + [cv] + (H["part_zero"] if st < 4 else []) + sc_last[sw],
            stream=sw)
        sc_last[sw] = [sc]
        yet_cons[sl] = [sc]
        xmt_cons[sl] = [cv] + evs
    return [t for s in SWS for t in sc_last[s]]



def emit_moe_reduce(p: Prog):
    x3own = p.get_in("x3own", [D, T], F32)
    parts = [p.get_in("part%d" % j, [D, T], BF16) for j in range(8)]
    x4_fm = p.get_out("x4_fm", [D, T], F32)
    xs = p.sb("xs", [128, 2 * 512], F32)
    pb = p.sb("pb", [128, 2 * 512], BF16)
    acc = p.sb("acc", [128, 2 * 512], F32)
    xc = [[], []]
    pc = [[], []]
    ac = [[], []]
    for kc in range(8):
        for tt in range(4):
            sl = (kc * 4 + tt) % 2
            t_x = p.dma("sync", (lambda k, t_, s: lambda e: e.dma_start(
                out=xs[:, s * 512:(s + 1) * 512],
                in_=x3own[k * 128:(k + 1) * 128, t_ * 512:(t_ + 1) * 512]))(kc, tt, sl),
                deps=xc[sl], stream="x%d" % sl)
            last = None
            for j in range(8):
                t_p = p.dma("gpsimd", (lambda jj, k, t_, s: lambda e: e.dma_start(
                    out=pb[:, s * 512:(s + 1) * 512],
                    in_=parts[jj][k * 128:(k + 1) * 128, t_ * 512:(t_ + 1) * 512]))(j, kc, tt, sl),
                    deps=pc[sl], stream="p%d" % sl)
                if j == 0:
                    a = p.op("vector", (lambda s: lambda e: e.tensor_add(
                        acc[:, s * 512:(s + 1) * 512], xs[:, s * 512:(s + 1) * 512],
                        pb[:, s * 512:(s + 1) * 512]))(sl), deps=[t_x, t_p] + ac[sl])
                else:
                    a = p.op("vector", (lambda s: lambda e: e.tensor_add(
                        acc[:, s * 512:(s + 1) * 512], acc[:, s * 512:(s + 1) * 512],
                        pb[:, s * 512:(s + 1) * 512]))(sl), deps=[t_p, last])
                pc[sl] = [a]
                last = a
            xc[sl] = [last]
            st = p.out_dma("sync", (lambda k, t_, s: lambda e: e.dma_start(
                out=x4_fm[k * 128:(k + 1) * 128, t_ * 512:(t_ + 1) * 512],
                in_=acc[:, s * 512:(s + 1) * 512]))(kc, tt, sl), deps=[last], stream="st%d" % sl)
            ac[sl] = [st]


def build_moe_reduce():
    p = Prog("moer")
    emit_moe_reduce(p)
    return p


def prep_moe_dense_inputs(inputs, xln4_all, meta_all):
    tris = np_bf((np.arange(128)[:, None] < np.arange(128)[None, :]).astype(np.float32))
    cone255 = np.zeros((128, 255), dtype=np.float32)
    cone255[:, 127] = 1.0
    iota8 = np.broadcast_to(np.arange(8, dtype=np.float32)[None, :], (128, 8)).copy()
    iota128 = np.eye(128, dtype=np.float32)
    in_maps = []
    for c in range(NCORE):
        in_maps.append(dict(
            xg=xln4_all, meta_all=meta_all,
            w1e=inputs["L1_ffn_W1"][c], w3e=inputs["L1_ffn_W3"][c],
            w2e=inputs["L1_ffn_W2"][c],
            myexp=np.full((128, 1), float(c), dtype=np.float32),
            c_tris=tris, c_cone255=np_bf(cone255), c_iota8=iota8,
            c_iota128=iota128,
        ))
    return in_maps


_BUILT = {}


def _get(name, builder):
    if name not in _BUILT:
        _BUILT[name] = builder().build()
    return _BUILT[name]


def kernel(**inputs):
    inputs = {k: np.asarray(v) for k, v in inputs.items()}
    if KERNEL_MODE == "fused":
        return kernel_fused(inputs)
    cores = list(range(NCORE))

    nc1 = _get("l1", build_l0_attn)
    r1 = run_bass_kernel_spmd(nc1, prep_l0_attn_inputs(inputs), core_ids=cores)
    x1 = np.empty((4, NSEQ, D), dtype=np.float32)
    for c in cores:
        b, half = c // 2, c % 2
        x1[b, half * T:(half + 1) * T, :] = r1.results[c]["x1_fm"].T

    nc2 = _get("l2", build_l0_ffn)
    r2 = run_bass_kernel_spmd(nc2, prep_l0_ffn_inputs(inputs, x1), core_ids=cores)
    x2 = np.empty((4, NSEQ, D), dtype=np.float32)
    for c in cores:
        b, half = c // 2, c % 2
        x2[b, half * T:(half + 1) * T, :] = r2.results[c]["x2_fm"].T

    nc3 = _get("l3", build_l1_attn)
    r3 = run_bass_kernel_spmd(nc3, prep_l1_attn_inputs(inputs, x2), core_ids=cores)
    o_outs = [r3.results[c]["o_out"] for c in cores]

    nc4a = _get("l4a", build_l1_proj_router)
    r4a = run_bass_kernel_spmd(nc4a, prep_l1_proj_router_inputs(inputs, x2, o_outs),
                               core_ids=cores)
    x3 = {c: r4a.results[c]["x3_fm"] for c in cores}
    xln4_all = np.concatenate([r4a.results[c]["xln4"] for c in cores], axis=1)
    meta_all = np.concatenate([r4a.results[c]["meta"] for c in cores], axis=0)

    nc4b = _get("l4b", build_moe_dense)
    r4b = run_bass_kernel_spmd(nc4b, prep_moe_dense_inputs(inputs, xln4_all, meta_all),
                               core_ids=cores)
    parts = [r4b.results[c]["part"] for c in cores]

    nc4c = _get("l4c", build_moe_reduce)
    in4c = []
    for c in cores:
        m = dict(x3own=x3[c])
        for j in range(8):
            m["part%d" % j] = np.ascontiguousarray(parts[j][:, c * T:(c + 1) * T])
        in4c.append(m)
    r4c = run_bass_kernel_spmd(nc4c, in4c, core_ids=cores)
    out = np.empty((4, NSEQ, D), dtype=np.float32)
    for c in cores:
        b, half = c // 2, c % 2
        out[b, half * T:(half + 1) * T, :] = r4c.results[c]["x4_fm"].T
    return out


PAIRS_RG = [[0, 1], [2, 3], [4, 5], [6, 7]]
FULL_RG = [list(range(8))]


def _copy_dram(p, dst_ap, src_ap, stream, deps=(), eng="gpsimd"):
    return p.dma(eng, lambda e: e.dma_start(out=dst_ap, in_=src_ap),
                 deps=list(deps), stream=stream)


TRUNC = None
TRUNC2 = None
SKIP_B_V = False
SKIP_B_QK = False
DDS = 16384


def _trunc_out(p):
    x4 = p.outp("x4_fm", [D, T], F32)
    z = p.sb("tz", [128, 512], F32)
    tm = p.op("vector", lambda e: e.memset(z[:, :], 0.0))
    for kc in range(8):
        for tt in range(4):
            p.out_dma("sync", (lambda k, t_: lambda e: e.dma_start(
                out=x4[k * 128:(k + 1) * 128, t_ * 512:(t_ + 1) * 512],
                in_=z[:, :]))(kc, tt), deps=[tm], stream="fs0")


def emit_fused(p: Prog):
    x1_d = p.dram("x1_d", [D, T], F32)
    p.io = {"x1_fm": x1_d}
    emit_l0_attn(p)
    p.new_stage()
    if TRUNC == 0:
        _trunc_out(p)
        return

    hb_in = p.dram("hb_in", [D, 1536], BF16)
    hb_out = p.dram("hb_out", [2 * D, 1536], BF16)
    x1h_d = p.dram("x1h_d", [D, W2], F32)
    hm = p.inp("halo_mask", [128, 1], F32)
    t1 = _copy_dram(p, hb_in[:, :], x1_d[:, 512:2048], "cp1")
    tc1 = p.coll(lambda e: e.collective_compute(
        "AllGather", mybir.AluOpType.bypass, replica_groups=PAIRS_RG,
        ins=[hb_in[:, :].opt()], outs=[hb_out[:, :].opt()]), deps=[t1], stream="cc1")
    mb = p.sb("mb", [128, 2 * 512], BF16)
    mf = p.sb("mf", [128, 2 * 512], F32)
    hm_sb = p.sb("hm_sb", [128, 1], F32)
    tl = emit_load2(p, hm_sb, (0, 1), hm[:, :], stream="konst")
    mcons = [[], []]
    fcons = [[], []]
    for kc in range(8):
        for tt in range(3):
            sl = (kc * 3 + tt) % 2
            ld = p.dma("sync", (lambda k, t_, s: lambda e: e.dma_start(
                out=mb[:, s * 512:(s + 1) * 512],
                in_=hb_out[k * 128:(k + 1) * 128, t_ * 512:(t_ + 1) * 512]))(kc, tt, sl),
                deps=[tc1] + mcons[sl], stream="hl%d" % sl)
            op = p.op("vector", (lambda s: lambda e: e.tensor_scalar_mul(
                mf[:, s * 512:(s + 1) * 512], mb[:, s * 512:(s + 1) * 512],
                hm_sb[:, 0:1]))(sl), deps=[ld, tl] + fcons[sl])
            mcons[sl] = [op]
            st = p.dma("sync", (lambda k, t_, s: lambda e: e.dma_start(
                out=x1h_d[k * 128:(k + 1) * 128, t_ * 512:(t_ + 1) * 512],
                in_=mf[:, s * 512:(s + 1) * 512]))(kc, tt, sl), deps=[op],
                stream="hs%d" % sl)
            fcons[sl] = [st]
    tow = _copy_dram(p, x1h_d[:, 1536:3584], x1_d[:, :], "cp2", eng="sync")
    p.new_stage()
    if TRUNC == 1:
        _trunc_out(p)
        return

    x2_d = p.dram("x2_d", [D, T], F32)
    p.io = {"x1h": x1h_d, "x2_fm": x2_d}
    emit_l0_ffn(p)
    p.new_stage()
    if TRUNC == 2:
        _trunc_out(p)
        return

    xf_in = p.dram("xf_in", [D, T], F32)
    xf_out = p.dram("xf_out", [2 * D, T], F32)
    xfull_d = p.dram("xfull_d", [D, W3], F32)
    t2 = _copy_dram(p, xf_in[:, :], x2_d[:, :], "cp3", eng="sync")
    tc2 = p.coll(lambda e: e.collective_compute(
        "AllGather", mybir.AluOpType.bypass, replica_groups=PAIRS_RG,
        ins=[xf_in[:, :].opt()], outs=[xf_out[:, :].opt()]), deps=[t2], stream="cc2")
    t3 = _copy_dram(p, xfull_d[:, 0:T], xf_out[0:D, :], "cp4", deps=[tc2], eng="sync")
    t4 = _copy_dram(p, xfull_d[:, T:2 * T], xf_out[D:2 * D, :], "cp4", deps=[tc2], eng="sync")
    p.new_stage()
    if TRUNC == 3:
        _trunc_out(p)
        return

    o_out_d = p.dram("o_out_d", [512, W3], BF16)
    p.io = {"xfull": xfull_d, "o_out": o_out_d}
    emit_l1_attn(p)
    p.new_stage()
    if TRUNC == 4:
        _trunc_out(p)
        return

    ag_o = p.dram("ag_o", [2 * 512, W3], BF16)
    o_mine_d = p.dram("o_mine_d", [D, T], BF16)
    wm0 = p.inp("win_m0", [128, 1], F32)
    wm1 = p.inp("win_m1", [128, 1], F32)
    tc3 = p.coll(lambda e: e.collective_compute(
        "AllGather", mybir.AluOpType.bypass, replica_groups=PAIRS_RG,
        ins=[o_out_d[:, :].opt()], outs=[ag_o[:, :].opt()]), deps=[], stream="cc3")
    wa_sb = p.sb("wa_sb", [128, 2], F32)
    twm = emit_load2(p, wa_sb, (0, 1), wm0[:, :], stream="konst")
    twm2 = emit_load2(p, wa_sb, (1, 2), wm1[:, :], stream="konst")
    ma = p.sb("ma", [128, 2 * 512], BF16)
    mb2 = p.sb("mb2", [128, 2 * 512], BF16)
    mo = p.sb("mo", [128, 2 * 512], BF16)
    mac = [[], []]
    moc = [[], []]
    for kc in range(8):
        for tt in range(4):
            sl = (kc * 4 + tt) % 2
            la = p.dma("sync", (lambda k, t_, s: lambda e: e.dma_start(
                out=ma[:, s * 512:(s + 1) * 512],
                in_=ag_o[k * 128:(k + 1) * 128, t_ * 512:(t_ + 1) * 512]))(kc, tt, sl),
                deps=[tc3] + mac[sl], stream="oa%d" % sl)
            lb = p.dma("sync", (lambda k, t_, s: lambda e: e.dma_start(
                out=mb2[:, s * 512:(s + 1) * 512],
                in_=ag_o[k * 128:(k + 1) * 128, T + t_ * 512:T + (t_ + 1) * 512]))(kc, tt, sl),
                deps=[tc3] + mac[sl], stream="ob%d" % sl)
            o1 = p.op("vector", (lambda s: lambda e: e.tensor_scalar_mul(
                mo[:, s * 512:(s + 1) * 512], ma[:, s * 512:(s + 1) * 512],
                wa_sb[:, 0:1]))(sl), deps=[la, twm] + moc[sl])
            o2 = p.op("vector", (lambda s: lambda e: e.scalar_tensor_tensor(
                mo[:, s * 512:(s + 1) * 512], mb2[:, s * 512:(s + 1) * 512],
                wa_sb[:, 1:2], mo[:, s * 512:(s + 1) * 512],
                ALU.mult, ALU.add))(sl), deps=[o1, lb, twm2])
            mac[sl] = [o2]
            st = p.dma("sync", (lambda k, t_, s: lambda e: e.dma_start(
                out=o_mine_d[k * 128:(k + 1) * 128, t_ * 512:(t_ + 1) * 512],
                in_=mo[:, s * 512:(s + 1) * 512]))(kc, tt, sl), deps=[o2],
                stream="os%d" % sl)
            moc[sl] = [st]
    p.new_stage()
    if TRUNC == 5:
        _trunc_out(p)
        return

    x3_keep = p.dram("x3_keep", [D, T], F32)
    xln4_d = p.dram("xln4_d", [D, T], BF16)
    meta_d = p.dram("meta_d", [T, 4], F32)
    p.io = {"x2own": x2_d, "o_mine": o_mine_d, "x3_fm": x3_keep,
            "xln4": xln4_d, "meta": meta_d}
    emit_l1_proj_router(p)
    p.new_stage()
    if TRUNC == 6:
        _trunc_out(p)
        return

    meta_ag = p.dram("meta_ag", [TG, 4], F32)
    tc5 = p.coll(lambda e: e.collective_compute(
        "AllGather", mybir.AluOpType.bypass, replica_groups=FULL_RG,
        ins=[meta_d[:, :].opt()], outs=[meta_ag[:, :].opt()]), deps=[], stream="cc5")
    if MOE_MODE == "dense":
        xg_ag = p.dram("xg_ag", [8 * D, T], BF16)
        xg_d = p.dram("xg_d", [D, TG], BF16)
        tc4 = p.coll(lambda e: e.collective_compute(
            "AllGather", mybir.AluOpType.bypass, replica_groups=FULL_RG,
            ins=[xln4_d[:, :].opt()], outs=[xg_ag[:, :].opt()]), deps=[], stream="cc4")
        xgc = []
        for r in range(8):
            xgc.append(_copy_dram(p, xg_d[:, r * T:(r + 1) * T],
                                  xg_ag[r * D:(r + 1) * D, :], "cp5", deps=[tc4], eng="sync"))
    else:
        xt_tok = p.dram("xt_tok", [T, D], BF16)
        xg_tok = p.dram("xg_tok", [TG, D], BF16)
        idbf_in = p.inp("c_idbf", [128, 128], F32)
        idb7 = p.sb("idb7", [128, 128], F32)
        tid = emit_load2(p, idb7, (0, 128), idbf_in[:, :], stream="konst")
        lxb = p.sb("lxb", [128, 2 * 128], BF16)
        lxf = p.sb("lxf", [128, 2 * 128], F32)
        xtb = p.sb("xtb", [128, 8 * 128], BF16)
        ps7t = [p.ps(f"ps{i}", [128, 512], F32) for i in range(8)]
        lx_cons = [[], []]
        tb_rot = [[], []]
        st_prev = []
        stv_all = []
        for tb in range(16):
            evs7 = []
            for kc in range(8):
                sl = (tb * 8 + kc) % 2
                ld = p.dma("sync", (lambda k, t_, s: lambda e: e.dma_start(
                    out=lxb[:, s * 128:(s + 1) * 128],
                    in_=xln4_d[k * 128:(k + 1) * 128, t_ * 128:(t_ + 1) * 128]))(kc, tb, sl),
                    deps=lx_cons[sl], stream="oa%d" % sl)
                cpf = p.op("vector", (lambda s: lambda e: e.tensor_copy(
                    lxf[:, s * 128:(s + 1) * 128], lxb[:, s * 128:(s + 1) * 128]))(sl),
                    deps=[ld] + tb_rot[sl])
                tr = p.op("tensor", (lambda s, b: lambda e: e.transpose(
                    ps7t[2 + b][:, 0:128], lxf[:, s * 128:(s + 1) * 128],
                    idb7[:, :]))(sl, sl), deps=[cpf, tid] + tb_rot[sl])
                ev = p.op("scalar", (lambda k, b: lambda e: e.copy(
                    xtb[:, k * 128:(k + 1) * 128], ps7t[2 + b][:, 0:128]))(kc, sl),
                    deps=[tr] + st_prev)
                lx_cons[sl] = [cpf]
                tb_rot[sl] = [ev]
                evs7.append(ev)
            st_prev = []
            stv = p.dma("sync", (lambda t_: lambda e: e.dma_start(
                out=xt_tok[t_ * 128:(t_ + 1) * 128, :], in_=xtb[:, :]))(tb),
                deps=evs7, stream="os0")
            st_prev = [stv]
            stv_all.append(stv)
        tc4 = p.coll(lambda e: e.collective_compute(
            "AllGather", mybir.AluOpType.bypass, replica_groups=FULL_RG,
            ins=[xt_tok[:, :].opt()], outs=[xg_tok[:, :].opt()]), deps=stv_all,
            stream="cc4", sem="cc2")
    if MOE_MODE == "dense":
        p.new_stage()
        if TRUNC == 7:
            _trunc_out(p)
            return
    if MOE_MODE == "dense":
        part_rs = p.dram("part_rs", [8 * D, T], BF16)
        p.io = {"xg": xg_d, "meta_all": meta_ag, "part": part_rs,
                "part_rs_layout": True}
        emit_moe_dense(p)
        p.new_stage()
        if TRUNC == 8:
            _trunc_out(p)
            return

        rs_out = p.dram("rs_out", [D, T], BF16)
        tc6 = p.coll(lambda e: e.collective_compute(
            "ReduceScatter", mybir.AluOpType.add, replica_groups=FULL_RG,
            ins=[part_rs[:, :].opt()], outs=[rs_out[:, :].opt()]), deps=[], stream="cc6")
        x4 = p.outp("x4_fm", [D, T], F32)
        fx = p.sb("fx", [128, 2 * 512], F32)
        fp = p.sb("fp", [128, 2 * 512], BF16)
        fo = p.sb("fo", [128, 2 * 512], F32)
        xc = [[], []]
        pc = [[], []]
        oc = [[], []]
        for kc in range(8):
            for tt in range(4):
                sl = (kc * 4 + tt) % 2
                t_x = p.dma("sync", (lambda k, t_, s: lambda e: e.dma_start(
                    out=fx[:, s * 512:(s + 1) * 512],
                    in_=x3_keep[k * 128:(k + 1) * 128, t_ * 512:(t_ + 1) * 512]))(kc, tt, sl),
                    deps=xc[sl], stream="fx%d" % sl)
                t_p = p.dma("sync", (lambda k, t_, s: lambda e: e.dma_start(
                    out=fp[:, s * 512:(s + 1) * 512],
                    in_=rs_out[k * 128:(k + 1) * 128, t_ * 512:(t_ + 1) * 512]))(kc, tt, sl),
                    deps=[tc6] + pc[sl], stream="fp%d" % sl)
                a = p.op("vector", (lambda s: lambda e: e.tensor_add(
                    fo[:, s * 512:(s + 1) * 512], fx[:, s * 512:(s + 1) * 512],
                    fp[:, s * 512:(s + 1) * 512]))(sl), deps=[t_x, t_p] + oc[sl])
                xc[sl] = [a]
                pc[sl] = [a]
                st = p.out_dma("sync", (lambda k, t_, s: lambda e: e.dma_start(
                    out=x4[k * 128:(k + 1) * 128, t_ * 512:(t_ + 1) * 512],
                    in_=fo[:, s * 512:(s + 1) * 512]))(kc, tt, sl), deps=[a],
                    stream="fs%d" % sl)
                oc[sl] = [st]
        return

    p.soft_stage()
    part_tok = p.dram("part_tok", [TG, D], BF16)
    p.io = {"xg_tok": xg_tok, "meta_all": meta_ag, "part": part_tok,
            "meta_dep": [tc5], "xg_dep": [tc4], "ps23_seed": tb_rot,
            "st_seed": st_prev}
    H = emit_moe_ep_a(p)
    p.new_stage()
    if TRUNC == 8:
        _trunc_out(p)
        return

    ep_sc = emit_moe_ep_b(p, H)
    p.new_stage()

    rs_tok = p.dram("rs_tok", [T, D], BF16)
    tc6 = p.coll(lambda e: e.collective_compute(
        "ReduceScatter", mybir.AluOpType.add, replica_groups=FULL_RG,
        ins=[part_tok[:, :].opt()], outs=[rs_tok[:, :].opt()]), deps=[], stream="cc6")
    x4 = p.outp("x4_fm", [D, T], F32)
    idb9 = p.sb("idb9", [128, 128], F32)
    tid9 = emit_load2(p, idb9, (0, 128), idbf_in[:, :], stream="konst")
    fx = p.sb("fx", [128, 2 * 512], F32)
    rsb = p.sb("rsb", [128, 2 * 128], BF16)
    rsf = p.sb("rsf", [128, 2 * 128], F32)
    fo = p.sb("fo", [128, 2 * 512], F32)
    ps9 = [p.ps(f"ps{i}", [128, 512], F32) for i in range(8)]
    xc = [[], []]
    rc = [[], []]
    oc = [[], []]
    tb_rot9 = [[], []]
    for kc in range(8):
        for tt in range(4):
            sl = (kc * 4 + tt) % 2
            t_x = p.dma("sync", (lambda k, t_, s: lambda e: e.dma_start(
                out=fx[:, s * 512:(s + 1) * 512],
                in_=x3_keep[k * 128:(k + 1) * 128, t_ * 512:(t_ + 1) * 512]))(kc, tt, sl),
                deps=xc[sl], stream="fx%d" % sl)
            adds = []
            for j in range(4):
                sj = (tt * 4 + j) % 2
                lr = p.dma("sync", (lambda k, t_, jj, s: lambda e: e.dma_start(
                    out=rsb[:, s * 128:(s + 1) * 128],
                    in_=rs_tok[t_ * 512 + jj * 128:t_ * 512 + (jj + 1) * 128,
                               k * 128:(k + 1) * 128]))(kc, tt, j, sj),
                    deps=[tc6] + rc[sj], stream="fp%d" % sj)
                cpf = p.op("vector", (lambda s: lambda e: e.tensor_copy(
                    rsf[:, s * 128:(s + 1) * 128], rsb[:, s * 128:(s + 1) * 128]))(sj),
                    deps=[lr] + tb_rot9[sj])
                tr = p.op("tensor", (lambda s, b: lambda e: e.transpose(
                    ps9[2 + b][:, 0:128], rsf[:, s * 128:(s + 1) * 128],
                    idb9[:, :]))(sj, sj), deps=[cpf, tid9] + tb_rot9[sj])
                a = p.op("vector", (lambda s, jj, b: lambda e: e.tensor_add(
                    fo[:, s * 512 + jj * 128:s * 512 + (jj + 1) * 128],
                    fx[:, s * 512 + jj * 128:s * 512 + (jj + 1) * 128],
                    ps9[2 + b][:, 0:128]))(sl, j, sj),
                    deps=[tr, t_x] + oc[sl])
                rc[sj] = [cpf]
                tb_rot9[sj] = [a]
                adds.append(a)
            xc[sl] = adds
            st = p.out_dma("sync", (lambda k, t_, s: lambda e: e.dma_start(
                out=x4[k * 128:(k + 1) * 128, t_ * 512:(t_ + 1) * 512],
                in_=fo[:, s * 512:(s + 1) * 512]))(kc, tt, sl), deps=adds,
                stream="fs%d" % sl)
            oc[sl] = [st]


def build_fused():
    p = Prog("mkfused", fused=True)
    emit_fused(p)
    return p


def prep_fused_inputs(inputs):
    m1 = prep_l0_attn_inputs(inputs)
    dummy_x = np.zeros((4, NSEQ, D), dtype=np.float32)
    m2 = prep_l0_ffn_inputs(inputs, dummy_x)
    m3 = prep_l1_attn_inputs(inputs, dummy_x)
    m4a = prep_l1_proj_router_inputs(inputs, dummy_x, [np.zeros((512, W3), dtype=BF)] * 8)
    m4b = prep_moe_dense_inputs(inputs, np.zeros((D, TG), dtype=BF),
                                np.zeros((TG, 4), dtype=np.float32))
    in_maps = []
    for c in range(NCORE):
        b, half = c // 2, c % 2
        m = {}
        for stage, mm in (("s0", m1[c]), ("s2", m2[c]), ("s4", m3[c]),
                          ("s6", m4a[c]), ("s8", m4b[c])):
            for k, v in mm.items():
                if k in ("x_fm",):
                    m[stage + "_" + k] = v
                elif k in ("x1h", "xfull", "x2own", "o_mine", "xg", "meta_all"):
                    continue
                else:
                    m[stage + "_" + k] = v
        m["halo_mask"] = np.full((128, 1), 1.0 if half == 1 else 0.0, dtype=np.float32)
        m["win_m0"] = np.full((128, 1), 1.0 if half == 0 else 0.0, dtype=np.float32)
        m["win_m1"] = np.full((128, 1), 1.0 if half == 1 else 0.0, dtype=np.float32)
        if MOE_MODE == "ep":
            m["c_idbf"] = np.eye(128, dtype=np.float32)
            m["s8_c_pcol"] = np.arange(128, dtype=np.float32).reshape(128, 1)
        in_maps.append(m)
    return in_maps


KERNEL_MODE = "fused"
MOE_MODE = "ep"


def kernel_fused(inputs):
    nc = _get("fused", build_fused)
    in_maps = prep_fused_inputs(inputs)
    res = run_bass_kernel_spmd(nc, in_maps, core_ids=list(range(NCORE)))
    out = np.empty((4, NSEQ, D), dtype=np.float32)
    for c in range(NCORE):
        b, half = c // 2, c % 2
        out[b, half * T:(half + 1) * T, :] = res.results[c]["x4_fm"].T
    return out


# version 08
# speedup vs baseline: 1.0034x; 1.0034x over previous
import numpy as np
import ml_dtypes
import concourse.bass as bass
import concourse.mybir as mybir
from concourse.bass_utils import run_bass_kernel_spmd
from concourse.bass_core_rust import IndirectOffsetOnAxis

F32 = mybir.dt.float32
BF16 = mybir.dt.bfloat16
I32 = mybir.dt.int32
AF = mybir.ActivationFunctionType
ALU = mybir.AluOpType
AX = mybir.AxisListType
BF = ml_dtypes.bfloat16

NCORE = 8
T = 2048
D = 1024
NSEQ = 4096
MASK_C = float(2 ** 18)

ENGINES = ("sync", "tensor", "vector", "scalar", "gpsimd")
DEBUG_DUMPS = False


class Tok:
    __slots__ = ("sem", "val", "clk")

    def __init__(self, sem, val, clk=None):
        self.sem = sem
        self.val = val
        self.clk = clk


class Prog:

    def __init__(self, name="prog", fused=False):
        self.nc = bass.Bass("TRN2", target_bir_lowering=False, name=name,
                            dynamic_dma_scratch_size=DDS if fused else 16384)
        self.queues = {e: [] for e in ENGINES}
        self.sems = {}
        self.cnt = {}
        self.waited = {e: {} for e in ENGINES}
        for e in ENGINES:
            self._mksem("p_" + e)
        for e in ("sync", "scalar", "gpsimd"):
            self._mksem("d_" + e)
        self._mksem("cc")
        self.inputs = {}
        self.outputs = {}
        self.out_toks = []
        self.fused = fused
        self.stage = 0
        self._arena_base = None
        self._arena_off = 0
        self._shared_ps = None
        self.io = None
        self._mksem("bar")
        self.clock = {e: {} for e in ENGINES}
        self.hb_viol = []

    def new_stage(self):
        self.stage += 1
        k = self.stage
        pre = [Tok(s, self.cnt[s]) for s in self.cnt
               if self.cnt[s] > 0 and s != "bar"]
        for e in ENGINES:
            self.op(e, (lambda ee: lambda eng: eng.engine_nop()
                        if hasattr(eng, "engine_nop") else eng.nop())(e),
                    deps=pre, inc=("bar", 1))
        bar_t = Tok("bar", self.cnt["bar"])
        for e in ENGINES:
            self.queues[e].append(([bar_t], lambda eng: None, self.sems["bar"], 0, e))
        self._arena_off = 0
        return bar_t

    def _mksem(self, name):
        self.sems[name] = self.nc.alloc_semaphore(name)
        self.cnt[name] = 0

    def inp(self, name, shape, dtype):
        ap = self.nc.dram_tensor(name, list(shape), dtype, kind="ExternalInput").ap()
        self.inputs[name] = ap
        return ap

    def outp(self, name, shape, dtype):
        ap = self.nc.dram_tensor(name, list(shape), dtype, kind="ExternalOutput").ap()
        self.outputs[name] = ap
        return ap

    def get_in(self, name, shape, dtype):
        if self.io is not None and name in self.io:
            return self.io[name]
        if self.fused:
            return self.inp("s%d_%s" % (self.stage, name), shape, dtype)
        return self.inp(name, shape, dtype)

    def get_out(self, name, shape, dtype):
        if self.io is not None and name in self.io:
            return self.io[name]
        return self.outp(name, shape, dtype)

    def dram(self, name, shape, dtype, addr_space="Local"):
        if self.fused:
            name = "s%d_%s" % (self.stage, name)
        return self.nc.dram_tensor(name, list(shape), dtype, addr_space=addr_space).ap()

    ARENA = 212000

    def sb(self, name, shape, dtype):
        if not self.fused:
            return self.nc.alloc_sbuf_tensor(name, list(shape), dtype)
        if self._arena_base is None:
            self._arena_base = self.nc.bump_sbuf(self.ARENA)[0]
        nbytes = int(np.prod(shape[1:])) * (2 if dtype == BF16 else 4)
        off = (self._arena_off + 31) // 32 * 32
        self._arena_off = off + nbytes
        assert self._arena_off <= self.ARENA, (name, self._arena_off)
        return self.nc.alloc_sbuf_tensor_at(
            "s%d_%s" % (self.stage, name), list(shape), dtype,
            offset=self._arena_base + off)

    def ps(self, name, shape, dtype=F32):
        if not self.fused:
            return self.nc.alloc_psum_tensor(name, list(shape), dtype)
        key = (name, tuple(shape), dtype)
        if self._shared_ps is None:
            self._shared_ps = {}
        if key not in self._shared_ps:
            self._shared_ps[key] = self.nc.alloc_psum_tensor(
                "ps_%s" % name, list(shape), dtype)
        return self._shared_ps[key]

    def op(self, eng, fn, deps=(), inc=None, n=1):
        if inc is None:
            inc = ("p_" + eng, 1)
        sem_name, amount = inc
        deps = [Tok(d.sem, self.cnt[d.sem], getattr(d, "clk", None))
                if (d.sem.startswith("d_") or d.sem.startswith("cc")) else d
                for d in deps if d is not None]
        ec = self.clock[eng]
        upd = {}
        for d in deps:
            if d.sem[0] == "d" and d.sem[1] == "_" and d.val > ec.get(d.sem, 0) \
                    and d.val > upd.get(d.sem, 0):
                upd[d.sem] = d.val
            if d.clk:
                for s, v in d.clk.items():
                    if v > ec.get(s, 0) and v > upd.get(s, 0):
                        upd[s] = v
        if upd:
            ec = dict(ec)
            ec.update(upd)
            self.clock[eng] = ec
        if sem_name[0] == "d" and sem_name[1] == "_":
            prior = self.cnt[sem_name]
            if ec.get(sem_name, 0) < prior:
                import traceback
                fr = traceback.extract_stack()[-3]
                self.hb_viol.append(
                    "HB RACE: %s DMA on %s: prior count %d, queue %s only knows %d"
                    " (stage %d, %s:%d)" % (eng, sem_name, prior, eng,
                                            ec.get(sem_name, 0), self.stage,
                                            fr.filename.split("/")[-1], fr.lineno))
        self.cnt[sem_name] += amount * n
        tokclk = ec
        if sem_name[0] == "d" and sem_name[1] == "_":
            tokclk = dict(ec)
            tokclk[sem_name] = self.cnt[sem_name]
        tok = Tok(sem_name, self.cnt[sem_name], tokclk)
        self.queues[eng].append((deps, fn, self.sems[sem_name], amount, eng))
        return tok

    STREAM_ALIAS = {
        "cp1": "cpg", "cp2": "cp", "cp3": "cp", "cp4": "cp", "cp5": "cp",
        "hl0": "rhb0", "hl1": "rhb1", "hs0": "st0", "hs1": "st1",
        "oa0": "xs0", "oa1": "xs1", "ob0": "ys0", "ob1": "ys1",
        "os0": "st0", "os1": "st1",
        "fx0": "xs0", "fx1": "xs1", "fp0": "ys0", "fp1": "ys1",
        "fs0": "st0", "fs1": "st1",
        "mrt0": "mrt", "mrt1": "mrt", "mst": "mrt", "gw": "mrt", "ow": "mrt",
        "irtl00": "irt0", "irtl01": "irt0", "irtl10": "irt1", "irtl11": "irt1",
        "yrd0": "xs0", "yrd1": "xs1", "fyr0": "xs0", "fyr1": "xs1",
        "fys0": "yst0", "fys1": "yst1", "fy20": "ys0", "fy21": "ys1",
        "fst0": "st0", "fst1": "st1", "frb0": "rhb0",
        "hlst0": "shst0", "hlst1": "shst1",
        "oos0": "ost0", "oos1": "ost1",
        "w2s": "wv", "w1l": "wblk", "w3l": "wv", "w2l": "konstg",
        "xgr0": "rhb0", "xgr1": "rhb1", "hrr": "rhb0",
        "mt0": "xt0", "mt1": "ys1", "or0": "irt0", "or1": "irt1",
        "grl0": "irt0", "grl1": "irt1",
        "pst0": "yst0", "pst1": "yst1",
        "rr0": "rhb0", "rr1": "rhb1",
    }

    def dma(self, eng, fn, deps=(), stream=None):
        stream = self.STREAM_ALIAS.get(stream, stream)
        name = "d_" + (stream if stream is not None else eng)
        if name not in self.sems:
            self._mksem(name)
        return self.op(eng, fn, deps=deps, inc=(name, 16))

    def coll(self, fn, deps=(), stream="cc", sem="cc"):
        if sem not in self.sems:
            self._mksem(sem)
        return self.op("gpsimd", fn, deps=deps, inc=(sem, 1))

    def soft_stage(self):
        self.stage += 1

    def out_dma(self, eng, fn, deps=(), stream=None):
        t = self.dma(eng, fn, deps=deps, stream=stream)
        self.out_toks.append(t)
        return t

    def barrier_tok(self):
        return [Tok(s, self.cnt[s]) for s in self.cnt if self.cnt[s] > 0]

    def build(self):
        nc = self.nc
        with nc.Block() as block:
            def run(eng_name):
                def body(eng):
                    waited = self.waited[eng_name]
                    for deps, fn, sem, amount, _ in self.queues[eng_name]:
                        for d in deps:
                            if waited.get(d.sem, 0) >= d.val:
                                continue
                            eng.wait_ge(self.sems[d.sem], d.val)
                            waited[d.sem] = d.val
                        inst = fn(eng)
                        if inst is not None:
                            inst.then_inc(sem, amount)
                    if eng_name == "sync":
                        fin = {}
                        for t in self.out_toks:
                            fin[t.sem] = max(fin.get(t.sem, 0), self.cnt[t.sem])
                        for s_, v_ in fin.items():
                            if waited.get(s_, 0) < v_:
                                eng.wait_ge(self.sems[s_], v_)
                                waited[s_] = v_
                return body
            block.sync(run("sync"))
            block.tensor(run("tensor"))
            block.vector(run("vector"))
            block.scalar(run("scalar"))
            block.gpsimd(run("gpsimd"))
        return nc



def emit_load2(p, dst_sb, dst_cols, src_ap, deps=(), eng="sync", stream=None):
    def fn(e):
        return e.dma_start(out=dst_sb[:, dst_cols[0]:dst_cols[1]], in_=src_ap)
    return p.dma(eng, fn, deps=deps, stream=stream)


def emit_ln_stats(p, x_sb, xsq_sb, nchunk, t, ones_sb, stat_ps, mean_sb, rstd_sb,
                  scale_sb, dep, eps, extra_rms_eps=None, sq_scr=None):
    raise NotImplementedError



def np_bf(x):
    return np.asarray(x, dtype=BF)


def causal_mask_V(delta, qn=512):
    j = np.arange(128)[:, None]
    q = np.arange(qn)[None, :]
    return np_bf(np.where(q + delta < j, -MASK_C, 0.0))


def anticausal_mask_V(delta, qn=512):
    j = np.arange(128)[:, None]
    q = np.arange(qn)[None, :]
    return np_bf(np.where(j < q + delta, -MASK_C, 0.0))


def rope_tables(positions, dh=64, theta=10000.0):
    inv = 1.0 / theta ** (np.arange(0, dh, 2, dtype=np.float64) / dh)
    fr = positions[None, :].astype(np.float64) * inv[:, None]
    cos = np.repeat(np.cos(fr), 2, axis=0)
    sin = np.repeat(np.sin(fr), 2, axis=0)
    return (np.tile(cos, (2, 1)).astype(np.float32),
            np.tile(sin, (2, 1)).astype(np.float32))


def rot_interleaved_R():
    R = np.zeros((128, 128), dtype=np.float32)
    for i in range(64):
        R[2 * i + 1, 2 * i] = -1.0
        R[2 * i, 2 * i + 1] = 1.0
    return np_bf(R)


def rot_half_R():
    R = np.zeros((128, 128), dtype=np.float32)
    for h in range(2):
        b = h * 64
        for i in range(32):
            R[b + 32 + i, b + i] = -1.0
            R[b + i, b + 32 + i] = 1.0
    return np_bf(R)


def xpos_tables(positions, dh=64, window=512, nseq=4096):
    half = np.arange(0, dh, 2, dtype=np.float64)
    inv = 1.0 / 10000.0 ** (half / dh)
    t = positions.astype(np.float64)
    fr = t[None, :] * inv[:, None]
    fr = np.concatenate([fr, fr], axis=0)
    base = (half + 0.4 * dh) / (1.4 * dh)
    power = (t - nseq // 2) / float(window // 2)
    scale = base[:, None] ** power[None, :]
    scale = np.concatenate([scale, scale], axis=0)
    cos, sin = np.cos(fr), np.sin(fr)
    out = (cos * scale, sin * scale, cos / scale, sin / scale)
    return [np.tile(o, (2, 1)).astype(np.float32) for o in out]




def emit_module_ln(p, x_fm, bk, xs, sqs, cones_f, onesrow_f, rt_dram, rt_row,
                   Sbb, Bbb, app_scr, xo, nw_sb=None, rms_eps=None, eps=1e-5,
                   pre_deps=(), tag="", W=None, src_bf16=False, gains=None,
                   out_dram=None, ost=None, out_gpsimd=False, out_dram2=None):
    W = T if W is None else W
    ntt = W // 512
    pre = list(pre_deps)
    slot_cons = [pre, pre]
    sq_cons = [[], []]
    last_m1 = last_m2 = None
    for kc in range(8):
        for tt in range(ntt):
            sl = (kc * ntt + tt) % 2
            ssl = sl
            ld = p.dma("sync", (lambda kcc, ttt, s: lambda e: e.dma_start(
                out=xs[:, s * 512:(s + 1) * 512],
                in_=x_fm[kcc * 128:(kcc + 1) * 128, ttt * 512:(ttt + 1) * 512]))(kc, tt, sl),
                deps=slot_cons[sl], stream="xs%d" % sl)
            slot_cons[sl] = []
            t_sq = p.op("scalar", (lambda s, s2: lambda e: e.activation(
                sqs[:, s2 * 512:(s2 + 1) * 512],
                xs[:, s * 512:(s + 1) * 512],
                AF.Square))(sl, ssl), deps=[ld] + sq_cons[ssl] + (pre if kc == 0 else []))
            sq_cons[ssl] = []
            first = (kc == 0 and tt == 0)
            last = (kc == 7 and tt == ntt - 1)
            t_m1 = p.op("tensor", (lambda s, ttt, f, l: lambda e: e.matmul(
                bk["sx"][0:ntt, :], cones_f[:, ntt - 1 - ttt: 2 * ntt - 1 - ttt],
                xs[:, s * 512:(s + 1) * 512],
                start=f, stop=l))(sl, tt, first, last),
                deps=[ld] + (bk.get("sx_free", []) if first else []))
            t_m2 = p.op("tensor", (lambda ttt, s2, f, l: lambda e: e.matmul(
                bk["sx2"][0:ntt, :], cones_f[:, ntt - 1 - ttt: 2 * ntt - 1 - ttt],
                sqs[:, s2 * 512:(s2 + 1) * 512],
                start=f, stop=l))(tt, ssl, first, last),
                deps=[t_sq] + (bk.get("sx2_free", []) if first else []))
            sq_cons[ssl].append(t_m2)
            slot_cons[sl] += [t_sq, t_m1]
            last_m1, last_m2 = t_m1, t_m2
    sA = sqs[0:ntt, 0:512]
    sB = sqs[0:ntt, 512:1024]
    sC = app_scr[0:ntt, 0:512]
    sD = app_scr[0:ntt, 512:1024]
    t1 = p.op("vector", lambda e: e.tensor_scalar_mul(sA, bk["sx"][0:ntt, :], 1.0 / D), deps=[last_m1, last_m2])
    t2 = p.op("vector", lambda e: e.tensor_scalar_mul(sB, bk["sx2"][0:ntt, :], 1.0 / D), deps=[last_m2])
    t3 = p.op("vector", lambda e: e.tensor_mul(sC, sA, sA), deps=[t1])
    t4 = p.op("vector", lambda e: e.tensor_sub(sC, sB, sC), deps=[t2, t3])
    t5 = p.op("vector", lambda e: e.tensor_scalar_add(sD, sC, eps), deps=[t4])
    t6 = p.op("scalar", lambda e: e.activation(sD, sD, AF.Sqrt), deps=[t5])
    t7 = p.op("vector", lambda e: e.reciprocal(sD, sD), deps=[t6])
    toks_scale = t7
    if rms_eps is not None:
        t8 = p.op("vector", lambda e: e.tensor_mul(sC, sC, sD), deps=[t7, t4])
        t9 = p.op("vector", lambda e: e.tensor_mul(sC, sC, sD), deps=[t8])
        t10 = p.op("vector", lambda e: e.tensor_scalar_add(sC, sC, rms_eps), deps=[t9])
        t11 = p.op("scalar", lambda e: e.activation(sC, sC, AF.Sqrt), deps=[t10])
        t12 = p.op("vector", lambda e: e.reciprocal(sC, sC), deps=[t11])
        toks_scale = p.op("vector", lambda e: e.tensor_mul(sD, sD, sC), deps=[t12])
    t13 = p.op("vector", lambda e: e.scalar_tensor_tensor(
        sA, sA, -1.0, sD, ALU.mult, ALU.mult), deps=[toks_scale, t3])
    w1 = p.dma("sync", lambda e: e.dma_start(out=rt_dram[0:ntt, :], in_=sD), deps=[toks_scale, t13], stream="rtw" + tag)
    w2 = p.dma("sync", lambda e: e.dma_start(out=rt_dram[ntt:2 * ntt, :], in_=sA), deps=[t13], stream="rtw" + tag)
    xln_toks = {}
    xln_toks2 = {}
    slot_cons = [[], []]
    scr_cons = [[], []]
    ost_cons = [[], []]
    bc_evac = {}
    rtslot_busy = [[], []]
    bcslot_cons = [[], []]
    for tt in range(ntt):
        bsl = tt % 2
        sl_ = 0
        r1 = p.dma("sync", (lambda ttt, s: lambda e: e.dma_start(
            out=rt_row[0:1, s * 1024:s * 1024 + 512],
            in_=rt_dram[ttt:ttt + 1, :]))(tt, sl_), deps=[w1] + rtslot_busy[sl_],
            stream="rt%d%s" % (sl_, tag))
        r2 = p.dma("sync", (lambda ttt, s: lambda e: e.dma_start(
            out=rt_row[0:1, s * 1024 + 512:s * 1024 + 1024],
            in_=rt_dram[ntt + ttt:ntt + ttt + 1, :]))(tt, sl_), deps=[w2] + rtslot_busy[sl_],
            stream="rt%d%s" % (sl_, tag))
        m1 = p.op("tensor", (lambda s: lambda e: e.matmul(
            bk["b1"][:, :], onesrow_f[:, :], rt_row[0:1, s * 1024:s * 1024 + 512],
            start=True, stop=True))(sl_), deps=[r1] + (bk.get("b1_free", []) if tt == 0 else []) + list(bc_evac.get((tt - 1, 0), [])))
        e1 = p.op("scalar", (lambda b_: lambda e: e.copy(
            Sbb[:, b_ * 512:(b_ + 1) * 512], bk["b1"][:, :]))(bsl),
            deps=[m1] + bcslot_cons[bsl])
        m2 = p.op("tensor", (lambda s: lambda e: e.matmul(
            bk["b2"][:, :], onesrow_f[:, :], rt_row[0:1, s * 1024 + 512:s * 1024 + 1024],
            start=True, stop=True))(sl_), deps=[r2] + (bk.get("b2_free", []) if tt == 0 else []) + list(bc_evac.get((tt - 1, 1), [])))
        e2 = p.op("scalar", (lambda b_: lambda e: e.copy(
            Bbb[:, b_ * 512:(b_ + 1) * 512], bk["b2"][:, :]))(bsl),
            deps=[m2] + bcslot_cons[bsl])
        bc_evac[(tt, 0)] = [e1]
        bc_evac[(tt, 1)] = [e2]
        rtslot_busy[sl_] = [m1, m2]
        bcslot_cons[bsl] = []
        for kc in range(8):
            sl = (tt * 8 + kc) % 2
            ssl = sl
            ldb = p.dma("sync", (lambda kcc, ttt, s: lambda e: e.dma_start(
                out=xs[:, s * 512:(s + 1) * 512],
                in_=x_fm[kcc * 128:(kcc + 1) * 128, ttt * 512:(ttt + 1) * 512]))(kc, tt, sl),
                deps=slot_cons[sl] + ([last_m1, last_m2] if tt * 8 + kc < 2 else []), stream="xs%d" % sl)
            slot_cons[sl] = []
            if nw_sb is not None:
                d1 = p.op("vector", (lambda kcc, s, b_, s2: lambda e: e.scalar_tensor_tensor(
                    app_scr[:, s2 * 512:(s2 + 1) * 512],
                    xs[:, s * 512:(s + 1) * 512],
                    nw_sb[:, kcc:kcc + 1],
                    Sbb[:, b_ * 512:(b_ + 1) * 512], ALU.mult, ALU.mult))(kc, sl, bsl, ssl),
                    deps=[ldb, e1, w1, w2] + scr_cons[ssl])
                d2 = p.op("vector", (lambda kcc, ttt, b_, s2: lambda e: e.scalar_tensor_tensor(
                    xo[:, kcc * W + ttt * 512: kcc * W + (ttt + 1) * 512],
                    Bbb[:, b_ * 512:(b_ + 1) * 512],
                    nw_sb[:, kcc:kcc + 1],
                    app_scr[:, s2 * 512:(s2 + 1) * 512], ALU.mult, ALU.add))(kc, tt, bsl, ssl),
                    deps=[d1, e2])
            elif out_dram is not None:
                d1 = p.op("vector", (lambda s, b_, s2: lambda e: e.tensor_mul(
                    app_scr[:, s2 * 512:(s2 + 1) * 512],
                    xs[:, s * 512:(s + 1) * 512],
                    Sbb[:, b_ * 512:(b_ + 1) * 512]))(sl, bsl, ssl),
                    deps=[ldb, e1, w1, w2] + scr_cons[ssl])
                d2 = p.op("vector", (lambda b_, s2: lambda e: e.tensor_add(
                    ost[:, s2 * 512:(s2 + 1) * 512],
                    app_scr[:, s2 * 512:(s2 + 1) * 512],
                    Bbb[:, b_ * 512:(b_ + 1) * 512]))(bsl, ssl),
                    deps=[d1, e2] + ost_cons[ssl])
                dst = p.dma("gpsimd" if out_gpsimd else "sync",
                    (lambda kcc, ttt, s2: lambda e: e.dma_start(
                    out=out_dram[kcc * 128:(kcc + 1) * 128, ttt * 512:(ttt + 1) * 512],
                    in_=ost[:, s2 * 512:(s2 + 1) * 512]))(kc, tt, ssl),
                    deps=[d2], stream=("lngst%d%s" if out_gpsimd else "lnst%d%s") % (ssl, tag))
                ost_cons[ssl] = [dst]
                if out_dram2 is not None:
                    dst2 = p.dma("sync", (lambda kcc, ttt, s2: lambda e: e.dma_start(
                        out=out_dram2[kcc * 128:(kcc + 1) * 128, ttt * 512:(ttt + 1) * 512],
                        in_=ost[:, s2 * 512:(s2 + 1) * 512]))(kc, tt, ssl),
                        deps=[d2], stream="ln2st%d%s" % (ssl, tag))
                    ost_cons[ssl].append(dst2)
                    xln_toks2[(kc, tt)] = dst2
                d2 = dst
            else:
                d1 = p.op("vector", (lambda s, b_, s2: lambda e: e.tensor_mul(
                    app_scr[:, s2 * 512:(s2 + 1) * 512],
                    xs[:, s * 512:(s + 1) * 512],
                    Sbb[:, b_ * 512:(b_ + 1) * 512]))(sl, bsl, ssl),
                    deps=[ldb, e1, w1, w2] + scr_cons[ssl])
                d2 = p.op("vector", (lambda kcc, ttt, b_, s2: lambda e: e.tensor_add(
                    xo[:, kcc * W + ttt * 512: kcc * W + (ttt + 1) * 512],
                    app_scr[:, s2 * 512:(s2 + 1) * 512],
                    Bbb[:, b_ * 512:(b_ + 1) * 512]))(kc, tt, bsl, ssl),
                    deps=[d1, e2])
            scr_cons[ssl] = [d2]
            slot_cons[sl].append(d2)
            bcslot_cons[bsl].append(d2)
            xln_toks[(kc, tt)] = d2
    if out_dram2 is not None:
        return xln_toks, [w1, w2] + [t for v in bc_evac.values() for t in v], xln_toks2
    return xln_toks, [w1, w2] + [t for v in bc_evac.values() for t in v]
def emit_l0_attn(p: Prog):
    x_fm = p.get_in("x_fm", [D, T], F32)
    wqkv = p.get_in("wqkv", [D, 3 * D], F32)
    wout = p.get_in("wout", [D, D], F32)
    nw = p.get_in("nw", [128, 8], F32)
    bout = p.get_in("bout", [128, 8], F32)
    pmkT = p.get_in("pmkT", [D, 16], F32)
    pmv65 = p.get_in("pmv65", [16, 16 * 65], F32)
    rc = p.get_in("rc", [128, T], BF16)
    rs = p.get_in("rs", [128, T], BF16)
    cU = p.get_in("c_U", [128, 128], BF16)
    cVfam = p.get_in("c_Vfam", [128, 4 * 512], BF16)
    cselb = p.get_in("c_selb", [128, 31], F32)
    cR = p.get_in("c_R", [128, 128], BF16)
    ccones = p.get_in("c_cones", [128, 7], F32)
    x1_fm = p.get_out("x1_fm", [D, T], F32)
    mrow_d = p.dram("mrow_d", [16, 1], F32)
    rt_d = p.dram("rt_d", [8, 512], F32)

    xs = p.sb("xs", [128, 2 * T], F32)
    sqs = p.sb("sqs", [128, 2 * 512], F32)
    xo = p.sb("xo", [128, 8 * T], BF16)
    Wst = p.sb("Wst", [128, 4096], BF16)
    Qf = p.sb("Qf", [128, 8 * T], BF16)
    Kf = p.sb("Kf", [128, 8 * T], BF16)
    V65 = p.sb("V65", [128, 16 * 1040], BF16)
    rcb = p.sb("rcb", [128, T], BF16)
    rsb = p.sb("rsb", [128, T], BF16)
    Ub = p.sb("Ub", [128, 128], BF16)
    Vfb = p.sb("Vfb", [128, 4 * 512], BF16)
    selb = p.sb("selb", [128, 31], F32)
    Rb = p.sb("Rb", [128, 128], BF16)
    conesb = p.sb("conesb", [128, 7], F32)
    pmk_sb = p.sb("pmk_sb", [128, 8 * 16], BF16)
    pmv_sb = p.sb("pmv_sb", [16, 16 * 65], BF16)
    nw_sb = p.sb("nw_sb", [128, 8], F32)
    bout_sb = p.sb("bout_sb", [128, 8], F32)
    onesrow_f = p.sb("onesrow_f", [1, 128], F32)
    onesrow_b2 = p.sb("onesrow_b2", [1, 128], BF16)
    bvrow_sb = p.sb("bvrow_sb", [1, 512], BF16)
    rt_row = p.sb("rt_row", [1, 1024], F32)
    Sbb = p.sb("Sbb", [128, 2 * 512], BF16)
    Bbb = p.sb("Bbb", [128, 2 * 512], BF16)
    app_scr = p.sb("app_scr", [128, 2 * 512], F32)
    qsmax = p.sb("qsmax", [16, 1], F32)
    ksmax = p.sb("ksmax", [16, 1], F32)
    hred = p.sb("hred", [16, 1], F32)
    negM = p.sb("negM", [128, 16], F32)
    mrow = p.sb("mrow", [1, 16], F32)
    Pt = p.sb("Pt", [128, 2 * 512], BF16)
    Ppm = p.sb("Ppm", [16, 512], BF16)
    xr = p.sb("xr", [128, 2 * 512], BF16)
    ost = p.sb("ost", [128, 2 * 512], F32)
    recb = app_scr
    rrow = app_scr

    ps = [p.ps(f"ps{i}", [128, 512], F32) for i in range(8)]

    c1 = [emit_load2(p, rcb, (0, T), rc[:, :], stream="konst"),
          emit_load2(p, rsb, (0, T), rs[:, :], stream="konst"),
          emit_load2(p, Ub, (0, 128), cU[:, :], stream="konst"),
          emit_load2(p, Vfb, (0, 4 * 512), cVfam[:, :], stream="konst"),
          emit_load2(p, selb, (0, 31), cselb[:, :], stream="konst"),
          emit_load2(p, Rb, (0, 128), cR[:, :], stream="konst"),
          emit_load2(p, conesb, (0, 7), ccones[:, :], stream="konst"),
          emit_load2(p, nw_sb, (0, 8), nw[:, :], stream="konst"),
          emit_load2(p, bout_sb, (0, 8), bout[:, :], stream="konst")]
    c2 = [p.dma("gpsimd", lambda e: e.dma_start(
              out=pmk_sb[:, :].rearrange("p (c m) -> p c m", c=8),
              in_=pmkT.rearrange("(c p) m -> p c m", p=128)), stream="konstg"),
          p.dma("gpsimd", lambda e: e.dma_start(out=pmv_sb[:, :], in_=pmv65[:, :]), stream="konstg")]
    c3 = [p.op("vector", lambda e: e.memset(onesrow_f[:, :], 1.0)),
          p.op("vector", lambda e: e.memset(V65[:, :], 1.0))]
    t_ones = c3[0]
    t_v65ones = c3[1]
    t_cones = c1[6]

    bk = {"sx": ps[0], "sx2": ps[1], "b1": ps[2], "b2": ps[3],
          "sx_free": [], "sx2_free": [], "b1_free": [], "b2_free": []}
    xln_toks, ln_done = emit_module_ln(p, x_fm, bk, xs, sqs, conesb, onesrow_f, rt_d,
                              rt_row, Sbb, Bbb, app_scr, xo, nw_sb=nw_sb,
                              rms_eps=1e-6, eps=1e-5,
                              pre_deps=[t_cones, t_ones, c1[7]], tag="l0a")
    xln_chunk = [xln_toks[(kc, T // 512 - 1)] for kc in range(8)]
    xln_all = [xln_toks[(kc, tt)] for kc in range(8) for tt in range(T // 512)]

    wv_ld = [None, None]
    pv_rot = [[], []]
    v_mms = []
    for ft in range(2):
        wv_ld[ft] = p.dma("gpsimd", (lambda f: lambda e: e.dma_start(
            out=Wst[:, :].rearrange("p (c m) -> p c m", c=8),
            in_=wqkv[:, 2 * D + f * 512:2 * D + (f + 1) * 512].rearrange(
                "(c p) m -> p c m", p=128)))(ft), deps=list(v_mms), stream="wv")
        v_mms = []
        for tc in range(16):
            bnk = tc % 2
            mm = None
            for kc in range(8):
                mm = p.op("tensor", (lambda tcc, k, b: lambda e: e.matmul(
                    ps[6 + b][:, :],
                    xo[:, k * T + tcc * 128:k * T + (tcc + 1) * 128],
                    Wst[:, k * 512:(k + 1) * 512],
                    start=(k == 0), stop=(k == 7)))(tc, kc, bnk),
                    deps=([wv_ld[ft], xln_chunk[kc]] + (pv_rot[bnk] if kc == 0 else [])))
            v_mms.append(mm)
            pv_rot[bnk] = []
            for hh in range(8):
                h = ft * 8 + hh
                ev = p.op("scalar", (lambda tcc, hhh, hg, b: lambda e: e.copy(
                    V65[:, tcc * 1040 + hg * 65:tcc * 1040 + hg * 65 + 64],
                    ps[6 + b][:, hhh * 64:(hhh + 1) * 64]))(tc, hh, h, bnk),
                    deps=[mm, t_v65ones])
                pv_rot[bnk].append(ev)
    v65_done = [t for b in pv_rot for t in b]

    qk_rot = [[], []]
    wcol_ld = {}
    qk_evac = {}
    for fc in range(16):
        sl = fc % 2
        if fc >= 2:
            dep_w = [qk_evac[(fc - 2, tt)] for tt in range(4)]
        else:
            dep_w = list(v65_done)
        wcol_ld[fc] = p.dma("gpsimd", (lambda f, s: lambda e: e.dma_start(
            out=Wst[:, s * 1024:s * 1024 + 1024].rearrange("p (c m) -> p c m", c=8),
            in_=wqkv[:, f * 128:(f + 1) * 128].rearrange("(c p) m -> p c m", p=128)))(fc, sl),
            deps=dep_w, stream="w%d" % sl)
        for tt in range(4):
            bnk = (fc * 4 + tt) % 2
            mm = None
            for kc in range(8):
                mm = p.op("tensor", (lambda f, s, t_, k, b: lambda e: e.matmul(
                    ps[4 + b][:, :],
                    Wst[:, s * 1024 + k * 128:s * 1024 + (k + 1) * 128],
                    xo[:, k * T + t_ * 512:k * T + (t_ + 1) * 512],
                    start=(k == 0), stop=(k == 7)))(fc, sl, tt, kc, bnk),
                    deps=[wcol_ld[fc], xln_toks[(kc, tt)]] + (qk_rot[bnk] if kc == 0 else []))
            dstT = Qf if fc < 8 else Kf
            dc = fc if fc < 8 else fc - 8
            ev = p.op("scalar", (lambda dT, d, t_, b: lambda e: e.copy(
                dT[:, d * T + t_ * 512:d * T + (t_ + 1) * 512], ps[4 + b][:, :]))(dstT, dc, tt, bnk),
                deps=[mm])
            qk_rot[bnk] = [ev]
            qk_evac[(fc, tt)] = ev

    rope_done = {}
    rot_free = []
    xr_cons = [[], []]
    for ci in range(16):
        src = Qf if ci < 8 else Kf
        cc = ci % 8
        for tt in range(4):
            dep0 = [qk_evac[(ci, tt)], c1[5]]
            m = p.op("tensor", (lambda s_, c_, t_: lambda e: e.matmul(
                ps[6][:, :], Rb[:, :], s_[:, c_ * T + t_ * 512:c_ * T + (t_ + 1) * 512],
                start=True, stop=True))(src, cc, tt), deps=dep0 + rot_free)
            a1 = p.op("vector", (lambda s_, c_, t_, sl: lambda e: e.tensor_mul(
                xr[:, sl * 512:(sl + 1) * 512],
                s_[:, c_ * T + t_ * 512:c_ * T + (t_ + 1) * 512],
                rcb[:, t_ * 512:(t_ + 1) * 512]))(src, cc, tt, (ci * 4 + tt) % 2),
                deps=[qk_evac[(ci, tt)], c1[0]] + xr_cons[(ci * 4 + tt) % 2])
            a2 = p.op("vector", (lambda s_, c_, t_, sl: lambda e: e.scalar_tensor_tensor(
                s_[:, c_ * T + t_ * 512:c_ * T + (t_ + 1) * 512],
                ps[6][:, :], 1.0, rsb[:, t_ * 512:(t_ + 1) * 512],
                ALU.mult, ALU.mult))(src, cc, tt, 0), deps=[m, c1[1], a1])
            a3 = p.op("vector", (lambda s_, c_, t_, sl: lambda e: e.tensor_add(
                s_[:, c_ * T + t_ * 512:c_ * T + (t_ + 1) * 512],
                s_[:, c_ * T + t_ * 512:c_ * T + (t_ + 1) * 512],
                xr[:, sl * 512:(sl + 1) * 512]))(src, cc, tt, (ci * 4 + tt) % 2),
                deps=[a2])
            rot_free = [a2]
            xr_cons[(ci * 4 + tt) % 2] = [a3]
            rope_done[(ci, tt)] = a3

    dbg_toks = []
    if DEBUG_DUMPS:
        dxln = p.get_out("dbg_xln", [D, T], BF16)
        dq = p.get_out("dbg_q", [D, T], BF16)
        dk = p.get_out("dbg_k", [D, T], BF16)
        dv = p.get_out("dbg_v65", [128, 16 * 1040], BF16)
        for kc in range(8):
            dbg_toks.append(p.out_dma("scalar", (lambda k: lambda e: e.dma_start(
                out=dxln[k * 128:(k + 1) * 128, :], in_=xo[:, k * T:(k + 1) * T]))(kc),
                deps=[xln_toks[(kc, tt)] for tt in range(4)], stream="dbg"))
            dbg_toks.append(p.out_dma("scalar", (lambda k: lambda e: e.dma_start(
                out=dq[k * 128:(k + 1) * 128, :], in_=Qf[:, k * T:(k + 1) * T]))(kc),
                deps=[rope_done[(kc, tt)] for tt in range(4)], stream="dbg"))
            dbg_toks.append(p.out_dma("scalar", (lambda k: lambda e: e.dma_start(
                out=dk[k * 128:(k + 1) * 128, :], in_=Kf[:, k * T:(k + 1) * T]))(kc),
                deps=[rope_done[(kc + 8, tt)] for tt in range(4)], stream="dbg"))
        dbg_toks.append(p.out_dma("scalar", lambda e: e.dma_start(
            out=dv[:, :], in_=V65[:, :]), deps=v65_done, stream="dbg"))

    sq_slot_cons = [list(ln_done), list(ln_done)]
    bank_free = []
    last_up = None
    for ti, (tens, outmax, cbase) in enumerate(
            [(Qf, qsmax, 0), (Kf, ksmax, 8)] ):
        for tt in range(4):
            mm = None
            for kc in range(8):
                sl = kc % 2
                sq = p.op("scalar", (lambda s_, c_, t_, s2: lambda e: e.activation(
                    sqs[:, s2 * 512:(s2 + 1) * 512],
                    s_[:, c_ * T + t_ * 512:c_ * T + (t_ + 1) * 512],
                    AF.Square))(tens, kc, tt, sl),
                    deps=[rope_done[(kc + cbase, tt)]] + sq_slot_cons[sl])
                mm = p.op("tensor", (lambda k, s2, f, l: lambda e: e.matmul(
                    ps[2][0:16, :], selb[:, 15 - 2 * k:31 - 2 * k],
                    sqs[:, s2 * 512:(s2 + 1) * 512],
                    start=f, stop=l))(kc, sl, kc == 0, kc == 7),
                    deps=[sq, c1[4]] + (bank_free if kc == 0 else []))
                sq_slot_cons[sl] = [mm]
            rd = p.op("vector", lambda e: e.reduce_max(
                hred[:, :], ps[2][0:16, :], axis=AX.X), deps=[mm] + ([last_up] if last_up else []))
            if tt == 0:
                up = p.op("vector", (lambda om: lambda e: e.tensor_copy(
                    om[:, :], hred[:, :]))(outmax), deps=[rd])
            else:
                up = p.op("vector", (lambda om: lambda e: e.tensor_max(
                    om[:, :], om[:, :], hred[:, :]))(outmax), deps=[rd])
            bank_free = [rd]
            last_up = up
    pm_mm = None
    for kc in range(8):
        sl = kc % 2
        sq = p.op("scalar", (lambda k, s2: lambda e: e.activation(
            sqs[:, s2 * 512:s2 * 512 + 16], pmk_sb[:, k * 16:(k + 1) * 16],
            AF.Square))(kc, sl), deps=[c2[0]] + sq_slot_cons[sl])
        pm_mm = p.op("tensor", (lambda k, s2, f, l: lambda e: e.matmul(
            ps[3][0:16, 0:16], selb[:, 15 - 2 * k:31 - 2 * k],
            sqs[:, s2 * 512:s2 * 512 + 16],
            start=f, stop=l))(kc, sl, kc == 0, kc == 7),
            deps=[sq])
        sq_slot_cons[sl] = [pm_mm]
    rd = p.op("vector", lambda e: e.reduce_max(hred[:, :], ps[3][0:16, 0:16], axis=AX.X),
              deps=[pm_mm, last_up])
    up = p.op("vector", lambda e: e.tensor_max(ksmax[:, :], ksmax[:, :], hred[:, :]),
              deps=[rd])
    s1 = p.op("scalar", lambda e: e.activation(qsmax[:, :], qsmax[:, :], AF.Sqrt),
              deps=[up])
    s2_ = p.op("scalar", lambda e: e.activation(ksmax[:, :], ksmax[:, :], AF.Sqrt), deps=[up, s1])
    s3 = p.op("vector", lambda e: e.tensor_mul(hred[:, :], qsmax[:, :], ksmax[:, :]),
              deps=[s1, s2_, up])
    s4 = p.op("vector", lambda e: e.tensor_scalar_mul(hred[:, :], hred[:, :], -0.1250002), deps=[s3])
    w = p.dma("sync", lambda e: e.dma_start(out=mrow_d[:, :], in_=hred[:, :]), deps=[s4], stream="mrt")
    r = p.dma("sync", lambda e: e.dma_start(
        out=mrow[:, :], in_=mrow_d.rearrange("p o -> o p")), deps=[w], stream="mrt")
    bcm = p.op("tensor", lambda e: e.matmul(ps[2][:, 0:16], onesrow_f[:, :], mrow[:, :],
                                            start=True, stop=True), deps=[r, rd, t_ones])
    negm_tok = p.op("scalar", lambda e: e.copy(negM[:, :], ps[2][:, 0:16]), deps=[bcm])

    ofm_toks = {}
    sfree = [[negm_tok], [negm_tok]]
    pt_cons = [[], []]
    ppm_cons = []
    pm_free = [negm_tok]
    o_free = [up]
    rb_free = v65_done[-1:]
    rb_free = [rope_done[(15, 3)]]
    for h in range(16):
        base = (h % 2) * 64
        ch = h // 2
        for seg in range(4):
            pvs = []
            for kc in range(4):
                b = kc % 2
                t_s = p.op("tensor", (lambda bb, c_, s_, k, b_: lambda e: e.matmul(
                    ps[b_][:, :],
                    Kf[bb:bb + 64, c_ * T + s_ * 512 + k * 128:c_ * T + s_ * 512 + (k + 1) * 128],
                    Qf[bb:bb + 64, c_ * T + s_ * 512:c_ * T + (s_ + 1) * 512],
                    start=True, stop=False))(base, ch, seg, kc, b),
                    deps=[rope_done[(ch, seg)], rope_done[(ch + 8, seg)]] + sfree[b])
                t_mk = p.op("tensor", (lambda k, b_: lambda e: e.matmul(
                    ps[b_][:, :], Ub[:, :], Vfb[:, k * 512:(k + 1) * 512],
                    start=False, stop=True))(kc, b), deps=[t_s, c1[2], c1[3]])
                t_e = p.op("scalar", (lambda h_, k, b_: lambda e: e.activation(
                    Pt[:, b_ * 512:(b_ + 1) * 512], ps[b_][:, :], AF.Exp,
                    bias=negM[:, h_:h_ + 1], scale=0.125))(h, kc, b),
                    deps=[t_mk] + pt_cons[b])
                sfree[b] = [t_e]
                t_pv = p.op("tensor", (lambda h_, c_, s_, k, b_: lambda e: e.matmul(
                    ps[3][0:65, :],
                    V65[:, (s_ * 4 + k) * 1040 + h_ * 65:(s_ * 4 + k) * 1040 + h_ * 65 + 65],
                    Pt[:, b_ * 512:(b_ + 1) * 512],
                    start=(k == 0), stop=False))(h, ch, seg, kc, b),
                    deps=[t_e] + v65_done + (o_free if kc == 0 else []))
                pt_cons[b] = [t_pv]
                pvs.append(t_pv)
            t_spm = p.op("tensor", (lambda bb, c_, s_: lambda e: e.matmul(
                ps[2][0:16, :], pmk_sb[bb:bb + 64, c_ * 16:(c_ + 1) * 16],
                Qf[bb:bb + 64, c_ * T + s_ * 512:c_ * T + (s_ + 1) * 512],
                start=True, stop=True))(base, ch, seg),
                deps=[c2[0], rope_done[(ch, seg)]] + pm_free)
            t_epm = p.op("scalar", (lambda h_: lambda e: e.activation(
                Ppm[:, :], ps[2][0:16, :], AF.Exp,
                bias=negM[0:16, h_:h_ + 1], scale=0.125))(h),
                deps=[t_spm] + ppm_cons)
            pm_free = [t_epm]
            t_pvm = p.op("tensor", (lambda h_: lambda e: e.matmul(
                ps[3][0:65, :], pmv_sb[0:16, h_ * 65:h_ * 65 + 65], Ppm[0:16, :],
                start=False, stop=True))(h), deps=[t_epm, c2[1]] + pvs)
            ppm_cons = [t_pvm]
            t_rc = p.op("vector", lambda e: e.reciprocal(rrow[0:1, 0:512], ps[3][64:65, :]),
                        deps=[t_pvm])
            t_bc = p.op("tensor", lambda e: e.matmul(
                ps[6][0:64, :], onesrow_f[0:1, 0:64], rrow[0:1, 0:512], start=True, stop=True),
                deps=[t_rc, t_ones] + rb_free)
            t_ev = p.op("scalar", lambda e: e.copy(recb[0:64, 512:1024], ps[6][0:64, :]), deps=[t_bc])
            rb_free = [t_ev]
            t_on = p.op("vector", (lambda bb, c_, s_: lambda e: e.tensor_mul(
                xo[bb:bb + 64, c_ * T + s_ * 512:c_ * T + (s_ + 1) * 512],
                ps[3][0:64, :], recb[0:64, 512:1024]))(base, ch, seg),
                deps=[t_ev, qk_evac[(15, 3)], v65_done[-1]] + dbg_toks[-1:])
            o_free = [t_on]
            ofm_toks[(h, seg)] = t_on

    if DEBUG_DUMPS:
        dofm = p.get_out("dbg_ofm", [D, T], BF16)
        dnegm = p.get_out("dbg_negm", [128, 16], F32)
        for kc in range(8):
            dbg_toks.append(p.out_dma("scalar", (lambda k: lambda e: e.dma_start(
                out=dofm[k * 128:(k + 1) * 128, :], in_=xo[:, k * T:(k + 1) * T]))(kc),
                deps=[ofm_toks[(2 * kc, sg)] for sg in range(4)] +
                     [ofm_toks[(2 * kc + 1, sg)] for sg in range(4)], stream="dbg"))
        dbg_toks.append(p.out_dma("scalar", lambda e: e.dma_start(
            out=dnegm[:, :], in_=negM[:, :]), deps=[negm_tok], stream="dbg"))

    op_rot = [[], []]
    wo_ld = {}
    xsl = {}
    st_cons = [[], []]
    xs_cons = [[], []]
    for fc in range(8):
        sl = fc % 2
        depw = [wo_ld[fc - 2]] if fc >= 2 else []
        if fc >= 2:
            depw = op_rot[0] + op_rot[1]
        elif fc < 2:
            depw = [qk_evac[(15, 3)]]
        wo_ld[fc] = p.dma("gpsimd", (lambda f, s: lambda e: e.dma_start(
            out=Wst[:, s * 1024:s * 1024 + 1024].rearrange("p (c m) -> p c m", c=8),
            in_=wout[:, f * 128:(f + 1) * 128].rearrange("(c p) m -> p c m", p=128)))(fc, sl),
            deps=depw, stream="w%d" % sl)
        xsl[fc] = p.dma("sync", (lambda f, s: lambda e: e.dma_start(
            out=xs[:, s * T:(s + 1) * T], in_=x_fm[f * 128:(f + 1) * 128, :]))(fc, sl),
            deps=xs_cons[sl], stream="xs%d" % sl)
        xs_cons[sl] = []
        for tt in range(4):
            bnk = (fc * 4 + tt) % 2
            mm = None
            for ic in range(8):
                deps = [wo_ld[fc]] + [ofm_toks[(2 * ic, tt)], ofm_toks[(2 * ic + 1, tt)]]
                mm = p.op("tensor", (lambda f, s, t_, k, b: lambda e: e.matmul(
                    ps[4 + b][:, :],
                    Wst[:, s * 1024 + k * 128:s * 1024 + (k + 1) * 128],
                    xo[:, k * T + t_ * 512:k * T + (t_ + 1) * 512],
                    start=(k == 0), stop=(k == 7)))(fc, sl, tt, ic, bnk),
                    deps=deps + (op_rot[bnk] if ic == 0 else []))
            sslot = (fc * 4 + tt) % 2
            ev = p.op("vector", (lambda f, s, t_, b, s2: lambda e: e.scalar_tensor_tensor(
                ost[:, s2 * 512:(s2 + 1) * 512], ps[4 + b][:, :],
                bout_sb[:, f:f + 1],
                xs[:, s * T + t_ * 512:s * T + (t_ + 1) * 512],
                ALU.add, ALU.add))(fc, sl, tt, bnk, sslot),
                deps=[mm, xsl[fc], c1[8], ofm_toks[(15, 3)]] + st_cons[sslot])
            op_rot[bnk] = [ev]
            xs_cons[sl].append(ev)
            st = p.out_dma("sync", (lambda f, t_, s2: lambda e: e.dma_start(
                out=x1_fm[f * 128:(f + 1) * 128, t_ * 512:(t_ + 1) * 512],
                in_=ost[:, s2 * 512:(s2 + 1) * 512]))(fc, tt, sslot),
                deps=[ev], stream="st%d" % sslot)
            st_cons[sslot] = [st]


def prep_l0_attn_inputs(inputs):
    pm_k = inputs["L0_attn_pm_k"]
    pm_v = inputs["L0_attn_pm_v"]
    pmkT = np.ascontiguousarray(np.transpose(pm_k, (0, 2, 1)).reshape(D, 16)).astype(np.float32)
    pmv65 = np.zeros((16, 16 * 65), dtype=np.float32)
    for h in range(16):
        pmv65[:, h * 65:h * 65 + 64] = pm_v[h]
        pmv65[:, h * 65 + 64] = 1.0
    U = np_bf((np.arange(128)[:, None] <= np.arange(128)[None, :]).astype(np.float32))
    Vfam = np.concatenate([causal_mask_V(-128 * kc) for kc in range(4)], axis=1)
    selb = np.zeros((128, 31), dtype=np.float32)
    selb[0:64, 15] = 1.0
    selb[64:128, 16] = 1.0
    cones = np.zeros((128, 7), dtype=np.float32)
    cones[:, 3] = 1.0
    R = rot_interleaved_R()
    nw = np.ascontiguousarray(inputs["L0_attn_norm_w"].reshape(8, 128).T).astype(np.float32)
    bout = np.ascontiguousarray(inputs["L0_attn_b_out"].reshape(8, 128).T).astype(np.float32)
    x = inputs["L0_attn_seq"]
    in_maps = []
    for c in range(NCORE):
        b, half = c // 2, c % 2
        s = half * T
        rc_, rs_ = rope_tables(np.arange(s, s + T))
        in_maps.append(dict(
            x_fm=np.ascontiguousarray(x[b, s:s + T, :].T),
            wqkv=inputs["L0_attn_w_qkv"],
            wout=inputs["L0_attn_w_out"],
            nw=nw, bout=bout, pmkT=pmkT, pmv65=pmv65,
            rc=np_bf(rc_), rs=np_bf(rs_),
            c_U=U, c_Vfam=Vfam, c_selb=selb, c_R=R, c_cones=cones,
        ))
    return in_maps


def build_l0_attn():
    p = Prog("l0attn")
    emit_l0_attn(p)
    return p


W2 = 3584
NT2 = 7
QKSC = 8.0


def emit_l0_ffn(p: Prog):
    x1h = p.get_in("x1h", [D, W2], F32)
    fw_qkv = p.get_in("fw_qkv", [2 * D, 1536], F32)
    fw_o = p.get_in("fw_o", [2 * 512, D], F32)
    fw_1 = p.get_in("fw_1", [2 * D, 5460], F32)
    fw_2 = p.get_in("fw_2", [2 * 2730, D], F32)
    f_qs = p.get_in("f_qs", [128, 2], F32)
    f_ks = p.get_in("f_ks", [128, 2], F32)
    f_lg = p.get_in("f_lg", [128, 16], F32)
    f_lb = p.get_in("f_lb", [128, 16], F32)
    f_b1a = p.get_in("f_b1a", [128, 44], F32)
    f_b1g = p.get_in("f_b1g", [128, 44], F32)
    f_b2 = p.get_in("f_b2", [128, 16], F32)
    xq_cs = p.get_in("xq_cs", [128, W2], BF16)
    xq_ss = p.get_in("xq_ss", [128, W2], BF16)
    xk_cs = p.get_in("xk_cs", [128, W2], BF16)
    xk_ss = p.get_in("xk_ss", [128, W2], BF16)
    cU = p.get_in("c_U", [128, 128], BF16)
    cVfam = p.get_in("c_Vfam", [128, 4 * 512], BF16)
    cUa = p.get_in("c_Ua", [128, 128], BF16)
    cVafam = p.get_in("c_Vafam", [128, 4 * 512], BF16)
    cselb = p.get_in("c_selb", [128, 31], F32)
    csel2T = p.get_in("c_sel2T", [8, 4 * 128], F32)
    cR2 = p.get_in("c_R2", [128, 128], BF16)
    ccones = p.get_in("c_cones", [128, 13], F32)
    cpw = p.get_in("c_pw", [1, 7 * 128], BF16)
    cconesb = p.get_in("c_conesb", [128, 13], BF16)
    x2_fm = p.get_out("x2_fm", [D, T], F32)

    y_d = p.dram("y_d", [D, W2], F32)
    sh_d = p.dram("sh_d", [D, W2], BF16)
    o_d = p.dram("o_d", [512, W2], BF16)
    hl_d = p.dram("hl_d", [D, W2], BF16)
    rt_d = p.dram("rt_d", [2 * NT2, 512], F32)
    mrow_d = p.dram("mrow_d", [8, 1], F32)

    xs = p.sb("xs", [128, 2 * 512], F32)
    ys = p.sb("ys", [128, 2 * 513], F32)
    sqs = p.sb("sqs", [128, 2 * 512], F32)
    app_scr = p.sb("app_scr", [128, 2 * 512], F32)
    rt_row = p.sb("rt_row", [1, 1024], F32)
    ost = p.sb("ost", [128, 2 * 512], F32)
    Sbb = p.sb("Sbb", [128, 2 * 512], BF16)
    Bbb = p.sb("Bbb", [128, 2 * 512], BF16)
    Qf = p.sb("Qf", [128, 4 * W2], BF16)
    Kf = p.sb("Kf", [128, 4 * W2], BF16)
    V65 = p.sb("V65", [128, 28 * 520], BF16)
    xtb = p.sb("xtb", [128, 1024], BF16)
    Wblk = p.sb("Wblk", [128, 4096], BF16)
    Wv = p.sb("Wv", [128, 8 * 512], BF16)
    rhb = p.sb("rhb", [128, 8 * 512], BF16)
    shm = p.sb("shm", [128, 2 * 512], BF16)
    Wo = p.sb("Wo", [128, 2 * 512], BF16)
    wa = p.sb("wa", [128, 2 * 1024], BF16)
    wg = p.sb("wg", [128, 2 * 1024], BF16)
    w2b = p.sb("w2b", [128, 2816], BF16)
    h_sb = p.sb("h_sb", [128, 22 * 512], BF16)
    Pt = p.sb("Pt", [128, 2 * 512], BF16)
    Ub = p.sb("Ub", [128, 128], BF16)
    Vfb = p.sb("Vfb", [128, 4 * 512], BF16)
    Uab = p.sb("Uab", [128, 128], BF16)
    Vafb = p.sb("Vafb", [128, 4 * 512], BF16)
    selb = p.sb("selb", [128, 31], F32)
    sel2T = p.sb("sel2T", [8, 4 * 128], F32)
    R2b = p.sb("R2b", [128, 128], BF16)
    conesb = p.sb("conesb", [128, 13], F32)
    pwb = p.sb("pwb", [1, 7 * 128], BF16)
    onesrow_f = p.sb("onesrow_f", [1, 128], F32)
    onesrow_b2 = p.sb("onesrow_b2", [1, 128], BF16)
    bvrow_sb = p.sb("bvrow_sb", [1, 512], BF16)
    onesrow_b = p.sb("onesrow_b", [1, 512], BF16)
    qs_sb = p.sb("qs_sb", [128, 2], F32)
    ks_sb = p.sb("ks_sb", [128, 2], F32)
    lg_sb = p.sb("lg_sb", [128, 16], F32)
    lb_sb = p.sb("lb_sb", [128, 16], F32)
    b1a_sb = p.sb("b1a_sb", [128, 44], F32)
    b1g_sb = p.sb("b1g_sb", [128, 44], F32)
    b2_sb = p.sb("b2_sb", [128, 16], F32)
    nrm = p.sb("nrm", [16, 512], F32)
    nq2 = p.sb("nq2", [16, 512], F32)
    qs2_sb = p.sb("qs2_sb", [128, 4], F32)
    qsmax = p.sb("qsmax", [16, 1], F32)
    ksmax = p.sb("ksmax", [16, 1], F32)
    hred = p.sb("hred", [16, 1], F32)
    negM = p.sb("negM", [128, 8], F32)
    mrow = p.sb("mrow", [1, 8], F32)
    arecb = p.sb("arecb", [64, 512], F32)
    arec = nq2
    gact = p.sb("gact", [128, 2 * 512], BF16)
    conesbb = p.sb("conesbb", [128, 13], BF16)
    hlb = gact

    ps = [p.ps(f"ps{i}", [128, 512], F32) for i in range(8)]

    cl = [emit_load2(p, Ub, (0, 128), cU[:, :], stream="konst"),
          emit_load2(p, Vfb, (0, 2048), cVfam[:, :], stream="konst"),
          emit_load2(p, Uab, (0, 128), cUa[:, :], stream="konst"),
          emit_load2(p, Vafb, (0, 2048), cVafam[:, :], stream="konst"),
          emit_load2(p, selb, (0, 31), cselb[:, :], stream="konst"),
          emit_load2(p, sel2T, (0, 512), csel2T[:, :], stream="konst"),
          emit_load2(p, R2b, (0, 128), cR2[:, :], stream="konst"),
          emit_load2(p, conesb, (0, 13), ccones[:, :], stream="konst"),
          emit_load2(p, pwb, (0, 896), cpw[:, :], stream="konst"),
          emit_load2(p, qs_sb, (0, 2), f_qs[:, :], stream="konst"),
          emit_load2(p, ks_sb, (0, 2), f_ks[:, :], stream="konst"),
          emit_load2(p, lg_sb, (0, 16), f_lg[:, :], stream="konst"),
          emit_load2(p, lb_sb, (0, 16), f_lb[:, :], stream="konst"),
          emit_load2(p, b1a_sb, (0, 44), f_b1a[:, :], stream="konst"),
          emit_load2(p, b1g_sb, (0, 44), f_b1g[:, :], stream="konst"),
          emit_load2(p, b2_sb, (0, 16), f_b2[:, :], stream="konst"),
          emit_load2(p, conesbb, (0, 13), cconesb[:, :], stream="konst")]
    mos = [p.op("vector", lambda e: e.memset(onesrow_f[:, :], 1.0)),
           p.op("vector", lambda e: e.memset(onesrow_b[:, :], 1.0)),
           p.op("vector", lambda e: e.memset(V65[:, :], 1.0))]
    t_ones = mos[0]
    t_onesb = mos[1]
    t_v65ones = mos[2]

    bk = {"sx": ps[0], "sx2": ps[1], "b1": ps[2], "b2": ps[3]}
    y0_toks, ln_done = emit_module_ln(
        p, x1h, bk, xs, sqs, conesb, onesrow_f, rt_d, rt_row, Sbb, Bbb,
        app_scr, None, nw_sb=None, rms_eps=None, eps=1e-5,
        pre_deps=[cl[7], t_ones], tag="m", W=W2, out_dram=y_d, ost=ost)

    def stream_tiles_ln(src_dram, tiles, tag2, extra_pre, gains=None, out_d=None,
                        src16=True):
        raise NotImplementedError

    prev_done = [y0_toks[(kc, tt)] for kc in range(8) for tt in range(NT2)]

    resid_out_toks = None
    for l in range(2):
        RQ = list(range(2 * l, NT2))
        RWn = list(range(1 + 2 * l, NT2))
        RFr = list(range(1 + 2 * l, NT2))

        if TRUNC2 is not None and TRUNC2 <= l * 6 + 0:
            return
        sh_toks = {}
        ys_cons = [[], []]
        shm_cons = [[], []]
        for ti, tt in enumerate(RQ):
            for kc in range(8):
                sl = (ti * 8 + kc) % 2
                lo = tt * 512 - 1
                if lo < 0:
                    t_l = p.dma("sync", (lambda k, s: lambda e: e.dma_start(
                        out=ys[:, s * 513 + 1:(s + 1) * 513],
                        in_=y_d[k * 128:(k + 1) * 128, 0:512]))(kc, sl),
                        deps=ys_cons[sl] + prev_done, stream="ys%d" % sl)
                    t_z = p.op("vector", (lambda s: lambda e: e.memset(
                        ys[:, s * 513:s * 513 + 1], 0.0))(sl), deps=ys_cons[sl])
                    dep_src = [t_l, t_z]
                else:
                    t_l = p.dma("sync", (lambda k, s, o: lambda e: e.dma_start(
                        out=ys[:, s * 513:(s + 1) * 513],
                        in_=y_d[k * 128:(k + 1) * 128, o:o + 513]))(kc, sl, lo),
                        deps=ys_cons[sl] + prev_done, stream="ys%d" % sl)
                    dep_src = [t_l]
                off = 1 if kc < 4 else 0
                t_c = p.op("vector", (lambda s, s2, o: lambda e: e.tensor_copy(
                    shm[:, s2 * 512:(s2 + 1) * 512],
                    ys[:, s * 513 + o:s * 513 + o + 512]))(sl, sl, off),
                    deps=dep_src + shm_cons[sl])
                ys_cons[sl] = [t_c]
                t_s = p.dma("scalar", (lambda k, ttt, s2: lambda e: e.dma_start(
                    out=sh_d[k * 128:(k + 1) * 128, ttt * 512:(ttt + 1) * 512],
                    in_=shm[:, s2 * 512:(s2 + 1) * 512]))(kc, tt, sl),
                    deps=[t_c], stream="shst%d" % sl)
                shm_cons[sl] = [t_s]
                sh_toks[(kc, tt)] = t_s

        if TRUNC2 is not None and TRUNC2 <= l * 6 + 1:
            return
        wv_ld = p.dma("gpsimd", (lambda ll: lambda e: e.dma_start(
            out=Wv[:, :].rearrange("p (c m) -> p c m", c=8),
            in_=fw_qkv[ll * D:(ll + 1) * D, 1024:1536].rearrange(
                "(c p) m -> p c m", p=128)))(l), stream="konstg",
            deps=prev_done)
        qk_evac = {}
        v_evac = {}
        rhb_cons = [[], []]
        blk_rot = [[], []]
        vps_rot = [[], []]
        wblk_cons = []
        for blk in range(2):
            wsl = 0
            wblk_ld = p.dma("gpsimd", (lambda ll, b: lambda e: e.dma_start(
                out=Wblk[:, :].rearrange("p (c m) -> p c m", c=8),
                in_=fw_qkv[ll * D:(ll + 1) * D, b * 512:(b + 1) * 512].rearrange(
                    "(c p) m -> p c m", p=128)))(l, blk),
                deps=prev_done + wblk_cons, stream="wblk")
            for ti, tt in enumerate(RQ):
                rsl = 0
                t_r = p.dma("sync", (lambda ttt: lambda e: e.dma_start(
                    out=rhb[:, :].rearrange("p (c m) -> p c m", c=8),
                    in_=sh_d[:, ttt * 512:(ttt + 1) * 512].rearrange(
                        "(c p) m -> p c m", p=128)))(tt),
                    deps=rhb_cons[0] + [sh_toks[(kc, tt)] for kc in range(8)],
                    stream="rhb0")
                rhb_cons[0] = []
                for fcq in range(4 if not SKIP_B_QK else 0):
                    bnk = (blk * 28 + ti * 4 + fcq) % 2
                    mm = None
                    for kc in range(8):
                        mm = p.op("tensor", (lambda f, k, b: lambda e: e.matmul(
                            ps[4 + b][:, :],
                            Wblk[:, k * 512 + f * 128:k * 512 + (f + 1) * 128],
                            rhb[:, k * 512:(k + 1) * 512],
                            start=(k == 0), stop=(k == 7)))(fcq, kc, bnk),
                            deps=[wblk_ld, t_r] + (blk_rot[bnk] if kc == 0 else []))
                    dstT = Qf if blk == 0 else Kf
                    ev = p.op("scalar", (lambda dT, d, t_, b: lambda e: e.copy(
                        dT[:, d * W2 + t_ * 512:d * W2 + (t_ + 1) * 512], ps[4 + b][:, :]))(
                        dstT, fcq, tt, bnk), deps=[mm])
                    blk_rot[bnk] = [ev]
                    rhb_cons[0].append(ev)
                    wblk_cons.append(ev)
                    qk_evac[(blk, fcq, tt)] = ev
                if blk == 0 and not SKIP_B_V:
                    for j in range(4):
                        vb = (ti * 4 + j) % 2
                        mmv = None
                        for kc in range(8):
                            mmv = p.op("tensor", (lambda jj, k, b: lambda e: e.matmul(
                                ps[6 + b][:, :],
                                rhb[:, k * 512 + jj * 128:k * 512 + (jj + 1) * 128],
                                Wv[:, k * 512:(k + 1) * 512],
                                start=(k == 0), stop=(k == 7)))(j, kc, vb),
                                deps=[wv_ld, t_r] + (vps_rot[vb] if kc == 0 else []))
                        vps_rot[vb] = []
                        tg = tt * 4 + j
                        for hh in range(8):
                            evv = p.op("scalar", (lambda tgg, hhh, b: lambda e: e.copy(
                                V65[:, tgg * 520 + hhh * 65:tgg * 520 + hhh * 65 + 64],
                                ps[6 + b][:, hhh * 64:(hhh + 1) * 64]))(tg, hh, vb),
                                deps=[mmv, t_v65ones])
                            vps_rot[vb].append(evv)
                            rhb_cons[0].append(evv)
                            v_evac[(tg, hh)] = evv
        v_all = [v_evac[k] for k in v_evac]
        qk_all = [qk_evac[k] for k in qk_evac]
        last_qk = qk_evac.get((1, 3, RQ[-1]))

        if TRUNC2 is not None and TRUNC2 <= l * 6 + 2:
            return
        t_qs2 = p.op("vector", (lambda ll: lambda e: e.tensor_mul(
            qs2_sb[:, 0:1], qs_sb[:, ll:ll + 1], qs_sb[:, ll:ll + 1]))(l),
            deps=[cl[9], last_qk])
        t_ks2 = p.op("vector", (lambda ll: lambda e: e.tensor_mul(
            qs2_sb[:, 1:2], ks_sb[:, ll:ll + 1], ks_sb[:, ll:ll + 1]))(l),
            deps=[cl[10], t_qs2])
        xpos_done = {}
        nrm_cons = []
        n2_cons = []
        mmax_init = {0: False, 1: False}
        m_bank_free = []
        sq_cons = [[], []]
        xt_cons = [[], []]
        rot_free = []
        scr_cons = [[], []]
        rec_free = []
        for qk in range(2):
            src = Qf if qk == 0 else Kf
            cstab = xq_cs if qk == 0 else xk_cs
            sstab = xq_ss if qk == 0 else xk_ss
            scap = qs_sb if qk == 0 else ks_sb
            s2col = qk
            outmax = qsmax if qk == 0 else ksmax
            for ti, tt in enumerate(RQ):
                xsl = 0
                t_xt = p.dma("sync", (lambda t_, ct: lambda e: e.dma_start(
                    out=xtb[:, 0:512], in_=ct[:, t_ * 512:(t_ + 1) * 512]))(
                    tt, cstab), deps=xt_cons[0], stream="xt0")
                t_xt2 = p.dma("sync", (lambda t_, st_: lambda e: e.dma_start(
                    out=xtb[:, 512:1024], in_=st_[:, t_ * 512:(t_ + 1) * 512]))(
                    tt, sstab), deps=xt_cons[0], stream="xt0")
                xt_cons[0] = []
                mmN = mmM = None
                for cc in range(4):
                    ssl = cc % 2
                    t_sq = p.op("scalar", (lambda s_, c_, t_, s2: lambda e: e.activation(
                        sqs[:, s2 * 512:(s2 + 1) * 512],
                        s_[:, c_ * W2 + t_ * 512:c_ * W2 + (t_ + 1) * 512],
                        AF.Square))(src, cc, tt, ssl),
                        deps=[qk_evac[(qk, cc, tt)]] + sq_cons[ssl])
                    mmN = p.op("tensor", (lambda c_, s2, f: lambda e: e.matmul(
                        ps[2][0:16, :], selb[:, 15 - 2 * c_:31 - 2 * c_],
                        sqs[:, s2 * 512:(s2 + 1) * 512],
                        start=f, stop=(c_ == 3)))(cc, ssl, cc == 0),
                        deps=[t_sq, cl[4]] + (rec_free if cc == 0 else []))
                    t_w = p.op("vector", (lambda s2, sc: lambda e: e.tensor_scalar_mul(
                        sqs[:, s2 * 512:(s2 + 1) * 512],
                        sqs[:, s2 * 512:(s2 + 1) * 512],
                        qs2_sb[:, sc:sc + 1]))(ssl, s2col),
                        deps=[mmN, t_qs2, t_ks2])
                    mmM = p.op("tensor", (lambda c_, s2, f: lambda e: e.matmul(
                        ps[3][0:16, :], selb[:, 15 - 2 * c_:31 - 2 * c_],
                        sqs[:, s2 * 512:(s2 + 1) * 512],
                        start=f, stop=(c_ == 3)))(cc, ssl, cc == 0),
                        deps=[t_w] + (m_bank_free if cc == 0 else []))
                    sq_cons[ssl] = [mmM]
                r1 = p.op("scalar", lambda e: e.activation(nrm[:, :], ps[2][0:16, :], AF.Sqrt),
                          deps=[mmN] + nrm_cons)
                r2 = p.op("vector", lambda e: e.tensor_scalar_max(nrm[:, :], nrm[:, :], 1e-12),
                          deps=[r1])
                r3 = p.op("vector", lambda e: e.reciprocal(nrm[:, :], nrm[:, :]), deps=[r2])
                n1 = p.op("vector", lambda e: e.tensor_mul(nq2[:, :], ps[3][0:16, :], nrm[0:16, :]),
                          deps=[r3, mmM] + n2_cons)
                n2 = p.op("vector", lambda e: e.tensor_mul(nq2[:, :], nq2[:, :], nrm[0:16, :]),
                          deps=[n1])
                m_bank_free = [n1]
                n3 = p.op("vector", lambda e: e.reduce_max(hred[:, :], nq2[:, :], axis=AX.X),
                          deps=[n2])
                if not mmax_init[qk]:
                    n4 = p.op("vector", (lambda om: lambda e: e.tensor_copy(om[:, :], hred[:, :]))(outmax),
                              deps=[n3])
                    mmax_init[qk] = True
                else:
                    n4 = p.op("vector", (lambda om: lambda e: e.tensor_max(om[:, :], om[:, :], hred[:, :]))(outmax),
                              deps=[n3])
                n2_cons = [n4]
                for cc in range(4):
                    bcm = p.op("tensor", (lambda c_: lambda e: e.matmul(
                        ps[2][:, :], sel2T[:, c_ * 128:(c_ + 1) * 128], nrm[0:8, :],
                        start=True, stop=True))(cc),
                        deps=[r3, cl[5], mmN, n1])
                    qn = p.op("vector", (lambda s_, c_, t_, sc_, ll: lambda e: e.scalar_tensor_tensor(
                        s_[:, c_ * W2 + t_ * 512:c_ * W2 + (t_ + 1) * 512],
                        s_[:, c_ * W2 + t_ * 512:c_ * W2 + (t_ + 1) * 512],
                        sc_[:, ll:ll + 1], ps[2][:, :], ALU.mult, ALU.mult))(src, cc, tt, scap, l),
                        deps=[bcm] + sq_cons[0] + sq_cons[1] + [n2])
                    rec_free = [qn]
                    rmm = p.op("tensor", (lambda s_, c_, t_: lambda e: e.matmul(
                        ps[6][:, :], R2b[:, :],
                        s_[:, c_ * W2 + t_ * 512:c_ * W2 + (t_ + 1) * 512],
                        start=True, stop=True))(src, cc, tt),
                        deps=[qn, cl[6]] + rot_free)
                    a1 = p.op("vector", (lambda s_, c_, t_, s2: lambda e: e.tensor_mul(
                        app_scr[:, s2 * 512:(s2 + 1) * 512],
                        s_[:, c_ * W2 + t_ * 512:c_ * W2 + (t_ + 1) * 512],
                        xtb[:, 0:512]))(src, cc, tt, cc % 2),
                        deps=[qn, t_xt] + scr_cons[cc % 2])
                    a2 = p.op("vector", (lambda s_, c_, t_: lambda e: e.scalar_tensor_tensor(
                        s_[:, c_ * W2 + t_ * 512:c_ * W2 + (t_ + 1) * 512],
                        ps[6][:, :], 1.0,
                        xtb[:, 512:1024], ALU.mult, ALU.mult))(src, cc, tt),
                        deps=[rmm, t_xt2, a1])
                    rot_free = [a2]
                    a3 = p.op("vector", (lambda s_, c_, t_, s2: lambda e: e.tensor_add(
                        s_[:, c_ * W2 + t_ * 512:c_ * W2 + (t_ + 1) * 512],
                        s_[:, c_ * W2 + t_ * 512:c_ * W2 + (t_ + 1) * 512],
                        app_scr[:, s2 * 512:(s2 + 1) * 512]))(src, cc, tt, cc % 2),
                        deps=[a2])
                    scr_cons[cc % 2] = [a3]
                    xt_cons[0].append(a3)
                    xpos_done[(qk, cc, tt)] = a3
                nrm_cons = [a3, n2]
        s3 = p.op("vector", lambda e: e.tensor_mul(hred[:, :], qsmax[:, :], ksmax[:, :]),
                  deps=n2_cons + [xpos_done[(1, 3, RQ[-1])]])
        s4 = p.op("scalar", lambda e: e.activation(hred[:, :], hred[:, :], AF.Sqrt), deps=[s3])
        s5 = p.op("vector", lambda e: e.tensor_scalar_mul(hred[:, :], hred[:, :], -QKSC * 1.000001),
                  deps=[s4])
        w = p.dma("sync", lambda e: e.dma_start(out=mrow_d[:, :], in_=hred[0:8, :]), deps=[s5],
                  stream="mrt%d" % l)
        r = p.dma("sync", lambda e: e.dma_start(
            out=mrow[:, :], in_=mrow_d.rearrange("p o -> o p")), deps=[w], stream="mrt%d" % l)
        bcm = p.op("tensor", lambda e: e.matmul(ps[2][:, 0:8], onesrow_f[:, :], mrow[:, :],
                                                start=True, stop=True),
                   deps=[r, t_ones, xpos_done[(1, 3, RQ[-1])]])
        negm_tok = p.op("scalar", lambda e: e.copy(negM[:, :], ps[2][:, 0:8]), deps=[bcm])

        if TRUNC2 is not None and TRUNC2 <= l * 6 + 3:
            return
        on_toks = {}
        sfree = [[negm_tok], [negm_tok]]
        pt_cons = [[], []]
        o_free = [negm_tok]
        rb_free = [xpos_done[(1, 3, RQ[-1])]]
        shm_cons2 = [[], []]
        ost_free = []
        for h in range(8):
            base = (h % 2) * 64
            ch = h // 2
            for w in RWn:
                kgs = [((w - 1) * 4 + kc, kc, True) for kc in range(4)] + \
                      [(w * 4 + kc, kc, False) for kc in range(4)]
                pvs = []
                for i, (kg, kc, prev) in enumerate(kgs):
                    b = i % 2
                    t_s = p.op("tensor", (lambda bb, c_, w_, kg_, b_: lambda e: e.matmul(
                        ps[b_][:, :],
                        Kf[bb:bb + 64, c_ * W2 + kg_ * 128:c_ * W2 + (kg_ + 1) * 128],
                        Qf[bb:bb + 64, c_ * W2 + w_ * 512:c_ * W2 + (w_ + 1) * 512],
                        start=True, stop=False))(base, ch, w, kg, b),
                        deps=[xpos_done[(0, ch, w)], xpos_done[(1, ch, (kg // 4))]] + sfree[b])
                    if prev:
                        t_mk = p.op("tensor", (lambda k, b_: lambda e: e.matmul(
                            ps[b_][:, :], Uab[:, :], Vafb[:, k * 512:(k + 1) * 512],
                            start=False, stop=False))(kc, b), deps=[t_s, cl[2], cl[3]])
                        t_mk = p.op("tensor", (lambda w_, b_: lambda e: e.matmul(
                            ps[b_][:, :], pwb[0:1, w_ * 128:(w_ + 1) * 128], onesrow_b[:, :],
                            start=False, stop=True))(w, b), deps=[t_mk, cl[8], t_onesb])
                    else:
                        t_mk = p.op("tensor", (lambda k, b_: lambda e: e.matmul(
                            ps[b_][:, :], Ub[:, :], Vfb[:, k * 512:(k + 1) * 512],
                            start=False, stop=True))(kc, b), deps=[t_s, cl[0], cl[1]])
                    t_e = p.op("scalar", (lambda h_, b_: lambda e: e.activation(
                        Pt[:, b_ * 512:(b_ + 1) * 512], ps[b_][:, :], AF.Exp,
                        bias=negM[:, h_:h_ + 1], scale=QKSC))(h, b),
                        deps=[t_mk] + pt_cons[b])
                    sfree[b] = [t_e]
                    t_pv = p.op("tensor", (lambda h_, kg_, b_, f, lst: lambda e: e.matmul(
                        ps[3][0:65, :],
                        V65[:, kg_ * 520 + h_ * 65:kg_ * 520 + h_ * 65 + 65],
                        Pt[:, b_ * 512:(b_ + 1) * 512],
                        start=f, stop=lst))(h, kg, b, i == 0, i == 7),
                        deps=[t_e, v_evac[(kg, h)]] + (o_free if i == 0 else []))
                    pt_cons[b] = [t_pv]
                    pvs.append(t_pv)
                t_rc = p.op("vector", lambda e: e.reciprocal(arec[0:1, :], ps[3][64:65, :]),
                            deps=[pvs[-1]] + ost_free)
                t_bc = p.op("tensor", lambda e: e.matmul(
                    ps[6][0:64, :], onesrow_f[0:1, 0:64], arec[0:1, :],
                    start=True, stop=True), deps=[t_rc, t_ones] + rb_free)
                t_ev = p.op("scalar", lambda e: e.copy(arecb[:, :], ps[6][0:64, :]),
                            deps=[t_bc])
                rb_free = [t_ev]
                sl2 = (h * len(RWn) + w) % 2
                t_on = p.op("vector", (lambda s2: lambda e: e.tensor_mul(
                    shm[0:64, s2 * 512:(s2 + 1) * 512],
                    ps[3][0:64, :], arecb[:, :]))(sl2),
                    deps=[t_ev] + shm_cons2[sl2])
                o_free = [t_on]
                ost_free = [t_on]
                t_os = p.dma("scalar", (lambda h_, w_, s2: lambda e: e.dma_start(
                    out=o_d[h_ * 64:(h_ + 1) * 64, w_ * 512:(w_ + 1) * 512],
                    in_=shm[0:64, s2 * 512:(s2 + 1) * 512]))(h, w, sl2),
                    deps=[t_on], stream="ost%d" % sl2)
                shm_cons2[sl2] = [t_os]
                on_toks[(h, w)] = t_os

        if TRUNC2 is not None and TRUNC2 <= l * 6 + 4:
            return
        resid_done = {}
        wo_cons = [[], []]
        rhb_cons2 = [[], []]
        op_rot = [[], []]
        ys_cons2 = [[], []]
        ost_cons2 = [[], []]
        for fi, fc in enumerate(range(8)):
            wsl = fc % 2
            t_wo = p.dma("gpsimd", (lambda ll, f, s: lambda e: e.dma_start(
                out=Wo[:, s * 512:s * 512 + 512].rearrange("p (c m) -> p c m", c=4),
                in_=fw_o[ll * 512:(ll + 1) * 512, f * 128:(f + 1) * 128].rearrange(
                    "(c p) m -> p c m", p=128)))(l, fc, wsl),
                deps=wo_cons[wsl], stream="wo%d" % wsl)
            wo_cons[wsl] = []
            for ti, tt in enumerate(RWn):
                rsl = 0
                t_r = p.dma("sync", (lambda ttt: lambda e: e.dma_start(
                    out=rhb[:, 0:2048].rearrange("p (c m) -> p c m", c=4),
                    in_=o_d[:, ttt * 512:(ttt + 1) * 512].rearrange(
                        "(c p) m -> p c m", p=128)))(tt),
                    deps=rhb_cons2[0] + [on_toks[(h, tt)] for h in range(8)],
                    stream="orb0")
                rhb_cons2[0] = []
                ysl = (fi * len(RWn) + ti) % 2
                t_y = p.dma("sync", (lambda f, ttt, s: lambda e: e.dma_start(
                    out=xs[:, s * 512:(s + 1) * 512],
                    in_=y_d[f * 128:(f + 1) * 128, ttt * 512:(ttt + 1) * 512]))(fc, tt, ysl),
                    deps=ys_cons2[ysl] + prev_done, stream="yrd%d" % ysl)
                ys_cons2[ysl] = []
                bnk = (fi * len(RWn) + ti) % 2
                mm = None
                for ic in range(4):
                    mm = p.op("tensor", (lambda s, k, b: lambda e: e.matmul(
                        ps[4 + b][:, :],
                        Wo[:, s * 512 + k * 128:s * 512 + (k + 1) * 128],
                        rhb[:, k * 512:(k + 1) * 512],
                        start=(k == 0), stop=(k == 3)))(wsl, ic, bnk),
                        deps=[t_wo, t_r] + (op_rot[bnk] if ic == 0 else []))
                osl = (fi * len(RWn) + ti) % 2
                ev = p.op("vector", (lambda b, s, s2: lambda e: e.tensor_add(
                    ost[:, s2 * 512:(s2 + 1) * 512], ps[4 + b][:, :],
                    xs[:, s * 512:(s + 1) * 512]))(bnk, ysl, osl),
                    deps=[mm, t_y] + ost_cons2[osl])
                op_rot[bnk] = [ev]
                rhb_cons2[0].append(ev)
                ys_cons2[ysl].append(ev)
                t_st = p.dma("sync", (lambda f, ttt, s2: lambda e: e.dma_start(
                    out=y_d[f * 128:(f + 1) * 128, ttt * 512:(ttt + 1) * 512],
                    in_=ost[:, s2 * 512:(s2 + 1) * 512]))(fc, tt, osl),
                    deps=[ev], stream="yst%d" % osl)
                ost_cons2[osl] = [t_st]
                wo_cons[wsl].append(ev)
                resid_done[(fc, tt)] = t_st
        prev_done = [resid_done[k] for k in resid_done]

        if TRUNC2 is not None and TRUNC2 <= l * 6 + 5:
            return
        sh2_toks = {}
        ys_cons = [prev_done, prev_done]
        shm_c = [shm_cons2[0] + [on_toks[(7, RWn[-1])]], shm_cons2[1]]
        for ti, tt in enumerate(RFr):
            for kc in range(8):
                sl = (ti * 8 + kc) % 2
                lo = tt * 512 - 1
                t_l = p.dma("sync", (lambda k, s, o: lambda e: e.dma_start(
                    out=ys[:, s * 513:(s + 1) * 513],
                    in_=y_d[k * 128:(k + 1) * 128, o:o + 513]))(kc, sl, lo),
                    deps=ys_cons[sl] + prev_done, stream="ys%d" % sl)
                off = 1 if kc < 4 else 0
                t_c = p.op("vector", (lambda s, o: lambda e: e.tensor_copy(
                    shm[:, s * 512:(s + 1) * 512],
                    ys[:, s * 513 + o:s * 513 + o + 512]))(sl, off),
                    deps=[t_l] + shm_c[sl])
                ys_cons[sl] = [t_c]
                t_s = p.dma("scalar", (lambda k, ttt, s2: lambda e: e.dma_start(
                    out=sh_d[k * 128:(k + 1) * 128, ttt * 512:(ttt + 1) * 512],
                    in_=shm[:, s2 * 512:(s2 + 1) * 512]))(kc, tt, sl),
                    deps=[t_c], stream="shst%d" % sl)
                shm_c[sl] = [t_s]
                sh2_toks[(kc, tt)] = t_s

        nt = len(RFr)
        sl_cons = [[], []]
        sqc = [[], []]
        lm1 = lm2 = None
        for kc in range(8):
            for ti, tt in enumerate(RFr):
                sl = (kc * nt + ti) % 2
                ld = p.dma("gpsimd", (lambda k, ttt, s: lambda e: e.dma_start(
                    out=hlb[:, s * 512:(s + 1) * 512],
                    in_=sh_d[k * 128:(k + 1) * 128, ttt * 512:(ttt + 1) * 512]))(kc, tt, sl),
                    deps=sl_cons[sl] + [sh2_toks[(kc, tt)]], stream="hlb%d" % sl)
                sl_cons[sl] = []
                t_sq = p.op("scalar", (lambda s: lambda e: e.activation(
                    sqs[:, s * 512:(s + 1) * 512], hlb[:, s * 512:(s + 1) * 512],
                    AF.Square))(sl), deps=[ld] + sqc[sl])
                sqc[sl] = []
                first = (kc == 0 and ti == 0)
                lastt = (kc == 7 and ti == nt - 1)
                lm1 = p.op("tensor", (lambda s, ii, f, lst: lambda e: e.matmul(
                    ps[0][0:nt, :], conesbb[:, 6 - ii: 6 - ii + nt],
                    hlb[:, s * 512:(s + 1) * 512], start=f, stop=lst))(sl, ti, first, lastt),
                    deps=[ld, cl[16]] + ([negm_tok] if first else []))
                lm2 = p.op("tensor", (lambda s, ii, f, lst: lambda e: e.matmul(
                    ps[1][0:nt, :], conesb[:, 6 - ii: 6 - ii + nt],
                    sqs[:, s * 512:(s + 1) * 512], start=f, stop=lst))(sl, ti, first, lastt),
                    deps=[t_sq, cl[7]] + ([negm_tok] if first else []))
                sqc[sl] = [lm2]
                sl_cons[sl] += [t_sq, lm1]
        sA = sqs[0:nt, 0:512]
        sB = sqs[0:nt, 512:1024]
        sC = app_scr[0:nt, 0:512]
        sD = app_scr[0:nt, 512:1024]
        u1 = p.op("vector", lambda e: e.tensor_scalar_mul(sA, ps[0][0:nt, :], 1.0 / D), deps=[lm1, lm2])
        u2 = p.op("vector", lambda e: e.tensor_scalar_mul(sB, ps[1][0:nt, :], 1.0 / D), deps=[lm2, u1])
        u3 = p.op("vector", lambda e: e.tensor_mul(sC, sA, sA), deps=[u1])
        u4 = p.op("vector", lambda e: e.tensor_sub(sC, sB, sC), deps=[u2, u3])
        u5 = p.op("vector", lambda e: e.tensor_scalar_add(sD, sC, 1e-5), deps=[u4])
        u6 = p.op("scalar", lambda e: e.activation(sD, sD, AF.Sqrt), deps=[u5])
        u7 = p.op("vector", lambda e: e.reciprocal(sD, sD), deps=[u6])
        u8 = p.op("vector", lambda e: e.scalar_tensor_tensor(sA, sA, -1.0, sD, ALU.mult, ALU.mult),
                  deps=[u7, u3])
        v1 = p.dma("sync", (lambda: lambda e: e.dma_start(out=rt_d[0:nt, :], in_=sD))(),
                   deps=[u7, u8], stream="irt%d" % l)
        v2 = p.dma("sync", (lambda: lambda e: e.dma_start(out=rt_d[7:7 + nt, :], in_=sA))(),
                   deps=[u8], stream="irt%d" % l)
        hl_toks = {}
        sl_cons = [[], []]
        scrc = [[], []]
        shm_c3 = [shm_c[0], shm_c[1]]
        rtb = [[], []]
        bcc = [[], []]
        ps23_free = []
        for ti, tt in enumerate(RFr):
            sl_ = 0
            bsl = ti % 2
            q1 = p.dma("sync", (lambda ii, s: lambda e: e.dma_start(
                out=rt_row[0:1, s * 1024:s * 1024 + 512], in_=rt_d[ii:ii + 1, :]))(ti, sl_),
                deps=[v1] + rtb[sl_], stream="irtl%d%d" % (sl_, l))
            q2 = p.dma("sync", (lambda ii, s: lambda e: e.dma_start(
                out=rt_row[0:1, s * 1024 + 512:s * 1024 + 1024], in_=rt_d[7 + ii:8 + ii, :]))(ti, sl_),
                deps=[v2] + rtb[sl_], stream="irtl%d%d" % (sl_, l))
            b1_ = p.op("tensor", (lambda s: lambda e: e.matmul(
                ps[2][:, :], onesrow_f[:, :], rt_row[0:1, s * 1024:s * 1024 + 512],
                start=True, stop=True))(sl_),
                deps=[q1] + ([negm_tok] if ti == 0 else []) + ps23_free)
            e1 = p.op("scalar", (lambda b_: lambda e: e.copy(
                Sbb[:, b_ * 512:(b_ + 1) * 512], ps[2][:, :]))(bsl), deps=[b1_] + bcc[bsl])
            b2_ = p.op("tensor", (lambda s: lambda e: e.matmul(
                ps[3][:, :], onesrow_f[:, :], rt_row[0:1, s * 1024 + 512:s * 1024 + 1024],
                start=True, stop=True))(sl_),
                deps=[q2] + ([negm_tok] if ti == 0 else []) + ps23_free)
            e2 = p.op("scalar", (lambda b_: lambda e: e.copy(
                Bbb[:, b_ * 512:(b_ + 1) * 512], ps[3][:, :]))(bsl), deps=[b2_] + bcc[bsl])
            ps23_free = [e1, e2]
            rtb[sl_] = [b1_, b2_]
            bcc[bsl] = []
            for kc in range(8):
                sl = (ti * 8 + kc) % 2
                ld = p.dma("gpsimd", (lambda k, ttt, s: lambda e: e.dma_start(
                    out=hlb[:, s * 512:(s + 1) * 512],
                    in_=sh_d[k * 128:(k + 1) * 128, ttt * 512:(ttt + 1) * 512]))(kc, tt, sl),
                    deps=sl_cons[sl] + [lm1, lm2], stream="hlb%d" % sl)
                sl_cons[sl] = []
                g1 = p.op("vector", (lambda k, b_, s, ll: lambda e: e.scalar_tensor_tensor(
                    app_scr[:, s * 512:(s + 1) * 512],
                    hlb[:, s * 512:(s + 1) * 512],
                    lg_sb[:, ll * 8 + k:ll * 8 + k + 1],
                    Sbb[:, b_ * 512:(b_ + 1) * 512], ALU.mult, ALU.mult))(kc, bsl, sl, l),
                    deps=[ld, e1, v1, v2, cl[11]] + scrc[sl])
                g2 = p.op("vector", (lambda k, b_, s, ll: lambda e: e.scalar_tensor_tensor(
                    app_scr[:, s * 512:(s + 1) * 512],
                    Bbb[:, b_ * 512:(b_ + 1) * 512],
                    lg_sb[:, ll * 8 + k:ll * 8 + k + 1],
                    app_scr[:, s * 512:(s + 1) * 512], ALU.mult, ALU.add))(kc, bsl, sl, l),
                    deps=[g1, e2])
                g3 = p.op("vector", (lambda k, s, ll: lambda e: e.tensor_scalar_add(
                    shm[:, s * 512:(s + 1) * 512],
                    app_scr[:, s * 512:(s + 1) * 512],
                    lb_sb[:, ll * 8 + k:ll * 8 + k + 1]))(kc, sl, l),
                    deps=[g2, cl[12]] + shm_c3[sl])
                scrc[sl] = [g3]
                sl_cons[sl].append(g3)
                bcc[bsl].append(g3)
                t_s = p.dma("scalar", (lambda k, ttt, s: lambda e: e.dma_start(
                    out=hl_d[k * 128:(k + 1) * 128, ttt * 512:(ttt + 1) * 512],
                    in_=shm[:, s * 512:(s + 1) * 512]))(kc, tt, sl),
                    deps=[g3], stream="hlst%d" % sl)
                shm_c3[sl] = [t_s]
                hl_toks[(kc, tt)] = t_s

        wac = [[], []]
        h_cons = {}
        rhb_c3 = [[], []]
        ysc = [[], []]
        ostc = [[], []]
        w2c = [[], []]
        gact_c = [[], []]
        ffn_done = {}
        for ti, tt in enumerate(RFr):
            rsl = 0
            t_r = p.dma("sync", (lambda ttt: lambda e: e.dma_start(
                out=rhb[:, :].rearrange("p (c m) -> p c m", c=8),
                in_=hl_d[:, ttt * 512:(ttt + 1) * 512].rearrange(
                    "(c p) m -> p c m", p=128)))(tt),
                deps=rhb_c3[0] + [hl_toks[(kc, tt)] for kc in range(8)], stream="frb0")
            rhb_c3[0] = []
            for hc in range(22):
                wd = 128 if hc < 21 else 42
                wsl = hc % 2
                depw = wac[wsl]
                t_wa = p.dma("gpsimd", (lambda ll, h_, s, w_: lambda e: e.dma_start(
                    out=wa[:, s * 1024:s * 1024 + 8 * w_].rearrange("p (c m) -> p c m", c=8),
                    in_=fw_1[ll * D:(ll + 1) * D, h_ * 128:h_ * 128 + w_].rearrange(
                        "(c p) m -> p c m", p=128)))(l, hc, wsl, wd),
                    deps=depw, stream="wa%d" % wsl)
                t_wg = p.dma("gpsimd", (lambda ll, h_, s, w_: lambda e: e.dma_start(
                    out=wg[:, s * 1024:s * 1024 + 8 * w_].rearrange("p (c m) -> p c m", c=8),
                    in_=fw_1[ll * D:(ll + 1) * D, 2730 + h_ * 128:2730 + h_ * 128 + w_].rearrange(
                        "(c p) m -> p c m", p=128)))(l, hc, wsl, wd),
                    deps=depw, stream="wg%d" % wsl)
                wac[wsl] = []
                bnk = hc % 2
                mma = mmg = None
                for kc in range(8):
                    mma = p.op("tensor", (lambda s, k, b, w_: lambda e: e.matmul(
                        ps[4 + b][0:w_, :],
                        wa[:, s * 1024 + k * w_:s * 1024 + (k + 1) * w_],
                        rhb[:, k * 512:(k + 1) * 512],
                        start=(k == 0), stop=(k == 7)))(wsl, kc, bnk, wd),
                        deps=[t_wa, t_r] + (h_cons.get((bnk, "a"), []) if kc == 0 else []))
                    mmg = p.op("tensor", (lambda s, k, b, w_: lambda e: e.matmul(
                        ps[6 + b][0:w_, :],
                        wg[:, s * 1024 + k * w_:s * 1024 + (k + 1) * w_],
                        rhb[:, k * 512:(k + 1) * 512],
                        start=(k == 0), stop=(k == 7)))(wsl, kc, bnk, wd),
                        deps=[t_wg, t_r] + (h_cons.get((bnk, "g"), []) if kc == 0 else []))
                gsl = hc % 2
                t_g = p.op("scalar", (lambda b, s, w_, h_, ll: lambda e: e.activation(
                    gact[0:w_, s * 512:s * 512 + 512], ps[6 + b][0:w_, :], AF.Gelu,
                    bias=b1g_sb[0:w_, ll * 22 + h_:ll * 22 + h_ + 1]))(bnk, gsl, wd, hc, l),
                    deps=[mmg, cl[14]] + gact_c[gsl])
                t_h = p.op("vector", (lambda b, s, w_, h_, ll: lambda e: e.scalar_tensor_tensor(
                    h_sb[0:w_, h_ * 512:(h_ + 1) * 512], ps[4 + b][0:w_, :],
                    b1a_sb[0:w_, ll * 22 + h_:ll * 22 + h_ + 1],
                    gact[0:w_, s * 512:s * 512 + 512], ALU.add, ALU.mult))(bnk, gsl, wd, hc, l),
                    deps=[t_g, mma, cl[13]] + h_cons.get(("h", hc), []))
                gact_c[gsl] = [t_h]
                h_cons[(bnk, "a")] = [t_h]
                h_cons[(bnk, "g")] = [t_g]
                h_cons[("h", hc)] = []
                wac[wsl] = [mma, mmg]
                h_cons.setdefault("tile", []).append(t_h)
            h_all = h_cons.pop("tile")
            rhb_c3[0] = list(h_all)
            for fc in range(8):
                wsl2 = 0
                t_w2a = p.dma("gpsimd", (lambda ll, f: lambda e: e.dma_start(
                    out=w2b[:, 0:2688].rearrange("p (c m) -> p c m", c=21),
                    in_=fw_2[ll * 2730:ll * 2730 + 2688, f * 128:(f + 1) * 128].rearrange(
                        "(c p) m -> p c m", p=128)))(l, fc),
                    deps=w2c[0], stream="w2s")
                t_w2b = p.dma("gpsimd", (lambda ll, f: lambda e: e.dma_start(
                    out=w2b[0:42, 2688:2816],
                    in_=fw_2[ll * 2730 + 2688:ll * 2730 + 2730, f * 128:(f + 1) * 128]))(l, fc),
                    deps=w2c[0], stream="w2s")
                w2c[0] = []
                ysl = fc % 2
                t_y = p.dma("sync", (lambda f, ttt, s: lambda e: e.dma_start(
                    out=xs[:, s * 512:(s + 1) * 512],
                    in_=y_d[f * 128:(f + 1) * 128, ttt * 512:(ttt + 1) * 512]))(fc, tt, ysl),
                    deps=ysc[ysl] + [resid_done[(fc, tt)]] if l == 0 or True else [], stream="fyr%d" % ysl)
                ysc[ysl] = []
                bnk2 = fc % 2
                mm = None
                for hc in range(22):
                    wd = 128 if hc < 21 else 42
                    mm = p.op("tensor", (lambda h_, b, w_: lambda e: e.matmul(
                        ps[0 + b][:, :],
                        w2b[0:w_, h_ * 128:h_ * 128 + 128],
                        h_sb[0:w_, h_ * 512:(h_ + 1) * 512],
                        start=(h_ == 0), stop=(h_ == 21)))(hc, bnk2, wd),
                        deps=[t_w2a, t_w2b, h_all[hc]] + (ostc[bnk2] if hc == 0 else []))
                w2c[0] = [mm]
                osl = fc % 2
                ev = p.op("vector", (lambda b, sy, s2, f, ll: lambda e: e.scalar_tensor_tensor(
                    ost[:, s2 * 512:(s2 + 1) * 512], ps[0 + b][:, :],
                    b2_sb[:, ll * 8 + f:ll * 8 + f + 1],
                    xs[:, sy * 512:(sy + 1) * 512], ALU.add, ALU.add))(bnk2, ysl, osl, fc, l),
                    deps=[mm, t_y, cl[15]])
                ostc[bnk2] = [ev]
                ysc[ysl].append(ev)
                t_st = p.dma("sync", (lambda f, ttt, s2: lambda e: e.dma_start(
                    out=y_d[f * 128:(f + 1) * 128, ttt * 512:(ttt + 1) * 512],
                    in_=ost[:, s2 * 512:(s2 + 1) * 512]))(fc, tt, osl),
                    deps=[ev], stream="fys%d" % osl)
                ostc[bnk2] = [ev, t_st]
                ysc[ysl].append(t_st)
                ffn_done[(fc, tt)] = t_st
            for hc in range(22):
                h_cons[("h", hc)] = [mm]
        prev_done = [ffn_done[k] for k in ffn_done]

    if TRUNC2 is not None and TRUNC2 <= 12:
        return
    f_ys = [[], []]
    f_xs = [[], []]
    f_os = [[], []]
    for kc in range(8):
        for tt in range(4):
            sl = (kc * 4 + tt) % 2
            t_a = p.dma("sync", (lambda k, t_, s: lambda e: e.dma_start(
                out=xs[:, s * 512:(s + 1) * 512],
                in_=x1h[k * 128:(k + 1) * 128, 1536 + t_ * 512:1536 + (t_ + 1) * 512]))(kc, tt, sl),
                deps=f_xs[sl] + prev_done, stream="fx%d" % sl)
            t_b = p.dma("sync", (lambda k, t_, s: lambda e: e.dma_start(
                out=ys[:, s * 513:s * 513 + 512],
                in_=y_d[k * 128:(k + 1) * 128, 1536 + t_ * 512:1536 + (t_ + 1) * 512]))(kc, tt, sl),
                deps=f_ys[sl] + prev_done, stream="fy2%d" % sl)
            ev = p.op("vector", (lambda s: lambda e: e.tensor_add(
                ost[:, s * 512:(s + 1) * 512], xs[:, s * 512:(s + 1) * 512],
                ys[:, s * 513:s * 513 + 512]))(sl), deps=[t_a, t_b] + f_os[sl])
            f_xs[sl] = [ev]
            f_ys[sl] = [ev]
            st = p.out_dma("sync", (lambda k, t_, s: lambda e: e.dma_start(
                out=x2_fm[k * 128:(k + 1) * 128, t_ * 512:(t_ + 1) * 512],
                in_=ost[:, s * 512:(s + 1) * 512]))(kc, tt, sl),
                deps=[ev], stream="fst%d" % sl)
            f_os[sl] = [st]


def build_l0_ffn():
    p = Prog("l0ffn")
    emit_l0_ffn(p)
    return p


def prep_l0_ffn_inputs(inputs, x1_full):
    C22 = float(2 ** 22)
    U = np_bf((np.arange(128)[:, None] <= np.arange(128)[None, :]).astype(np.float32))
    Ua = np_bf((np.arange(128)[:, None] > np.arange(128)[None, :]).astype(np.float32))
    j = np.arange(128)[:, None]
    q = np.arange(512)[None, :]
    Vfam = np.concatenate(
        [np_bf(np.where(q + (-128 * kc) < j, -C22, 0.0)) for kc in range(4)], axis=1)
    Vafam = np.concatenate(
        [np_bf(np.where(j <= q + (-128 * kc), -C22, 0.0)) for kc in range(4)], axis=1)
    selb = np.zeros((128, 31), dtype=np.float32)
    selb[0:64, 15] = 1.0
    selb[64:128, 16] = 1.0
    sel2T = np.zeros((8, 4 * 128), dtype=np.float32)
    for cc in range(4):
        sel2T[2 * cc, cc * 128:cc * 128 + 64] = 1.0
        sel2T[2 * cc + 1, cc * 128 + 64:cc * 128 + 128] = 1.0
    cones = np.zeros((128, 13), dtype=np.float32)
    cones[:, 6] = 1.0
    R2 = rot_half_R()
    qs = inputs["L0_ffn_q_scale"]
    ks = inputs["L0_ffn_k_scale"]
    f_qs = np.stack([np.tile(qs[l], 2) for l in range(2)], axis=1).astype(np.float32)
    f_ks = np.stack([np.tile(ks[l], 2) for l in range(2)], axis=1).astype(np.float32)
    def chunked(v, n):
        return np.ascontiguousarray(v.reshape(n, 128).T).astype(np.float32)
    f_lg = np.concatenate([chunked(inputs["L0_ffn_ln_g"][l], 8) for l in range(2)], axis=1)
    f_lb = np.concatenate([chunked(inputs["L0_ffn_ln_b"][l], 8) for l in range(2)], axis=1)
    b1 = inputs["L0_ffn_b1"]
    f_b1a = np.zeros((128, 44), dtype=np.float32)
    f_b1g = np.zeros((128, 44), dtype=np.float32)
    for l in range(2):
        for hc in range(22):
            wd = 128 if hc < 21 else 42
            f_b1a[0:wd, l * 22 + hc] = b1[l, hc * 128:hc * 128 + wd]
            f_b1g[0:wd, l * 22 + hc] = b1[l, 2730 + hc * 128:2730 + hc * 128 + wd]
    f_b2 = np.concatenate([chunked(inputs["L0_ffn_b2"][l], 8) for l in range(2)], axis=1)
    fw_qkv = np.ascontiguousarray(inputs["L0_ffn_wqkv"].reshape(2 * D, 1536))
    fw_o = np.ascontiguousarray(inputs["L0_ffn_wo"].reshape(2 * 512, D))
    fw_1 = np.ascontiguousarray(inputs["L0_ffn_w1"].reshape(2 * D, 5460))
    fw_2 = np.ascontiguousarray(inputs["L0_ffn_w2"].reshape(2 * 2730, D))
    in_maps = []
    for c in range(NCORE):
        b, half = c // 2, c % 2
        s = half * T
        pos = np.arange(s - 1536, s + 2048)
        tabs = xpos_tables(pos)
        x1h = np.zeros((D, W2), dtype=np.float32)
        lo = max(0, s - 1536)
        x1h[:, lo - (s - 1536):] = x1_full[b, lo:s + 2048, :].T
        pw = np.zeros((1, 7 * 128), dtype=np.float32)
        if half == 0:
            pw[0, 3 * 128:4 * 128] = -C22
        in_maps.append(dict(
            x1h=x1h, fw_qkv=fw_qkv, fw_o=fw_o, fw_1=fw_1, fw_2=fw_2,
            f_qs=f_qs, f_ks=f_ks, f_lg=f_lg, f_lb=f_lb,
            f_b1a=f_b1a, f_b1g=f_b1g, f_b2=f_b2,
            xq_cs=np_bf(tabs[0]), xq_ss=np_bf(tabs[1]),
            xk_cs=np_bf(tabs[2]), xk_ss=np_bf(tabs[3]),
            c_U=U, c_Vfam=Vfam, c_Ua=Ua, c_Vafam=Vafam,
            c_selb=selb, c_sel2T=sel2T, c_R2=R2, c_cones=cones,
            c_conesb=np_bf(cones), c_pw=np_bf(pw),
        ))
    return in_maps


W3 = 4096


def emit_l1_attn(p: Prog):
    xfull = p.get_in("xfull", [D, W3], F32)
    wq = p.get_in("wq", [D, 512], F32)
    wk = p.get_in("wk", [D, 512], F32)
    wv = p.get_in("wv", [D, 512], F32)
    bq = p.get_in("bq", [128, 4], F32)
    bk = p.get_in("bk", [128, 4], F32)
    bv = p.get_in("bv", [128, 4], F32)
    bv_row = p.get_in("bv_row", [1, 512], BF16)
    cU = p.get_in("c_U", [128, 128], BF16)
    cVfam = p.get_in("c_Vfam", [128, 4 * 512], BF16)
    cselb = p.get_in("c_selb", [128, 31], F32)
    ccones = p.get_in("c_cones", [128, 15], F32)
    o_out = p.get_out("o_out", [512, W3], BF16)
    xln_d = p.dram("xln_d", [D, W3], F32)
    rt_d = p.dram("rt_d", [16, 512], F32)
    mrow_d = p.dram("mrow_d", [8, 1], F32)

    xs = p.sb("xs", [128, 2 * 512], F32)
    sqs = p.sb("sqs", [128, 2 * 512], F32)
    app_scr = p.sb("app_scr", [128, 2 * 512], F32)
    rt_row = p.sb("rt_row", [1, 1024], F32)
    ost = p.sb("ost", [128, 2 * 512], F32)
    Sbb = p.sb("Sbb", [128, 2 * 512], BF16)
    Bbb = p.sb("Bbb", [128, 2 * 512], BF16)
    Qf = p.sb("Qf", [128, 4 * W3], BF16)
    Kf = p.sb("Kf", [128, 4 * W3], BF16)
    V65 = p.sb("V65", [128, 32 * 520], BF16)
    Wqb = p.sb("Wqb", [128, 8 * 512], BF16)
    Wkb = p.sb("Wkb", [128, 8 * 512], BF16)
    Wvb = p.sb("Wvb", [128, 8 * 512], BF16)
    rhb = p.sb("rhb", [128, 2 * 8 * 512], BF16)
    Ub = p.sb("Ub", [128, 128], BF16)
    Vfb = p.sb("Vfb", [128, 4 * 512], BF16)
    selb = p.sb("selb", [128, 31], F32)
    conesb = p.sb("conesb", [128, 15], F32)
    onesrow_f = p.sb("onesrow_f", [1, 128], F32)
    onesrow_b2 = p.sb("onesrow_b2", [1, 128], BF16)
    bvrow_sb = p.sb("bvrow_sb", [1, 512], BF16)
    bq_sb = p.sb("bq_sb", [128, 4], F32)
    bk_sb = p.sb("bk_sb", [128, 4], F32)
    bv_sb = p.sb("bv_sb", [128, 4], F32)
    qsmax = p.sb("qsmax", [16, 1], F32)
    ksmax = p.sb("ksmax", [16, 1], F32)
    hred = p.sb("hred", [16, 1], F32)
    negM = p.sb("negM", [128, 8], F32)
    mrow = p.sb("mrow", [1, 8], F32)
    Pt = p.sb("Pt", [128, 2 * 512], BF16)
    arecb = p.sb("arecb", [64, 512], F32)
    arec = p.sb("arec", [1, 512], F32)
    oshm = p.sb("oshm", [128, 2 * 512], BF16)

    ps = [p.ps(f"ps{i}", [128, 512], F32) for i in range(8)]

    cl = [emit_load2(p, Ub, (0, 128), cU[:, :], stream="konst"),
          emit_load2(p, Vfb, (0, 2048), cVfam[:, :], stream="konst"),
          emit_load2(p, selb, (0, 31), cselb[:, :], stream="konst"),
          emit_load2(p, conesb, (0, 15), ccones[:, :], stream="konst"),
          emit_load2(p, bq_sb, (0, 4), bq[:, :], stream="konst"),
          emit_load2(p, bk_sb, (0, 4), bk[:, :], stream="konst"),
          emit_load2(p, bv_sb, (0, 4), bv[:, :], stream="konst"),
          emit_load2(p, bvrow_sb, (0, 512), bv_row[:, :], stream="konst")]
    wl = [p.dma("gpsimd", (lambda w_, dst: lambda e: e.dma_start(
        out=dst[:, :].rearrange("p (c m) -> p c m", c=8),
        in_=w_.rearrange("(c p) m -> p c m", p=128)))(w_, dst), stream="konstg")
        for (w_, dst) in ((wq, Wqb), (wk, Wkb), (wv, Wvb))]
    t_ones = p.op("vector", lambda e: e.memset(onesrow_f[:, :], 1.0))
    t_onesb2 = p.op("vector", lambda e: e.memset(onesrow_b2[:, :], 1.0))
    t_v65 = p.op("vector", lambda e: e.memset(V65[:, :], 1.0))

    bkk = {"sx": ps[0], "sx2": ps[1], "b1": ps[2], "b2": ps[3]}
    xln_toks, ln_done = emit_module_ln(
        p, xfull, bkk, xs, sqs, conesb, onesrow_f, rt_d, rt_row, Sbb, Bbb,
        app_scr, None, nw_sb=None, rms_eps=None, eps=1e-5,
        pre_deps=[cl[3], t_ones], tag="m", W=W3, out_dram=xln_d, ost=ost)
    prev = [xln_toks[(kc, tt)] for kc in range(8) for tt in range(8)]

    qk_evac = {}
    v_evac = {}
    rhb_cons = [[], []]
    rot = [[], []]
    vrot = [[], []]
    for tt in range(8):
        rsl = tt % 2
        t_r = p.dma("gpsimd", (lambda ttt, s: lambda e: e.dma_start(
            out=rhb[:, s * 4096:(s + 1) * 4096].rearrange("p (c m) -> p c m", c=8),
            in_=xln_d[:, ttt * 512:(ttt + 1) * 512].rearrange(
                "(c p) m -> p c m", p=128)))(tt, rsl),
            deps=rhb_cons[rsl] + prev, stream="rhbg%d" % rsl)
        rhb_cons[rsl] = []
        for (ix, Wb, dstT, bias) in ((0, Wqb, Qf, bq_sb), (1, Wkb, Kf, bk_sb)):
            for fcq in range(4):
                bnk = fcq % 2
                mm = None
                for kc in range(8):
                    mm = p.op("tensor", (lambda W_, f, k, b, r: lambda e: e.matmul(
                        ps[4 + b][:, :],
                        W_[:, k * 512 + f * 128:k * 512 + (f + 1) * 128],
                        rhb[:, r * 4096 + k * 512:r * 4096 + (k + 1) * 512],
                        start=(k == 0), stop=(k == 7)))(Wb, fcq, kc, bnk, rsl),
                        deps=[wl[ix], t_r] + (rot[bnk] if kc == 0 else []))
                ev = p.op("vector", (lambda dT, f, t_, b, bi: lambda e: e.tensor_scalar_add(
                    dT[:, f * W3 + t_ * 512:f * W3 + (t_ + 1) * 512], ps[4 + b][:, :],
                    bi[:, f:f + 1]))(dstT, fcq, tt, bnk, bias),
                    deps=[mm, cl[4 + ix]])
                rot[bnk] = [ev]
                rhb_cons[rsl].append(ev)
                qk_evac[(ix, fcq, tt)] = ev
        for j in range(4):
            vb = j % 2
            mmv = None
            for kc in range(8):
                mmv = p.op("tensor", (lambda jj, k, b, r: lambda e: e.matmul(
                    ps[6 + b][:, :],
                    rhb[:, r * 4096 + k * 512 + jj * 128:r * 4096 + k * 512 + (jj + 1) * 128],
                    Wvb[:, k * 512:(k + 1) * 512],
                    start=(k == 0), stop=False))(j, kc, vb, rsl),
                    deps=[wl[2], t_r] + (vrot[vb] if kc == 0 else []))
            mmv = p.op("tensor", (lambda b: lambda e: e.matmul(
                ps[6 + b][:, :], onesrow_b2[:, :], bvrow_sb[0:1, :],
                start=False, stop=True))(vb), deps=[mmv, t_onesb2, cl[7]])
            tg = tt * 4 + j
            vrot[vb] = []
            for hh in range(8):
                evv = p.op("scalar", (lambda tgg, hhh, b: lambda e: e.copy(
                    V65[:, tgg * 520 + hhh * 65:tgg * 520 + hhh * 65 + 64],
                    ps[6 + b][:, hhh * 64:(hhh + 1) * 64]))(tg, hh, vb),
                    deps=[mmv, t_v65])
                vrot[vb].append(evv)
                rhb_cons[rsl].append(evv)
                v_evac[(tg, hh)] = evv


    sqc = [[], []]
    bankf = []
    lastup = None
    for ix, (tens, outmax) in enumerate(((Qf, qsmax), (Kf, ksmax))):
        for tt in range(8):
            mm = None
            for cc in range(4):
                sl = cc % 2
                sq = p.op("scalar", (lambda s_, c_, t_, s2: lambda e: e.activation(
                    sqs[:, s2 * 512:(s2 + 1) * 512],
                    s_[:, c_ * W3 + t_ * 512:c_ * W3 + (t_ + 1) * 512],
                    AF.Square))(tens, cc, tt, sl),
                    deps=[qk_evac[(ix, cc, tt)]] + sqc[sl] + ln_done[-1:])
                mm = p.op("tensor", (lambda c_, s2, f, lst: lambda e: e.matmul(
                    ps[2][0:16, :], selb[:, 15 - 2 * c_:31 - 2 * c_],
                    sqs[:, s2 * 512:(s2 + 1) * 512],
                    start=f, stop=lst))(cc, sl, cc == 0, cc == 3),
                    deps=[sq, cl[2]] + (bankf if cc == 0 else []))
                sqc[sl] = [mm]
            rd = p.op("vector", lambda e: e.reduce_max(hred[:, :], ps[2][0:16, :], axis=AX.X),
                      deps=[mm] + ([lastup] if lastup else []))
            if tt == 0:
                up = p.op("vector", (lambda om: lambda e: e.tensor_copy(om[:, :], hred[:, :]))(outmax),
                          deps=[rd])
            else:
                up = p.op("vector", (lambda om: lambda e: e.tensor_max(om[:, :], om[:, :], hred[:, :]))(outmax),
                          deps=[rd])
            bankf = [rd]
            lastup = up
    s3 = p.op("vector", lambda e: e.tensor_mul(hred[:, :], qsmax[:, :], ksmax[:, :]), deps=[lastup])
    s4 = p.op("scalar", lambda e: e.activation(hred[:, :], hred[:, :], AF.Sqrt), deps=[s3])
    s5 = p.op("vector", lambda e: e.tensor_scalar_mul(hred[:, :], hred[:, :], -0.1250002), deps=[s4])
    w = p.dma("sync", lambda e: e.dma_start(out=mrow_d[:, :], in_=hred[0:8, :]), deps=[s5], stream="mrt")
    r = p.dma("sync", lambda e: e.dma_start(out=mrow[:, :], in_=mrow_d.rearrange("p o -> o p")),
              deps=[w], stream="mrt")
    bcm = p.op("tensor", lambda e: e.matmul(ps[2][:, 0:8], onesrow_f[:, :], mrow[:, :],
                                            start=True, stop=True), deps=[r, t_ones, bankf[0]])
    negm_tok = p.op("scalar", lambda e: e.copy(negM[:, :], ps[2][:, 0:8]), deps=[bcm])

    sfree = [[negm_tok], [negm_tok]]
    pt_cons = [[], []]
    o_free = [negm_tok]
    rb_free = [qk_evac[(1, 3, 7)]]
    oshm_c = [[], []]
    ost_free = []
    for h in range(8):
        base = (h % 2) * 64
        ch = h // 2
        for qt in range(8):
            nkc = 4 * (qt + 1)
            pvs = []
            for i in range(nkc):
                b = i % 2
                t_s = p.op("tensor", (lambda bb, c_, q_, kg_, b_: lambda e: e.matmul(
                    ps[b_][:, :],
                    Kf[bb:bb + 64, c_ * W3 + kg_ * 128:c_ * W3 + (kg_ + 1) * 128],
                    Qf[bb:bb + 64, c_ * W3 + q_ * 512:c_ * W3 + (q_ + 1) * 512],
                    start=True, stop=(kg_ < q_ * 4)))(base, ch, qt, i, b),
                    deps=[qk_evac[(0, ch, qt)], qk_evac[(1, ch, i // 4)]] + sfree[b])
                if i >= qt * 4:
                    t_s = p.op("tensor", (lambda k, b_: lambda e: e.matmul(
                        ps[b_][:, :], Ub[:, :], Vfb[:, k * 512:(k + 1) * 512],
                        start=False, stop=True))(i - qt * 4, b), deps=[t_s, cl[0], cl[1]])
                t_e = p.op("scalar", (lambda h_, b_: lambda e: e.activation(
                    Pt[:, b_ * 512:(b_ + 1) * 512], ps[b_][:, :], AF.Exp,
                    bias=negM[:, h_:h_ + 1], scale=0.125))(h, b),
                    deps=[t_s] + pt_cons[b])
                sfree[b] = [t_e]
                t_pv = p.op("tensor", (lambda h_, kg_, b_, f, lst: lambda e: e.matmul(
                    ps[3][0:65, :],
                    V65[:, kg_ * 520 + h_ * 65:kg_ * 520 + h_ * 65 + 65],
                    Pt[:, b_ * 512:(b_ + 1) * 512],
                    start=f, stop=lst))(h, i, b, i == 0, i == nkc - 1),
                    deps=[t_e, v_evac[(i, h)]] + (o_free if i == 0 else []))
                pt_cons[b] = [t_pv]
                pvs.append(t_pv)
            t_rc = p.op("vector", lambda e: e.reciprocal(arec[0:1, :], ps[3][64:65, :]),
                        deps=[pvs[-1]] + ost_free)
            t_bc = p.op("tensor", lambda e: e.matmul(
                ps[6][0:64, :], onesrow_f[0:1, 0:64], arec[0:1, :], start=True, stop=True),
                deps=[t_rc, t_ones] + rb_free)
            t_ev = p.op("scalar", lambda e: e.copy(arecb[:, :], ps[6][0:64, :]), deps=[t_bc])
            rb_free = [t_ev]
            sl2 = (h * 8 + qt) % 2
            t_on = p.op("vector", (lambda s2: lambda e: e.tensor_mul(
                oshm[0:64, s2 * 512:(s2 + 1) * 512], ps[3][0:64, :], arecb[:, :]))(sl2),
                deps=[t_ev] + oshm_c[sl2])
            o_free = [t_on]
            ost_free = [t_on]
            t_os = p.out_dma("scalar", (lambda h_, q_, s2: lambda e: e.dma_start(
                out=o_out[h_ * 64:(h_ + 1) * 64, q_ * 512:(q_ + 1) * 512],
                in_=oshm[0:64, s2 * 512:(s2 + 1) * 512]))(h, qt, sl2),
                deps=[t_on], stream="oos%d" % sl2)
            oshm_c[sl2] = [t_os]


def build_l1_attn():
    p = Prog("l1attn")
    emit_l1_attn(p)
    return p


def prep_l1_attn_inputs(inputs, x2_full):
    U = np_bf((np.arange(128)[:, None] <= np.arange(128)[None, :]).astype(np.float32))
    Vfam = np.concatenate([causal_mask_V(-128 * kc) for kc in range(4)], axis=1)
    selb = np.zeros((128, 31), dtype=np.float32)
    selb[0:64, 15] = 1.0
    selb[64:128, 16] = 1.0
    cones = np.zeros((128, 15), dtype=np.float32)
    cones[:, 7] = 1.0
    Wa = inputs["L1_attn_W_attn"]
    ba = inputs["L1_attn_b_attn"]
    in_maps = []
    for c in range(NCORE):
        b, hg = c // 2, c % 2
        cs = hg * 512
        in_maps.append(dict(
            xfull=np.ascontiguousarray(x2_full[b].T),
            wq=np.ascontiguousarray(Wa[:, cs:cs + 512]),
            wk=np.ascontiguousarray(Wa[:, 1024 + cs:1024 + cs + 512]),
            wv=np.ascontiguousarray(Wa[:, 2048 + cs:2048 + cs + 512]),
            bq=np.ascontiguousarray(ba[cs:cs + 512].reshape(4, 128).T).astype(np.float32),
            bk=np.ascontiguousarray(ba[1024 + cs:1024 + cs + 512].reshape(4, 128).T).astype(np.float32),
            bv=np.ascontiguousarray(ba[2048 + cs:2048 + cs + 512].reshape(4, 128).T).astype(np.float32),
            bv_row=np_bf(ba[2048 + cs:2048 + cs + 512].reshape(1, 512)),
            c_U=U, c_Vfam=Vfam, c_selb=selb, c_cones=cones,
        ))
    return in_maps



def emit_l1_proj_router(p: Prog):
    x2own = p.get_in("x2own", [D, T], F32)
    o_mine = p.get_in("o_mine", [D, T], BF16)
    wproj = p.get_in("wproj", [D, D], F32)
    bproj = p.get_in("bproj", [128, 8], F32)
    wr = p.get_in("wr", [D, 8], F32)
    ccones = p.get_in("c_cones", [128, 7], F32)
    ciota8 = p.get_in("c_iota8", [128, 8], F32)
    ciota8m = p.get_in("c_iota8m", [128, 8], F32)
    cid8 = p.get_in("c_id8", [8, 8], F32)
    x3_fm = p.get_out("x3_fm", [D, T], F32)
    xln4 = p.get_out("xln4", [D, T], BF16)
    meta = p.get_out("meta", [T, 4], F32)
    x3_d = p.dram("x3_d", [D, T], F32)
    xlnf_d = p.dram("xlnf_d", [D, T], F32)
    rt_d = p.dram("rt_d", [8, 512], F32)

    xs = p.sb("xs", [128, 2 * 512], F32)
    sqs = p.sb("sqs", [128, 2 * 512], F32)
    app_scr = p.sb("app_scr", [128, 2 * 512], F32)
    rt_row = p.sb("rt_row", [1, 1024], F32)
    ost = p.sb("ost", [128, 2 * 512], F32)
    Sbb = p.sb("Sbb", [128, 2 * 512], BF16)
    Bbb = p.sb("Bbb", [128, 2 * 512], BF16)
    rhb = p.sb("rhb", [128, 2 * 8 * 512], BF16)
    Wst = p.sb("Wst", [128, 2 * 1024], BF16)
    Wrb = p.sb("Wrb", [128, 64], F32)
    conesb = p.sb("conesb", [128, 7], F32)
    iota8 = p.sb("iota8", [128, 8], F32)
    iota8m = p.sb("iota8m", [128, 8], F32)
    id8 = p.sb("id8", [8, 8], F32)
    onesrow_f = p.sb("onesrow_f", [1, 128], F32)
    bproj_sb = p.sb("bproj_sb", [128, 8], F32)
    lgt = p.sb("lgt", [8, T], F32)
    xlnf = p.sb("xlnf", [128, 2 * 4096], F32)
    lgt2 = p.sb("lgt2", [128, 8], F32)
    l2s = p.sb("l2s", [128, 8], F32)
    m1s = p.sb("m1s", [128, 8], F32)
    mt = p.sb("mt", [128, 4], F32)

    ps = [p.ps(f"ps{i}", [128, 512], F32) for i in range(8)]

    cl = [emit_load2(p, conesb, (0, 7), ccones[:, :], stream="konst"),
          emit_load2(p, iota8, (0, 8), ciota8[:, :], stream="konst"),
          emit_load2(p, iota8m, (0, 8), ciota8m[:, :], stream="konst"),
          emit_load2(p, id8, (0, 8), cid8[:, :], stream="konst"),
          emit_load2(p, bproj_sb, (0, 8), bproj[:, :], stream="konst")]
    wr_ld = p.dma("sync", lambda e: e.dma_start(
        out=Wrb[:, :].rearrange("p (c m) -> p c m", c=8),
        in_=wr.rearrange("(c p) m -> p c m", p=128)), stream="konst")
    t_ones = p.op("vector", lambda e: e.memset(onesrow_f[:, :], 1.0))

    x3_toks = {}
    wcons = [[], []]
    rhbc = [[], []]
    oprot = [[], []]
    xsc = [[], []]
    ostc = [[], []]
    for fc in range(8):
        wsl = fc % 2
        t_w = p.dma("gpsimd", (lambda f, s: lambda e: e.dma_start(
            out=Wst[:, s * 1024:s * 1024 + 1024].rearrange("p (c m) -> p c m", c=8),
            in_=wproj[:, f * 128:(f + 1) * 128].rearrange("(c p) m -> p c m", p=128)))(fc, wsl),
            deps=wcons[wsl], stream="w%d" % wsl)
        wcons[wsl] = []
        for tt in range(4):
            rsl = (fc * 4 + tt) % 2
            t_r = p.dma("sync", (lambda ttt, s: lambda e: e.dma_start(
                out=rhb[:, s * 4096:(s + 1) * 4096].rearrange("p (c m) -> p c m", c=8),
                in_=o_mine[:, ttt * 512:(ttt + 1) * 512].rearrange(
                    "(c p) m -> p c m", p=128)))(tt, rsl),
                deps=rhbc[rsl], stream="rhb%d" % rsl)
            rhbc[rsl] = []
            t_x = p.dma("sync", (lambda f, ttt, s: lambda e: e.dma_start(
                out=xs[:, s * 512:(s + 1) * 512],
                in_=x2own[f * 128:(f + 1) * 128, ttt * 512:(ttt + 1) * 512]))(fc, tt, rsl),
                deps=xsc[rsl], stream="xs%d" % rsl)
            xsc[rsl] = []
            bnk = (fc * 4 + tt) % 2
            mm = None
            for kc in range(8):
                mm = p.op("tensor", (lambda s, k, b, r: lambda e: e.matmul(
                    ps[4 + b][:, :],
                    Wst[:, s * 1024 + k * 128:s * 1024 + (k + 1) * 128],
                    rhb[:, r * 4096 + k * 512:r * 4096 + (k + 1) * 512],
                    start=(k == 0), stop=(k == 7)))(wsl, kc, bnk, rsl),
                    deps=[t_w, t_r] + (oprot[bnk] if kc == 0 else []))
            ev = p.op("vector", (lambda b, s, f: lambda e: e.scalar_tensor_tensor(
                ost[:, s * 512:(s + 1) * 512], ps[4 + b][:, :],
                bproj_sb[:, f:f + 1],
                xs[:, s * 512:(s + 1) * 512], ALU.add, ALU.add))(bnk, rsl, fc),
                deps=[mm, t_x, cl[4]] + ostc[rsl])
            oprot[bnk] = [ev]
            rhbc[rsl].append(ev)
            xsc[rsl].append(ev)
            wcons[wsl].append(ev)
            st1 = p.out_dma("sync", (lambda f, ttt, s: lambda e: e.dma_start(
                out=x3_fm[f * 128:(f + 1) * 128, ttt * 512:(ttt + 1) * 512],
                in_=ost[:, s * 512:(s + 1) * 512]))(fc, tt, rsl), deps=[ev], stream="st%d" % rsl)
            st2 = p.dma("sync", (lambda f, ttt, s: lambda e: e.dma_start(
                out=x3_d[f * 128:(f + 1) * 128, ttt * 512:(ttt + 1) * 512],
                in_=ost[:, s * 512:(s + 1) * 512]))(fc, tt, rsl), deps=[ev], stream="st%d" % rsl)
            ostc[rsl] = [st1, st2]
            x3_toks[(fc, tt)] = st2

    bkk = {"sx": ps[0], "sx2": ps[1], "b1": ps[2], "b2": ps[3]}
    xln_toks, ln_done, xln_toks2 = emit_module_ln(
        p, x3_d, bkk, xs, sqs, conesb, onesrow_f, rt_d, rt_row, Sbb, Bbb,
        app_scr, None, nw_sb=None, rms_eps=None, eps=1e-5,
        pre_deps=[cl[0], t_ones] + [x3_toks[k] for k in x3_toks], tag="m",
        W=T, out_dram=xln4, ost=ost, out_gpsimd=True, out_dram2=xlnf_d)
    for k in xln_toks:
        p.out_toks.append(xln_toks[k])

    lg_evac = {}
    lgc = []
    rhbc2 = [[], []]
    for tt in range(4):
        rsl = tt % 2
        t_r = p.dma("sync", (lambda ttt, s: lambda e: e.dma_start(
            out=xlnf[:, s * 4096:(s + 1) * 4096].rearrange("p (c m) -> p c m", c=8),
            in_=xlnf_d[:, ttt * 512:(ttt + 1) * 512].rearrange(
                "(c p) m -> p c m", p=128)))(tt, rsl),
            deps=[xln_toks2[(kc, tt)] for kc in range(8)] + rhbc2[rsl], stream="rr%d" % rsl)
        mm = None
        for kc in range(8):
            mm = p.op("tensor", (lambda k, r: lambda e: e.matmul(
                ps[4][0:8, :], Wrb[:, k * 8:(k + 1) * 8],
                xlnf[:, r * 4096 + k * 512:r * 4096 + (k + 1) * 512],
                start=(k == 0), stop=(k == 7)))(kc, rsl),
                deps=[wr_ld, t_r] + (lgc if kc == 0 else []))
        ev = p.op("vector", (lambda ttt: lambda e: e.tensor_copy(
            lgt[0:8, ttt * 512:(ttt + 1) * 512], ps[4][0:8, :]))(tt), deps=[mm])
        lgc = [ev]
        rhbc2[rsl] = [mm]
        lg_evac[tt] = ev
    c1_ = p.op("vector", lambda e: e.tensor_scalar_min(lgt[0:8, :], lgt[0:8, :], 10000.0),
               deps=list(lg_evac.values()))
    c2_ = p.op("vector", lambda e: e.tensor_scalar_max(lgt[0:8, :], lgt[0:8, :], -10000.0),
               deps=[c1_])
    tp_free = []
    mtc = []
    for j in range(16):
        tp = p.op("tensor", (lambda jj: lambda e: e.transpose(
            ps[5][0:128, 0:8], lgt[0:8, jj * 128:(jj + 1) * 128], id8[:, :]))(j),
            deps=[c2_, cl[3]] + tp_free)
        g0 = p.op("vector", lambda e: e.tensor_copy(lgt2[:, :], ps[5][0:128, 0:8]), deps=[tp] + mtc)
        tp_free = [g0]
        g1 = p.op("vector", lambda e: e.reduce_max(m1s[:, 0:1], lgt2[:, :], axis=AX.X), deps=[g0])
        g2 = p.op("vector", lambda e: e.tensor_scalar(l2s[:, :], lgt2[:, :], m1s[:, 0:1], None,
                                                      ALU.is_equal), deps=[g1])
        g3 = p.op("vector", lambda e: e.tensor_mul(l2s[:, :], l2s[:, :], iota8m[:, :]),
                  deps=[g2, cl[2]])
        g4 = p.op("vector", lambda e: e.tensor_reduce(m1s[:, 1:2], l2s[:, :], AX.X, ALU.min),
                  deps=[g3])
        g5 = p.op("vector", lambda e: e.tensor_scalar_add(m1s[:, 1:2], m1s[:, 1:2], 999.0),
                  deps=[g4])
        g6 = p.op("vector", lambda e: e.tensor_scalar(l2s[:, :], iota8[:, :], m1s[:, 1:2], None,
                                                      ALU.is_equal), deps=[g5, cl[1]])
        g7 = p.op("vector", lambda e: e.scalar_tensor_tensor(l2s[:, :], l2s[:, :], -1e30,
                                                             lgt2[:, :], ALU.mult, ALU.add),
                  deps=[g6])
        g8 = p.op("vector", lambda e: e.reduce_max(m1s[:, 2:3], l2s[:, :], axis=AX.X), deps=[g7])
        g9 = p.op("vector", lambda e: e.tensor_scalar(l2s[:, :], l2s[:, :], m1s[:, 2:3], None,
                                                      ALU.is_equal), deps=[g8])
        g10 = p.op("vector", lambda e: e.tensor_mul(l2s[:, :], l2s[:, :], iota8m[:, :]), deps=[g9])
        g11 = p.op("vector", lambda e: e.tensor_reduce(m1s[:, 3:4], l2s[:, :], AX.X, ALU.min),
                   deps=[g10])
        g12 = p.op("vector", lambda e: e.tensor_scalar_add(m1s[:, 3:4], m1s[:, 3:4], 999.0),
                   deps=[g11])
        g13 = p.op("vector", lambda e: e.tensor_sub(m1s[:, 4:5], m1s[:, 2:3], m1s[:, 0:1]),
                   deps=[g12, g8, g1])
        g14 = p.op("scalar", lambda e: e.activation(m1s[:, 4:5], m1s[:, 4:5], AF.Exp), deps=[g13])
        g15 = p.op("vector", lambda e: e.tensor_scalar_add(m1s[:, 5:6], m1s[:, 4:5],
                                                           1.0 + 1e-12), deps=[g14])
        g16 = p.op("vector", lambda e: e.reciprocal(m1s[:, 5:6], m1s[:, 5:6]), deps=[g15])
        g17 = p.op("vector", lambda e: e.tensor_mul(m1s[:, 6:7], m1s[:, 4:5], m1s[:, 5:6]),
                   deps=[g16])
        h1 = p.op("vector", lambda e: e.tensor_copy(mt[:, 0:1], m1s[:, 1:2]), deps=[g17] + mtc)
        h2 = p.op("vector", lambda e: e.tensor_copy(mt[:, 1:2], m1s[:, 3:4]), deps=[g12, h1])
        h3 = p.op("vector", lambda e: e.tensor_copy(mt[:, 2:3], m1s[:, 5:6]), deps=[g16, h2])
        h4 = p.op("vector", lambda e: e.tensor_copy(mt[:, 3:4], m1s[:, 6:7]), deps=[g17, h3])
        st = p.out_dma("sync", (lambda jj: lambda e: e.dma_start(
            out=meta[jj * 128:(jj + 1) * 128, :], in_=mt[:, :]))(j), deps=[h4], stream="mst")
        mtc = [st]


def build_l1_proj_router():
    p = Prog("l1pr")
    emit_l1_proj_router(p)
    return p


def prep_l1_proj_router_inputs(inputs, x2_full, o_outs):
    cones = np.zeros((128, 7), dtype=np.float32)
    cones[:, 3] = 1.0
    iota8 = np.broadcast_to(np.arange(8, dtype=np.float32)[None, :], (128, 8)).copy()
    in_maps = []
    for c in range(NCORE):
        b, half = c // 2, c % 2
        s = half * T
        o_mine = np.concatenate([
            np.asarray(o_outs[2 * b])[:, s:s + T],
            np.asarray(o_outs[2 * b + 1])[:, s:s + T]], axis=0)
        in_maps.append(dict(
            x2own=np.ascontiguousarray(x2_full[b, s:s + T, :].T),
            o_mine=np.ascontiguousarray(o_mine),
            wproj=inputs["L1_attn_W_proj"],
            bproj=np.ascontiguousarray(inputs["L1_attn_b_proj"].reshape(8, 128).T).astype(np.float32),
            wr=inputs["L1_ffn_Wr"],
            c_cones=cones, c_iota8=iota8, c_iota8m=iota8 - 999.0,
            c_id8=np.eye(8, dtype=np.float32),
        ))
    return in_maps


TG = 16384
CAP = 5120.0


def emit_moe_dense(p: Prog):
    xg = p.get_in("xg", [D, TG], BF16)
    meta_all = p.get_in("meta_all", [TG, 4], F32)
    w1e = p.get_in("w1e", [D, 4096], F32)
    w3e = p.get_in("w3e", [D, 4096], F32)
    w2e = p.get_in("w2e", [4096, D], F32)
    myexp = p.get_in("myexp", [128, 1], F32)
    ctris = p.get_in("c_tris", [128, 128], BF16)
    ccone255 = p.get_in("c_cone255", [128, 255], BF16)
    ciota8 = p.get_in("c_iota8", [128, 8], F32)
    part = p.get_out("part", [D, TG], BF16)
    h_d = p.dram("h_d", [4096, TG], BF16)
    g_d = p.dram("g_d", [128, 128], F32)
    offs_d = p.dram("offs_d", [128, 8], F32)

    w2sb = p.sb("w2sb", [128, 32 * 1024], BF16)
    hrb = p.sb("hrb", [128, 32 * 512], BF16)
    w1b = p.sb("w1b", [128, 8 * 1024], BF16)
    w3b = p.sb("w3b", [128, 8 * 1024], BF16)
    rhb = p.sb("rhb", [128, 2 * 8 * 512], BF16)
    hst = p.sb("hst", [128, 2 * 512], BF16)
    gact = p.sb("gact", [128, 2 * 512], BF16)
    cums = p.sb("cums", [128, 8 * 128], F32)
    g_sb = p.sb("g_sb", [128, 128], F32)
    gT_sb = p.sb("gT_sb", [128, 128], F32)
    grow = p.sb("grow", [1, 2 * 512], F32)
    gb = p.sb("gb", [128, 2 * 512], F32)
    mt = p.sb("mt", [128, 2 * 4], F32)
    oh1 = p.sb("oh1", [128, 8], F32)
    oh2 = p.sb("oh2", [128, 8], F32)
    obf = p.sb("obf", [128, 8], BF16)
    scr8 = p.sb("scr8", [128, 8 * 4], F32)
    trisb = p.sb("trisb", [128, 128], BF16)
    cone255 = p.sb("cone255", [128, 255], BF16)
    iota8 = p.sb("iota8", [128, 8], F32)
    idb = p.sb("idb", [128, 128], F32)
    myx = p.sb("myx", [128, 1], F32)
    totbf = p.sb("totbf", [128, 8], BF16)
    orow = p.sb("orow", [1, 2 * 8], F32)
    ost = p.sb("ost", [128, 2 * 512], BF16)
    onesr = p.sb("onesr", [1, 128], F32)

    ps = [p.ps(f"ps{i}", [128, 512], F32) for i in range(8)]

    cl = [emit_load2(p, trisb, (0, 128), ctris[:, :], stream="konst"),
          emit_load2(p, cone255, (0, 255), ccone255[:, :], stream="konst"),
          emit_load2(p, iota8, (0, 8), ciota8[:, :], stream="konst"),
          emit_load2(p, myx, (0, 1), myexp[:, :], stream="konst")]
    t_id = p.op("vector", lambda e: e.memset(idb[:, :], 0.0))
    t_onesr = p.op("vector", lambda e: e.memset(onesr[:, :], 1.0))
    ciota128 = p.get_in("c_iota128", [128, 128], F32)
    il = emit_load2(p, idb, (0, 128), ciota128[:, :], stream="konst", deps=[t_id])

    md = (p.io or {}).get("meta_dep", []) if p.io else []
    xgd = (p.io or {}).get("xg_dep", []) if p.io else []
    mtc = [[], []]
    cum_evac = {}
    ps_rot = [[], []]
    tot_last = None
    oh_cons = []
    for tile in range(128):
        sl = tile % 2
        ld = p.dma("sync", (lambda t_, s: lambda e: e.dma_start(
            out=mt[:, s * 4:(s + 1) * 4], in_=meta_all[t_ * 128:(t_ + 1) * 128, :]))(tile, sl),
            deps=mtc[sl] + (md if tile < 2 else []), stream="mt%d" % sl)
        o1 = p.op("vector", (lambda s: lambda e: e.tensor_scalar(
            oh1[:, :], iota8[:, :], mt[:, s * 4:s * 4 + 1], None, ALU.is_equal))(sl),
            deps=[ld, cl[2]] + oh_cons)
        o2 = p.op("vector", (lambda s: lambda e: e.tensor_scalar(
            oh2[:, :], iota8[:, :], mt[:, s * 4 + 1:s * 4 + 2], None, ALU.is_equal))(sl),
            deps=[ld, o1])
        ob = p.op("vector", lambda e: e.tensor_add(obf[:, :], oh1[:, :], oh2[:, :]),
                  deps=[o2])
        mtc[sl] = [o1, o2]
        oh_cons = [ob]
        bnk = tile % 2
        mm1 = p.op("tensor", (lambda b: lambda e: e.matmul(
            ps[4 + b][:, 0:8], trisb[:, :], obf[:, :], start=True, stop=True))(bnk),
            deps=[ob, cl[0]] + ps_rot[bnk])
        ev = p.op("vector", (lambda t_, b: lambda e: e.tensor_copy(
            cums[:, t_ * 8:(t_ + 1) * 8], ps[4 + b][:, 0:8]))(tile, bnk), deps=[mm1])
        ps_rot[bnk] = [ev]
        cum_evac[tile] = ev
        tot_last = p.op("tensor", (lambda t_: lambda e: e.matmul(
            ps[6][:, 0:8], cone255[:, 127 - t_:255 - t_], obf[:, :],
            start=(t_ == 0), stop=(t_ == 127)))(tile),
            deps=[ob, cl[1]])
        oh_cons.append(tot_last)
    tev = p.op("vector", lambda e: e.tensor_copy(scr8[:, 0:8], ps[6][:, 0:8]), deps=[tot_last])
    tbf = p.op("vector", lambda e: e.tensor_copy(totbf[:, :], scr8[:, 0:8]), deps=[tev])
    om = p.op("tensor", lambda e: e.matmul(ps[6][:, 0:8], trisb[:, :], totbf[:, :],
                                           start=True, stop=True), deps=[tbf, cl[0], tev])
    oev = p.op("vector", lambda e: e.tensor_copy(scr8[:, 8:16], ps[6][:, 0:8]), deps=[om])
    ow = p.dma("sync", lambda e: e.dma_start(out=offs_d[:, :], in_=scr8[:, 8:16]), deps=[oev],
               stream="ow")

    g_done = {}
    mtc = [[], []]
    orc = [[], []]
    gcons = []
    for tile in range(128):
        sl = tile % 2
        ld = p.dma("sync", (lambda t_, s: lambda e: e.dma_start(
            out=mt[:, s * 4:(s + 1) * 4], in_=meta_all[t_ * 128:(t_ + 1) * 128, :]))(tile, sl),
            deps=mtc[sl], stream="mt%d" % sl)
        orl = p.dma("sync", (lambda t_, s: lambda e: e.dma_start(
            out=orow[0:1, s * 8:(s + 1) * 8], in_=offs_d[t_:t_ + 1, :]))(tile, sl),
            deps=[ow] + orc[sl], stream="or%d" % sl)
        o1 = p.op("vector", (lambda s: lambda e: e.tensor_scalar(
            oh1[:, :], iota8[:, :], mt[:, s * 4:s * 4 + 1], None, ALU.is_equal))(sl),
            deps=[ld] + gcons)
        o2 = p.op("vector", (lambda s: lambda e: e.tensor_scalar(
            oh2[:, :], iota8[:, :], mt[:, s * 4 + 1:s * 4 + 2], None, ALU.is_equal))(sl),
            deps=[ld, o1])
        bcm = p.op("tensor", (lambda s: lambda e: e.matmul(
            ps[5][:, 0:8], onesr[:, :], orow[0:1, s * 8:(s + 1) * 8],
            start=True, stop=True))(sl), deps=[orl, t_onesr] + gcons)
        cg = p.op("vector", (lambda t_: lambda e: e.tensor_add(
            scr8[:, 16:24], cums[:, t_ * 8:(t_ + 1) * 8], ps[5][:, 0:8]))(tile),
            deps=[bcm, cum_evac[tile]] + gcons)
        x1_ = p.op("vector", lambda e: e.tensor_mul(scr8[:, 24:32], scr8[:, 16:24], oh1[:, :]),
                   deps=[cg, o1])
        r1 = p.op("vector", lambda e: e.reduce_sum(g_sb[0:128, 0:1] if False else scr8[:, 0:1],
                                                   scr8[:, 24:32], axis=AX.X), deps=[x1_])
        k1 = p.op("vector", lambda e: e.tensor_scalar(scr8[:, 1:2], scr8[:, 0:1], CAP, None,
                                                      ALU.is_lt), deps=[r1])
        x2_ = p.op("vector", lambda e: e.tensor_mul(scr8[:, 24:32], scr8[:, 16:24], oh2[:, :]),
                   deps=[cg, o2, r1])
        r2 = p.op("vector", lambda e: e.reduce_sum(scr8[:, 2:3], scr8[:, 24:32], axis=AX.X),
                  deps=[x2_])
        k2 = p.op("vector", lambda e: e.tensor_scalar(scr8[:, 3:4], scr8[:, 2:3], CAP, None,
                                                      ALU.is_lt), deps=[r2])
        m1_ = p.op("vector", (lambda s: lambda e: e.tensor_scalar(
            scr8[:, 4:5], mt[:, s * 4:s * 4 + 1], myx[:, 0:1], None, ALU.is_equal))(sl),
            deps=[ld, cl[3], k1])
        m2_ = p.op("vector", (lambda s: lambda e: e.tensor_scalar(
            scr8[:, 5:6], mt[:, s * 4 + 1:s * 4 + 2], myx[:, 0:1], None, ALU.is_equal))(sl),
            deps=[ld, m1_])
        g1 = p.op("vector", (lambda s: lambda e: e.tensor_mul(
            scr8[:, 4:5], scr8[:, 4:5], mt[:, s * 4 + 2:s * 4 + 3]))(sl), deps=[m1_, k1])
        g1b = p.op("vector", lambda e: e.tensor_mul(scr8[:, 4:5], scr8[:, 4:5], scr8[:, 1:2]),
                   deps=[g1])
        g2 = p.op("vector", (lambda s: lambda e: e.tensor_mul(
            scr8[:, 5:6], scr8[:, 5:6], mt[:, s * 4 + 3:s * 4 + 4]))(sl), deps=[m2_, k2])
        g2b = p.op("vector", lambda e: e.tensor_mul(scr8[:, 5:6], scr8[:, 5:6], scr8[:, 3:4]),
                   deps=[g2])
        gf = p.op("vector", (lambda t_: lambda e: e.tensor_add(
            g_sb[:, t_:t_ + 1], scr8[:, 4:5], scr8[:, 5:6]))(tile), deps=[g1b, g2b])
        mtc[sl] = [m1_, m2_, g1, g2]
        orc[sl] = [bcm]
        gcons = [gf]
        g_done[tile] = gf
    gt = p.op("tensor", lambda e: e.transpose(ps[5][:, 0:128], g_sb[:, :], idb[:, :]),
              deps=[il] + list(g_done.values()))
    gte = p.op("vector", lambda e: e.tensor_copy(gT_sb[:, :], ps[5][:, 0:128]), deps=[gt])
    gw = p.dma("sync", lambda e: e.dma_start(out=g_d[:, :], in_=gT_sb[:, :]), deps=[gte],
               stream="gw")

    h_toks = {}
    rhbc = [[], []]
    hstc = [[], []]
    gac = [[], []]
    wcons = []
    for hb in range(4):
        lw1 = p.dma("gpsimd", (lambda b: lambda e: e.dma_start(
            out=w1b[:, :].rearrange("p (c m) -> p c m", c=8),
            in_=w1e[:, b * 1024:(b + 1) * 1024].rearrange("(c p) m -> p c m", p=128)))(hb),
            deps=wcons, stream="w1l")
        lw3 = p.dma("gpsimd", (lambda b: lambda e: e.dma_start(
            out=w3b[:, :].rearrange("p (c m) -> p c m", c=8),
            in_=w3e[:, b * 1024:(b + 1) * 1024].rearrange("(c p) m -> p c m", p=128)))(hb),
            deps=wcons, stream="w3l")
        wcons = []
        for tt in range(32):
            rsl = tt % 2
            t_r = p.dma("sync", (lambda ttt, s: lambda e: e.dma_start(
                out=rhb[:, s * 4096:(s + 1) * 4096].rearrange("p (c m) -> p c m", c=8),
                in_=xg[:, ttt * 512:(ttt + 1) * 512].rearrange(
                    "(c p) m -> p c m", p=128)))(tt, rsl),
                deps=rhbc[rsl], stream="xgr%d" % rsl)
            rhbc[rsl] = []
            for hcs in range(8):
                bnk = hcs % 2
                mma = mmg = None
                for kc in range(8):
                    mma = p.op("tensor", (lambda s_, k, b, r: lambda e: e.matmul(
                        ps[4 + b][:, :],
                        w1b[:, s_ * 1024 + k * 128:s_ * 1024 + (k + 1) * 128] if False else
                        w1b[:, k * 1024 + s_ * 128:k * 1024 + (s_ + 1) * 128],
                        rhb[:, r * 4096 + k * 512:r * 4096 + (k + 1) * 512],
                        start=(k == 0), stop=(k == 7)))(hcs, kc, bnk, rsl),
                        deps=[lw1, t_r] + (gac[bnk] if kc == 0 else []))
                    mmg = p.op("tensor", (lambda s_, k, b, r: lambda e: e.matmul(
                        ps[6 + b][:, :],
                        w3b[:, k * 1024 + s_ * 128:k * 1024 + (s_ + 1) * 128],
                        rhb[:, r * 4096 + k * 512:r * 4096 + (k + 1) * 512],
                        start=(k == 0), stop=(k == 7)))(hcs, kc, bnk, rsl),
                        deps=[lw3, t_r] + (gac[bnk] if kc == 0 else []))
                sl2 = (tt * 8 + hcs) % 2
                t_g = p.op("scalar", (lambda b, s2: lambda e: e.activation(
                    gact[:, s2 * 512:(s2 + 1) * 512], ps[4 + b][:, :], AF.Silu))(bnk, sl2),
                    deps=[mma] + hstc[sl2])
                t_h = p.op("vector", (lambda b, s2: lambda e: e.tensor_mul(
                    hst[:, s2 * 512:(s2 + 1) * 512], gact[:, s2 * 512:(s2 + 1) * 512],
                    ps[6 + b][:, :]))(bnk, sl2), deps=[t_g, mmg])
                gac[bnk] = [t_h]
                rhbc[rsl].append(t_h)
                wcons.append(t_h)
                t_s = p.dma("scalar", (lambda b, h_, ttt, s2: lambda e: e.dma_start(
                    out=h_d[b * 1024 + h_ * 128:b * 1024 + (h_ + 1) * 128,
                            ttt * 512:(ttt + 1) * 512],
                    in_=hst[:, s2 * 512:(s2 + 1) * 512]))(hb, hcs, tt, sl2),
                    deps=[t_h], stream="hs%d" % sl2)
                hstc[sl2] = [t_s]
                h_toks[(hb, hcs, tt)] = t_s

    lw2 = p.dma("gpsimd", lambda e: e.dma_start(
        out=w2sb[:, :].rearrange("p (c m) -> p c m", c=32),
        in_=w2e.rearrange("(c p) m -> p c m", p=128)), stream="w2l")
    hrc = []
    gbc = []
    oc = [[], []]
    prot = [[], []]
    for tt in range(32):
        t_h = p.dma("sync", (lambda ttt: lambda e: e.dma_start(
            out=hrb[:, :].rearrange("p (c m) -> p c m", c=32),
            in_=h_d[:, ttt * 512:(ttt + 1) * 512].rearrange(
                "(c p) m -> p c m", p=128)))(tt),
            deps=hrc + [h_toks[(hb, hcs, tt)] for hb in range(4) for hcs in range(8)],
            stream="hrr")
        hrc = []
        gsl = tt % 2
        grl = p.dma("sync", (lambda ttt, s: lambda e: e.dma_start(
            out=grow[0:1, s * 512:(s + 1) * 512],
            in_=g_d[ttt * 4:(ttt + 1) * 4, :].rearrange("a b -> (a b)").rearrange("(o n) -> o n", o=1)))(tt, gsl),
            deps=[gw] + gbc, stream="grl%d" % gsl)
        bcg = p.op("tensor", (lambda s: lambda e: e.matmul(
            ps[5][:, :], onesr[:, :], grow[0:1, s * 512:(s + 1) * 512],
            start=True, stop=True))(gsl), deps=[grl, t_onesr] + gbc)
        geb = p.op("scalar", (lambda s: lambda e: e.copy(gb[:, s * 512:(s + 1) * 512],
                                                         ps[5][:, :]))(tt % 2),
                   deps=[bcg] + oc[tt % 2])
        gbc = [geb]
        for fc in range(8):
            bnk = fc % 2
            mm = None
            for hc in range(32):
                mm = p.op("tensor", (lambda f, h_, b: lambda e: e.matmul(
                    ps[0 + b][:, :],
                    w2sb[:, h_ * 1024 + f * 128:h_ * 1024 + (f + 1) * 128],
                    hrb[:, h_ * 512:(h_ + 1) * 512],
                    start=(h_ == 0), stop=(h_ == 31)))(fc, hc, bnk),
                    deps=[lw2, t_h] + (prot[bnk] if hc == 0 else []))
            sl2 = (tt * 8 + fc) % 2
            ev = p.op("vector", (lambda b, s, s2: lambda e: e.tensor_mul(
                ost[:, s2 * 512:(s2 + 1) * 512], ps[0 + b][:, :],
                gb[:, s * 512:(s + 1) * 512]))(bnk, tt % 2, sl2),
                deps=[mm, geb] + oc[sl2])
            prot[bnk] = [ev]
            hrc.append(ev)
            if p.io is not None and p.io.get("part_rs_layout"):
                ro = (tt // 4) * 1024 + fc * 128
                co = (tt % 4) * 512
            else:
                ro = fc * 128
                co = tt * 512
            st = p.out_dma("sync", (lambda r_, c_, s2: lambda e: e.dma_start(
                out=part[r_:r_ + 128, c_:c_ + 512],
                in_=ost[:, s2 * 512:(s2 + 1) * 512]))(ro, co, sl2), deps=[ev],
                stream="pst%d" % sl2)
            oc[sl2] = [st, ev]


def build_moe_dense():
    p = Prog("moed")
    emit_moe_dense(p)
    return p



NS = 5120
SWS = ["lngst", "cpg", "rhbg0", "rhbg1"]



def _bc_reg(p, e, val):
    if not hasattr(p, "_bc"):
        p._bc = {}
    if val not in p._bc:
        r = e.alloc_register()
        e.reg_mov(r, val)
        p._bc[val] = r
    return p._bc[val]


def emit_moe_ep_a(p: Prog):
    from concourse.bass_core_rust import IndirectOffsetOnAxis as IOA
    xg_tok = p.io["xg_tok"]
    meta_all = p.io["meta_all"]
    part = p.io["part"]
    w1e = p.get_in("w1e", [D, 4096], F32)
    w3e = p.get_in("w3e", [D, 4096], F32)
    w2e = p.get_in("w2e", [4096, D], F32)
    myexp = p.get_in("myexp", [128, 1], F32)
    ctris = p.get_in("c_tris", [128, 128], BF16)
    ccone255 = p.get_in("c_cone255", [128, 255], BF16)
    ciota8 = p.get_in("c_iota8", [128, 8], F32)
    ciota128 = p.get_in("c_iota128", [128, 128], F32)
    cpcol = p.get_in("c_pcol", [128, 1], F32)
    xm_d = p.dram("xm_d", [NS, 2], F32)
    he_d = p.dram("he_d", [4096, NS], BF16)
    offs_d = p.dram("offs_d", [128, 8], F32)

    cums = p.sb("cums", [128, 8 * 128], F32)
    mt = p.sb("mt", [128, 2 * 4], F32)
    oh1 = p.sb("oh1", [128, 8], F32)
    oh2 = p.sb("oh2", [128, 8], F32)
    obf = p.sb("obf", [128, 8], BF16)
    scr8 = p.sb("scr8", [128, 8 * 4], F32)
    trisb = p.sb("trisb", [128, 128], BF16)
    cone255 = p.sb("cone255", [128, 255], BF16)
    iota8 = p.sb("iota8", [128, 8], F32)
    myx = p.sb("myx", [128, 1], F32)
    pcol = p.sb("pcol", [128, 1], F32)
    totbf = p.sb("totbf", [128, 8], BF16)
    orow = p.sb("orow", [1, 2 * 8], F32)
    onesr = p.sb("onesr", [1, 128], F32)
    gm = p.sb("gm", [128, 4 * 2], F32)
    iix = p.sb("iix", [128, 4], I32)
    zb = p.sb("zb", [128, 1024], BF16)
    zf = p.sb("zf", [128, 80], F32)
    w1b = p.sb("w1b", [128, 8 * 4096], BF16)
    w3b = p.sb("w3b", [128, 8 * 4096], BF16)
    rxe = p.sb("rxe", [128, 4 * 1024], BF16)
    rxf = p.sb("rxf", [128, 4 * 1024], F32)
    rhf = p.sb("rhf", [128, 8 * 512], BF16)
    xms = p.sb("xms", [128, 8], F32)
    iig = p.sb("iig", [128, 4], I32)
    gsc = p.sb("gsc", [128, 8], F32)
    hst = p.sb("hst", [128, 2 * 512], BF16)
    gact = p.sb("gact", [128, 2 * 512], BF16)

    ps = [p.ps(f"ps{i}", [128, 512], F32) for i in range(8)]

    cl = [emit_load2(p, trisb, (0, 128), ctris[:, :], stream="konst"),
          emit_load2(p, cone255, (0, 255), ccone255[:, :], stream="konst"),
          emit_load2(p, iota8, (0, 8), ciota8[:, :], stream="konst"),
          emit_load2(p, myx, (0, 1), myexp[:, :], stream="konst"),
          emit_load2(p, pcol, (0, 1), cpcol[:, :], stream="konst")]
    t_onesr = p.op("vector", lambda e: e.memset(onesr[:, :], 1.0))
    idf32 = p.sb("idf32", [128, 128], F32)
    cl.append(emit_load2(p, idf32, (0, 128), ciota128[:, :], stream="konst"))
    t_idbf = cl[5]

    t_zb = p.op("vector", lambda e: e.memset(zb[:, :], 0.0))
    t_zf = p.op("vector", lambda e: e.memset(zf[:, :], 0.0))
    ZST = ["rr0", "rr1", "ys0", "yst0", "yst1", "cp"]
    zch = {s: [] for s in ZST}
    part_zero = []
    for rb in range(128):
        s = ZST[rb % 6]
        t = p.dma("sync", (lambda r: lambda e: e.dma_start(
            out=part[r * 128:(r + 1) * 128, :], in_=zb[:, :]))(rb),
            deps=[t_zb] + zch[s], stream=s)
        zch[s] = [t]
        part_zero.append(t)
    xm_zero = p.dma("sync", lambda e: e.dma_start(
        out=xm_d[:, :].rearrange("(c p) m -> p c m", p=128), in_=zf[:, :].rearrange(
            "p (c m) -> p c m", c=40)), deps=[t_zf], stream="cp")
    zch["cp"].append(xm_zero)

    md = (p.io or {}).get("meta_dep", []) if p.io else []
    xgd = (p.io or {}).get("xg_dep", []) if p.io else []
    mtc = [[], []]
    cum_evac = {}
    ps_rot = [[], []]
    tot_last = None
    oh_cons = []
    for tile in range(128):
        sl = tile % 2
        ld = p.dma("sync", (lambda t_, s: lambda e: e.dma_start(
            out=mt[:, s * 4:(s + 1) * 4], in_=meta_all[t_ * 128:(t_ + 1) * 128, :]))(tile, sl),
            deps=mtc[sl] + (md if tile < 2 else []), stream="mt%d" % sl)
        o1 = p.op("vector", (lambda s: lambda e: e.tensor_scalar(
            oh1[:, :], iota8[:, :], mt[:, s * 4:s * 4 + 1], None, ALU.is_equal))(sl),
            deps=[ld, cl[2]] + oh_cons)
        o2 = p.op("vector", (lambda s: lambda e: e.tensor_scalar(
            oh2[:, :], iota8[:, :], mt[:, s * 4 + 1:s * 4 + 2], None, ALU.is_equal))(sl),
            deps=[ld, o1])
        ob = p.op("vector", lambda e: e.tensor_add(obf[:, :], oh1[:, :], oh2[:, :]),
                  deps=[o2])
        mtc[sl] = [o1, o2]
        oh_cons = [ob]
        bnk = tile % 2
        mm1 = p.op("tensor", (lambda b: lambda e: e.matmul(
            ps[4 + b][:, 0:8], trisb[:, :], obf[:, :], start=True, stop=True))(bnk),
            deps=[ob, cl[0]] + ps_rot[bnk])
        ev = p.op("vector", (lambda t_, b: lambda e: e.tensor_copy(
            cums[:, t_ * 8:(t_ + 1) * 8], ps[4 + b][:, 0:8]))(tile, bnk), deps=[mm1])
        ps_rot[bnk] = [ev]
        cum_evac[tile] = ev
        tot_last = p.op("tensor", (lambda t_: lambda e: e.matmul(
            ps[6][:, 0:8], cone255[:, 127 - t_:255 - t_], obf[:, :],
            start=(t_ == 0), stop=(t_ == 127)))(tile),
            deps=[ob, cl[1]])
        oh_cons.append(tot_last)
    tev = p.op("vector", lambda e: e.tensor_copy(scr8[:, 0:8], ps[6][:, 0:8]), deps=[tot_last])
    tbf = p.op("vector", lambda e: e.tensor_copy(totbf[:, :], scr8[:, 0:8]), deps=[tev])
    om = p.op("tensor", lambda e: e.matmul(ps[6][:, 0:8], trisb[:, :], totbf[:, :],
                                           start=True, stop=True), deps=[tbf, cl[0], tev])
    oev = p.op("vector", lambda e: e.tensor_copy(scr8[:, 8:16], ps[6][:, 0:8]), deps=[om])
    ow = p.dma("sync", lambda e: e.dma_start(out=offs_d[:, :], in_=scr8[:, 8:16]), deps=[oev],
               stream="ow")

    mid = (p.io or {}).get("mid_cb")
    if mid is not None:
        xgd, seed23, stseed = mid()
    else:
        seed23, stseed = [[], []], []

    mtc = [[], []]
    orc = [[], []]
    gcons = []
    sc_last = {s: [] for s in SWS}
    ps5_free = []
    orl_last = None
    for tile in range(128):
        sl = tile % 2
        sw = SWS[tile % 4]
        ld = p.dma("sync", (lambda t_, s: lambda e: e.dma_start(
            out=mt[:, s * 4:(s + 1) * 4], in_=meta_all[t_ * 128:(t_ + 1) * 128, :]))(tile, sl),
            deps=mtc[sl], stream="mt%d" % sl)
        orl = p.dma("sync", (lambda t_, s: lambda e: e.dma_start(
            out=orow[0:1, s * 8:(s + 1) * 8], in_=offs_d[t_:t_ + 1, :]))(tile, sl),
            deps=[ow] + orc[sl], stream="or%d" % sl)
        orl_last = orl
        o1 = p.op("vector", (lambda s: lambda e: e.tensor_scalar(
            oh1[:, :], iota8[:, :], mt[:, s * 4:s * 4 + 1], None, ALU.is_equal))(sl),
            deps=[ld] + gcons)
        o2 = p.op("vector", (lambda s: lambda e: e.tensor_scalar(
            oh2[:, :], iota8[:, :], mt[:, s * 4 + 1:s * 4 + 2], None, ALU.is_equal))(sl),
            deps=[ld, o1])
        bcm = p.op("tensor", (lambda s: lambda e: e.matmul(
            ps[5][:, 0:8], onesr[:, :], orow[0:1, s * 8:(s + 1) * 8],
            start=True, stop=True))(sl), deps=[orl, t_onesr] + gcons)
        cg = p.op("vector", (lambda t_: lambda e: e.tensor_add(
            scr8[:, 16:24], cums[:, t_ * 8:(t_ + 1) * 8], ps[5][:, 0:8]))(tile),
            deps=[bcm, cum_evac[tile]] + gcons)
        ps5_free = [cg]
        x1_ = p.op("vector", lambda e: e.tensor_mul(scr8[:, 24:32], scr8[:, 16:24], oh1[:, :]),
                   deps=[cg, o1])
        r1 = p.op("vector", lambda e: e.reduce_sum(scr8[:, 0:1], scr8[:, 24:32], axis=AX.X),
                  deps=[x1_])
        k1 = p.op("vector", lambda e: e.tensor_scalar(scr8[:, 1:2], scr8[:, 0:1], CAP, None,
                                                      ALU.is_lt), deps=[r1])
        x2_ = p.op("vector", lambda e: e.tensor_mul(scr8[:, 24:32], scr8[:, 16:24], oh2[:, :]),
                   deps=[cg, o2, r1])
        r2 = p.op("vector", lambda e: e.reduce_sum(scr8[:, 2:3], scr8[:, 24:32], axis=AX.X),
                  deps=[x2_])
        k2 = p.op("vector", lambda e: e.tensor_scalar(scr8[:, 3:4], scr8[:, 2:3], CAP, None,
                                                      ALU.is_lt), deps=[r2])
        m1_ = p.op("vector", (lambda s: lambda e: e.tensor_scalar(
            scr8[:, 4:5], mt[:, s * 4:s * 4 + 1], myx[:, 0:1], None, ALU.is_equal))(sl),
            deps=[ld, cl[3], k1])
        m2_ = p.op("vector", (lambda s: lambda e: e.tensor_scalar(
            scr8[:, 5:6], mt[:, s * 4 + 1:s * 4 + 2], myx[:, 0:1], None, ALU.is_equal))(sl),
            deps=[ld, m1_])
        s1 = p.op("vector", lambda e: e.tensor_mul(scr8[:, 6:7], scr8[:, 4:5], scr8[:, 1:2]),
                  deps=[m1_, k1])
        s2 = p.op("vector", lambda e: e.tensor_mul(scr8[:, 7:8], scr8[:, 5:6], scr8[:, 3:4]),
                  deps=[m2_, k2])
        g1 = p.op("vector", (lambda s: lambda e: e.tensor_mul(
            scr8[:, 4:5], scr8[:, 4:5], mt[:, s * 4 + 2:s * 4 + 3]))(sl), deps=[m1_, k1, s1])
        g1b = p.op("vector", lambda e: e.tensor_mul(scr8[:, 4:5], scr8[:, 4:5], scr8[:, 1:2]),
                   deps=[g1])
        g2 = p.op("vector", (lambda s: lambda e: e.tensor_mul(
            scr8[:, 5:6], scr8[:, 5:6], mt[:, s * 4 + 3:s * 4 + 4]))(sl), deps=[m2_, g1, s2])
        g2b = p.op("vector", lambda e: e.tensor_mul(scr8[:, 5:6], scr8[:, 5:6], scr8[:, 3:4]),
                   deps=[g2])
        sel = p.op("vector", lambda e: e.tensor_add(scr8[:, 8:9], scr8[:, 6:7], scr8[:, 7:8]),
                   deps=[s1, s2])
        v1 = p.op("vector", lambda e: e.tensor_mul(scr8[:, 9:10], scr8[:, 0:1], scr8[:, 6:7]),
                  deps=[s1, r1, g1b])
        v2 = p.op("vector", lambda e: e.tensor_mul(scr8[:, 10:11], scr8[:, 2:3], scr8[:, 7:8]),
                  deps=[s2, r2, g2b])
        pv_ = p.op("vector", lambda e: e.tensor_add(scr8[:, 11:12], scr8[:, 9:10],
                                                    scr8[:, 10:11]), deps=[v1, v2])
        tk = p.op("vector", (lambda t_: lambda e: e.tensor_scalar(
            scr8[:, 14:15], pcol[:, 0:1], float(t_ * 128), None, ALU.add))(tile),
            deps=[cl[4]] + gcons)
        av = p.op("vector", lambda e: e.tensor_scalar(scr8[:, 12:13], scr8[:, 8:9], 0.0, None,
                                                      ALU.is_equal), deps=[sel])
        bv = p.op("vector", lambda e: e.tensor_scalar(scr8[:, 15:16], scr8[:, 14:15], 1.0e6,
                                                      None, ALU.add), deps=[tk])
        ab = p.op("vector", lambda e: e.tensor_mul(scr8[:, 13:14], scr8[:, 12:13],
                                                   scr8[:, 15:16]), deps=[av, bv])
        ixf = p.op("vector", lambda e: e.tensor_add(scr8[:, 9:10], scr8[:, 11:12],
                                                    scr8[:, 13:14]), deps=[pv_, ab])
        sl4 = tile % 4
        cv = p.op("vector", (lambda s: lambda e: e.tensor_copy(
            iix[:, s:s + 1], scr8[:, 9:10]))(sl4), deps=[ixf] + sc_last[sw])
        gmc = p.op("vector", (lambda s: lambda e: e.tensor_add(
            gm[:, 2 * s:2 * s + 1], scr8[:, 4:5], scr8[:, 5:6]))(sl4),
            deps=[g1b, g2b] + sc_last[sw])
        gmt = p.op("vector", (lambda s: lambda e: e.tensor_scalar(
            gm[:, 2 * s + 1:2 * s + 2], scr8[:, 14:15], 1.0, None, ALU.add))(sl4),
            deps=[tk] + sc_last[sw])
        sc = p.dma("gpsimd", (lambda s: lambda e: e.indirect_dma_start(
            out=xm_d[:, :], out_offset=IOA(iix[:, s:s + 1], 0),
            in_=gm[:, 2 * s:2 * s + 2], in_offset=None,
            bounds_check=_bc_reg(p, e, NS - 1), oob_is_err=False))(sl4),
            deps=[cv, gmc, gmt, xm_zero] + sc_last[sw], stream=sw)
        sc_last[sw] = [sc]
        mtc[sl] = [m1_, m2_, g1, g2, o1, o2, tk]
        orc[sl] = [bcm]
        gcons = [cv, gmt, gmc]
    all_sc = [t for s in SWS for t in sc_last[s]]

    lw1 = p.dma("gpsimd", lambda e: e.dma_start(
        out=w1b[:, :].rearrange("p (c m) -> p c m", c=8),
        in_=w1e.rearrange("(c p) m -> p c m", p=128)), stream="w1l")
    lw3 = p.dma("gpsimd", lambda e: e.dma_start(
        out=w3b[:, :].rearrange("p (c m) -> p c m", c=8),
        in_=w3e.rearrange("(c p) m -> p c m", p=128)), stream="w3l")
    rxe_cons = []
    rxf_cons = []
    rhf_cons = []
    xms_cons = []
    gath_last = {s: list(sc_last[s]) for s in SWS}
    trot = [list(seed23[i]) for i in range(2)]
    mrot = [[], []]
    hstc = [list(stseed), []]
    he_toks = []
    for st5 in range(10):
        lm = p.dma("sync", (lambda t_: lambda e: e.dma_start(
            out=xms[:, :].rearrange("p (c m) -> p c m", c=4),
            in_=xm_d[t_ * 512:(t_ + 1) * 512, :].rearrange("(c p) m -> p c m", p=128)))(st5),
            deps=all_sc + xms_cons + [orl_last], stream="or0")
        gths = []
        for j in range(4):
            sw = SWS[j]
            v = p.op("vector", (lambda jj: lambda e: e.tensor_scalar(
                gsc[:, jj:jj + 1], xms[:, 2 * jj + 1:2 * jj + 2], 0.0, None,
                ALU.is_equal))(j), deps=[lm] + xms_cons)
            m_ = p.op("vector", (lambda jj: lambda e: e.tensor_scalar(
                gsc[:, 4 + jj:5 + jj], gsc[:, jj:jj + 1], 1000001.0, None,
                ALU.mult))(j), deps=[v])
            a_ = p.op("vector", (lambda jj: lambda e: e.tensor_add(
                gsc[:, 4 + jj:5 + jj], gsc[:, 4 + jj:5 + jj],
                xms[:, 2 * jj + 1:2 * jj + 2]))(j), deps=[m_])
            a2 = p.op("vector", (lambda jj: lambda e: e.tensor_scalar(
                gsc[:, 4 + jj:5 + jj], gsc[:, 4 + jj:5 + jj], -1.0, None,
                ALU.add))(j), deps=[a_])
            cvg = p.op("vector", (lambda jj: lambda e: e.tensor_copy(
                iig[:, jj:jj + 1], gsc[:, 4 + jj:5 + jj]))(j),
                deps=[a2] + rxe_cons)
            gt_ = p.dma("gpsimd", (lambda jj: lambda e: e.indirect_dma_start(
                out=rxe[:, jj * 1024:(jj + 1) * 1024], out_offset=None,
                in_=xg_tok[:, :], in_offset=IOA(iig[:, jj:jj + 1], 0),
                bounds_check=_bc_reg(p, e, TG - 1), oob_is_err=False))(j),
                deps=[cvg] + gath_last[sw] + rxe_cons +
                     (xgd if st5 == 0 else []), stream=sw)
            gath_last[sw] = [gt_]
            gths.append(gt_)
        xms_cons = gths
        cjs = []
        for j in range(4):
            cj = p.op("vector", (lambda jj: lambda e: e.tensor_copy(
                rxf[:, jj * 1024:(jj + 1) * 1024],
                rxe[:, jj * 1024:(jj + 1) * 1024]))(j),
                deps=[gths[j]] + rxf_cons)
            cjs.append(cj)
        rxf_cons = []
        evs = {}
        for j in range(4):
            for kc in range(8):
                b2 = (j * 8 + kc) % 2
                tr = p.op("tensor", (lambda jj, k, b: lambda e: e.transpose(
                    ps[2 + b][:, 0:128], rxf[:, jj * 1024 + k * 128:jj * 1024 + (k + 1) * 128],
                    idf32[:, :]))(j, kc, b2),
                    deps=[cjs[j], t_idbf] + trot[b2])
                ev = p.op("scalar", (lambda jj, k, b: lambda e: e.copy(
                    rhf[:, k * 512 + jj * 128:k * 512 + (jj + 1) * 128],
                    ps[2 + b][:, 0:128]))(j, kc, b2), deps=[tr] + rhf_cons)
                trot[b2] = [ev]
                evs[(j, kc)] = ev
        rhf_cons = []
        rxe_cons = list(cjs)
        rxf_cons = [evs[(j, kc)] for j in range(4) for kc in range(8)]
        ev_all = list(rxf_cons)
        mm_last = None
        for hc in range(32):
            bnk = hc % 2
            mma = mmg = None
            for kc in range(8):
                mma = p.op("tensor", (lambda h_, k, b: lambda e: e.matmul(
                    ps[4 + b][:, :], w1b[:, k * 4096 + h_ * 128:k * 4096 + (h_ + 1) * 128],
                    rhf[:, k * 512:(k + 1) * 512],
                    start=(k == 0), stop=(k == 7)))(hc, kc, bnk),
                    deps=[lw1] + ev_all + (mrot[bnk] if kc == 0 else []) +
                         ([ps_rot[bnk][0]] if (st5 == 0 and hc < 2 and ps_rot[bnk]) else []) +
                         (ps5_free if (st5 == 0 and hc == 1 and kc == 0) else []))
                mmg = p.op("tensor", (lambda h_, k, b: lambda e: e.matmul(
                    ps[6 + b][:, :], w3b[:, k * 4096 + h_ * 128:k * 4096 + (h_ + 1) * 128],
                    rhf[:, k * 512:(k + 1) * 512],
                    start=(k == 0), stop=(k == 7)))(hc, kc, bnk),
                    deps=[lw3] + ev_all + (mrot[bnk] if kc == 0 else []) +
                         ([oev] if (st5 == 0 and hc < 2) else []))
            ev_all = []
            sl2 = hc % 2
            t_g = p.op("scalar", (lambda b, s2: lambda e: e.activation(
                gact[:, s2 * 512:(s2 + 1) * 512], ps[4 + b][:, :], AF.Silu))(bnk, sl2),
                deps=[mma] + hstc[sl2])
            t_h = p.op("vector", (lambda b, s2: lambda e: e.tensor_mul(
                hst[:, s2 * 512:(s2 + 1) * 512], gact[:, s2 * 512:(s2 + 1) * 512],
                ps[6 + b][:, :]))(bnk, sl2), deps=[t_g, mmg])
            mrot[bnk] = [t_h]
            mm_last = mma
            rhf_cons.append(t_h)
            t_s = p.dma("scalar", (lambda h_, t_, s2: lambda e: e.dma_start(
                out=he_d[h_ * 128:(h_ + 1) * 128, t_ * 512:(t_ + 1) * 512],
                in_=hst[:, s2 * 512:(s2 + 1) * 512]))(hc, st5, sl2),
                deps=[t_h], stream="hs%d" % sl2)
            hstc[sl2] = [t_s]
            he_toks.append(t_s)
        rhf_cons = rhf_cons[-2:]
    return dict(he_d=he_d, xm_d=xm_d, w2e=w2e, he_toks=he_toks,
                part_zero=part_zero, all_sc=all_sc)


def emit_moe_ep_b(p: Prog, H):
    from concourse.bass_core_rust import IndirectOffsetOnAxis as IOA
    he_d, xm_d, w2e = H["he_d"], H["xm_d"], H["w2e"]
    part = p.io["part"]
    w2sb = p.sb("w2sb", [128, 32 * 1024], BF16)
    hle = p.sb("hle", [128, 2 * 4096], BF16)
    yet = p.sb("yet", [128, 2 * 1024], BF16)
    xmt = p.sb("xmt", [128, 2 * 2], F32)
    iib = p.sb("iib", [128, 4], I32)
    scb = p.sb("scb", [128, 8], F32)
    ps = [p.ps(f"ps{i}", [128, 512], F32) for i in range(8)]

    lw2 = p.dma("gpsimd", lambda e: e.dma_start(
        out=w2sb[:, :].rearrange("p (c m) -> p c m", c=32),
        in_=w2e.rearrange("(c p) m -> p c m", p=128)), stream="w2l")
    hle_cons = [[], []]
    xmt_cons = [[], []]
    yet_cons = [[], []]
    sc_last = {s: [] for s in SWS}
    prot = [[], []]
    for st in range(40):
        sl = st % 2
        sw = SWS[st % 4]
        ld_h = p.dma("sync", (lambda t_, s: lambda e: e.dma_start(
            out=hle[:, s * 4096:(s + 1) * 4096].rearrange("p (c m) -> p c m", c=32),
            in_=he_d[:, t_ * 128:(t_ + 1) * 128].rearrange("(c p) m -> p c m", p=128)))(st, sl),
            deps=hle_cons[sl] + (H["he_toks"] if st < 2 else []),
            stream=["hrr", "xgr1"][sl])
        ld_m = p.dma("sync", (lambda t_, s: lambda e: e.dma_start(
            out=xmt[:, 2 * s:2 * s + 2], in_=xm_d[t_ * 128:(t_ + 1) * 128, :]))(st, sl),
            deps=xmt_cons[sl] + (H["all_sc"] if st < 2 else []),
            stream=["or0", "or1"][sl])
        v = p.op("vector", (lambda s: lambda e: e.tensor_scalar(
            scb[:, s:s + 1], xmt[:, 2 * s + 1:2 * s + 2], 0.0, None, ALU.is_equal))(sl),
            deps=[ld_m])
        m_ = p.op("vector", (lambda s: lambda e: e.tensor_scalar(
            scb[:, 2 + s:3 + s], scb[:, s:s + 1], 1000001.0, None, ALU.mult))(sl),
            deps=[v])
        a_ = p.op("vector", (lambda s: lambda e: e.tensor_add(
            scb[:, 2 + s:3 + s], scb[:, 2 + s:3 + s], xmt[:, 2 * s + 1:2 * s + 2]))(sl),
            deps=[m_])
        a2 = p.op("vector", (lambda s: lambda e: e.tensor_scalar(
            scb[:, 2 + s:3 + s], scb[:, 2 + s:3 + s], -1.0, None, ALU.add))(sl),
            deps=[a_])
        sl4 = st % 4
        cv = p.op("vector", (lambda s, s4: lambda e: e.tensor_copy(
            iib[:, s4:s4 + 1], scb[:, 2 + s:3 + s]))(sl, sl4), deps=[a2] + sc_last[sw])
        evs = []
        for dc in range(2):
            mm = None
            for hc in range(32):
                mm = p.op("tensor", (lambda d_, h_, s: lambda e: e.matmul(
                    ps[d_][:, :], hle[:, s * 4096 + h_ * 128:s * 4096 + (h_ + 1) * 128],
                    w2sb[:, h_ * 1024 + d_ * 512:h_ * 1024 + d_ * 512 + 512],
                    start=(h_ == 0), stop=(h_ == 31)))(dc, hc, sl),
                    deps=[lw2, ld_h] + (prot[dc] if hc == 0 else []))
            ev = p.op("vector", (lambda d_, s: lambda e: e.tensor_scalar(
                yet[:, s * 1024 + d_ * 512:s * 1024 + (d_ + 1) * 512], ps[d_][:, :],
                xmt[:, 2 * s:2 * s + 1], None, ALU.mult))(dc, sl),
                deps=[mm, ld_m] + yet_cons[sl])
            prot[dc] = [ev]
            evs.append(ev)
        hle_cons[sl] = [mm]
        sc = p.dma("gpsimd", (lambda s, s4: lambda e: e.indirect_dma_start(
            out=part[:, :], out_offset=IOA(iib[:, s4:s4 + 1], 0),
            in_=yet[:, s * 1024:(s + 1) * 1024], in_offset=None,
            bounds_check=_bc_reg(p, e, TG - 1), oob_is_err=False))(sl, sl4),
            deps=evs + [cv] + (H["part_zero"] if st < 4 else []) + sc_last[sw],
            stream=sw)
        sc_last[sw] = [sc]
        yet_cons[sl] = [sc]
        xmt_cons[sl] = [cv] + evs
    return [t for s in SWS for t in sc_last[s]]



def emit_moe_reduce(p: Prog):
    x3own = p.get_in("x3own", [D, T], F32)
    parts = [p.get_in("part%d" % j, [D, T], BF16) for j in range(8)]
    x4_fm = p.get_out("x4_fm", [D, T], F32)
    xs = p.sb("xs", [128, 2 * 512], F32)
    pb = p.sb("pb", [128, 2 * 512], BF16)
    acc = p.sb("acc", [128, 2 * 512], F32)
    xc = [[], []]
    pc = [[], []]
    ac = [[], []]
    for kc in range(8):
        for tt in range(4):
            sl = (kc * 4 + tt) % 2
            t_x = p.dma("sync", (lambda k, t_, s: lambda e: e.dma_start(
                out=xs[:, s * 512:(s + 1) * 512],
                in_=x3own[k * 128:(k + 1) * 128, t_ * 512:(t_ + 1) * 512]))(kc, tt, sl),
                deps=xc[sl], stream="x%d" % sl)
            last = None
            for j in range(8):
                t_p = p.dma("gpsimd", (lambda jj, k, t_, s: lambda e: e.dma_start(
                    out=pb[:, s * 512:(s + 1) * 512],
                    in_=parts[jj][k * 128:(k + 1) * 128, t_ * 512:(t_ + 1) * 512]))(j, kc, tt, sl),
                    deps=pc[sl], stream="p%d" % sl)
                if j == 0:
                    a = p.op("vector", (lambda s: lambda e: e.tensor_add(
                        acc[:, s * 512:(s + 1) * 512], xs[:, s * 512:(s + 1) * 512],
                        pb[:, s * 512:(s + 1) * 512]))(sl), deps=[t_x, t_p] + ac[sl])
                else:
                    a = p.op("vector", (lambda s: lambda e: e.tensor_add(
                        acc[:, s * 512:(s + 1) * 512], acc[:, s * 512:(s + 1) * 512],
                        pb[:, s * 512:(s + 1) * 512]))(sl), deps=[t_p, last])
                pc[sl] = [a]
                last = a
            xc[sl] = [last]
            st = p.out_dma("sync", (lambda k, t_, s: lambda e: e.dma_start(
                out=x4_fm[k * 128:(k + 1) * 128, t_ * 512:(t_ + 1) * 512],
                in_=acc[:, s * 512:(s + 1) * 512]))(kc, tt, sl), deps=[last], stream="st%d" % sl)
            ac[sl] = [st]


def build_moe_reduce():
    p = Prog("moer")
    emit_moe_reduce(p)
    return p


def prep_moe_dense_inputs(inputs, xln4_all, meta_all):
    tris = np_bf((np.arange(128)[:, None] < np.arange(128)[None, :]).astype(np.float32))
    cone255 = np.zeros((128, 255), dtype=np.float32)
    cone255[:, 127] = 1.0
    iota8 = np.broadcast_to(np.arange(8, dtype=np.float32)[None, :], (128, 8)).copy()
    iota128 = np.eye(128, dtype=np.float32)
    in_maps = []
    for c in range(NCORE):
        in_maps.append(dict(
            xg=xln4_all, meta_all=meta_all,
            w1e=inputs["L1_ffn_W1"][c], w3e=inputs["L1_ffn_W3"][c],
            w2e=inputs["L1_ffn_W2"][c],
            myexp=np.full((128, 1), float(c), dtype=np.float32),
            c_tris=tris, c_cone255=np_bf(cone255), c_iota8=iota8,
            c_iota128=iota128,
        ))
    return in_maps


_BUILT = {}


def _get(name, builder):
    if name not in _BUILT:
        _BUILT[name] = builder().build()
    return _BUILT[name]


def kernel(**inputs):
    inputs = {k: np.asarray(v) for k, v in inputs.items()}
    if KERNEL_MODE == "fused":
        return kernel_fused(inputs)
    cores = list(range(NCORE))

    nc1 = _get("l1", build_l0_attn)
    r1 = run_bass_kernel_spmd(nc1, prep_l0_attn_inputs(inputs), core_ids=cores)
    x1 = np.empty((4, NSEQ, D), dtype=np.float32)
    for c in cores:
        b, half = c // 2, c % 2
        x1[b, half * T:(half + 1) * T, :] = r1.results[c]["x1_fm"].T

    nc2 = _get("l2", build_l0_ffn)
    r2 = run_bass_kernel_spmd(nc2, prep_l0_ffn_inputs(inputs, x1), core_ids=cores)
    x2 = np.empty((4, NSEQ, D), dtype=np.float32)
    for c in cores:
        b, half = c // 2, c % 2
        x2[b, half * T:(half + 1) * T, :] = r2.results[c]["x2_fm"].T

    nc3 = _get("l3", build_l1_attn)
    r3 = run_bass_kernel_spmd(nc3, prep_l1_attn_inputs(inputs, x2), core_ids=cores)
    o_outs = [r3.results[c]["o_out"] for c in cores]

    nc4a = _get("l4a", build_l1_proj_router)
    r4a = run_bass_kernel_spmd(nc4a, prep_l1_proj_router_inputs(inputs, x2, o_outs),
                               core_ids=cores)
    x3 = {c: r4a.results[c]["x3_fm"] for c in cores}
    xln4_all = np.concatenate([r4a.results[c]["xln4"] for c in cores], axis=1)
    meta_all = np.concatenate([r4a.results[c]["meta"] for c in cores], axis=0)

    nc4b = _get("l4b", build_moe_dense)
    r4b = run_bass_kernel_spmd(nc4b, prep_moe_dense_inputs(inputs, xln4_all, meta_all),
                               core_ids=cores)
    parts = [r4b.results[c]["part"] for c in cores]

    nc4c = _get("l4c", build_moe_reduce)
    in4c = []
    for c in cores:
        m = dict(x3own=x3[c])
        for j in range(8):
            m["part%d" % j] = np.ascontiguousarray(parts[j][:, c * T:(c + 1) * T])
        in4c.append(m)
    r4c = run_bass_kernel_spmd(nc4c, in4c, core_ids=cores)
    out = np.empty((4, NSEQ, D), dtype=np.float32)
    for c in cores:
        b, half = c // 2, c % 2
        out[b, half * T:(half + 1) * T, :] = r4c.results[c]["x4_fm"].T
    return out


PAIRS_RG = [[0, 1], [2, 3], [4, 5], [6, 7]]
FULL_RG = [list(range(8))]


def _copy_dram(p, dst_ap, src_ap, stream, deps=(), eng="gpsimd"):
    return p.dma(eng, lambda e: e.dma_start(out=dst_ap, in_=src_ap),
                 deps=list(deps), stream=stream)


TRUNC = None
TRUNC2 = None
SKIP_B_V = False
SKIP_B_QK = False
DDS = 16384


def _trunc_out(p):
    x4 = p.outp("x4_fm", [D, T], F32)
    z = p.sb("tz", [128, 512], F32)
    tm = p.op("vector", lambda e: e.memset(z[:, :], 0.0))
    for kc in range(8):
        for tt in range(4):
            p.out_dma("sync", (lambda k, t_: lambda e: e.dma_start(
                out=x4[k * 128:(k + 1) * 128, t_ * 512:(t_ + 1) * 512],
                in_=z[:, :]))(kc, tt), deps=[tm], stream="fs0")


def emit_fused(p: Prog):
    x1_d = p.dram("x1_d", [D, T], F32)
    p.io = {"x1_fm": x1_d}
    emit_l0_attn(p)
    p.new_stage()
    if TRUNC == 0:
        _trunc_out(p)
        return

    hb_in = p.dram("hb_in", [D, 1536], BF16)
    hb_out = p.dram("hb_out", [2 * D, 1536], BF16)
    x1h_d = p.dram("x1h_d", [D, W2], F32)
    hm = p.inp("halo_mask", [128, 1], F32)
    t1 = _copy_dram(p, hb_in[:, :], x1_d[:, 512:2048], "cp1")
    tc1 = p.coll(lambda e: e.collective_compute(
        "AllGather", mybir.AluOpType.bypass, replica_groups=PAIRS_RG,
        ins=[hb_in[:, :].opt()], outs=[hb_out[:, :].opt()]), deps=[t1], stream="cc1")
    mb = p.sb("mb", [128, 2 * 512], BF16)
    mf = p.sb("mf", [128, 2 * 512], F32)
    hm_sb = p.sb("hm_sb", [128, 1], F32)
    tl = emit_load2(p, hm_sb, (0, 1), hm[:, :], stream="konst")
    mcons = [[], []]
    fcons = [[], []]
    for kc in range(8):
        for tt in range(3):
            sl = (kc * 3 + tt) % 2
            ld = p.dma("sync", (lambda k, t_, s: lambda e: e.dma_start(
                out=mb[:, s * 512:(s + 1) * 512],
                in_=hb_out[k * 128:(k + 1) * 128, t_ * 512:(t_ + 1) * 512]))(kc, tt, sl),
                deps=[tc1] + mcons[sl], stream="hl%d" % sl)
            op = p.op("vector", (lambda s: lambda e: e.tensor_scalar_mul(
                mf[:, s * 512:(s + 1) * 512], mb[:, s * 512:(s + 1) * 512],
                hm_sb[:, 0:1]))(sl), deps=[ld, tl] + fcons[sl])
            mcons[sl] = [op]
            st = p.dma("sync", (lambda k, t_, s: lambda e: e.dma_start(
                out=x1h_d[k * 128:(k + 1) * 128, t_ * 512:(t_ + 1) * 512],
                in_=mf[:, s * 512:(s + 1) * 512]))(kc, tt, sl), deps=[op],
                stream="hs%d" % sl)
            fcons[sl] = [st]
    tow = _copy_dram(p, x1h_d[:, 1536:3584], x1_d[:, :], "cp2", eng="sync")
    p.new_stage()
    if TRUNC == 1:
        _trunc_out(p)
        return

    x2_d = p.dram("x2_d", [D, T], F32)
    p.io = {"x1h": x1h_d, "x2_fm": x2_d}
    emit_l0_ffn(p)
    p.new_stage()
    if TRUNC == 2:
        _trunc_out(p)
        return

    xf_in = p.dram("xf_in", [D, T], F32)
    xf_out = p.dram("xf_out", [2 * D, T], F32)
    xfull_d = p.dram("xfull_d", [D, W3], F32)
    t2 = _copy_dram(p, xf_in[:, :], x2_d[:, :], "cp3", eng="sync")
    tc2 = p.coll(lambda e: e.collective_compute(
        "AllGather", mybir.AluOpType.bypass, replica_groups=PAIRS_RG,
        ins=[xf_in[:, :].opt()], outs=[xf_out[:, :].opt()]), deps=[t2], stream="cc2")
    t3 = _copy_dram(p, xfull_d[:, 0:T], xf_out[0:D, :], "cp4", deps=[tc2], eng="sync")
    t4 = _copy_dram(p, xfull_d[:, T:2 * T], xf_out[D:2 * D, :], "cp4", deps=[tc2], eng="sync")
    p.new_stage()
    if TRUNC == 3:
        _trunc_out(p)
        return

    o_out_d = p.dram("o_out_d", [512, W3], BF16)
    p.io = {"xfull": xfull_d, "o_out": o_out_d}
    emit_l1_attn(p)
    p.new_stage()
    if TRUNC == 4:
        _trunc_out(p)
        return

    ag_o = p.dram("ag_o", [2 * 512, W3], BF16)
    o_mine_d = p.dram("o_mine_d", [D, T], BF16)
    wm0 = p.inp("win_m0", [128, 1], F32)
    wm1 = p.inp("win_m1", [128, 1], F32)
    tc3 = p.coll(lambda e: e.collective_compute(
        "AllGather", mybir.AluOpType.bypass, replica_groups=PAIRS_RG,
        ins=[o_out_d[:, :].opt()], outs=[ag_o[:, :].opt()]), deps=[], stream="cc3")
    wa_sb = p.sb("wa_sb", [128, 2], F32)
    twm = emit_load2(p, wa_sb, (0, 1), wm0[:, :], stream="konst")
    twm2 = emit_load2(p, wa_sb, (1, 2), wm1[:, :], stream="konst")
    ma = p.sb("ma", [128, 2 * 512], BF16)
    mb2 = p.sb("mb2", [128, 2 * 512], BF16)
    mo = p.sb("mo", [128, 2 * 512], BF16)
    mac = [[], []]
    moc = [[], []]
    for kc in range(8):
        for tt in range(4):
            sl = (kc * 4 + tt) % 2
            la = p.dma("sync", (lambda k, t_, s: lambda e: e.dma_start(
                out=ma[:, s * 512:(s + 1) * 512],
                in_=ag_o[k * 128:(k + 1) * 128, t_ * 512:(t_ + 1) * 512]))(kc, tt, sl),
                deps=[tc3] + mac[sl], stream="oa%d" % sl)
            lb = p.dma("sync", (lambda k, t_, s: lambda e: e.dma_start(
                out=mb2[:, s * 512:(s + 1) * 512],
                in_=ag_o[k * 128:(k + 1) * 128, T + t_ * 512:T + (t_ + 1) * 512]))(kc, tt, sl),
                deps=[tc3] + mac[sl], stream="ob%d" % sl)
            o1 = p.op("vector", (lambda s: lambda e: e.tensor_scalar_mul(
                mo[:, s * 512:(s + 1) * 512], ma[:, s * 512:(s + 1) * 512],
                wa_sb[:, 0:1]))(sl), deps=[la, twm] + moc[sl])
            o2 = p.op("vector", (lambda s: lambda e: e.scalar_tensor_tensor(
                mo[:, s * 512:(s + 1) * 512], mb2[:, s * 512:(s + 1) * 512],
                wa_sb[:, 1:2], mo[:, s * 512:(s + 1) * 512],
                ALU.mult, ALU.add))(sl), deps=[o1, lb, twm2])
            mac[sl] = [o2]
            st = p.dma("sync", (lambda k, t_, s: lambda e: e.dma_start(
                out=o_mine_d[k * 128:(k + 1) * 128, t_ * 512:(t_ + 1) * 512],
                in_=mo[:, s * 512:(s + 1) * 512]))(kc, tt, sl), deps=[o2],
                stream="os%d" % sl)
            moc[sl] = [st]
    p.new_stage()
    if TRUNC == 5:
        _trunc_out(p)
        return

    x3_keep = p.dram("x3_keep", [D, T], F32)
    xln4_d = p.dram("xln4_d", [D, T], BF16)
    meta_d = p.dram("meta_d", [T, 4], F32)
    p.io = {"x2own": x2_d, "o_mine": o_mine_d, "x3_fm": x3_keep,
            "xln4": xln4_d, "meta": meta_d}
    emit_l1_proj_router(p)
    p.new_stage()
    if TRUNC == 6:
        _trunc_out(p)
        return

    meta_ag = p.dram("meta_ag", [TG, 4], F32)
    tc5 = p.coll(lambda e: e.collective_compute(
        "AllGather", mybir.AluOpType.bypass, replica_groups=FULL_RG,
        ins=[meta_d[:, :].opt()], outs=[meta_ag[:, :].opt()]), deps=[], stream="cc5")
    if MOE_MODE == "dense":
        xg_ag = p.dram("xg_ag", [8 * D, T], BF16)
        xg_d = p.dram("xg_d", [D, TG], BF16)
        tc4 = p.coll(lambda e: e.collective_compute(
            "AllGather", mybir.AluOpType.bypass, replica_groups=FULL_RG,
            ins=[xln4_d[:, :].opt()], outs=[xg_ag[:, :].opt()]), deps=[], stream="cc4")
        xgc = []
        for r in range(8):
            xgc.append(_copy_dram(p, xg_d[:, r * T:(r + 1) * T],
                                  xg_ag[r * D:(r + 1) * D, :], "cp5", deps=[tc4], eng="sync"))
    else:
        xt_tok = p.dram("xt_tok", [T, D], BF16)
        xg_tok = p.dram("xg_tok", [TG, D], BF16)
        idbf_in = p.inp("c_idbf", [128, 128], F32)

        def mid_cb():
            idb7 = p.sb("idb7", [128, 128], F32)
            tid = emit_load2(p, idb7, (0, 128), idbf_in[:, :], stream="konst")
            lxb = p.sb("lxb", [128, 2 * 128], BF16)
            lxf = p.sb("lxf", [128, 2 * 128], F32)
            xtb = p.sb("xtb", [128, 8 * 128], BF16)
            ps7t = [p.ps(f"ps{i}", [128, 512], F32) for i in range(8)]
            lx_cons = [[], []]
            tb_rot = [[], []]
            st_prev = []
            stv_all = []
            for tb in range(16):
                evs7 = []
                for kc in range(8):
                    sl = (tb * 8 + kc) % 2
                    ld = p.dma("sync", (lambda k, t_, s: lambda e: e.dma_start(
                        out=lxb[:, s * 128:(s + 1) * 128],
                        in_=xln4_d[k * 128:(k + 1) * 128, t_ * 128:(t_ + 1) * 128]))(kc, tb, sl),
                        deps=lx_cons[sl], stream="oa%d" % sl)
                    cpf = p.op("vector", (lambda s: lambda e: e.tensor_copy(
                        lxf[:, s * 128:(s + 1) * 128], lxb[:, s * 128:(s + 1) * 128]))(sl),
                        deps=[ld] + tb_rot[sl])
                    tr = p.op("tensor", (lambda s, b: lambda e: e.transpose(
                        ps7t[2 + b][:, 0:128], lxf[:, s * 128:(s + 1) * 128],
                        idb7[:, :]))(sl, sl), deps=[cpf, tid] + tb_rot[sl])
                    ev = p.op("scalar", (lambda k, b: lambda e: e.copy(
                        xtb[:, k * 128:(k + 1) * 128], ps7t[2 + b][:, 0:128]))(kc, sl),
                        deps=[tr] + st_prev)
                    lx_cons[sl] = [cpf]
                    tb_rot[sl] = [ev]
                    evs7.append(ev)
                st_prev = []
                stv = p.dma("sync", (lambda t_: lambda e: e.dma_start(
                    out=xt_tok[t_ * 128:(t_ + 1) * 128, :], in_=xtb[:, :]))(tb),
                    deps=evs7, stream="os0")
                st_prev = [stv]
                stv_all.append(stv)
            tc4 = p.coll(lambda e: e.collective_compute(
                "AllGather", mybir.AluOpType.bypass, replica_groups=FULL_RG,
                ins=[xt_tok[:, :].opt()], outs=[xg_tok[:, :].opt()]), deps=stv_all,
                stream="cc4", sem="cc2")
            return [tc4], tb_rot, st_prev
    if MOE_MODE == "dense":
        p.new_stage()
        if TRUNC == 7:
            _trunc_out(p)
            return
    if MOE_MODE == "dense":
        part_rs = p.dram("part_rs", [8 * D, T], BF16)
        p.io = {"xg": xg_d, "meta_all": meta_ag, "part": part_rs,
                "part_rs_layout": True}
        emit_moe_dense(p)
        p.new_stage()
        if TRUNC == 8:
            _trunc_out(p)
            return

        rs_out = p.dram("rs_out", [D, T], BF16)
        tc6 = p.coll(lambda e: e.collective_compute(
            "ReduceScatter", mybir.AluOpType.add, replica_groups=FULL_RG,
            ins=[part_rs[:, :].opt()], outs=[rs_out[:, :].opt()]), deps=[], stream="cc6")
        x4 = p.outp("x4_fm", [D, T], F32)
        fx = p.sb("fx", [128, 2 * 512], F32)
        fp = p.sb("fp", [128, 2 * 512], BF16)
        fo = p.sb("fo", [128, 2 * 512], F32)
        xc = [[], []]
        pc = [[], []]
        oc = [[], []]
        for kc in range(8):
            for tt in range(4):
                sl = (kc * 4 + tt) % 2
                t_x = p.dma("sync", (lambda k, t_, s: lambda e: e.dma_start(
                    out=fx[:, s * 512:(s + 1) * 512],
                    in_=x3_keep[k * 128:(k + 1) * 128, t_ * 512:(t_ + 1) * 512]))(kc, tt, sl),
                    deps=xc[sl], stream="fx%d" % sl)
                t_p = p.dma("sync", (lambda k, t_, s: lambda e: e.dma_start(
                    out=fp[:, s * 512:(s + 1) * 512],
                    in_=rs_out[k * 128:(k + 1) * 128, t_ * 512:(t_ + 1) * 512]))(kc, tt, sl),
                    deps=[tc6] + pc[sl], stream="fp%d" % sl)
                a = p.op("vector", (lambda s: lambda e: e.tensor_add(
                    fo[:, s * 512:(s + 1) * 512], fx[:, s * 512:(s + 1) * 512],
                    fp[:, s * 512:(s + 1) * 512]))(sl), deps=[t_x, t_p] + oc[sl])
                xc[sl] = [a]
                pc[sl] = [a]
                st = p.out_dma("sync", (lambda k, t_, s: lambda e: e.dma_start(
                    out=x4[k * 128:(k + 1) * 128, t_ * 512:(t_ + 1) * 512],
                    in_=fo[:, s * 512:(s + 1) * 512]))(kc, tt, sl), deps=[a],
                    stream="fs%d" % sl)
                oc[sl] = [st]
        return

    p.soft_stage()
    part_tok = p.dram("part_tok", [TG, D], BF16)
    p.io = {"xg_tok": xg_tok, "meta_all": meta_ag, "part": part_tok,
            "meta_dep": [tc5], "mid_cb": mid_cb}
    H = emit_moe_ep_a(p)
    p.new_stage()
    if TRUNC == 8:
        _trunc_out(p)
        return

    ep_sc = emit_moe_ep_b(p, H)
    p.new_stage()

    rs_tok = p.dram("rs_tok", [T, D], BF16)
    tc6 = p.coll(lambda e: e.collective_compute(
        "ReduceScatter", mybir.AluOpType.add, replica_groups=FULL_RG,
        ins=[part_tok[:, :].opt()], outs=[rs_tok[:, :].opt()]), deps=[], stream="cc6")
    x4 = p.outp("x4_fm", [D, T], F32)
    idb9 = p.sb("idb9", [128, 128], F32)
    tid9 = emit_load2(p, idb9, (0, 128), idbf_in[:, :], stream="konst")
    fx = p.sb("fx", [128, 2 * 512], F32)
    rsb = p.sb("rsb", [128, 2 * 128], BF16)
    rsf = p.sb("rsf", [128, 2 * 128], F32)
    fo = p.sb("fo", [128, 2 * 512], F32)
    ps9 = [p.ps(f"ps{i}", [128, 512], F32) for i in range(8)]
    xc = [[], []]
    rc = [[], []]
    oc = [[], []]
    tb_rot9 = [[], []]
    for kc in range(8):
        for tt in range(4):
            sl = (kc * 4 + tt) % 2
            t_x = p.dma("sync", (lambda k, t_, s: lambda e: e.dma_start(
                out=fx[:, s * 512:(s + 1) * 512],
                in_=x3_keep[k * 128:(k + 1) * 128, t_ * 512:(t_ + 1) * 512]))(kc, tt, sl),
                deps=xc[sl], stream="fx%d" % sl)
            adds = []
            for j in range(4):
                sj = (tt * 4 + j) % 2
                lr = p.dma("sync", (lambda k, t_, jj, s: lambda e: e.dma_start(
                    out=rsb[:, s * 128:(s + 1) * 128],
                    in_=rs_tok[t_ * 512 + jj * 128:t_ * 512 + (jj + 1) * 128,
                               k * 128:(k + 1) * 128]))(kc, tt, j, sj),
                    deps=[tc6] + rc[sj], stream="fp%d" % sj)
                cpf = p.op("vector", (lambda s: lambda e: e.tensor_copy(
                    rsf[:, s * 128:(s + 1) * 128], rsb[:, s * 128:(s + 1) * 128]))(sj),
                    deps=[lr] + tb_rot9[sj])
                tr = p.op("tensor", (lambda s, b: lambda e: e.transpose(
                    ps9[2 + b][:, 0:128], rsf[:, s * 128:(s + 1) * 128],
                    idb9[:, :]))(sj, sj), deps=[cpf, tid9] + tb_rot9[sj])
                a = p.op("vector", (lambda s, jj, b: lambda e: e.tensor_add(
                    fo[:, s * 512 + jj * 128:s * 512 + (jj + 1) * 128],
                    fx[:, s * 512 + jj * 128:s * 512 + (jj + 1) * 128],
                    ps9[2 + b][:, 0:128]))(sl, j, sj),
                    deps=[tr, t_x] + oc[sl])
                rc[sj] = [cpf]
                tb_rot9[sj] = [a]
                adds.append(a)
            xc[sl] = adds
            st = p.out_dma("sync", (lambda k, t_, s: lambda e: e.dma_start(
                out=x4[k * 128:(k + 1) * 128, t_ * 512:(t_ + 1) * 512],
                in_=fo[:, s * 512:(s + 1) * 512]))(kc, tt, sl), deps=adds,
                stream="fs%d" % sl)
            oc[sl] = [st]


def build_fused():
    p = Prog("mkfused", fused=True)
    emit_fused(p)
    return p


def prep_fused_inputs(inputs):
    m1 = prep_l0_attn_inputs(inputs)
    dummy_x = np.zeros((4, NSEQ, D), dtype=np.float32)
    m2 = prep_l0_ffn_inputs(inputs, dummy_x)
    m3 = prep_l1_attn_inputs(inputs, dummy_x)
    m4a = prep_l1_proj_router_inputs(inputs, dummy_x, [np.zeros((512, W3), dtype=BF)] * 8)
    m4b = prep_moe_dense_inputs(inputs, np.zeros((D, TG), dtype=BF),
                                np.zeros((TG, 4), dtype=np.float32))
    in_maps = []
    for c in range(NCORE):
        b, half = c // 2, c % 2
        m = {}
        for stage, mm in (("s0", m1[c]), ("s2", m2[c]), ("s4", m3[c]),
                          ("s6", m4a[c]), ("s8", m4b[c])):
            for k, v in mm.items():
                if k in ("x_fm",):
                    m[stage + "_" + k] = v
                elif k in ("x1h", "xfull", "x2own", "o_mine", "xg", "meta_all"):
                    continue
                else:
                    m[stage + "_" + k] = v
        m["halo_mask"] = np.full((128, 1), 1.0 if half == 1 else 0.0, dtype=np.float32)
        m["win_m0"] = np.full((128, 1), 1.0 if half == 0 else 0.0, dtype=np.float32)
        m["win_m1"] = np.full((128, 1), 1.0 if half == 1 else 0.0, dtype=np.float32)
        if MOE_MODE == "ep":
            m["c_idbf"] = np.eye(128, dtype=np.float32)
            m["s8_c_pcol"] = np.arange(128, dtype=np.float32).reshape(128, 1)
        in_maps.append(m)
    return in_maps


KERNEL_MODE = "fused"
MOE_MODE = "ep"


def kernel_fused(inputs):
    nc = _get("fused", build_fused)
    in_maps = prep_fused_inputs(inputs)
    res = run_bass_kernel_spmd(nc, in_maps, core_ids=list(range(NCORE)))
    out = np.empty((4, NSEQ, D), dtype=np.float32)
    for c in range(NCORE):
        b, half = c // 2, c % 2
        out[b, half * T:(half + 1) * T, :] = res.results[c]["x4_fm"].T
    return out
